# version 09
# baseline (speedup 1.0000x reference)
import numpy as np
import ml_dtypes
import concourse.bass as bass
import concourse.mybir as mybir
from concourse.bass_utils import run_bass_kernel_spmd
from concourse.alu_op_type import AluOpType as OP

F32 = mybir.dt.float32
BF16 = mybir.dt.bfloat16
I32 = mybir.dt.int32
U8 = mybir.dt.uint8
AF = mybir.ActivationFunctionType
AX = mybir.AxisListType.X

L, D, H = 4096, 1024, 4
TMTOK = 512
NMEG = L // TMTOK
NT = TMTOK // 128
W = TMTOK
WP = W + 64
IFF = 2816
NG_S, NG_L, NG_W = 1, 8, 16
NGF = NG_S + NG_L + NG_W
RW = W + 64

DEBUG = False
SUB56 = 1
N_MEG_BUILD = NMEG
N_FFN_TILES = L // 128
PH = 99

ENGINES = ("tensor", "vector", "scalar", "gpsimd", "sync")
ESEM = {"tensor": "dt", "vector": "dv", "scalar": "dc", "gpsimd": "dg", "sync": "dy"}
NPOOL = 8
NEPOOL = 12
DMAPOOL = {q: [f"{q}{i}" for i in range(NPOOL)] for q in ("ms", "mg", "mh")}
ENGPOOL = {b: [f"{b}{i}" for i in range(NEPOOL)] for b in ESEM.values()}
ALLSEMS = [s for q in DMAPOOL.values() for s in q] + [s for p in ENGPOOL.values() for s in p]
QENG = {"ms": "sync", "mg": "gpsimd", "mh": "scalar"}


class Sch:
    def __init__(self):
        self.ops = {e: [] for e in ENGINES}
        self.cnt = {}

    def qtot(self, q):
        return [(sem, self.cnt[sem]) for sem in DMAPOOL[q] if self.cnt.get(sem, 0) > 0]

    def emit(self, eng, fn, waits=(), sig=None, amt=1):
        w = []
        for t in waits:
            if t is None:
                continue
            if isinstance(t, list):
                w.extend(t)
            else:
                w.append(t)
        tok = None
        if sig is not None:
            if sig in ENGPOOL:
                if not hasattr(self, "erot"):
                    self.erot = {}
                self.erot[sig] = (self.erot.get(sig, -1) + 1) % NEPOOL
                sig = ENGPOOL[sig][self.erot[sig]] if False else f"{sig}{self.erot[sig]}"
            self.cnt[sig] = self.cnt.get(sig, 0) + amt
            tok = (sig, self.cnt[sig])
        self.ops[eng].append((fn, w, sig, amt))
        return tok

    def dma(self, q, dst, src, waits=(), **kw):
        return self.dma_burst(q, [(dst, src, kw)], waits=waits)

    def dma_burst(self, q, items, waits=()):
        if not hasattr(self, "rot"):
            self.rot = {}
        self.rot[q] = (self.rot.get(q, -1) + 1) % NPOOL
        sem = DMAPOOL[q][self.rot[q]]
        eng = QENG[q]
        prev = self.cnt.get(sem, 0)
        w = [t for t in waits if t is not None]
        if prev > 0:
            w = w + [(sem, prev)]
        tok = None
        for i, (dst, src, kw) in enumerate(items):
            tok = self.emit(eng, lambda e, d=dst, s=src, kw=kw: e.dma_start(out=d, in_=s, **kw),
                            waits=(w if i == 0 else ()), sig=sem, amt=16)
        return tok

    def barrier(self):
        toks = {}
        for e in ENGINES:
            toks[e] = self.emit(e, lambda eng: eng.nop(), sig=ESEM[e])
        dt_ = [(sm, self.cnt[sm]) for sm in ALLSEMS if self.cnt.get(sm, 0) > 0]
        for e in ENGINES:
            self.emit(e, None, waits=[toks[o] for o in ENGINES if o != e] + dt_)

    def run(self, eng_name, eng, sems):
        seen = {}
        for fn, wl, sig, amt in self.ops[eng_name]:
            for s, v in wl:
                if seen.get(s, -1) < v:
                    eng.wait_ge(sems[s], v)
                    seen[s] = v
            ins = fn(eng) if fn is not None else None
            if sig is not None:
                ins.then_inc(sems[sig], amt)


class Arena:
    def __init__(self, tile, total, name):
        self.tile, self.total, self.name = tile, total, name
        self.blocks = {}
        self.hw = 0

    def alloc(self, key, size):
        assert key not in self.blocks, key
        taken = sorted(self.blocks.values())
        off = 0
        for (o, s) in taken:
            if off + size <= o:
                break
            off = o + s
        assert off + size <= self.total, f"{self.name} OOM: {key}({size})"
        self.blocks[key] = (off, size)
        self.hw = max(self.hw, off + size)
        return self.tile[:, off:off + size]

    def free(self, *keys):
        for k in keys:
            del self.blocks[k]


def prep_host(inp):
    f32 = np.float32
    bf = ml_dtypes.bfloat16
    out = {}
    out["qwT"] = np.ascontiguousarray(np.asarray(inp["L0_attn_q_w"], f32).T)
    out["kwT"] = np.ascontiguousarray(np.asarray(inp["L0_attn_k_w"], f32).T)
    out["vwT"] = np.ascontiguousarray(np.asarray(inp["L0_attn_v_w"], f32).T)
    bsp = np.zeros((D, 128), f32)
    for h in range(H):
        bsp[:, 32 * h] = np.asarray(inp["L0_attn_b_w"], f32)[h, :]
    out["bwTsp"] = bsp
    cds = np.zeros((3, 8, 4, 128, 32), f32)
    for ti, nm in enumerate(("convq_w", "convk_w", "convv_w")):
        cw = np.asarray(inp["L0_attn_" + nm], f32)
        for c in range(8):
            for g in range(4):
                d_ = cw[c * 128:(c + 1) * 128, g]
                for i4 in range(4):
                    cds[ti, c, g, 32 * i4:32 * i4 + 32, :] = np.diag(d_[32 * i4:32 * i4 + 32])
    out["cd"] = cds
    firs = [np.asarray(inp["L0_attn_fir_s"], f32).reshape(D, -1),
            np.asarray(inp["L0_attn_fir_l"], f32).reshape(D, -1),
            np.asarray(inp["L0_attn_fir_wd"], f32).reshape(D, -1)]
    groups = []
    for fi, filt in enumerate(firs):
        K = filt.shape[1]
        for Gf in range((K + 3) // 4):
            groups.append((fi, Gf, 64 - 4 * Gf))
    assert len(groups) == NGF
    fd = np.zeros((8, 4, NGF, 128, 32), f32)
    for gi, (fi, Gf, _t) in enumerate(groups):
        filt = firs[fi]
        K = filt.shape[1]
        for c in range(8):
            for s4 in range(4):
                for g in range(4):
                    k = K - 1 - 4 * Gf + g - 3
                    if 0 <= k < K:
                        d_ = filt[c * 128 + s4 * 32: c * 128 + s4 * 32 + 32, k]
                        fd[c, s4, gi, 32 * g:32 * g + 32, :] = np.diag(d_)
    out["fird"] = fd
    g1T = np.zeros((1152, 2048), f32)
    g1T[:1084, :] = np.asarray(inp["L0_attn_g1_w"], f32).T
    out["g1Tp"] = g1T
    out["g1b"] = np.ascontiguousarray(np.asarray(inp["L0_attn_g1_b"], f32).reshape(16, 128).T)
    out["g2T"] = np.ascontiguousarray(np.asarray(inp["L0_attn_g2_w"], f32).T)
    out["g2b"] = np.asarray(inp["L0_attn_g2_b"], f32).reshape(1, 20)
    out["vbias"] = np.asarray(inp["L0_attn_value_bias"], f32).reshape(1, 4)
    out["ltemp"] = np.asarray(inp["L0_attn_log_temp"], f32).reshape(1, 4)
    out["flog"] = np.asarray(inp["L0_attn_floor_logit"], f32).reshape(1, 4)
    out["onorm_rep"] = np.ascontiguousarray(np.tile(np.asarray(inp["L0_attn_onorm_w"], f32).reshape(1, 256), (128, 4)))
    out["owT"] = np.ascontiguousarray(np.asarray(inp["L0_attn_o_w"], f32).T)
    out["wg"] = np.ascontiguousarray(np.asarray(inp["L0_ffn_w_gate"], f32))
    out["wu"] = np.ascontiguousarray(np.asarray(inp["L0_ffn_w_up"], f32))
    out["wdn"] = np.ascontiguousarray(np.asarray(inp["L0_ffn_w_down"], f32))
    out["eye128"] = np.eye(128, dtype=bf)
    out["ident32"] = np.eye(128, dtype=f32)
    out["eye4"] = np.tile(np.eye(128, dtype=f32), (1, 4)).astype(bf)
    out["allones"] = np.ones((128, 128), dtype=bf)
    out["zeros128"] = np.zeros((128, 128), dtype=bf)
    oc = np.zeros((128, 256), f32)
    oc[:, 127] = 1
    out["onescol"] = oc.astype(bf)
    r = np.arange(128)
    out["triuI"] = (r[:, None] <= r[None, :]).astype(bf)
    bdm = (r[:, None] // 32 == r[None, :] // 32) & (r[:, None] > r[None, :])
    out["negbd"] = (-bdm.astype(f32)).astype(bf)
    out["blkLow"] = ((r[:, None] // 32) > (r[None, :] // 32)).astype(bf)
    out["blkUp"] = ((r[:, None] // 32) < (r[None, :] // 32)).astype(bf)
    hc4 = np.zeros((8, 4, 128, 16), f32)
    for c in range(8):
        for slot in range(4):
            hc4[c, slot, :, 4 * slot + c // 2] = 1.0
    out["headcol4"] = hc4.astype(bf)
    imask = np.zeros((1, 20), f32)
    imask[0, 4::5] = 1.0
    out["i4mask"] = imask
    return out, groups


def build(groups):
    global PH, SUB56
    nc = bass.Bass("TRN2", target_bir_lowering=False)
    S = Sch()
    dr = {}

    def din(name, shape, dt=F32):
        dr[name] = nc.dram_tensor(name, list(shape), dt, kind="ExternalInput").ap()

    din("x", (L, D))
    for nm in ("qwT", "kwT", "vwT"):
        din(nm, (D, D))
    din("bwTsp", (D, 128))
    din("cd", (3, 8, 4, 128, 32))
    din("fird", (8, 4, NGF, 128, 32))
    din("g1Tp", (1152, 2048))
    din("g1b", (128, 16))
    din("g2T", (2048, 20))
    din("g2b", (1, 20)); din("vbias", (1, 4)); din("ltemp", (1, 4)); din("flog", (1, 4))
    din("onorm_rep", (128, D))
    din("owT", (D, D))
    din("wg", (IFF, D)); din("wu", (IFF, D)); din("wdn", (D, IFF))
    for nm in ("eye128", "allones", "triuI", "negbd", "blkLow", "blkUp", "zeros128"):
        din(nm, (128, 128), BF16)
    din("ident32", (128, 128), F32)
    din("eye4", (128, 512), BF16)
    din("onescol", (128, 256), BF16)
    din("headcol4", (8, 4, 128, 16), BF16)
    din("i4mask", (1, 20))

    y = nc.dram_tensor("y", [L, D], F32, kind="ExternalOutput").ap()
    vdram = nc.dram_tensor("vdram", [D, 128 + L], BF16, kind="Internal").ap()
    xqdram = nc.dram_tensor("xqdram", [L, D], BF16, kind="Internal").ap()
    sdram = nc.dram_tensor("sdram", [L, D], F32, kind="Internal").ap()
    dbg = {}
    if DEBUG:
        for nm, shape in [("dbg_xn", (L, D)), ("dbg_q", (D, L)), ("dbg_k", (D, L)),
                          ("dbg_v", (D, L)), ("dbg_beta", (128, L)),
                          ("dbg_qh", (D, L)), ("dbg_kb", (D, L)), ("dbg_negw", (D, L)),
                          ("dbg_u", (L, D)), ("dbg_delta", (L, D)),
                          ("dbg_short", (L, D)), ("dbg_long", (L, D)), ("dbg_wide", (L, D)),
                          ("dbg_stat", (L, 64)), ("dbg_logits", (L, 20)),
                          ("dbg_otm", (L, D)), ("dbg_xq", (L, D)),
                          ("dbg_hq", (L, IFF)), ("dbg_h", (L, IFF)), ("dbg_preq", (D, L)), ("dbg_sums", (128, L))]:
            dbg[nm] = nc.dram_tensor(nm, list(shape), F32, kind="ExternalOutput").ap()

    ctx = []

    def sbuf(name, shape, dt):
        t = nc.sbuf_tensor(name, list(shape), dt)
        h_ = t.__enter__()
        ctx.append(t)
        return h_

    def psum(name, shape, dt=F32):
        t = nc.psum_tensor(name, list(shape), dt)
        h_ = t.__enter__()
        ctx.append(t)
        return h_

    AB = Arena(sbuf("AB", [128, 79360], BF16), 79360, "AB")
    AFp = Arena(sbuf("AFv", [128, 10240], F32), 10240, "AFp")
    AI = Arena(sbuf("AIv", [128, 2816], I32), 2816, "AI")
    AU = Arena(sbuf("AUv", [128, 1408], U8), 1408, "AU")

    PS_S = [psum(f"psS{h}", [128, 512]) for h in range(H)]
    PW = [psum(f"psW{i}", [128, 512]) for i in range(4)]

    C = {}
    for nm, w_ in [("eye128", 128), ("allones", 128), ("triuI", 128), ("negbd", 128),
                   ("blkLow", 128), ("blkUp", 128), ("zeros128", 128), ("eye4", 512), ("onescol", 256)]:
        C[nm] = AB.alloc(nm, w_)
    C["ident32"] = AFp.alloc("ident32", 128)
    C["onorm_rep"] = AFp.alloc("onorm_rep", D)
    C["g1b"] = AFp.alloc("g1b", 16)
    C["cd"] = AB.alloc("cd", 3 * 8 * 4 * 32)
    C["g2T"] = AB.alloc("g2T", 16 * 20)
    C["hc4"] = AB.alloc("hc4", 8 * 4 * 16)
    sm = AFp.alloc("smallrows", 64)

    tld = []
    for nm in ("eye128", "allones", "triuI", "negbd", "blkLow", "blkUp", "zeros128", "eye4", "onescol"):
        tld.append(S.dma("ms", C[nm][:], dr[nm]))
    tld.append(S.dma("ms", C["ident32"][:], dr["ident32"]))
    tld.append(S.dma("ms", C["onorm_rep"][:], dr["onorm_rep"]))
    tld.append(S.dma("ms", C["g1b"][:], dr["g1b"]))
    tld.append(S.dma("mg", C["cd"][:].rearrange("p (t c g w) -> p t c g w", t=3, c=8, g=4),
                     dr["cd"].rearrange("t c g p w -> p t c g w")))
    tld.append(S.dma("mg", C["g2T"][:].rearrange("p (c w) -> p c w", c=16),
                     dr["g2T"].rearrange("(c p) w -> p c w", p=128)))
    tld.append(S.dma("mg", C["hc4"][:].rearrange("p (c s w) -> p c s w", c=8, s=4),
                     dr["headcol4"].rearrange("c s p w -> p c s w")))
    tld.append(S.dma("ms", sm[0:1, 0:20], dr["g2b"]))
    tld.append(S.dma("ms", sm[0:1, 20:24], dr["vbias"]))
    tld.append(S.dma("ms", sm[0:1, 24:28], dr["ltemp"]))
    tld.append(S.dma("ms", sm[0:1, 28:32], dr["flog"]))
    tld.append(S.dma("ms", sm[0:1, 32:52], dr["i4mask"]))

    zt = AB.alloc("zt", 128)
    tz = S.emit("vector", lambda e: e.memset(zt[:], 0.0), sig="dv")
    for c in range(8):
        S.dma("mg", vdram[c * 128:(c + 1) * 128, 0:128], zt[:], waits=[tz])

    grow = AFp.alloc("grow", 80)
    fscr = AFp.alloc("fscr", 20)
    vb_sp = sm[0:1, 20:24].broadcast_to([1, 4, 5])
    lt_sp = sm[0:1, 24:28].broadcast_to([1, 4, 5])
    fl_sp = sm[0:1, 28:32].broadcast_to([1, 4, 5])
    g3 = lambda ap: ap.rearrange("p (a b) -> p a b", b=5)
    w_all = list(tld)
    t1 = S.emit("vector", lambda e: e.tensor_tensor(g3(grow[0:1, 0:20]), vb_sp, g3(sm[0:1, 32:52]), OP.mult),
                waits=w_all, sig="dv")
    t1 = S.emit("vector", lambda e: e.tensor_tensor(grow[0:1, 0:20], grow[0:1, 0:20], sm[0:1, 0:20], OP.add),
                waits=[t1], sig="dv")
    t2 = S.emit("scalar", lambda e: e.activation(g3(grow[0:1, 20:40]), lt_sp, AF.Exp, scale=-1.0), waits=w_all, sig="dc")
    tf1 = S.emit("scalar", lambda e: e.activation(g3(fscr[0:1, :]), fl_sp, AF.Sigmoid), waits=w_all, sig="dc")
    t4 = S.emit("vector", lambda e: e.tensor_scalar(grow[0:1, 40:60], fscr[0:1, :], -4.0, 1.0, OP.mult, OP.add),
                waits=[tf1], sig="dv")
    tf2 = S.emit("vector", lambda e: e.tensor_tensor(grow[0:1, 60:80], fscr[0:1, :], sm[0:1, 32:52], OP.mult),
                 waits=[tf1], sig="dv")
    tf3 = S.emit("vector", lambda e: e.scalar_tensor_tensor(grow[0:1, 60:80], grow[0:1, 60:80], -1.0, fscr[0:1, :],
                                                            OP.mult, OP.add), waits=[tf2], sig="dv")
    growb = AB.alloc("growb", 80)
    tcast = S.emit("vector", lambda e: e.tensor_copy(growb[0:1, :], grow[0:1, :]), waits=[t1, t2, t4, tf3], sig="dv")
    gconst = AFp.alloc("gconst", 80)
    tgc = None
    for i in range(4):
        tmm = S.emit("tensor", lambda e, i=i: e.matmul(PW[3][:, 0:20], C["allones"][0:1, :],
                                                       growb[0:1, i * 20:(i + 1) * 20], start=True, stop=True),
                     waits=[tcast, tgc], sig="dt")
        tgc = S.emit("vector", lambda e, i=i: e.tensor_copy(gconst[:, i * 20:(i + 1) * 20], PW[3][:, 0:20]),
                     waits=[tmm], sig="dv")
    AFp.free("grow", "fscr")
    AB.free("growb", "zt")
    S.barrier()

    xn_fm = AB.alloc("xn_fm", 8 * W)
    qh_fm = AB.alloc("qh_fm", 8 * W)
    kh_fm = AB.alloc("kh_fm", 8 * W)
    vf_fm = AB.alloc("vf_fm", 8 * W)
    S_bf = AB.alloc("S_bf", H * 512)
    tails = AB.alloc("tails", 3 * 8 * 64)
    statF = AB.alloc("statF", NT * 128)
    S32 = AFp.alloc("S32", H * 512)

    tsz = S.emit("vector", lambda e: e.memset(statF[:], 0.0), sig="dv")

    TP = dict(tok_prev=None)
    tp = {}

    def fmsl(tile, c, tt):
        return tile[:, c * W + tt * 128: c * W + tt * 128 + 128]

    def bcast128(constv):
        return constv.rearrange("p (x w) -> p x w", x=1).broadcast_to([128, 4, 128]).rearrange("p a w -> p (a w)")

    def emit_quant(src, dst, Fn, wtok, key, scratch=None, dst_war=None, fast=False):
        nblk = Fn // 16
        nb2 = (nblk + 15) // 16 * 16
        qbm = AFp.alloc(key + "bm", 2 * nb2)
        qv = AFp.alloc(key + "v", Fn)
        qA = AFp.alloc(key + "A", Fn)
        qBC = AFp.alloc(key + "BC", Fn)
        qt2 = scratch[:, 0:Fn] if scratch is not None else AFp.alloc(key + "t2", Fn)
        qi = AI.alloc(key + "i", Fn)
        qi2 = AI.alloc(key + "i2", Fn)
        qm = AU.alloc(key + "m", Fn)
        t0 = S.emit("vector", lambda e: e.tensor_reduce(qbm[:, 0:nblk], src.rearrange("p (b k) -> p b k", k=16),
                                                        axis=AX, op=OP.max, apply_absolute_value=True),
                    waits=[wtok], sig="dv")
        t1 = S.emit("vector", lambda e: e.tensor_scalar(qbm[:, nb2:nb2 + nblk], qbm[:, 0:nblk], 1.0 / 6.0, 1e-12,
                                                        OP.mult, OP.max), waits=[t0], sig="dv")
        t2 = S.emit("vector", lambda e: e.reciprocal(qbm[:, 0:nblk], qbm[:, nb2:nb2 + nblk]), waits=[t1], sig="dv")
        rv = qbm[:, 0:nblk].broadcast_to([128, nblk, 16])
        sv = qbm[:, nb2:nb2 + nblk].broadcast_to([128, nblk, 16])
        sview = src.rearrange("p (b k) -> p b k", k=16)
        t3 = S.emit("vector", lambda e: e.tensor_tensor(qv[:, :].rearrange("p (b k) -> p b k", k=16), sview, rv, OP.mult),
                    waits=[t2], sig="dv")
        t4 = S.emit("vector", lambda e: e.tensor_scalar(qi[:, :], qv[:, :], 2.0, None, OP.mult), waits=[t3], sig="dv")
        t5 = S.emit("vector", lambda e: e.tensor_scalar(qA[:, :], qi[:, :], 0.5, None, OP.mult), waits=[t4], sig="dv")
        t7 = S.emit("vector", lambda e: e.tensor_scalar(qi2[:, :], qv[:, :], 4.0, -4.0, OP.min, OP.max),
                    waits=[t3], sig="dv")
        qi3 = AI.alloc(key + "i3", Fn) if fast else qi
        t8 = S.emit("vector", lambda e, qi3=qi3: e.tensor_scalar(qi3[:, :], qv[:, :], 0.1, None, OP.mult),
                    waits=([t3] if fast else [t3, t5]), sig="dv")
        t10 = S.emit("vector", lambda e, qi3=qi3: e.scalar_tensor_tensor(qBC[:, :], qi3[:, :], 2.0, qi2[:, :],
                                                                         OP.mult, OP.add),
                     waits=[t7, t8], sig="dv")
        if fast:
            t11a = S.emit("scalar", lambda e: e.activation(qt2[:, :], qv[:, :], AF.Square),
                          waits=[t3], sig="dc")
        else:
            t11a = S.emit("vector", lambda e: e.tensor_tensor(qt2[:, :], qv[:, :], qv[:, :], OP.mult),
                          waits=[t3], sig="dv")
        t11 = S.emit("vector", lambda e: e.tensor_scalar(qm[:, :], qt2[:, :], 4.0, None, OP.is_gt),
                     waits=[t11a], sig="dv")
        t12 = S.emit("vector", lambda e: e.copy_predicated(qA[:, :], qm[:, :], qBC[:, :]),
                     waits=[t10, t11, t5], sig="dv")
        qAv = qA[:, :].rearrange("p (b k) -> p b k", k=16)
        t13 = S.emit("vector", lambda e: e.tensor_tensor(dst.rearrange("p (b k) -> p b k", k=16), qAv, sv, OP.mult),
                     waits=[t12, dst_war], sig="dv")
        AFp.free(key + "bm", key + "v", key + "A", key + "BC")
        if scratch is None:
            AFp.free(key + "t2")
        AI.free(key + "i", key + "i2")
        if fast:
            AI.free(key + "i3")
        AU.free(key + "m")
        return t13, t0

    for mt in range(N_MEG_BUILD):
        mb = mt * TMTOK
        first = (mt == 0)

        x_tm = AFp.alloc("x_tm", 1024)
        xn32 = AFp.alloc("xn32", 1024)
        lnS = AFp.alloc("lnS", 16)
        tmpS = AFp.alloc("tmpS", 1024)
        xn_tt = AB.alloc("xn_tt", 1024)
        xq_tm = AB.alloc("xq_tm", 1024)

        for tt in range(NT):
            r0 = mb + tt * 128
            txl = S.dma("mg", x_tm[:, :], dr["x"][r0:r0 + 128, :],
                        waits=[tp.get("xtmfree"), tp.get("mttail"), tp.get("sdma")])
            t1 = S.emit("scalar", lambda e: e.activation(tmpS[:, :], x_tm[:, :], AF.Square, accum_out=lnS[:, 0:1]),
                        waits=[txl], sig="dc")
            t2 = S.emit("vector", lambda e: e.tensor_reduce(lnS[:, 1:2], x_tm[:, :], axis=AX, op=OP.add),
                        waits=[txl], sig="dv")
            t3 = S.emit("vector", lambda e: e.tensor_scalar(lnS[:, 2:3], lnS[:, 1:2], 1.0 / D, None, OP.mult),
                        waits=[t2], sig="dv")
            t4 = S.emit("vector", lambda e: e.tensor_tensor(lnS[:, 3:4], lnS[:, 2:3], lnS[:, 2:3], OP.mult),
                        waits=[t3], sig="dv")
            t5 = S.emit("vector", lambda e: e.scalar_tensor_tensor(lnS[:, 4:5], lnS[:, 0:1], 1.0 / D, lnS[:, 3:4],
                                                                   OP.mult, OP.subtract), waits=[t1, t4], sig="dv")
            t6 = S.emit("vector", lambda e: e.tensor_scalar(lnS[:, 5:6], lnS[:, 4:5], 1e-5, None, OP.add),
                        waits=[t5], sig="dv")
            t7 = S.emit("vector", lambda e: e.reciprocal(lnS[:, 6:7], lnS[:, 5:6]), waits=[t6], sig="dv")
            t8 = S.emit("scalar", lambda e: e.activation(lnS[:, 7:8], lnS[:, 6:7], AF.Sqrt), waits=[t7], sig="dc")
            t9 = S.emit("vector", lambda e: e.tensor_scalar(lnS[:, 8:9], lnS[:, 2:3], -1.0, None, OP.mult),
                        waits=[t3], sig="dv")
            t10 = S.emit("vector", lambda e: e.tensor_scalar(xn32[:, :], x_tm[:, :], lnS[:, 8:9], lnS[:, 7:8],
                                                             OP.add, OP.mult),
                         waits=[t8, t9, tp.get("qprev"), tp.get("qprev2")], sig="dv")
            tp["xtmfree"] = t10
            t11 = S.emit("vector", lambda e: e.tensor_copy(xn_tt[:, :], xn32[:, :]),
                         waits=[t10, tp.get("xnxpo")], sig="dv")
            tq, tq0 = emit_quant(xn32[:, :], xq_tm[:, :], 1024, t10, "q1", scratch=tmpS,
                                 dst_war=tp.get("xqout"))
            tp["qprev"] = tq0
            if DEBUG:
                tp["qprev2"] = S.dma("mg", dbg["dbg_xn"][r0:r0 + 128, :], xn32[:, :], waits=[t10, tq])
            tp["xqout"] = S.dma("mg", xqdram[r0:r0 + 128, :], xq_tm[:, :], waits=[tq])
            if DEBUG:
                tp["xqout"] = S.dma("mg", dbg["dbg_xq"][r0:r0 + 128, :], xq_tm[:, :],
                                    waits=[tq, tp.get("xqout")])
            for c in range(8):
                S.dma("ms", xn_fm[:, c * W + tt * 128: c * W + tt * 128 + 128],
                      xn_tt[:, c * 128:(c + 1) * 128], waits=[t11, tp.get("xnfree")], transpose=True)
            tp["xnxpo"] = S.qtot("ms")
        tok_xpo = S.qtot("ms")
        AFp.free("x_tm", "xn32", "lnS", "tmpS")
        AB.free("xn_tt")
        tp.pop("xtmfree", None)
        tp.pop("xnxpo", None)

        if PH < 2: continue
        beta_sp = AFp.alloc("beta_sp", W)
        s2q_sp = AFp.alloc("s2q_sp", W)
        s2k_sp = AFp.alloc("s2k_sp", W)
        rkb_sp = AFp.alloc("rkb_sp", W)
        pre_q = AB.alloc("pre_q", 8 * WP)
        pre_k = AB.alloc("pre_k", 8 * WP)
        pre_v = AB.alloc("pre_v", 8 * WP)
        wbuf = AB.alloc("wbuf", 4 * 1024)
        tok_tail = []
        for ti, pre in enumerate((pre_q, pre_k, pre_v)):
            for c in range(8):
                tl = tails[:, (ti * 8 + c) * 64:(ti * 8 + c) * 64 + 64]
                if first:
                    tok_tail.append(S.emit("vector", lambda e, pre=pre, c=c: e.memset(pre[:, c * WP:c * WP + 64], 0.0),
                                           waits=[tp.get("mttail")], sig="dv"))
                else:
                    tok_tail.append(S.emit("vector", lambda e, pre=pre, c=c, tl=tl: e.tensor_copy(
                        pre[:, c * WP:c * WP + 64], tl), waits=[tp.get("mttail")], sig="dv"))
        tok_proj = {}
        order = [(n, dc) for n in ("q", "k", "v") for dc in range(8)]
        wsel = {"q": ("qwT", pre_q), "k": ("kwT", pre_k), "v": ("vwT", pre_v)}
        for oi, (name, dc) in enumerate(order):
            wnm, pre = wsel[name]
            buf = wbuf[:, (oi % 4) * 1024:(oi % 4) * 1024 + 1024]
            war = tok_proj.get(order[oi - 4]) if oi >= 4 else None
            twl = S.dma("mg", buf.rearrange("p (c w) -> p c w", c=8),
                        dr[wnm].rearrange("(c p) d -> p c d", p=128)[:, :, dc * 128:(dc + 1) * 128],
                        waits=[war, tp.get("mttail"), tp.get("sdma")])
            mmt = None
            for kc in range(8):
                mmt = S.emit("tensor", lambda e, kc=kc, buf=buf, oi=oi: e.matmul(
                    PW[oi % 2][:, 0:512], buf[:, kc * 128:(kc + 1) * 128], xn_fm[:, kc * W:(kc + 1) * W],
                    start=(kc == 0), stop=(kc == 7)),
                    waits=[twl, tok_xpo, tp.get(("projps", oi % 2))], sig="dt")
            ev = S.emit("scalar", lambda e, pre=pre, dc=dc, oi=oi: e.activation(
                pre[:, dc * WP + 64: dc * WP + 64 + W], PW[oi % 2][:, 0:512], AF.Identity),
                waits=[mmt] + tok_tail, sig="dc")
            tok_proj[(name, dc)] = ev
            tp[("projps", oi % 2)] = ev
        twb = S.dma("mg", wbuf[:, 0:1024].rearrange("p (c w) -> p c w", c=8),
                    dr["bwTsp"].rearrange("(c p) d -> p c d", p=128), waits=[tok_proj[("v", 6)]])
        bmm = None
        for kc in range(8):
            bmm = S.emit("tensor", lambda e, kc=kc: e.matmul(
                PW[2][:, 0:512], wbuf[:, kc * 128:(kc + 1) * 128], xn_fm[:, kc * W:(kc + 1) * W],
                start=(kc == 0), stop=(kc == 7)), waits=[twb, tp.get("ps2")], sig="dt")
        tok_beta = S.emit("scalar", lambda e: e.activation(beta_sp[:, :], PW[2][:, 0:512], AF.Sigmoid),
                          waits=[bmm], sig="dc")
        tp["ps2"] = tok_beta
        if DEBUG:
            for c in range(8):
                S.dma("mg", dbg["dbg_preq"][c * 128:(c + 1) * 128, mb:mb + W],
                      pre_q[:, c * WP + 64:c * WP + 64 + W], waits=[tok_proj[("q", c)]])
        tok_savetail = []
        for ti, (name, pre) in enumerate((("q", pre_q), ("k", pre_k), ("v", pre_v))):
            for c in range(8):
                tl = tails[:, (ti * 8 + c) * 64:(ti * 8 + c) * 64 + 64]
                tok_savetail.append(S.emit("vector", lambda e, pre=pre, c=c, tl=tl: e.tensor_copy(
                    tl, pre[:, c * WP + W: c * WP + W + 64]), waits=[tok_proj[(name, c)]], sig="dv"))

        if PH < 3: continue
        cdv = C["cd"][:].rearrange("p (t c g w) -> p t c g w", t=3, c=8, g=4)
        tok_conv = {}
        for wi, (name, pre, outt) in enumerate((("q", pre_q, qh_fm), ("k", pre_k, kh_fm), ("v", pre_v, vf_fm))):
            for c in range(8):
                mmt = S.emit("tensor", lambda e, c=c, pre=pre: e.matmul(
                    PW[3][:, 0:512], C["zeros128"][:, :], pre[:, c * WP + 64: c * WP + 64 + W],
                    start=True, stop=False), waits=[tok_proj[(name, c)], tp.get("ps3")], sig="dt")
                for s4 in range(4):
                    for g in range(4):
                        mmt = S.emit("tensor", lambda e, wi=wi, c=c, g=g, s4=s4, pre=pre: e.matmul(
                            PW[3][32 * s4:32 * s4 + 32, 0:512],
                            cdv[32 * s4:32 * s4 + 32, wi, c, g, :],
                            pre[32 * s4:32 * s4 + 32, c * WP + 61 + g: c * WP + 61 + g + W],
                            start=False, stop=False,
                            tile_position=(32 * s4, 32 * s4)),
                            waits=[tok_proj[(name, c)]], sig="dt")
                mmt = S.emit("tensor", lambda e, c=c, pre=pre: e.matmul(
                    PW[3][:, 0:512], C["zeros128"][:, :], pre[:, c * WP + 64: c * WP + 64 + W],
                    start=False, stop=True), sig="dt")
                tok_conv[(name, c)] = S.emit("scalar", lambda e, c=c, outt=outt: e.activation(
                    outt[:, c * W:(c + 1) * W], PW[3][:, 0:512], AF.Silu), waits=[mmt], sig="dc")
                tp["ps3"] = tok_conv[(name, c)]
        if DEBUG:
            for nm, key in (("dbg_q", "q"), ("dbg_k", "k"), ("dbg_v", "v")):
                tile = {"q": qh_fm, "k": kh_fm, "v": vf_fm}[key]
                for c in range(8):
                    S.dma("mg", dbg[nm][c * 128:(c + 1) * 128, mb:mb + W], tile[:, c * W:(c + 1) * W],
                          waits=[tok_conv[(key, c)]])
            S.dma("mg", dbg["dbg_beta"][:, mb:mb + W], beta_sp[:, :], waits=[tok_beta])
        if DEBUG:
            S.barrier()
        AB.free("pre_q", "pre_k", "pre_v", "wbuf")

        if PH < 4: continue
        k_tm = AB.alloc("k_tm", NT * 1024)
        v_tm = AB.alloc("v_tm", NT * 1024)
        vbT = AB.alloc("vbT", NT * 1024)
        kbT = AB.alloc("kbT", NT * 1024)
        khT = AB.alloc("khT", NT * 1024)
        kb_fm = AB.alloc("kb_fm", 8 * W)
        tok_vtm = {}
        tok_ktm = {}
        for c in range(8):
            S.dma("mg", vdram[c * 128:(c + 1) * 128, 128 + mb:128 + mb + W], vf_fm[:, c * W:(c + 1) * W],
                  waits=[tok_conv[("v", c)]])
            for tt in range(NT):
                tok_vtm[(tt, c)] = S.dma("ms", v_tm[:, tt * 1024 + c * 128: tt * 1024 + (c + 1) * 128],
                                         vf_fm[:, c * W + tt * 128: c * W + tt * 128 + 128],
                                         waits=[tok_conv[("v", c)], tp.get("mttail")], transpose=True)
                tok_ktm[(tt, c)] = S.dma("mh", k_tm[:, tt * 1024 + c * 128: tt * 1024 + (c + 1) * 128],
                                         kh_fm[:, c * W + tt * 128: c * W + tt * 128 + 128],
                                         waits=[tok_conv[("k", c)], tp.get("mttail")], transpose=True)
        tok_vdr_w = S.qtot("mg")
        tok_ktm_all = S.qtot("mh")
        tok_vtm_all = S.qtot("ms")

        if PH < 5: continue
        sq_t = AB.alloc("sq_t", 2 * W)
        tok_s2 = {}
        for ti, (name, src, dstsp) in enumerate((("q", qh_fm, s2q_sp), ("k", kh_fm, s2k_sp))):
            mmt = None
            sqt = {}
            for c in range(8):
                half = c % 2
                war = tok_s2.get((name, c - 2))
                if war is None and name == "k":
                    war = tok_s2.get(("q", 6 + c))
                sqt[c] = S.emit("scalar", lambda e, c=c, src=src, half=half: e.activation(
                    sq_t[:, half * W:(half + 1) * W], src[:, c * W:(c + 1) * W], AF.Square),
                    waits=[tok_conv[(name, c)], war], sig="dc")
                h = c // 2
                mmt = S.emit("tensor", lambda e, c=c, h=h, half=half: e.matmul(
                    PW[2][:, 0:512], C["onescol"][:, 127 - 32 * h:255 - 32 * h], sq_t[:, half * W:(half + 1) * W],
                    start=(c == 0), stop=(c == 7)),
                    waits=[sqt[c], tp.get("ps2")], sig="dt")
                tok_s2[(name, c)] = mmt
            tok_s2[name] = S.emit("vector", lambda e, dstsp=dstsp: e.tensor_copy(dstsp[:, :], PW[2][:, 0:512]),
                                  waits=[mmt], sig="dv")
            tp["ps2"] = tok_s2[name]
        AB.free("sq_t")
        tok_rsq = {}
        for name, sp in (("q", s2q_sp), ("k", s2k_sp)):
            a = S.emit("vector", lambda e, sp=sp: e.tensor_scalar(sp[:, :], sp[:, :], 1e-6, None, OP.add),
                       waits=[tok_s2[name]], sig="dv")
            b = S.emit("vector", lambda e, sp=sp: e.reciprocal(sp[:, :], sp[:, :]), waits=[a], sig="dv")
            tok_rsq[name] = S.emit("scalar", lambda e, sp=sp: e.activation(sp[:, :], sp[:, :], AF.Sqrt),
                                   waits=[b], sig="dc")
        tok_rkb = S.emit("vector", lambda e: e.tensor_tensor(rkb_sp[:, :], s2k_sp[:, :], beta_sp[:, :], OP.mult),
                         waits=[tok_rsq["k"], tok_beta], sig="dv")
        sprd3 = AB.alloc("sprd3", 3 * W)
        tok_sb = {}
        tok_sb["q"] = S.emit("vector", lambda e: e.tensor_copy(sprd3[:, 0:W], s2q_sp[:, :]),
                             waits=[tok_rsq["q"]], sig="dv")
        tok_sb["k"] = S.emit("vector", lambda e: e.tensor_copy(sprd3[:, W:2 * W], s2k_sp[:, :]),
                             waits=[tok_rsq["k"]], sig="dv")
        tok_sb["kb"] = S.emit("vector", lambda e: e.tensor_copy(sprd3[:, 2 * W:3 * W], rkb_sp[:, :]),
                              waits=[tok_rkb], sig="dv")
        tok_scaled = {}
        for c in range(8):
            h = c // 2
            tpos = (96, 0) if h == 3 else None
            bq = S.emit("tensor", lambda e, h=h, tpos=tpos: e.matmul(
                PW[2][:, 0:512], C["allones"][32 * h:32 * h + 1, :],
                sprd3[32 * h:32 * h + 1, 0:W], start=True, stop=True, tile_position=tpos),
                waits=[tok_sb["q"], tp.get("ps2")], sig="dt")
            tok_scaled[("q", c)] = S.emit("vector", lambda e, c=c: e.tensor_tensor(
                qh_fm[:, c * W:(c + 1) * W], qh_fm[:, c * W:(c + 1) * W], PW[2][:, 0:512], OP.mult),
                waits=[bq], sig="dv")
            tp["ps2"] = tok_scaled[("q", c)]
            bk = S.emit("tensor", lambda e, h=h, tpos=tpos: e.matmul(
                PW[3][:, 0:512], C["allones"][32 * h:32 * h + 1, :],
                sprd3[32 * h:32 * h + 1, 2 * W:3 * W], start=True, stop=True, tile_position=tpos),
                waits=[tok_sb["kb"], tp.get("ps3")], sig="dt")
            tok_scaled[("kb", c)] = S.emit("vector", lambda e, c=c: e.tensor_tensor(
                kb_fm[:, c * W:(c + 1) * W], kh_fm[:, c * W:(c + 1) * W], PW[3][:, 0:512], OP.mult),
                waits=[bk], sig="dv")
            bk2 = S.emit("tensor", lambda e, h=h, tpos=tpos: e.matmul(
                PW[3][:, 0:512], C["allones"][32 * h:32 * h + 1, :],
                sprd3[32 * h:32 * h + 1, W:2 * W], start=True, stop=True, tile_position=tpos),
                waits=[tok_sb["k"], tok_scaled[("kb", c)]], sig="dt")
            tok_scaled[("kh", c)] = S.emit("vector", lambda e, c=c: e.tensor_tensor(
                kh_fm[:, c * W:(c + 1) * W], kh_fm[:, c * W:(c + 1) * W], PW[3][:, 0:512], OP.mult),
                waits=[bk2, tok_ktm_all], sig="dv")
            tp["ps3"] = tok_scaled[("kh", c)]
        if DEBUG:
            for c in range(8):
                S.dma("mg", dbg["dbg_qh"][c * 128:(c + 1) * 128, mb:mb + W], qh_fm[:, c * W:(c + 1) * W],
                      waits=[tok_scaled[("q", c)]])
                S.dma("mg", dbg["dbg_kb"][c * 128:(c + 1) * 128, mb:mb + W], kb_fm[:, c * W:(c + 1) * W],
                      waits=[tok_scaled[("kb", c)]])

        if PH < 6: continue
        scT = AFp.alloc("scT", 3 * 128)
        tok_tmprod = {}
        for tt in range(NT):
            tsc = []
            for si, (sp, tok_src) in enumerate(((beta_sp, tok_beta), (s2k_sp, tok_rsq["k"]), (rkb_sp, tok_rkb))):
                tpx = S.emit("tensor", lambda e, sp=sp, tt=tt: e.transpose(
                    PW[2][:, 0:128], sp[:, tt * 128:(tt + 1) * 128], C["ident32"][:, :]),
                    waits=[tok_src, tp.get("ps2")], sig="dt")
                ev = S.emit("vector", lambda e, si=si: e.tensor_copy(scT[:, si * 128:(si + 1) * 128], PW[2][:, 0:128]),
                            waits=[tpx], sig="dv")
                tp["ps2"] = ev
                tsc.append(ev)

            def hview(si):
                return scT[:, si * 128: si * 128 + 128].rearrange("p (h r) -> p h r", h=4)[:, :, 0:1].broadcast_to(
                    [128, 4, 256])
            sl = slice(tt * 1024, (tt + 1) * 1024)
            wv_ = [tok_vtm[(tt, c)] for c in range(8)] + tsc
            wk_ = [tok_ktm[(tt, c)] for c in range(8)] + tsc
            if not SUB56:
                continue
            tok_tmprod[("vbT", tt)] = S.emit("vector", lambda e, sl=sl, tt=tt: e.tensor_tensor(
                vbT[:, sl].rearrange("p (h c) -> p h c", h=4), v_tm[:, sl].rearrange("p (h c) -> p h c", h=4),
                hview(0), OP.mult), waits=wv_, sig="dv")
            tok_tmprod[("kbT", tt)] = S.emit("vector", lambda e, sl=sl, tt=tt: e.tensor_tensor(
                kbT[:, sl].rearrange("p (h c) -> p h c", h=4), k_tm[:, sl].rearrange("p (h c) -> p h c", h=4),
                hview(2), OP.mult), waits=wk_, sig="dv")
            tok_tmprod[("khT", tt)] = S.emit("vector", lambda e, sl=sl, tt=tt: e.tensor_tensor(
                khT[:, sl].rearrange("p (h c) -> p h c", h=4), k_tm[:, sl].rearrange("p (h c) -> p h c", h=4),
                hview(1), OP.mult), waits=wk_, sig="dv")
            tp["ps2"] = tok_tmprod[("khT", tt)]
        AB.free("k_tm", "sprd3")
        AFp.free("scT", "beta_sp", "s2q_sp", "s2k_sp", "rkb_sp")
        S.barrier()

        if PH < 7: continue
        negw_fm = AB.alloc("negw_fm", 8 * W)
        delta_tm = AB.alloc("delta_tm", NT * 1024)
        B_b = AB.alloc("B_b", 512)
        BT_b = AB.alloc("BT_b", 512)
        pw_b = AB.alloc("pw_b", 512)
        pwT_b = AB.alloc("pwT_b", 512)
        P_b = AB.alloc("P_b", 2 * 512)
        invT_b = AB.alloc("invT_b", 512)
        Mt_b = AB.alloc("Mt_b", 512)
        Gnw_b = AB.alloc("Gnw_b", 512)
        GnwT_b = AB.alloc("GnwT_b", 512)
        T2t_b = AB.alloc("T2t_b", 512)
        Ht_b = AB.alloc("Ht_b", 512)
        u_b = AB.alloc("u_b", 1024)
        U0_b = AB.alloc("U0_b", 1024)
        uadj_b = AB.alloc("uadj_b", 1024)
        for tt in range(NT):
            fsc = first and tt == 0
            mmt = None
            for h in range(H):
                for half in range(2):
                    c = 2 * h + half
                    mmt = S.emit("tensor", lambda e, h=h, c=c, tt=tt, half=half: e.matmul(
                        PW[0][:, h * 128:(h + 1) * 128], fmsl(kb_fm, c, tt), fmsl(kh_fm, c, tt),
                        start=(h == 0 and half == 0), stop=(h == 3 and half == 1)),
                        waits=[tok_scaled[("kb", c)], tok_scaled[("kh", c)], tp.get("ps0")], sig="dt")
            tB = S.emit("vector", lambda e: e.tensor_tensor(
                B_b[:, :].rearrange("p (a w) -> p a w", a=4),
                PW[0][:, :].rearrange("p (a w) -> p a w", a=4),
                C["negbd"][:].rearrange("p (x w) -> p x w", x=1).broadcast_to([128, 4, 128]), OP.mult),
                waits=[mmt], sig="dv")
            tp["ps0"] = tB
            tBT = S.emit("vector", lambda e: e.transpose(BT_b[:, :], B_b[:, :]), waits=[tB], sig="dv")
            tPc = S.emit("vector", lambda e: e.tensor_tensor(P_b[:, 0:512], B_b[:, :], C["eye4"][:], OP.add),
                         waits=[tB], sig="dv")
            curP = 0
            tsq_src, tsq_srcT = tB, tBT
            src_pw, src_pwT = B_b, BT_b
            for lvl in range(4):
                mmq = None
                for h in range(H):
                    mmq = S.emit("tensor", lambda e, h=h, s1=src_pwT, s2=src_pw: e.matmul(
                        PW[1][:, h * 128:(h + 1) * 128], s1[:, h * 128:(h + 1) * 128], s2[:, h * 128:(h + 1) * 128],
                        start=(h == 0), stop=(h == 3)), waits=[tsq_src, tsq_srcT, tp.get("ps1")], sig="dt")
                tev = S.emit("vector", lambda e: e.tensor_copy(pw_b[:, :], PW[1][:, :]), waits=[mmq], sig="dv")
                tp["ps1"] = tev
                tevT = S.emit("vector", lambda e: e.transpose(pwT_b[:, :], pw_b[:, :]), waits=[tev], sig="dv")
                mmp = None
                for h in range(H):
                    mmp = S.emit("tensor", lambda e, h=h, curP=curP: e.matmul(
                        PW[2][:, h * 128:(h + 1) * 128], pwT_b[:, h * 128:(h + 1) * 128],
                        P_b[:, curP * 512 + h * 128: curP * 512 + (h + 1) * 128],
                        start=(h == 0), stop=False), waits=[tevT, tPc, tp.get("ps2")], sig="dt")
                for h in range(H):
                    mmp = S.emit("tensor", lambda e, h=h, curP=curP: e.matmul(
                        PW[2][:, h * 128:(h + 1) * 128], C["eye128"][:, :],
                        P_b[:, curP * 512 + h * 128: curP * 512 + (h + 1) * 128],
                        start=False, stop=(h == 3)), sig="dt")
                curP = 1 - curP
                tPc = S.emit("vector", lambda e, curP=curP: e.tensor_copy(P_b[:, curP * 512:(curP + 1) * 512],
                                                                          PW[2][:, :]), waits=[mmp], sig="dv")
                tp["ps2"] = tPc
                if lvl < 3:
                    tsq_src = S.emit("vector", lambda e: e.tensor_copy(B_b[:, :], pw_b[:, :]), waits=[tevT], sig="dv")
                    tsq_srcT = S.emit("vector", lambda e: e.tensor_copy(BT_b[:, :], pwT_b[:, :]), waits=[tevT], sig="dv")
                    src_pw, src_pwT = B_b, BT_b
            tinvT = S.emit("vector", lambda e, curP=curP: e.transpose(invT_b[:, :], P_b[:, curP * 512:(curP + 1) * 512]),
                           waits=[tPc], sig="dv")
            mmu = None
            for h in range(H):
                mmu = S.emit("tensor", lambda e, h=h, tt=tt: e.matmul(
                    PW[h // 2][:, (h % 2) * 256:(h % 2) * 256 + 256], invT_b[:, h * 128:(h + 1) * 128],
                    vbT[:, tt * 1024 + h * 256: tt * 1024 + (h + 1) * 256], start=True, stop=True),
                    waits=[tinvT, tok_tmprod[("vbT", tt)], tp.get("ps0"), tp.get("ps1")], sig="dt")
            tu1 = S.emit("vector", lambda e: e.tensor_copy(u_b[:, 0:512], PW[0][:, :]),
                         waits=[mmu, tp.get("dbgu")], sig="dv")
            tu2 = S.emit("vector", lambda e: e.tensor_copy(u_b[:, 512:1024], PW[1][:, :]), waits=[mmu], sig="dv")
            tp["ps0"] = tu1
            tp["ps1"] = tu2
            mmw = None
            for h in range(H):
                for half in range(2):
                    mmw = S.emit("tensor", lambda e, h=h, half=half, tt=tt: e.matmul(
                        PW[2 + h // 2][:, ((h % 2) * 2 + half) * 128: ((h % 2) * 2 + half) * 128 + 128],
                        kbT[:, tt * 1024 + h * 256 + half * 128: tt * 1024 + h * 256 + half * 128 + 128],
                        invT_b[:, h * 128:(h + 1) * 128], start=True, stop=True),
                        waits=[tinvT, tok_tmprod[("kbT", tt)], tp.get("ps2"), tp.get("ps3")], sig="dt")
            nw_lo = negw_fm[:, 0:4 * W].rearrange("p (c w) -> p c w", c=4)[:, :, tt * 128:tt * 128 + 128]
            nw_hi = negw_fm[:, 4 * W:8 * W].rearrange("p (c w) -> p c w", c=4)[:, :, tt * 128:tt * 128 + 128]
            tw1 = S.emit("vector", lambda e, nw_lo=nw_lo: e.tensor_scalar(
                nw_lo, PW[2][:, :].rearrange("p (c w) -> p c w", c=4), -1.0, None, OP.mult), waits=[mmw], sig="dv")
            tw2 = S.emit("vector", lambda e, nw_hi=nw_hi: e.tensor_scalar(
                nw_hi, PW[3][:, :].rearrange("p (c w) -> p c w", c=4), -1.0, None, OP.mult), waits=[mmw], sig="dv")
            tp["ps2"] = tw1
            tp["ps3"] = tw2
            if DEBUG:
                r0 = mb + tt * 128
                tp["dbgu"] = S.dma("mg", dbg["dbg_u"][r0:r0 + 128, :], u_b[:, :], waits=[tu1, tu2])
                for c in range(8):
                    tp["dbgu"] = S.dma("mg", dbg["dbg_negw"][c * 128:(c + 1) * 128, r0:r0 + 128],
                                       fmsl(negw_fm, c, tt), waits=[tw1, tw2, tp.get("dbgu")])
            mmt = None
            for h in range(H):
                for half in range(2):
                    c = 2 * h + half
                    mmt = S.emit("tensor", lambda e, h=h, c=c, tt=tt, half=half: e.matmul(
                        PW[0][:, h * 128:(h + 1) * 128], fmsl(kh_fm, c, tt), fmsl(qh_fm, c, tt),
                        start=(h == 0 and half == 0), stop=(h == 3 and half == 1)),
                        waits=[tu1, tok_scaled[("q", c)]], sig="dt")
            tMt = S.emit("vector", lambda e: e.tensor_tensor(
                Mt_b[:, :].rearrange("p (a w) -> p a w", a=4), PW[0][:, :].rearrange("p (a w) -> p a w", a=4),
                C["triuI"][:].rearrange("p (x w) -> p x w", x=1).broadcast_to([128, 4, 128]), OP.mult), waits=[mmt], sig="dv")
            tp["ps0"] = tMt
            mm1 = None
            for h in range(H):
                for half in range(2):
                    c = 2 * h + half
                    mm1 = S.emit("tensor", lambda e, h=h, c=c, tt=tt, half=half: e.matmul(
                        PW[1][:, h * 128:(h + 1) * 128], fmsl(negw_fm, c, tt), fmsl(kh_fm, c, tt),
                        start=(h == 0 and half == 0), stop=(h == 3 and half == 1)),
                        waits=[tw1, tw2, tu2], sig="dt")
            tGnw = S.emit("vector", lambda e: e.tensor_tensor(
                Gnw_b[:, :].rearrange("p (a w) -> p a w", a=4), PW[1][:, :].rearrange("p (a w) -> p a w", a=4),
                C["blkLow"][:].rearrange("p (x w) -> p x w", x=1).broadcast_to([128, 4, 128]), OP.mult), waits=[mm1], sig="dv")
            tp["ps1"] = tGnw
            mm2 = None
            for h in range(H):
                for half in range(2):
                    c = 2 * h + half
                    mm2 = S.emit("tensor", lambda e, h=h, c=c, tt=tt, half=half: e.matmul(
                        PW[2][:, h * 128:(h + 1) * 128], fmsl(kh_fm, c, tt), fmsl(negw_fm, c, tt),
                        start=(h == 0 and half == 0), stop=(h == 3 and half == 1)),
                        waits=[tw1, tw2, tp.get("ps2")], sig="dt")
            tGnwT = S.emit("vector", lambda e: e.tensor_tensor(
                GnwT_b[:, :].rearrange("p (a w) -> p a w", a=4), PW[2][:, :].rearrange("p (a w) -> p a w", a=4),
                C["blkUp"][:].rearrange("p (x w) -> p x w", x=1).broadcast_to([128, 4, 128]), OP.mult), waits=[mm2], sig="dv")
            tp["ps2"] = tGnwT
            mmt2 = None
            for h in range(H):
                mmt2 = S.emit("tensor", lambda e, h=h: e.matmul(
                    PW[3][:, h * 128:(h + 1) * 128], Gnw_b[:, h * 128:(h + 1) * 128],
                    GnwT_b[:, h * 128:(h + 1) * 128], start=(h == 0), stop=(h == 3)),
                    waits=[tGnw, tGnwT, tp.get("ps3")], sig="dt")
            tT2 = S.emit("vector", lambda e: e.tensor_copy(T2t_b[:, :], PW[3][:, :]), waits=[mmt2], sig="dv")
            tp["ps3"] = tT2
            mmh = None
            for h in range(H):
                hs = slice(h * 128, (h + 1) * 128)
                mmh = S.emit("tensor", lambda e, hs=hs: e.matmul(PW[0][:, hs], C["eye128"][:, :], C["eye128"][:, :],
                                                                start=True, stop=False),
                             waits=[tT2, tMt], sig="dt")
                S.emit("tensor", lambda e, hs=hs: e.matmul(PW[0][:, hs], C["eye128"][:, :], GnwT_b[:, hs],
                                                           start=False, stop=False), sig="dt")
                S.emit("tensor", lambda e, hs=hs: e.matmul(PW[0][:, hs], C["eye128"][:, :], T2t_b[:, hs],
                                                           start=False, stop=False), sig="dt")
                mmh = S.emit("tensor", lambda e, hs=hs: e.matmul(PW[0][:, hs], Gnw_b[:, hs], T2t_b[:, hs],
                                                                start=False, stop=True), sig="dt")
            tHt = S.emit("vector", lambda e: e.tensor_copy(Ht_b[:, :], PW[0][:, :]), waits=[mmh], sig="dv")
            tp["ps0"] = tHt
            mmu0 = None
            for h in range(H):
                pwb = PW[1 + h // 2]
                of = (h % 2) * 256
                mmu0 = S.emit("tensor", lambda e, h=h, pwb=pwb, of=of, fsc=fsc: e.matmul(
                    pwb[:, of:of + 256], C["eye128"][:, :], u_b[:, h * 256:(h + 1) * 256],
                    start=True, stop=fsc), waits=[tHt, tu1, tu2, tp.get("ps1"), tp.get("ps2")], sig="dt")
                if not fsc:
                    for half in range(2):
                        c = 2 * h + half
                        mmu0 = S.emit("tensor", lambda e, h=h, c=c, tt=tt, pwb=pwb, of=of, half=half: e.matmul(
                            pwb[:, of:of + 256], fmsl(negw_fm, c, tt),
                            S_bf[:, h * 512 + half * 256: h * 512 + half * 256 + 256],
                            start=False, stop=(half == 1)), sig="dt")
            tU1 = S.emit("vector", lambda e: e.tensor_copy(U0_b[:, 0:512], PW[1][:, :]), waits=[mmu0], sig="dv")
            tU2 = S.emit("vector", lambda e: e.tensor_copy(U0_b[:, 512:1024], PW[2][:, :]), waits=[mmu0], sig="dv")
            tp["ps1"] = tU1
            tp["ps2"] = tU2
            mmua = None
            for h in range(H):
                pwb = PW[1 + h // 2]
                of = (h % 2) * 256
                mmua = S.emit("tensor", lambda e, h=h, pwb=pwb, of=of: e.matmul(
                    pwb[:, of:of + 256], Ht_b[:, h * 128:(h + 1) * 128], U0_b[:, h * 256:(h + 1) * 256],
                    start=True, stop=True), waits=[tU1, tU2], sig="dt")
            tua1 = S.emit("vector", lambda e: e.tensor_copy(uadj_b[:, 0:512], PW[1][:, :]), waits=[mmua], sig="dv")
            tua2 = S.emit("vector", lambda e: e.tensor_copy(uadj_b[:, 512:1024], PW[2][:, :]), waits=[mmua], sig="dv")
            tp["ps1"] = tua1
            tp["ps2"] = tua2
            mmo = None
            for h in range(H):
                pwb = PW[1 + h // 2]
                of = (h % 2) * 256
                mmo = S.emit("tensor", lambda e, h=h, pwb=pwb, of=of, fsc=fsc: e.matmul(
                    pwb[:, of:of + 256], Mt_b[:, h * 128:(h + 1) * 128], uadj_b[:, h * 256:(h + 1) * 256],
                    start=True, stop=fsc), waits=[tua1, tua2], sig="dt")
                if not fsc:
                    for half in range(2):
                        c = 2 * h + half
                        mmo = S.emit("tensor", lambda e, h=h, c=c, tt=tt, pwb=pwb, of=of, half=half: e.matmul(
                            pwb[:, of:of + 256], fmsl(qh_fm, c, tt),
                            S_bf[:, h * 512 + half * 256: h * 512 + half * 256 + 256],
                            start=False, stop=(half == 1)), sig="dt")
            tod1 = S.emit("vector", lambda e, tt=tt: e.tensor_copy(delta_tm[:, tt * 1024:tt * 1024 + 512], PW[1][:, :]),
                          waits=[mmo], sig="dv")
            tod2 = S.emit("vector", lambda e, tt=tt: e.tensor_copy(delta_tm[:, tt * 1024 + 512:(tt + 1) * 1024],
                                                                   PW[2][:, :]), waits=[mmo], sig="dv")
            tp["ps1"] = tod1
            tp["ps2"] = tod2
            mms = None
            for h in range(H):
                for half in range(2):
                    mms = S.emit("tensor", lambda e, h=h, half=half, tt=tt: e.matmul(
                        PS_S[h][:, half * 256: half * 256 + 256],
                        khT[:, tt * 1024 + h * 256 + half * 128: tt * 1024 + h * 256 + half * 128 + 128],
                        uadj_b[:, h * 256:(h + 1) * 256], start=(half == 0), stop=(half == 1)),
                        waits=[tua1, tua2, tok_tmprod[("khT", tt)], tp.get("scast")], sig="dt")
            tsc = None
            for h in range(H):
                if fsc:
                    ta = S.emit("vector", lambda e, h=h: e.tensor_copy(S32[:, h * 512:(h + 1) * 512], PS_S[h][:, :]),
                                waits=[mms], sig="dv")
                else:
                    ta = S.emit("vector", lambda e, h=h: e.tensor_tensor(S32[:, h * 512:(h + 1) * 512],
                                                                         S32[:, h * 512:(h + 1) * 512],
                                                                         PS_S[h][:, :], OP.add),
                                waits=[mms], sig="dv")
                tsc = S.emit("vector", lambda e, h=h: e.tensor_copy(S_bf[:, h * 512:(h + 1) * 512],
                                                                    S32[:, h * 512:(h + 1) * 512]),
                             waits=[ta], sig="dv")
            tp["scast"] = tsc
            if DEBUG:
                r0 = mb + tt * 128
                tp["dbgu"] = S.dma("mg", dbg["dbg_delta"][r0:r0 + 128, :], delta_tm[:, tt * 1024:(tt + 1) * 1024],
                                   waits=[tod1, tod2, tp.get("dbgu")])
        AB.free("B_b", "BT_b", "pw_b", "pwT_b", "P_b", "invT_b", "Mt_b", "Gnw_b", "GnwT_b",
                "T2t_b", "Ht_b", "u_b", "U0_b", "uadj_b", "negw_fm", "kb_fm", "vbT", "kbT", "khT")
        S.barrier()

        if PH < 8: continue
        sh_tm = AB.alloc("sh_tm", NT * 1024)
        lo_tm = AB.alloc("lo_tm", NT * 1024)
        wd_tm = AB.alloc("wd_tm", NT * 1024)
        Rb = AB.alloc("Rb", 2 * 4 * RW)
        fdb = AB.alloc("fdb", 2 * 4 * NGF * 32)
        fir_fm = AB.alloc("fir_fm", 3 * W)
        sq_b = AB.alloc("sq_b", W)
        filt_last = [NG_S - 1, NG_S + NG_L - 1, NGF - 1]
        tok_fir_ev = {}
        vdview = vdram.rearrange("(c s i) t -> c i s t", c=8, s=4)
        mmsum = None
        first_summ = True
        for c in range(8):
            half = c % 2
            tRl = []
            for g in range(4):
                colb = mb + 61 + g
                tRl.append(S.dma("mh", Rb[32 * g:32 * g + 32, half * 4 * RW:(half + 1) * 4 * RW].rearrange(
                    "p (s t) -> p s t", s=4), vdview[c, :, :, colb:colb + RW],
                    waits=[tok_vdr_w, tok_fir_ev.get(c - 2)]))
            tfd = S.dma("mg", fdb[:, half * NGF * 128:(half + 1) * NGF * 128].rearrange(
                "p (s g w) -> p s g w", s=4, g=NGF), dr["fird"][c].rearrange("s g p w -> p s g w"),
                waits=[tok_fir_ev.get(c - 2)])
            mmf = {}
            for fi in range(3):
                mmf[fi] = S.emit("tensor", lambda e, fi=fi, half=half: e.matmul(
                    PW[fi][:, 0:512], C["zeros128"][:, :], Rb[:, half * 4 * RW: half * 4 * RW + 512],
                    start=True, stop=False), waits=tRl + [tfd, tp.get(("fps", fi))], sig="dt")
            for gi, (fi, Gf, tau) in enumerate(groups):
                for s4 in range(4):
                    mmf[fi] = S.emit("tensor", lambda e, fi=fi, gi=gi, s4=s4, tau=tau, half=half: e.matmul(
                        PW[fi][32 * s4:32 * s4 + 32, 0:512],
                        fdb[:, half * NGF * 128 + (s4 * NGF + gi) * 32: half * NGF * 128 + (s4 * NGF + gi) * 32 + 32],
                        Rb[:, half * 4 * RW + s4 * RW + tau: half * 4 * RW + s4 * RW + tau + 512],
                        start=False, stop=False, tile_position=(0, 32 * s4)),
                        waits=tRl + [tfd], sig="dt")
            for fi in range(3):
                mmf[fi] = S.emit("tensor", lambda e, fi=fi, half=half: e.matmul(
                    PW[fi][:, 0:512], C["zeros128"][:, :], Rb[:, half * 4 * RW: half * 4 * RW + 512],
                    start=False, stop=True), sig="dt")
            evs = []
            for fi in range(3):
                evs.append(S.emit("vector", lambda e, fi=fi: e.tensor_copy(fir_fm[:, fi * W:(fi + 1) * W], PW[fi][:, :]),
                                  waits=[mmf[fi], tp.get(("fxw", fi))], sig="dv"))
                tp[("fps", fi)] = evs[-1]
            tok_fir_ev[c] = evs[-1]
            if first_summ:
                mmsum = S.emit("tensor", lambda e: e.matmul(
                    PW[3][:, 0:512], C["zeros128"][:, :], vf_fm[:, 0:W],
                    start=True, stop=False), waits=[tp.get("sumps")] + [evs[0]], sig="dt")
                first_summ = False
            for fi in range(3):
                mmsum = S.emit("tensor", lambda e, c=c, fi=fi: e.matmul(
                    PW[3][0:16, 0:512], C["hc4"][:, (c * 4 + fi) * 16:(c * 4 + fi + 1) * 16],
                    fir_fm[:, fi * W:(fi + 1) * W],
                    start=False, stop=False), waits=[evs[fi], tp.get("sumps")], sig="dt")
            mmsum = S.emit("tensor", lambda e, c=c: e.matmul(
                PW[3][0:16, 0:512], C["hc4"][:, (c * 4 + 3) * 16:(c * 4 + 4) * 16], vf_fm[:, c * W:(c + 1) * W],
                start=False, stop=False), sig="dt")
            for fi in range(3):
                tsq = S.emit("vector", lambda e, fi=fi: e.tensor_tensor(
                    sq_b[:, :], fir_fm[:, fi * W:(fi + 1) * W], fir_fm[:, fi * W:(fi + 1) * W], OP.mult),
                    waits=[evs[fi], mmsum], sig="dv")
                mmsum = S.emit("tensor", lambda e, c=c, fi=fi: e.matmul(
                    PW[3][32:48, 0:512], C["hc4"][:, (c * 4 + fi) * 16:(c * 4 + fi + 1) * 16], sq_b[:, :],
                    start=False, stop=False), waits=[tsq], sig="dt")
            tsqv = S.emit("vector", lambda e, c=c: e.tensor_tensor(
                sq_b[:, :], vf_fm[:, c * W:(c + 1) * W], vf_fm[:, c * W:(c + 1) * W], OP.mult),
                waits=[mmsum], sig="dv")
            mmsum = S.emit("tensor", lambda e, c=c: e.matmul(
                PW[3][32:48, 0:512], C["hc4"][:, (c * 4 + 3) * 16:(c * 4 + 4) * 16], sq_b[:, :],
                start=False, stop=False), waits=[tsqv], sig="dt")
            if c == 7:
                mmsum = S.emit("tensor", lambda e: e.matmul(
                    PW[3][:, 0:512], C["zeros128"][:, :], vf_fm[:, 0:W],
                    start=False, stop=True), sig="dt")
            tp["sumps"] = mmsum
            for fi, dst in ((0, sh_tm), (1, lo_tm), (2, wd_tm)):
                tp[("fxw", fi)] = [S.dma("ms", dst[:, tt * 1024 + c * 128: tt * 1024 + (c + 1) * 128],
                                         fir_fm[:, fi * W + tt * 128: fi * W + tt * 128 + 128],
                                         waits=[evs[fi]], transpose=True) for tt in range(NT)]
        tok_fxall = S.qtot("ms") + S.qtot("mh")
        if DEBUG:
            for nm, dst in (("dbg_short", sh_tm), ("dbg_long", lo_tm), ("dbg_wide", wd_tm)):
                for tt in range(NT):
                    S.dma("mg", dbg[nm][mb + tt * 128: mb + tt * 128 + 128, :], dst[:, tt * 1024:(tt + 1) * 1024],
                          waits=[tok_fxall])
        AB.free("Rb", "fdb", "fir_fm", "sq_b")

        if PH < 9: continue
        stat64 = AFp.alloc("stat64", 64)
        sums16 = AFp.alloc("sums16", 512)
        trsp = AFp.alloc("trsp", 128)
        dsq = AFp.alloc("dsq", 1024)
        stt = S.emit("vector", lambda e: e.tensor_copy(sums16[:, :], PW[3][:, 0:512]), waits=[mmsum], sig="dv")
        if DEBUG:
            S.dma("mg", dbg["dbg_sums"][:, mb:mb + W], sums16[:, :], waits=[stt])
        tp["ps3"] = stt
        tok_stats = []
        for tt in range(NT):
            toks = []
            tp1 = S.emit("tensor", lambda e, tt=tt: e.transpose(PW[2][:, 0:128], sums16[:, tt * 128:(tt + 1) * 128],
                                                                C["ident32"][:, :]),
                         waits=[stt, tp.get("ps2")], sig="dt")
            ev1 = S.emit("vector", lambda e: e.tensor_copy(trsp[:, :], PW[2][:, 0:128]), waits=[tp1], sig="dv")
            tp["ps2"] = ev1
            for ti in range(4):
                off = [0, 12, 24, 48][ti]
                toks.append(S.emit("vector", lambda e, ti=ti, off=off: e.tensor_scalar(
                    stat64[:, off:off + 4], trsp[:, ti * 4:(ti + 1) * 4], 1.0 / 256.0, None, OP.mult),
                    waits=[ev1, tp.get("dbgstat")], sig="dv"))
                toks.append(S.emit("vector", lambda e, ti=ti, off=off: e.tensor_scalar(
                    stat64[:, off + 4:off + 8], trsp[:, 32 + ti * 4: 32 + (ti + 1) * 4], 1.0 / 256.0, 1e-8,
                    OP.mult, OP.max), waits=[ev1], sig="dv"))
            dslice = delta_tm[:, tt * 1024:(tt + 1) * 1024]
            td1 = S.emit("vector", lambda e, dslice=dslice: e.tensor_reduce(
                stat64[:, 60:64], dslice.rearrange("p (h c) -> p h c", h=4), axis=AX, op=OP.add),
                waits=[tp.get("dstat")], sig="dv")
            td2 = S.emit("vector", lambda e: e.tensor_scalar(stat64[:, 36:40], stat64[:, 60:64],
                                                             1.0 / 256.0, None, OP.mult), waits=[td1], sig="dv")
            td3 = S.emit("vector", lambda e, dslice=dslice: e.tensor_tensor(dsq[:, :], dslice, dslice, OP.mult),
                         waits=[td2], sig="dv")
            td4 = S.emit("vector", lambda e: e.tensor_reduce(
                stat64[:, 60:64], dsq[:, :].rearrange("p (h c) -> p h c", h=4), axis=AX, op=OP.add),
                waits=[td3], sig="dv")
            td5 = S.emit("vector", lambda e: e.tensor_scalar(stat64[:, 40:44], stat64[:, 60:64],
                                                             1.0 / 256.0, 1e-8, OP.mult, OP.max), waits=[td4], sig="dv")
            tp["dstat"] = td5
            toks.extend([td1, td2, td3, td4, td5])
            for tile, off in ((sh_tm, 0), (lo_tm, 12), (wd_tm, 24), (delta_tm, 36), (v_tm, 48)):
                toks.append(S.emit("vector", lambda e, tile=tile, off=off, tt=tt: e.tensor_reduce(
                    stat64[:, off + 8:off + 12], tile[:, tt * 1024:(tt + 1) * 1024].rearrange("p (h c) -> p h c", h=4),
                    axis=AX, op=OP.max, apply_absolute_value=True), waits=[tok_fxall], sig="dv"))
            sqs = []
            for coff in (4, 16, 28, 40, 52):
                sqs.append(S.emit("scalar", lambda e, coff=coff: e.activation(
                    stat64[:, coff:coff + 4], stat64[:, coff:coff + 4], AF.Sqrt), waits=toks, sig="dc"))
            tp2 = S.emit("tensor", lambda e: e.transpose(PW[2][0:64, 0:128], stat64[:, :], C["ident32"][:, :]),
                         waits=toks + sqs + [tp.get("ps2")], sig="dt")
            tst = S.emit("vector", lambda e, tt=tt: e.tensor_copy(statF[0:64, tt * 128:(tt + 1) * 128],
                                                                  PW[2][0:64, 0:128]), waits=[tp2], sig="dv")
            tp["ps2"] = tst
            tok_stats.append(tst)
            if DEBUG:
                tp["dbgstat"] = S.dma("mg", dbg["dbg_stat"][mb + tt * 128: mb + tt * 128 + 128, :], stat64[:, :],
                                      waits=toks + sqs + [tst])
        AFp.free("stat64", "sums16", "trsp", "dsq")

        if PH < 10: continue
        g1buf = AB.alloc("g1buf", 2 * 9 * 128)
        hdn_b = AB.alloc("hdn_b", 2 * W)
        logit_sp = AFp.alloc("logit_sp", W)
        tok_hdn = {}
        mml = None
        for dcg in range(16):
            buf = g1buf[:, (dcg % 2) * 9 * 128:((dcg % 2) + 1) * 9 * 128]
            twl = S.dma("mg", buf.rearrange("p (c w) -> p c w", c=9),
                        dr["g1Tp"].rearrange("(c p) d -> p c d", p=128)[:, :, dcg * 128:(dcg + 1) * 128],
                        waits=[tok_hdn.get(dcg - 2)])
            mmt = None
            for kc in range(9):
                rhs = xn_fm[:, kc * W:(kc + 1) * W] if kc < 8 else statF[:, 0:512]
                mmt = S.emit("tensor", lambda e, kc=kc, buf=buf, rhs=rhs, dcg=dcg: e.matmul(
                    PW[dcg % 2][:, 0:512], buf[:, kc * 128:(kc + 1) * 128], rhs,
                    start=(kc == 0), stop=(kc == 8)),
                    waits=[twl, tok_stats[-1], tp.get(("g1ps", dcg % 2))], sig="dt")
            ghl = S.emit("scalar", lambda e, dcg=dcg: e.activation(
                hdn_b[:, (dcg % 2) * W:((dcg % 2) + 1) * W], PW[dcg % 2][:, 0:512], AF.Gelu,
                bias=C["g1b"][:, dcg:dcg + 1]), waits=[mmt], sig="dc")
            tok_hdn[dcg] = ghl
            tp[("g1ps", dcg % 2)] = ghl
            mml = S.emit("tensor", lambda e, dcg=dcg: e.matmul(
                PW[2][0:20, 0:512], C["g2T"][:, dcg * 20:(dcg + 1) * 20],
                hdn_b[:, (dcg % 2) * W:((dcg % 2) + 1) * W],
                start=(dcg == 0), stop=(dcg == 15)), waits=[ghl, tp.get("ps2")], sig="dt")
        tok_lg = S.emit("vector", lambda e: e.tensor_copy(logit_sp[0:20, :], PW[2][0:20, 0:512]), waits=[mml], sig="dv")
        tp["ps2"] = tok_lg
        tp["xnfree"] = tok_lg
        AB.free("g1buf", "hdn_b")

        if PH < 11: continue
        o_tm = AB.alloc("o_tm", NT * 1024)
        lg_t = AFp.alloc("lg_t", 64)
        o32 = AFp.alloc("o32", 1024)
        tmpc = AFp.alloc("tmpc", 1024)
        for tt in range(NT):
            tpx = S.emit("tensor", lambda e, tt=tt: e.transpose(PW[3][:, 0:128], logit_sp[:, tt * 128:(tt + 1) * 128],
                                                                C["ident32"][:, :]),
                         waits=[tok_lg, tp.get("ps3"), tp.get("lgps")], sig="dt")
            t1 = S.emit("vector", lambda e: e.tensor_copy(lg_t[:, 0:20], PW[3][0:128, 0:20]),
                        waits=[tpx, tp.get("dbglg")], sig="dv")
            tp["ps3"] = t1
            t2 = S.emit("vector", lambda e: e.tensor_tensor(lg_t[:, 0:20], lg_t[:, 0:20], gconst[:, 0:20], OP.add),
                        waits=[t1], sig="dv")
            t3 = S.emit("vector", lambda e: e.tensor_tensor(lg_t[:, 0:20], lg_t[:, 0:20], gconst[:, 20:40], OP.mult),
                        waits=[t2], sig="dv")
            t4 = S.emit("vector", lambda e: e.tensor_reduce(lg_t[:, 20:24],
                                                            lg_t[:, 0:20].rearrange("p (h i) -> p h i", h=4),
                                                            axis=AX, op=OP.max), waits=[t3], sig="dv")
            mxv = lg_t[:, 20:24].broadcast_to([128, 4, 5])
            t5 = S.emit("vector", lambda e: e.tensor_tensor(lg_t[:, 0:20].rearrange("p (h i) -> p h i", h=4),
                                                            lg_t[:, 0:20].rearrange("p (h i) -> p h i", h=4),
                                                            mxv, OP.subtract), waits=[t4], sig="dv")
            t6 = S.emit("scalar", lambda e: e.activation(lg_t[:, 0:20], lg_t[:, 0:20], AF.Exp), waits=[t5], sig="dc")
            t7 = S.emit("vector", lambda e: e.tensor_reduce(lg_t[:, 20:24],
                                                            lg_t[:, 0:20].rearrange("p (h i) -> p h i", h=4),
                                                            axis=AX, op=OP.add), waits=[t6], sig="dv")
            t8 = S.emit("vector", lambda e: e.reciprocal(lg_t[:, 24:28], lg_t[:, 20:24]), waits=[t7], sig="dv")
            rv = lg_t[:, 24:28].broadcast_to([128, 4, 5])
            t9 = S.emit("vector", lambda e: e.tensor_tensor(lg_t[:, 0:20].rearrange("p (h i) -> p h i", h=4),
                                                            lg_t[:, 0:20].rearrange("p (h i) -> p h i", h=4),
                                                            rv, OP.mult), waits=[t8], sig="dv")
            t10 = S.emit("vector", lambda e: e.tensor_tensor(lg_t[:, 0:20], lg_t[:, 0:20], gconst[:, 40:60], OP.mult),
                         waits=[t9], sig="dv")
            t11 = S.emit("vector", lambda e: e.tensor_tensor(lg_t[:, 0:20], lg_t[:, 0:20], gconst[:, 60:80], OP.add),
                         waits=[t10], sig="dv")
            if DEBUG:
                tp["dbglg"] = S.dma("mg", dbg["dbg_logits"][mb + tt * 128:mb + tt * 128 + 128, 0:20],
                                    lg_t[:, 0:20], waits=[t11])
            tensors = [(sh_tm, 0), (lo_tm, 1), (wd_tm, 2), (delta_tm, 3), (v_tm, 4)]
            tsl = slice(tt * 1024, (tt + 1) * 1024)
            tokc = None
            for ti, (tile, i0) in enumerate(tensors):
                wv_ = lg_t[:, i0:i0 + 20].rearrange("p (h i) -> p h i", i=5)[:, :, 0:1].broadcast_to([128, 4, 256])
                if ti == 0:
                    tokc = S.emit("vector", lambda e, tile=tile, wv_=wv_, tsl=tsl: e.tensor_tensor(
                        o32[:, :].rearrange("p (h c) -> p h c", h=4),
                        tile[:, tsl].rearrange("p (h c) -> p h c", h=4), wv_, OP.mult),
                        waits=[t11, tok_fxall], sig="dv")
                else:
                    tokc = S.emit("vector", lambda e, tile=tile, wv_=wv_, tsl=tsl: e.tensor_tensor(
                        tmpc[:, :].rearrange("p (h c) -> p h c", h=4),
                        tile[:, tsl].rearrange("p (h c) -> p h c", h=4), wv_, OP.mult),
                        waits=[tokc], sig="dv")
                    tokc = S.emit("vector", lambda e: e.tensor_tensor(o32[:, :], o32[:, :], tmpc[:, :], OP.add),
                                  waits=[tokc], sig="dv")
            t12 = S.emit("vector", lambda e: e.tensor_tensor(tmpc[:, :], o32[:, :], o32[:, :], OP.mult),
                         waits=[tokc], sig="dv")
            t13 = S.emit("vector", lambda e: e.tensor_reduce(lg_t[:, 32:36],
                                                             tmpc[:, :].rearrange("p (h c) -> p h c", h=4),
                                                             axis=AX, op=OP.add), waits=[t12], sig="dv")
            t14 = S.emit("vector", lambda e: e.tensor_scalar(lg_t[:, 32:36], lg_t[:, 32:36], 1.0 / 256, 1e-5,
                                                             OP.mult, OP.add), waits=[t13], sig="dv")
            t15 = S.emit("vector", lambda e: e.reciprocal(lg_t[:, 36:40], lg_t[:, 32:36]), waits=[t14], sig="dv")
            t16 = S.emit("scalar", lambda e: e.activation(lg_t[:, 36:40], lg_t[:, 36:40], AF.Sqrt), waits=[t15], sig="dc")
            rsv = lg_t[:, 36:40].broadcast_to([128, 4, 256])
            t17 = S.emit("vector", lambda e: e.tensor_tensor(o32[:, :].rearrange("p (h c) -> p h c", h=4),
                                                             o32[:, :].rearrange("p (h c) -> p h c", h=4),
                                                             rsv, OP.mult), waits=[t16], sig="dv")
            t18 = S.emit("vector", lambda e, tsl=tsl: e.tensor_tensor(o_tm[:, tsl], o32[:, :], C["onorm_rep"][:, :],
                                                                      OP.mult), waits=[t17], sig="dv")
            tp[("otm", tt)] = t18
            if DEBUG:
                S.dma("mg", dbg["dbg_otm"][mb + tt * 128:mb + tt * 128 + 128, :], o_tm[:, tsl], waits=[t18])
        AFp.free("lg_t", "o32", "tmpc", "logit_sp")

        if PH < 12: continue
        o_fm = AB.alloc("o_fm", 8 * W)
        owb = AB.alloc("owb", 8 * 1024)
        AB.free("sh_tm", "lo_tm", "wd_tm", "delta_tm", "v_tm")
        s_t = AFp.alloc("s_t", 1024)
        towl = S.dma("mg", owb[:].rearrange("p (c w) -> p c w", c=8),
                     dr["owT"].rearrange("(c p) d -> p c d", p=128), waits=[tp[("otm", 0)]])
        tof = {}
        for tt in range(NT):
            for c in range(8):
                tof[(tt, c)] = S.dma("ms", o_fm[:, c * W + tt * 128: c * W + tt * 128 + 128],
                                     o_tm[:, tt * 1024 + c * 128: tt * 1024 + (c + 1) * 128],
                                     waits=[tp[("otm", tt)]], transpose=True)
        for tt in range(NT):
            mms = None
            for kc in range(8):
                for nn in range(2):
                    mms = S.emit("tensor", lambda e, kc=kc, nn=nn, tt=tt: e.matmul(
                        PW[nn][:, 0:512], o_fm[:, kc * W + tt * 128: kc * W + tt * 128 + 128],
                        owb[:, kc * 1024 + nn * 512: kc * 1024 + (nn + 1) * 512],
                        start=(kc == 0), stop=(kc == 7)),
                        waits=[towl, tof[(tt, kc)], tp.get("ps0"), tp.get("ps1")]
                        + ([tp.get(("g1ps", 0)), tp.get(("g1ps", 1))] if tt == 0 else []), sig="dt")
            ev1 = S.emit("vector", lambda e: e.tensor_copy(s_t[:, 0:512], PW[0][:, :]),
                         waits=[mms, tp.get("sdma")], sig="dv")
            ev2 = S.emit("vector", lambda e: e.tensor_copy(s_t[:, 512:1024], PW[1][:, :]), waits=[mms], sig="dv")
            tp["ps0"] = ev1
            tp["ps1"] = ev2
            tp["sdma"] = S.dma("mg", sdram[mb + tt * 128: mb + tt * 128 + 128, :], s_t[:, :], waits=[ev1, ev2])
            tp["mttail"] = mms
        tp[("projps", 0)] = tp["ps0"]
        tp[("projps", 1)] = tp["ps1"]
        AB.free("o_fm", "owb", "o_tm", "xq_tm")
        AFp.free("s_t")

    S.barrier()
    AB.free("xn_fm", "qh_fm", "kh_fm", "vf_fm", "S_bf", "tails", "statF", "cd", "g2T", "hc4")
    AFp.free("S32", "onorm_rep", "g1b", "gconst", "ident32", "smallrows")

    if PH < 13:
        return _finish(nc, S, ctx)
    wgq = AB.alloc("wgq", 8 * IFF)
    wuq = AB.alloc("wuq", 8 * IFF)
    wdq = AB.alloc("wdq", 22 * 1024)
    wld = AFp.alloc("wld", 2 * 1024)
    wq_b = AB.alloc("wq_b", 2 * 1024)
    prev_q = {}
    prev_x = {}
    i = 0
    for wnm, dst in (("wg", wgq), ("wu", wuq)):
        for ic2 in range(22):
            p = i % 2
            wldp = wld[:, p * 1024:(p + 1) * 1024]
            wqbp = wq_b[:, p * 1024:(p + 1) * 1024]
            tl = S.dma("mg", wldp, dr[wnm][ic2 * 128:(ic2 + 1) * 128, :], waits=[prev_q.get(p)])
            tqq, _ = emit_quant(wldp, wqbp, 1024, tl, "q2", dst_war=prev_x.get(p))
            prev_q[p] = tqq
            prev_x[p] = [S.dma("ms" if p == 0 else "mh",
                               dst[:, kc * IFF + ic2 * 128: kc * IFF + ic2 * 128 + 128],
                               wqbp[:, kc * 128:(kc + 1) * 128], waits=[tqq], transpose=True)
                         for kc in range(8)]
            i += 1
    AFp.free("wld")
    AB.free("wq_b")
    wld2 = AFp.alloc("wld2", 2 * (IFF // 2))
    wq_b2 = AB.alloc("wq_b2", 2 * IFF)
    prev_q2 = {0: [prev_q[0], prev_q[1]], 1: [prev_q[0], prev_q[1]]}
    prev_x2 = {0: prev_x[0] + prev_x[1], 1: prev_x[0] + prev_x[1]}
    for c8 in range(8):
        p = c8 % 2
        wldp = wld2[:, p * 1408:(p + 1) * 1408]
        wqbp = wq_b2[:, p * IFF:(p + 1) * IFF]
        tqh = []
        for hf in range(2):
            tl = S.dma("mg", wldp, dr["wdn"][c8 * 128:(c8 + 1) * 128, hf * 1408:(hf + 1) * 1408],
                       waits=[prev_q2.get(p)] + (tqh if hf else []))
            tqq, _ = emit_quant(wldp, wqbp[:, hf * 1408:(hf + 1) * 1408], IFF // 2, tl, "q3",
                                dst_war=prev_x2.get(p))
            tqh.append(tqq)
        prev_q2[p] = tqh
        prev_x2[p] = [S.dma("ms" if p == 0 else "mh",
                            wdq[:, icc * 1024 + c8 * 128: icc * 1024 + c8 * 128 + 128],
                            wqbp[:, icc * 128:(icc + 1) * 128], waits=tqh, transpose=True)
                      for icc in range(22)]
    AFp.free("wld2")
    AB.free("wq_b2")
    S.barrier()

    xq_t2 = AB.alloc("xq_t", 3 * 1024)
    hq_t2 = AB.alloc("hq_t", 5 * 512)
    hq_f2 = AB.alloc("hq_f", 5 * 512)
    gup2 = AFp.alloc("gup", 2 * 1024)
    hh2 = AFp.alloc("hh", 2 * 512)
    g_t2 = AFp.alloc("g_t", 2 * 1024)
    xs_t2 = AFp.alloc("xs_t", 2048)
    IC_OF = [0, 512, 1024, 1536, 2048, 2560]
    IC_SZ = [512, 512, 512, 512, 512, 256]
    GU = [(PW[0], PW[1]), (PS_S[0], PS_S[1])]
    DN = [(PW[2], PW[3]), (PS_S[2], PS_S[3])]

    def emit_dn(pend):
        tt, ici, q4, tok_hqf = pend
        pb = tt % 2
        bd0, bd1 = DN[pb]
        hq_f = hq_f2[:, q4 * 512:(q4 + 1) * 512]
        iof, isz = IC_OF[ici], IC_SZ[ici]
        mmd = None
        for j in range(isz // 128):
            icc = iof // 128 + j
            for nn in range(2):
                mmd = S.emit("tensor", lambda e, j=j, icc=icc, nn=nn, hq_f=hq_f, bd=(bd0, bd1)[nn]: e.matmul(
                    bd[:, 0:512], hq_f[:, j * 128:(j + 1) * 128],
                    wdq[:, icc * 1024 + nn * 512: icc * 1024 + (nn + 1) * 512],
                    start=(icc == 0), stop=(icc == 21)),
                    waits=([tok_hqf, tp.get(("dnev", pb))] if (j == 0 and ici == 0) else [tok_hqf]), sig="dt")
        tp[("dnmm", q4)] = mmd
        return mmd

    pend_dn = []
    for tt in range(N_FFN_TILES):
        r0 = tt * 128
        pb = tt % 2
        xq_t = xq_t2[:, (tt % 3) * 1024:(tt % 3 + 1) * 1024]
        g_t = g_t2[:, pb * 1024:(pb + 1) * 1024]
        hh = hh2[:, pb * 512:(pb + 1) * 512]
        xs_t = xs_t2
        wprev = tp.get(("ffn", tt - 2))
        txq = [S.dma("mh", xq_t[:, c * 128:(c + 1) * 128], xqdram[r0:r0 + 128, c * 128:(c + 1) * 128],
                     waits=[tp.get(("ffn", tt - 3))], transpose=True) for c in range(8)]
        for ici in range(6):
            iof, isz = IC_OF[ici], IC_SZ[ici]
            p2 = ici % 2
            q4 = (tt * 6 + ici) % 5
            bg, bu = GU[p2]
            gup = gup2[:, p2 * 1024:(p2 + 1) * 1024]
            hq_t = hq_t2[:, q4 * 512:(q4 + 1) * 512]
            hq_f = hq_f2[:, q4 * 512:(q4 + 1) * 512]
            mmg = None
            for kc in range(8):
                mmg = S.emit("tensor", lambda e, kc=kc, iof=iof, isz=isz, xq_t=xq_t, bg=bg: e.matmul(
                    bg[:, 0:isz], xq_t[:, kc * 128:(kc + 1) * 128], wgq[:, kc * IFF + iof: kc * IFF + iof + isz],
                    start=(kc == 0), stop=(kc == 7)),
                    waits=([txq, tp.get(("guev", p2))] if kc == 0 else []), sig="dt")
                mmg = S.emit("tensor", lambda e, kc=kc, iof=iof, isz=isz, xq_t=xq_t, bu=bu: e.matmul(
                    bu[:, 0:isz], xq_t[:, kc * 128:(kc + 1) * 128], wuq[:, kc * IFF + iof: kc * IFF + iof + isz],
                    start=(kc == 0), stop=(kc == 7)), sig="dt")
            if len(pend_dn) >= 3:
                emit_dn(pend_dn.pop(0))
            tsg = S.emit("scalar", lambda e, isz=isz, gup=gup, bg=bg: e.activation(gup[:, 0:isz], bg[:, 0:isz], AF.Silu),
                         waits=[mmg, tp.get(("guth", p2))], sig="dc")
            tup = S.emit("vector", lambda e, isz=isz, gup=gup, bu=bu: e.tensor_copy(gup[:, 512:512 + isz], bu[:, 0:isz]),
                         waits=[mmg, tp.get(("guth", p2))], sig="dv")
            tp[("guev", p2)] = [tsg, tup]
            th = S.emit("vector", lambda e, isz=isz, gup=gup, hh=hh: e.tensor_tensor(
                hh[:, 0:isz], gup[:, 0:isz], gup[:, 512:512 + isz], OP.mult),
                waits=[tsg, tup, tp.get(("dbgh", pb))], sig="dv")
            tp[("guth", p2)] = th
            thq, _ = emit_quant(hh[:, 0:isz], hq_t[:, 0:isz], isz, th, "q4", fast=True,
                                dst_war=[t for t in [tp.get(("dbghq", q4))] if t] + (tp.get(("hqx", q4)) or []))
            if DEBUG:
                tp[("dbgh", pb)] = S.dma("mg", dbg["dbg_h"][r0:r0 + 128, iof:iof + isz], hh[:, 0:isz],
                                         waits=[th, thq])
                tp[("dbghq", q4)] = S.dma("mg", dbg["dbg_hq"][r0:r0 + 128, iof:iof + isz], hq_t[:, 0:isz],
                                          waits=[thq])
            tok_hqf = [S.dma("ms", hq_f[:, j * 128:(j + 1) * 128], hq_t[:, j * 128:(j + 1) * 128],
                             waits=[thq, tp.get(("dnmm", q4))], transpose=True)
                       for j in range(isz // 128)]
            tp[("hqx", q4)] = tok_hqf
            pend_dn.append((tt, ici, q4, tok_hqf))
        for p_ in pend_dn:
            mmd = emit_dn(p_)
        pend_dn = []
        bd0, bd1 = DN[pb]
        tg1 = S.emit("scalar", lambda e, g_t=g_t, bd0=bd0: e.activation(g_t[:, 0:512], bd0[:, 0:512], AF.Silu),
                     waits=[mmd], sig="dc")
        tg2 = S.emit("scalar", lambda e, g_t=g_t, bd1=bd1: e.activation(g_t[:, 512:1024], bd1[:, 0:512], AF.Silu),
                     waits=[mmd], sig="dc")
        tp[("dnev", pb)] = [tg1, tg2]
        txl = S.dma("mg", xs_t[:, 0:1024], dr["x"][r0:r0 + 128, :], waits=[wprev, tp.get("xsfree")])
        tsl = S.dma("mg", xs_t[:, 1024:2048], sdram[r0:r0 + 128, :], waits=[wprev, tp.get("xsfree")])
        to1 = S.emit("vector", lambda e, g_t=g_t, xs_t=xs_t: e.tensor_tensor(
            g_t[:, :], xs_t[:, 1024:2048], g_t[:, :], OP.mult), waits=[tg1, tg2, tsl], sig="dv")
        to2 = S.emit("vector", lambda e, g_t=g_t, xs_t=xs_t: e.tensor_tensor(
            g_t[:, :], xs_t[:, 0:1024], g_t[:, :], OP.add), waits=[to1, txl], sig="dv")
        tp["xsfree"] = to2
        tp[("ffn", tt)] = S.dma("mg", y[r0:r0 + 128, :], g_t[:, :], waits=[to2])
    S.barrier()

    return _finish(nc, S, ctx)


def _finish(nc, S, ctx):
    sems = {}
    for s in ALLSEMS:
        t = nc.semaphore(s)
        h_ = t.__enter__()
        ctx.append(t)
        sems[s] = h_
    blk = nc.Block()
    b = blk.__enter__()
    ctx.append(blk)

    @b.tensor
    def _(eng):
        S.run("tensor", eng, sems)

    @b.vector
    def _(eng):
        S.run("vector", eng, sems)

    @b.scalar
    def _(eng):
        S.run("scalar", eng, sems)

    @b.gpsimd
    def _(eng):
        S.run("gpsimd", eng, sems)

    @b.sync
    def _(eng):
        S.run("sync", eng, sems)

    blk.__exit__(None, None, None)
    ctx.pop()
    return nc, ctx


_CACHE = {}


def kernel(**inputs):
    x_full = np.asarray(inputs["L0_attn_hidden_states"], np.float32)
    host, groups = prep_host(inputs)
    if "nc" not in _CACHE:
        nc, ctx = build(groups)
        _CACHE["nc"] = nc
        _CACHE["ctx"] = ctx
    nc = _CACHE["nc"]
    in_maps = []
    for core in range(8):
        m = dict(host)
        m["x"] = np.ascontiguousarray(x_full[core])
        in_maps.append(m)
    res = run_bass_kernel_spmd(nc, in_maps, core_ids=list(range(8)))
    return np.stack([np.asarray(r["y"], np.float32) for r in res.results], axis=0)


# version 10
# speedup vs baseline: 1.0497x; 1.0497x over previous
import numpy as np
import ml_dtypes
import concourse.bass as bass
import concourse.mybir as mybir
from concourse.bass_utils import run_bass_kernel_spmd
from concourse.alu_op_type import AluOpType as OP

F32 = mybir.dt.float32
BF16 = mybir.dt.bfloat16
I32 = mybir.dt.int32
U8 = mybir.dt.uint8
AF = mybir.ActivationFunctionType
AX = mybir.AxisListType.X

L, D, H = 4096, 1024, 4
TMTOK = 512
NMEG = L // TMTOK
NT = TMTOK // 128
W = TMTOK
WP = W + 64
IFF = 2816
NG_S, NG_L, NG_W = 1, 8, 16
NGF = NG_S + NG_L + NG_W
RW = W + 64

DEBUG = False
SUB56 = 1
N_MEG_BUILD = NMEG
N_FFN_TILES = L // 128
PH = 99

ENGINES = ("tensor", "vector", "scalar", "gpsimd", "sync")
ESEM = {"tensor": "dt", "vector": "dv", "scalar": "dc", "gpsimd": "dg", "sync": "dy"}
NPOOL = 8
NEPOOL = 12
DMAPOOL = {q: [f"{q}{i}" for i in range(NPOOL)] for q in ("ms", "mg", "mh")}
ENGPOOL = {b: [f"{b}{i}" for i in range(NEPOOL)] for b in ESEM.values()}
ALLSEMS = [s for q in DMAPOOL.values() for s in q] + [s for p in ENGPOOL.values() for s in p]
QENG = {"ms": "sync", "mg": "gpsimd", "mh": "scalar"}


class Sch:
    def __init__(self):
        self.ops = {e: [] for e in ENGINES}
        self.cnt = {}

    def qtot(self, q):
        return [(sem, self.cnt[sem]) for sem in DMAPOOL[q] if self.cnt.get(sem, 0) > 0]

    def emit(self, eng, fn, waits=(), sig=None, amt=1):
        w = []
        for t in waits:
            if t is None:
                continue
            if isinstance(t, list):
                w.extend(t)
            else:
                w.append(t)
        tok = None
        if sig is not None:
            if sig in ENGPOOL:
                if not hasattr(self, "erot"):
                    self.erot = {}
                self.erot[sig] = (self.erot.get(sig, -1) + 1) % NEPOOL
                sig = ENGPOOL[sig][self.erot[sig]] if False else f"{sig}{self.erot[sig]}"
            self.cnt[sig] = self.cnt.get(sig, 0) + amt
            tok = (sig, self.cnt[sig])
        self.ops[eng].append((fn, w, sig, amt))
        return tok

    def dma(self, q, dst, src, waits=(), **kw):
        return self.dma_burst(q, [(dst, src, kw)], waits=waits)

    def dma_burst(self, q, items, waits=()):
        if not hasattr(self, "rot"):
            self.rot = {}
        self.rot[q] = (self.rot.get(q, -1) + 1) % NPOOL
        sem = DMAPOOL[q][self.rot[q]]
        eng = QENG[q]
        prev = self.cnt.get(sem, 0)
        w = [t for t in waits if t is not None]
        if prev > 0:
            w = w + [(sem, prev)]
        tok = None
        for i, (dst, src, kw) in enumerate(items):
            tok = self.emit(eng, lambda e, d=dst, s=src, kw=kw: e.dma_start(out=d, in_=s, **kw),
                            waits=(w if i == 0 else ()), sig=sem, amt=16)
        return tok

    def barrier(self):
        toks = {}
        for e in ENGINES:
            toks[e] = self.emit(e, lambda eng: eng.nop(), sig=ESEM[e])
        dt_ = [(sm, self.cnt[sm]) for sm in ALLSEMS if self.cnt.get(sm, 0) > 0]
        for e in ENGINES:
            self.emit(e, None, waits=[toks[o] for o in ENGINES if o != e] + dt_)

    def run(self, eng_name, eng, sems):
        seen = {}
        for fn, wl, sig, amt in self.ops[eng_name]:
            for s, v in wl:
                if seen.get(s, -1) < v:
                    eng.wait_ge(sems[s], v)
                    seen[s] = v
            ins = fn(eng) if fn is not None else None
            if sig is not None:
                ins.then_inc(sems[sig], amt)


class Arena:
    def __init__(self, tile, total, name):
        self.tile, self.total, self.name = tile, total, name
        self.blocks = {}
        self.hw = 0

    def alloc(self, key, size):
        assert key not in self.blocks, key
        taken = sorted(self.blocks.values())
        off = 0
        for (o, s) in taken:
            if off + size <= o:
                break
            off = o + s
        assert off + size <= self.total, f"{self.name} OOM: {key}({size})"
        self.blocks[key] = (off, size)
        self.hw = max(self.hw, off + size)
        return self.tile[:, off:off + size]

    def free(self, *keys):
        for k in keys:
            del self.blocks[k]


def prep_host(inp):
    f32 = np.float32
    bf = ml_dtypes.bfloat16
    out = {}
    out["qwT"] = np.ascontiguousarray(np.asarray(inp["L0_attn_q_w"], f32).T)
    out["kwT"] = np.ascontiguousarray(np.asarray(inp["L0_attn_k_w"], f32).T)
    out["vwT"] = np.ascontiguousarray(np.asarray(inp["L0_attn_v_w"], f32).T)
    bsp = np.zeros((D, 128), f32)
    for h in range(H):
        bsp[:, 32 * h] = np.asarray(inp["L0_attn_b_w"], f32)[h, :]
    out["bwTsp"] = bsp
    cds = np.zeros((3, 8, 4, 128, 32), f32)
    for ti, nm in enumerate(("convq_w", "convk_w", "convv_w")):
        cw = np.asarray(inp["L0_attn_" + nm], f32)
        for c in range(8):
            for g in range(4):
                d_ = cw[c * 128:(c + 1) * 128, g]
                for i4 in range(4):
                    cds[ti, c, g, 32 * i4:32 * i4 + 32, :] = np.diag(d_[32 * i4:32 * i4 + 32])
    out["cd"] = cds
    firs = [np.asarray(inp["L0_attn_fir_s"], f32).reshape(D, -1),
            np.asarray(inp["L0_attn_fir_l"], f32).reshape(D, -1),
            np.asarray(inp["L0_attn_fir_wd"], f32).reshape(D, -1)]
    groups = []
    for fi, filt in enumerate(firs):
        K = filt.shape[1]
        for Gf in range((K + 3) // 4):
            groups.append((fi, Gf, 64 - 4 * Gf))
    assert len(groups) == NGF
    fd = np.zeros((8, 4, NGF, 128, 32), f32)
    for gi, (fi, Gf, _t) in enumerate(groups):
        filt = firs[fi]
        K = filt.shape[1]
        for c in range(8):
            for s4 in range(4):
                for g in range(4):
                    k = K - 1 - 4 * Gf + g - 3
                    if 0 <= k < K:
                        d_ = filt[c * 128 + s4 * 32: c * 128 + s4 * 32 + 32, k]
                        fd[c, s4, gi, 32 * g:32 * g + 32, :] = np.diag(d_)
    out["fird"] = fd
    g1T = np.zeros((1152, 2048), f32)
    g1T[:1084, :] = np.asarray(inp["L0_attn_g1_w"], f32).T
    out["g1Tp"] = g1T
    out["g1b"] = np.ascontiguousarray(np.asarray(inp["L0_attn_g1_b"], f32).reshape(16, 128).T)
    out["g2T"] = np.ascontiguousarray(np.asarray(inp["L0_attn_g2_w"], f32).T)
    out["g2b"] = np.asarray(inp["L0_attn_g2_b"], f32).reshape(1, 20)
    out["vbias"] = np.asarray(inp["L0_attn_value_bias"], f32).reshape(1, 4)
    out["ltemp"] = np.asarray(inp["L0_attn_log_temp"], f32).reshape(1, 4)
    out["flog"] = np.asarray(inp["L0_attn_floor_logit"], f32).reshape(1, 4)
    out["onorm_rep"] = np.ascontiguousarray(np.tile(np.asarray(inp["L0_attn_onorm_w"], f32).reshape(1, 256), (128, 4)))
    out["owT"] = np.ascontiguousarray(np.asarray(inp["L0_attn_o_w"], f32).T)
    out["wg"] = np.ascontiguousarray(np.asarray(inp["L0_ffn_w_gate"], f32))
    out["wu"] = np.ascontiguousarray(np.asarray(inp["L0_ffn_w_up"], f32))
    out["wdn"] = np.ascontiguousarray(np.asarray(inp["L0_ffn_w_down"], f32))
    out["eye128"] = np.eye(128, dtype=bf)
    out["ident32"] = np.eye(128, dtype=f32)
    out["eye4"] = np.tile(np.eye(128, dtype=f32), (1, 4)).astype(bf)
    out["allones"] = np.ones((128, 128), dtype=bf)
    out["zeros128"] = np.zeros((128, 128), dtype=bf)
    oc = np.zeros((128, 256), f32)
    oc[:, 127] = 1
    out["onescol"] = oc.astype(bf)
    r = np.arange(128)
    out["triuI"] = (r[:, None] <= r[None, :]).astype(bf)
    bdm = (r[:, None] // 32 == r[None, :] // 32) & (r[:, None] > r[None, :])
    out["negbd"] = (-bdm.astype(f32)).astype(bf)
    out["blkLow"] = ((r[:, None] // 32) > (r[None, :] // 32)).astype(bf)
    out["blkUp"] = ((r[:, None] // 32) < (r[None, :] // 32)).astype(bf)
    hc4 = np.zeros((8, 4, 128, 16), f32)
    for c in range(8):
        for slot in range(4):
            hc4[c, slot, :, 4 * slot + c // 2] = 1.0
    out["headcol4"] = hc4.astype(bf)
    imask = np.zeros((1, 20), f32)
    imask[0, 4::5] = 1.0
    out["i4mask"] = imask
    return out, groups


def build(groups):
    global PH, SUB56
    nc = bass.Bass("TRN2", target_bir_lowering=False)
    S = Sch()
    dr = {}

    def din(name, shape, dt=F32):
        dr[name] = nc.dram_tensor(name, list(shape), dt, kind="ExternalInput").ap()

    din("x", (L, D))
    for nm in ("qwT", "kwT", "vwT"):
        din(nm, (D, D))
    din("bwTsp", (D, 128))
    din("cd", (3, 8, 4, 128, 32))
    din("fird", (8, 4, NGF, 128, 32))
    din("g1Tp", (1152, 2048))
    din("g1b", (128, 16))
    din("g2T", (2048, 20))
    din("g2b", (1, 20)); din("vbias", (1, 4)); din("ltemp", (1, 4)); din("flog", (1, 4))
    din("onorm_rep", (128, D))
    din("owT", (D, D))
    din("wg", (IFF, D)); din("wu", (IFF, D)); din("wdn", (D, IFF))
    for nm in ("eye128", "allones", "triuI", "negbd", "blkLow", "blkUp", "zeros128"):
        din(nm, (128, 128), BF16)
    din("ident32", (128, 128), F32)
    din("eye4", (128, 512), BF16)
    din("onescol", (128, 256), BF16)
    din("headcol4", (8, 4, 128, 16), BF16)
    din("i4mask", (1, 20))

    y = nc.dram_tensor("y", [L, D], F32, kind="ExternalOutput").ap()
    vdram = nc.dram_tensor("vdram", [D, 128 + L], BF16, kind="Internal").ap()
    xqdram = nc.dram_tensor("xqdram", [L, D], BF16, kind="Internal").ap()
    sdram = nc.dram_tensor("sdram", [L, D], F32, kind="Internal").ap()
    dbg = {}
    if DEBUG:
        for nm, shape in [("dbg_xn", (L, D)), ("dbg_q", (D, L)), ("dbg_k", (D, L)),
                          ("dbg_v", (D, L)), ("dbg_beta", (128, L)),
                          ("dbg_qh", (D, L)), ("dbg_kb", (D, L)), ("dbg_negw", (D, L)),
                          ("dbg_u", (L, D)), ("dbg_delta", (L, D)),
                          ("dbg_short", (L, D)), ("dbg_long", (L, D)), ("dbg_wide", (L, D)),
                          ("dbg_stat", (L, 64)), ("dbg_logits", (L, 20)),
                          ("dbg_otm", (L, D)), ("dbg_xq", (L, D)),
                          ("dbg_hq", (L, IFF)), ("dbg_h", (L, IFF)), ("dbg_preq", (D, L)), ("dbg_sums", (128, L))]:
            dbg[nm] = nc.dram_tensor(nm, list(shape), F32, kind="ExternalOutput").ap()

    ctx = []

    def sbuf(name, shape, dt):
        t = nc.sbuf_tensor(name, list(shape), dt)
        h_ = t.__enter__()
        ctx.append(t)
        return h_

    def psum(name, shape, dt=F32):
        t = nc.psum_tensor(name, list(shape), dt)
        h_ = t.__enter__()
        ctx.append(t)
        return h_

    AB = Arena(sbuf("AB", [128, 79360], BF16), 79360, "AB")
    AFp = Arena(sbuf("AFv", [128, 10240], F32), 10240, "AFp")
    AI = Arena(sbuf("AIv", [128, 2816], I32), 2816, "AI")
    AU = Arena(sbuf("AUv", [128, 1408], U8), 1408, "AU")

    PS_S = [psum(f"psS{h}", [128, 512]) for h in range(H)]
    PW = [psum(f"psW{i}", [128, 512]) for i in range(4)]

    C = {}
    for nm, w_ in [("eye128", 128), ("allones", 128), ("triuI", 128), ("negbd", 128),
                   ("blkLow", 128), ("blkUp", 128), ("zeros128", 128), ("eye4", 512), ("onescol", 256)]:
        C[nm] = AB.alloc(nm, w_)
    C["ident32"] = AFp.alloc("ident32", 128)
    C["onorm_rep"] = AFp.alloc("onorm_rep", D)
    C["g1b"] = AFp.alloc("g1b", 16)
    C["cd"] = AB.alloc("cd", 3 * 8 * 4 * 32)
    C["g2T"] = AB.alloc("g2T", 16 * 20)
    C["hc4"] = AB.alloc("hc4", 8 * 4 * 16)
    sm = AFp.alloc("smallrows", 64)

    tld = []
    for nm in ("eye128", "allones", "triuI", "negbd", "blkLow", "blkUp", "zeros128", "eye4", "onescol"):
        tld.append(S.dma("ms", C[nm][:], dr[nm]))
    tld.append(S.dma("ms", C["ident32"][:], dr["ident32"]))
    tld.append(S.dma("ms", C["onorm_rep"][:], dr["onorm_rep"]))
    tld.append(S.dma("ms", C["g1b"][:], dr["g1b"]))
    tld.append(S.dma("mg", C["cd"][:].rearrange("p (t c g w) -> p t c g w", t=3, c=8, g=4),
                     dr["cd"].rearrange("t c g p w -> p t c g w")))
    tld.append(S.dma("mg", C["g2T"][:].rearrange("p (c w) -> p c w", c=16),
                     dr["g2T"].rearrange("(c p) w -> p c w", p=128)))
    tld.append(S.dma("mg", C["hc4"][:].rearrange("p (c s w) -> p c s w", c=8, s=4),
                     dr["headcol4"].rearrange("c s p w -> p c s w")))
    tld.append(S.dma("ms", sm[0:1, 0:20], dr["g2b"]))
    tld.append(S.dma("ms", sm[0:1, 20:24], dr["vbias"]))
    tld.append(S.dma("ms", sm[0:1, 24:28], dr["ltemp"]))
    tld.append(S.dma("ms", sm[0:1, 28:32], dr["flog"]))
    tld.append(S.dma("ms", sm[0:1, 32:52], dr["i4mask"]))

    zt = AB.alloc("zt", 128)
    tz = S.emit("vector", lambda e: e.memset(zt[:], 0.0), sig="dv")
    for c in range(8):
        S.dma("mg", vdram[c * 128:(c + 1) * 128, 0:128], zt[:], waits=[tz])

    grow = AFp.alloc("grow", 80)
    fscr = AFp.alloc("fscr", 20)
    vb_sp = sm[0:1, 20:24].broadcast_to([1, 4, 5])
    lt_sp = sm[0:1, 24:28].broadcast_to([1, 4, 5])
    fl_sp = sm[0:1, 28:32].broadcast_to([1, 4, 5])
    g3 = lambda ap: ap.rearrange("p (a b) -> p a b", b=5)
    w_all = list(tld)
    t1 = S.emit("vector", lambda e: e.tensor_tensor(g3(grow[0:1, 0:20]), vb_sp, g3(sm[0:1, 32:52]), OP.mult),
                waits=w_all, sig="dv")
    t1 = S.emit("vector", lambda e: e.tensor_tensor(grow[0:1, 0:20], grow[0:1, 0:20], sm[0:1, 0:20], OP.add),
                waits=[t1], sig="dv")
    t2 = S.emit("scalar", lambda e: e.activation(g3(grow[0:1, 20:40]), lt_sp, AF.Exp, scale=-1.0), waits=w_all, sig="dc")
    tf1 = S.emit("scalar", lambda e: e.activation(g3(fscr[0:1, :]), fl_sp, AF.Sigmoid), waits=w_all, sig="dc")
    t4 = S.emit("vector", lambda e: e.tensor_scalar(grow[0:1, 40:60], fscr[0:1, :], -4.0, 1.0, OP.mult, OP.add),
                waits=[tf1], sig="dv")
    tf2 = S.emit("vector", lambda e: e.tensor_tensor(grow[0:1, 60:80], fscr[0:1, :], sm[0:1, 32:52], OP.mult),
                 waits=[tf1], sig="dv")
    tf3 = S.emit("vector", lambda e: e.scalar_tensor_tensor(grow[0:1, 60:80], grow[0:1, 60:80], -1.0, fscr[0:1, :],
                                                            OP.mult, OP.add), waits=[tf2], sig="dv")
    growb = AB.alloc("growb", 80)
    tcast = S.emit("vector", lambda e: e.tensor_copy(growb[0:1, :], grow[0:1, :]), waits=[t1, t2, t4, tf3], sig="dv")
    gconst = AFp.alloc("gconst", 80)
    tgc = None
    for i in range(4):
        tmm = S.emit("tensor", lambda e, i=i: e.matmul(PW[3][:, 0:20], C["allones"][0:1, :],
                                                       growb[0:1, i * 20:(i + 1) * 20], start=True, stop=True),
                     waits=[tcast, tgc], sig="dt")
        tgc = S.emit("vector", lambda e, i=i: e.tensor_copy(gconst[:, i * 20:(i + 1) * 20], PW[3][:, 0:20]),
                     waits=[tmm], sig="dv")
    AFp.free("grow", "fscr")
    AB.free("growb", "zt")
    S.barrier()

    xn_fm = AB.alloc("xn_fm", 8 * W)
    qh_fm = AB.alloc("qh_fm", 8 * W)
    kh_fm = AB.alloc("kh_fm", 8 * W)
    vf_fm = AB.alloc("vf_fm", 8 * W)
    S_bf = AB.alloc("S_bf", H * 512)
    tails = AB.alloc("tails", 3 * 8 * 64)
    statF = AB.alloc("statF", NT * 128)
    S32 = AFp.alloc("S32", H * 512)

    tsz = S.emit("vector", lambda e: e.memset(statF[:], 0.0), sig="dv")

    TP = dict(tok_prev=None)
    tp = {}

    def fmsl(tile, c, tt):
        return tile[:, c * W + tt * 128: c * W + tt * 128 + 128]

    def bcast128(constv):
        return constv.rearrange("p (x w) -> p x w", x=1).broadcast_to([128, 4, 128]).rearrange("p a w -> p (a w)")

    def emit_quant(src, dst, Fn, wtok, key, scratch=None, dst_war=None, fast=False):
        nblk = Fn // 16
        nb2 = (nblk + 15) // 16 * 16
        qbm = AFp.alloc(key + "bm", 2 * nb2)
        qv = AFp.alloc(key + "v", Fn)
        qA = AFp.alloc(key + "A", Fn)
        qBC = AFp.alloc(key + "BC", Fn)
        qt2 = scratch[:, 0:Fn] if scratch is not None else AFp.alloc(key + "t2", Fn)
        qi = AI.alloc(key + "i", Fn)
        qi2 = AI.alloc(key + "i2", Fn)
        qm = AU.alloc(key + "m", Fn)
        t0 = S.emit("vector", lambda e: e.tensor_reduce(qbm[:, 0:nblk], src.rearrange("p (b k) -> p b k", k=16),
                                                        axis=AX, op=OP.max, apply_absolute_value=True),
                    waits=[wtok], sig="dv")
        t1 = S.emit("vector", lambda e: e.tensor_scalar(qbm[:, nb2:nb2 + nblk], qbm[:, 0:nblk], 1.0 / 6.0, 1e-12,
                                                        OP.mult, OP.max), waits=[t0], sig="dv")
        t2 = S.emit("vector", lambda e: e.reciprocal(qbm[:, 0:nblk], qbm[:, nb2:nb2 + nblk]), waits=[t1], sig="dv")
        rv = qbm[:, 0:nblk].broadcast_to([128, nblk, 16])
        sv = qbm[:, nb2:nb2 + nblk].broadcast_to([128, nblk, 16])
        sview = src.rearrange("p (b k) -> p b k", k=16)
        t3 = S.emit("vector", lambda e: e.tensor_tensor(qv[:, :].rearrange("p (b k) -> p b k", k=16), sview, rv, OP.mult),
                    waits=[t2], sig="dv")
        t4 = S.emit("vector", lambda e: e.tensor_scalar(qi[:, :], qv[:, :], 2.0, None, OP.mult), waits=[t3], sig="dv")
        t5 = S.emit("vector", lambda e: e.tensor_scalar(qA[:, :], qi[:, :], 0.5, None, OP.mult), waits=[t4], sig="dv")
        t7 = S.emit("vector", lambda e: e.tensor_scalar(qi2[:, :], qv[:, :], 4.0, -4.0, OP.min, OP.max),
                    waits=[t3], sig="dv")
        qi3 = AI.alloc(key + "i3", Fn) if fast else qi
        t8 = S.emit("vector", lambda e, qi3=qi3: e.tensor_scalar(qi3[:, :], qv[:, :], 0.1, None, OP.mult),
                    waits=([t3] if fast else [t3, t5]), sig="dv")
        t10 = S.emit("vector", lambda e, qi3=qi3: e.scalar_tensor_tensor(qBC[:, :], qi3[:, :], 2.0, qi2[:, :],
                                                                         OP.mult, OP.add),
                     waits=[t7, t8], sig="dv")
        if fast:
            t11a = S.emit("scalar", lambda e: e.activation(qt2[:, :], qv[:, :], AF.Square),
                          waits=[t3], sig="dc")
        else:
            t11a = S.emit("vector", lambda e: e.tensor_tensor(qt2[:, :], qv[:, :], qv[:, :], OP.mult),
                          waits=[t3], sig="dv")
        t11 = S.emit("vector", lambda e: e.tensor_scalar(qm[:, :], qt2[:, :], 4.0, None, OP.is_gt),
                     waits=[t11a], sig="dv")
        t12 = S.emit("vector", lambda e: e.copy_predicated(qA[:, :], qm[:, :], qBC[:, :]),
                     waits=[t10, t11, t5], sig="dv")
        qAv = qA[:, :].rearrange("p (b k) -> p b k", k=16)
        t13 = S.emit("vector", lambda e: e.tensor_tensor(dst.rearrange("p (b k) -> p b k", k=16), qAv, sv, OP.mult),
                     waits=[t12, dst_war], sig="dv")
        AFp.free(key + "bm", key + "v", key + "A", key + "BC")
        if scratch is None:
            AFp.free(key + "t2")
        AI.free(key + "i", key + "i2")
        if fast:
            AI.free(key + "i3")
        AU.free(key + "m")
        return t13, t0

    for mt in range(N_MEG_BUILD):
        mb = mt * TMTOK
        first = (mt == 0)

        x_tm = AFp.alloc("x_tm", 1024)
        xn32 = AFp.alloc("xn32", 1024)
        lnS = AFp.alloc("lnS", 16)
        tmpS = AFp.alloc("tmpS", 1024)
        xn_tt = AB.alloc("xn_tt", 1024)
        xq_tm = AB.alloc("xq_tm", 1024)

        for tt in range(NT):
            r0 = mb + tt * 128
            txl = S.dma("mg", x_tm[:, :], dr["x"][r0:r0 + 128, :],
                        waits=[tp.get("xtmfree"), tp.get("mttail"), tp.get("sdma")])
            t1 = S.emit("scalar", lambda e: e.activation(tmpS[:, :], x_tm[:, :], AF.Square, accum_out=lnS[:, 0:1]),
                        waits=[txl], sig="dc")
            t2 = S.emit("vector", lambda e: e.tensor_reduce(lnS[:, 1:2], x_tm[:, :], axis=AX, op=OP.add),
                        waits=[txl], sig="dv")
            t3 = S.emit("vector", lambda e: e.tensor_scalar(lnS[:, 2:3], lnS[:, 1:2], 1.0 / D, None, OP.mult),
                        waits=[t2], sig="dv")
            t4 = S.emit("vector", lambda e: e.tensor_tensor(lnS[:, 3:4], lnS[:, 2:3], lnS[:, 2:3], OP.mult),
                        waits=[t3], sig="dv")
            t5 = S.emit("vector", lambda e: e.scalar_tensor_tensor(lnS[:, 4:5], lnS[:, 0:1], 1.0 / D, lnS[:, 3:4],
                                                                   OP.mult, OP.subtract), waits=[t1, t4], sig="dv")
            t6 = S.emit("vector", lambda e: e.tensor_scalar(lnS[:, 5:6], lnS[:, 4:5], 1e-5, None, OP.add),
                        waits=[t5], sig="dv")
            t7 = S.emit("vector", lambda e: e.reciprocal(lnS[:, 6:7], lnS[:, 5:6]), waits=[t6], sig="dv")
            t8 = S.emit("scalar", lambda e: e.activation(lnS[:, 7:8], lnS[:, 6:7], AF.Sqrt), waits=[t7], sig="dc")
            t9 = S.emit("vector", lambda e: e.tensor_scalar(lnS[:, 8:9], lnS[:, 2:3], -1.0, None, OP.mult),
                        waits=[t3], sig="dv")
            t10 = S.emit("vector", lambda e: e.tensor_scalar(xn32[:, :], x_tm[:, :], lnS[:, 8:9], lnS[:, 7:8],
                                                             OP.add, OP.mult),
                         waits=[t8, t9, tp.get("qprev"), tp.get("qprev2")], sig="dv")
            tp["xtmfree"] = t10
            t11 = S.emit("vector", lambda e: e.tensor_copy(xn_tt[:, :], xn32[:, :]),
                         waits=[t10, tp.get("xnxpo")], sig="dv")
            tq, tq0 = emit_quant(xn32[:, :], xq_tm[:, :], 1024, t10, "q1", scratch=tmpS,
                                 dst_war=tp.get("xqout"))
            tp["qprev"] = tq0
            if DEBUG:
                tp["qprev2"] = S.dma("mg", dbg["dbg_xn"][r0:r0 + 128, :], xn32[:, :], waits=[t10, tq])
            tp["xqout"] = S.dma("mg", xqdram[r0:r0 + 128, :], xq_tm[:, :], waits=[tq])
            if DEBUG:
                tp["xqout"] = S.dma("mg", dbg["dbg_xq"][r0:r0 + 128, :], xq_tm[:, :],
                                    waits=[tq, tp.get("xqout")])
            for c in range(8):
                S.dma("ms", xn_fm[:, c * W + tt * 128: c * W + tt * 128 + 128],
                      xn_tt[:, c * 128:(c + 1) * 128], waits=[t11, tp.get("xnfree")], transpose=True)
            tp["xnxpo"] = S.qtot("ms")
        tok_xpo = S.qtot("ms")
        AFp.free("x_tm", "xn32", "lnS", "tmpS")
        AB.free("xn_tt")
        tp.pop("xtmfree", None)
        tp.pop("xnxpo", None)

        if PH < 2: continue
        beta_sp = AFp.alloc("beta_sp", W)
        s2q_sp = AFp.alloc("s2q_sp", W)
        s2k_sp = AFp.alloc("s2k_sp", W)
        rkb_sp = AFp.alloc("rkb_sp", W)
        pre_q = AB.alloc("pre_q", 8 * WP)
        pre_k = AB.alloc("pre_k", 8 * WP)
        pre_v = AB.alloc("pre_v", 8 * WP)
        wbuf = AB.alloc("wbuf", 4 * 1024)
        wstg = AB.alloc("wstg", 4 * 2048)
        tok_tail = []
        for ti, pre in enumerate((pre_q, pre_k, pre_v)):
            for c in range(8):
                tl = tails[:, (ti * 8 + c) * 64:(ti * 8 + c) * 64 + 64]
                if first:
                    tok_tail.append(S.emit("vector", lambda e, pre=pre, c=c: e.memset(pre[:, c * WP:c * WP + 64], 0.0),
                                           waits=[tp.get("mttail")], sig="dv"))
                else:
                    tok_tail.append(S.emit("vector", lambda e, pre=pre, c=c, tl=tl: e.tensor_copy(
                        pre[:, c * WP:c * WP + 64], tl), waits=[tp.get("mttail")], sig="dv"))
        tok_proj = {}
        order = [(n, dc) for n in ("q", "k", "v") for dc in range(8)]
        wsel = {"q": ("qwT", pre_q), "k": ("kwT", pre_k), "v": ("vwT", pre_v)}
        for oi, (name, dc) in enumerate(order):
            wnm, pre = wsel[name]
            buf = wbuf[:, (oi % 4) * 1024:(oi % 4) * 1024 + 1024]
            stg = wstg[:, (oi % 4) * 2048:(oi % 4) * 2048 + 2048].bitcast(F32)
            war = tok_proj.get(order[oi - 4]) if oi >= 4 else None
            tls = S.dma("ms", stg.rearrange("p (c w) -> p c w", c=8),
                        dr[wnm].rearrange("(c p) d -> p c d", p=128)[:, :, dc * 128:(dc + 1) * 128],
                        waits=[tp.get(("wcast", oi % 4)), tp.get("mttail"), tp.get("sdma")])
            twl = S.emit("scalar", lambda e, buf=buf, stg=stg: e.activation(buf[:, :], stg[:, :], AF.Identity),
                         waits=[tls, war], sig="dc")
            tp[("wcast", oi % 4)] = twl
            mmt = None
            for kc in range(8):
                mmt = S.emit("tensor", lambda e, kc=kc, buf=buf, oi=oi: e.matmul(
                    PW[oi % 2][:, 0:512], buf[:, kc * 128:(kc + 1) * 128], xn_fm[:, kc * W:(kc + 1) * W],
                    start=(kc == 0), stop=(kc == 7)),
                    waits=[twl, tok_xpo, tp.get(("projps", oi % 2))], sig="dt")
            ev = S.emit("scalar", lambda e, pre=pre, dc=dc, oi=oi: e.activation(
                pre[:, dc * WP + 64: dc * WP + 64 + W], PW[oi % 2][:, 0:512], AF.Identity),
                waits=[mmt] + tok_tail, sig="dc")
            tok_proj[(name, dc)] = ev
            tp[("projps", oi % 2)] = ev
        twb = S.dma("mg", wbuf[:, 0:1024].rearrange("p (c w) -> p c w", c=8),
                    dr["bwTsp"].rearrange("(c p) d -> p c d", p=128), waits=[tok_proj[("v", 6)]])
        bmm = None
        for kc in range(8):
            bmm = S.emit("tensor", lambda e, kc=kc: e.matmul(
                PW[2][:, 0:512], wbuf[:, kc * 128:(kc + 1) * 128], xn_fm[:, kc * W:(kc + 1) * W],
                start=(kc == 0), stop=(kc == 7)), waits=[twb, tp.get("ps2")], sig="dt")
        tok_beta = S.emit("scalar", lambda e: e.activation(beta_sp[:, :], PW[2][:, 0:512], AF.Sigmoid),
                          waits=[bmm], sig="dc")
        tp["ps2"] = tok_beta
        if DEBUG:
            for c in range(8):
                S.dma("mg", dbg["dbg_preq"][c * 128:(c + 1) * 128, mb:mb + W],
                      pre_q[:, c * WP + 64:c * WP + 64 + W], waits=[tok_proj[("q", c)]])
        tok_savetail = []
        for ti, (name, pre) in enumerate((("q", pre_q), ("k", pre_k), ("v", pre_v))):
            for c in range(8):
                tl = tails[:, (ti * 8 + c) * 64:(ti * 8 + c) * 64 + 64]
                tok_savetail.append(S.emit("vector", lambda e, pre=pre, c=c, tl=tl: e.tensor_copy(
                    tl, pre[:, c * WP + W: c * WP + W + 64]), waits=[tok_proj[(name, c)]], sig="dv"))

        if PH < 3: continue
        cdv = C["cd"][:].rearrange("p (t c g w) -> p t c g w", t=3, c=8, g=4)
        tok_conv = {}
        for wi, (name, pre, outt) in enumerate((("q", pre_q, qh_fm), ("k", pre_k, kh_fm), ("v", pre_v, vf_fm))):
            for c in range(8):
                mmt = S.emit("tensor", lambda e, c=c, pre=pre: e.matmul(
                    PW[3][:, 0:512], C["zeros128"][:, :], pre[:, c * WP + 64: c * WP + 64 + W],
                    start=True, stop=False), waits=[tok_proj[(name, c)], tp.get("ps3")], sig="dt")
                for s4 in range(4):
                    for g in range(4):
                        mmt = S.emit("tensor", lambda e, wi=wi, c=c, g=g, s4=s4, pre=pre: e.matmul(
                            PW[3][32 * s4:32 * s4 + 32, 0:512],
                            cdv[32 * s4:32 * s4 + 32, wi, c, g, :],
                            pre[32 * s4:32 * s4 + 32, c * WP + 61 + g: c * WP + 61 + g + W],
                            start=False, stop=False,
                            tile_position=(32 * s4, 32 * s4)),
                            waits=[tok_proj[(name, c)]], sig="dt")
                mmt = S.emit("tensor", lambda e, c=c, pre=pre: e.matmul(
                    PW[3][:, 0:512], C["zeros128"][:, :], pre[:, c * WP + 64: c * WP + 64 + W],
                    start=False, stop=True), sig="dt")
                tok_conv[(name, c)] = S.emit("scalar", lambda e, c=c, outt=outt: e.activation(
                    outt[:, c * W:(c + 1) * W], PW[3][:, 0:512], AF.Silu), waits=[mmt], sig="dc")
                tp["ps3"] = tok_conv[(name, c)]
        if DEBUG:
            for nm, key in (("dbg_q", "q"), ("dbg_k", "k"), ("dbg_v", "v")):
                tile = {"q": qh_fm, "k": kh_fm, "v": vf_fm}[key]
                for c in range(8):
                    S.dma("mg", dbg[nm][c * 128:(c + 1) * 128, mb:mb + W], tile[:, c * W:(c + 1) * W],
                          waits=[tok_conv[(key, c)]])
            S.dma("mg", dbg["dbg_beta"][:, mb:mb + W], beta_sp[:, :], waits=[tok_beta])
        if DEBUG:
            S.barrier()
        AB.free("pre_q", "pre_k", "pre_v", "wbuf", "wstg")

        if PH < 4: continue
        k_tm = AB.alloc("k_tm", NT * 1024)
        v_tm = AB.alloc("v_tm", NT * 1024)
        vbT = AB.alloc("vbT", NT * 1024)
        kbT = AB.alloc("kbT", NT * 1024)
        khT = AB.alloc("khT", NT * 1024)
        kb_fm = AB.alloc("kb_fm", 8 * W)
        tok_vtm = {}
        tok_ktm = {}
        for c in range(8):
            S.dma("mg", vdram[c * 128:(c + 1) * 128, 128 + mb:128 + mb + W], vf_fm[:, c * W:(c + 1) * W],
                  waits=[tok_conv[("v", c)]])
            for tt in range(NT):
                tok_vtm[(tt, c)] = S.dma("ms", v_tm[:, tt * 1024 + c * 128: tt * 1024 + (c + 1) * 128],
                                         vf_fm[:, c * W + tt * 128: c * W + tt * 128 + 128],
                                         waits=[tok_conv[("v", c)], tp.get("mttail")], transpose=True)
                tok_ktm[(tt, c)] = S.dma("mh", k_tm[:, tt * 1024 + c * 128: tt * 1024 + (c + 1) * 128],
                                         kh_fm[:, c * W + tt * 128: c * W + tt * 128 + 128],
                                         waits=[tok_conv[("k", c)], tp.get("mttail")], transpose=True)
        tok_vdr_w = S.qtot("mg")
        tok_ktm_all = S.qtot("mh")
        tok_vtm_all = S.qtot("ms")

        if PH < 5: continue
        sq_t = AB.alloc("sq_t", 2 * W)
        tok_s2 = {}
        for ti, (name, src, dstsp) in enumerate((("q", qh_fm, s2q_sp), ("k", kh_fm, s2k_sp))):
            mmt = None
            sqt = {}
            for c in range(8):
                half = c % 2
                war = tok_s2.get((name, c - 2))
                if war is None and name == "k":
                    war = tok_s2.get(("q", 6 + c))
                sqt[c] = S.emit("scalar", lambda e, c=c, src=src, half=half: e.activation(
                    sq_t[:, half * W:(half + 1) * W], src[:, c * W:(c + 1) * W], AF.Square),
                    waits=[tok_conv[(name, c)], war], sig="dc")
                h = c // 2
                mmt = S.emit("tensor", lambda e, c=c, h=h, half=half: e.matmul(
                    PW[2][:, 0:512], C["onescol"][:, 127 - 32 * h:255 - 32 * h], sq_t[:, half * W:(half + 1) * W],
                    start=(c == 0), stop=(c == 7)),
                    waits=[sqt[c], tp.get("ps2")], sig="dt")
                tok_s2[(name, c)] = mmt
            tok_s2[name] = S.emit("vector", lambda e, dstsp=dstsp: e.tensor_copy(dstsp[:, :], PW[2][:, 0:512]),
                                  waits=[mmt], sig="dv")
            tp["ps2"] = tok_s2[name]
        AB.free("sq_t")
        tok_rsq = {}
        for name, sp in (("q", s2q_sp), ("k", s2k_sp)):
            a = S.emit("vector", lambda e, sp=sp: e.tensor_scalar(sp[:, :], sp[:, :], 1e-6, None, OP.add),
                       waits=[tok_s2[name]], sig="dv")
            b = S.emit("vector", lambda e, sp=sp: e.reciprocal(sp[:, :], sp[:, :]), waits=[a], sig="dv")
            tok_rsq[name] = S.emit("scalar", lambda e, sp=sp: e.activation(sp[:, :], sp[:, :], AF.Sqrt),
                                   waits=[b], sig="dc")
        tok_rkb = S.emit("vector", lambda e: e.tensor_tensor(rkb_sp[:, :], s2k_sp[:, :], beta_sp[:, :], OP.mult),
                         waits=[tok_rsq["k"], tok_beta], sig="dv")
        sprd3 = AB.alloc("sprd3", 3 * W)
        tok_sb = {}
        tok_sb["q"] = S.emit("vector", lambda e: e.tensor_copy(sprd3[:, 0:W], s2q_sp[:, :]),
                             waits=[tok_rsq["q"]], sig="dv")
        tok_sb["k"] = S.emit("vector", lambda e: e.tensor_copy(sprd3[:, W:2 * W], s2k_sp[:, :]),
                             waits=[tok_rsq["k"]], sig="dv")
        tok_sb["kb"] = S.emit("vector", lambda e: e.tensor_copy(sprd3[:, 2 * W:3 * W], rkb_sp[:, :]),
                              waits=[tok_rkb], sig="dv")
        tok_scaled = {}
        for c in range(8):
            h = c // 2
            tpos = (96, 0) if h == 3 else None
            bq = S.emit("tensor", lambda e, h=h, tpos=tpos: e.matmul(
                PW[2][:, 0:512], C["allones"][32 * h:32 * h + 1, :],
                sprd3[32 * h:32 * h + 1, 0:W], start=True, stop=True, tile_position=tpos),
                waits=[tok_sb["q"], tp.get("ps2")], sig="dt")
            tok_scaled[("q", c)] = S.emit("vector", lambda e, c=c: e.tensor_tensor(
                qh_fm[:, c * W:(c + 1) * W], qh_fm[:, c * W:(c + 1) * W], PW[2][:, 0:512], OP.mult),
                waits=[bq], sig="dv")
            tp["ps2"] = tok_scaled[("q", c)]
            bk = S.emit("tensor", lambda e, h=h, tpos=tpos: e.matmul(
                PW[3][:, 0:512], C["allones"][32 * h:32 * h + 1, :],
                sprd3[32 * h:32 * h + 1, 2 * W:3 * W], start=True, stop=True, tile_position=tpos),
                waits=[tok_sb["kb"], tp.get("ps3")], sig="dt")
            tok_scaled[("kb", c)] = S.emit("vector", lambda e, c=c: e.tensor_tensor(
                kb_fm[:, c * W:(c + 1) * W], kh_fm[:, c * W:(c + 1) * W], PW[3][:, 0:512], OP.mult),
                waits=[bk], sig="dv")
            bk2 = S.emit("tensor", lambda e, h=h, tpos=tpos: e.matmul(
                PW[3][:, 0:512], C["allones"][32 * h:32 * h + 1, :],
                sprd3[32 * h:32 * h + 1, W:2 * W], start=True, stop=True, tile_position=tpos),
                waits=[tok_sb["k"], tok_scaled[("kb", c)]], sig="dt")
            tok_scaled[("kh", c)] = S.emit("vector", lambda e, c=c: e.tensor_tensor(
                kh_fm[:, c * W:(c + 1) * W], kh_fm[:, c * W:(c + 1) * W], PW[3][:, 0:512], OP.mult),
                waits=[bk2, tok_ktm_all], sig="dv")
            tp["ps3"] = tok_scaled[("kh", c)]
        if DEBUG:
            for c in range(8):
                S.dma("mg", dbg["dbg_qh"][c * 128:(c + 1) * 128, mb:mb + W], qh_fm[:, c * W:(c + 1) * W],
                      waits=[tok_scaled[("q", c)]])
                S.dma("mg", dbg["dbg_kb"][c * 128:(c + 1) * 128, mb:mb + W], kb_fm[:, c * W:(c + 1) * W],
                      waits=[tok_scaled[("kb", c)]])

        if PH < 6: continue
        scT = AFp.alloc("scT", 3 * 128)
        tok_tmprod = {}
        for tt in range(NT):
            tsc = []
            for si, (sp, tok_src) in enumerate(((beta_sp, tok_beta), (s2k_sp, tok_rsq["k"]), (rkb_sp, tok_rkb))):
                tpx = S.emit("tensor", lambda e, sp=sp, tt=tt: e.transpose(
                    PW[2][:, 0:128], sp[:, tt * 128:(tt + 1) * 128], C["ident32"][:, :]),
                    waits=[tok_src, tp.get("ps2")], sig="dt")
                ev = S.emit("vector", lambda e, si=si: e.tensor_copy(scT[:, si * 128:(si + 1) * 128], PW[2][:, 0:128]),
                            waits=[tpx], sig="dv")
                tp["ps2"] = ev
                tsc.append(ev)

            def hview(si):
                return scT[:, si * 128: si * 128 + 128].rearrange("p (h r) -> p h r", h=4)[:, :, 0:1].broadcast_to(
                    [128, 4, 256])
            sl = slice(tt * 1024, (tt + 1) * 1024)
            wv_ = [tok_vtm[(tt, c)] for c in range(8)] + tsc
            wk_ = [tok_ktm[(tt, c)] for c in range(8)] + tsc
            if not SUB56:
                continue
            tok_tmprod[("vbT", tt)] = S.emit("vector", lambda e, sl=sl, tt=tt: e.tensor_tensor(
                vbT[:, sl].rearrange("p (h c) -> p h c", h=4), v_tm[:, sl].rearrange("p (h c) -> p h c", h=4),
                hview(0), OP.mult), waits=wv_, sig="dv")
            tok_tmprod[("kbT", tt)] = S.emit("vector", lambda e, sl=sl, tt=tt: e.tensor_tensor(
                kbT[:, sl].rearrange("p (h c) -> p h c", h=4), k_tm[:, sl].rearrange("p (h c) -> p h c", h=4),
                hview(2), OP.mult), waits=wk_, sig="dv")
            tok_tmprod[("khT", tt)] = S.emit("vector", lambda e, sl=sl, tt=tt: e.tensor_tensor(
                khT[:, sl].rearrange("p (h c) -> p h c", h=4), k_tm[:, sl].rearrange("p (h c) -> p h c", h=4),
                hview(1), OP.mult), waits=wk_, sig="dv")
            tp["ps2"] = tok_tmprod[("khT", tt)]
        AB.free("k_tm", "sprd3")
        AFp.free("scT", "beta_sp", "s2q_sp", "s2k_sp", "rkb_sp")
        S.barrier()

        if PH < 7: continue
        negw_fm = AB.alloc("negw_fm", 8 * W)
        delta_tm = AB.alloc("delta_tm", NT * 1024)
        B_b = AB.alloc("B_b", 512)
        BT_b = AB.alloc("BT_b", 512)
        pw_b = AB.alloc("pw_b", 512)
        pwT_b = AB.alloc("pwT_b", 512)
        P_b = AB.alloc("P_b", 2 * 512)
        invT_b = AB.alloc("invT_b", 512)
        Mt_b = AB.alloc("Mt_b", 512)
        Gnw_b = AB.alloc("Gnw_b", 512)
        GnwT_b = AB.alloc("GnwT_b", 512)
        T2t_b = AB.alloc("T2t_b", 512)
        Ht_b = AB.alloc("Ht_b", 512)
        u_b = AB.alloc("u_b", 1024)
        U0_b = AB.alloc("U0_b", 1024)
        uadj_b = AB.alloc("uadj_b", 1024)
        for tt in range(NT):
            fsc = first and tt == 0
            mmt = None
            for h in range(H):
                for half in range(2):
                    c = 2 * h + half
                    mmt = S.emit("tensor", lambda e, h=h, c=c, tt=tt, half=half: e.matmul(
                        PW[0][:, h * 128:(h + 1) * 128], fmsl(kb_fm, c, tt), fmsl(kh_fm, c, tt),
                        start=(h == 0 and half == 0), stop=(h == 3 and half == 1)),
                        waits=[tok_scaled[("kb", c)], tok_scaled[("kh", c)], tp.get("ps0")], sig="dt")
            tB = S.emit("vector", lambda e: e.tensor_tensor(
                B_b[:, :].rearrange("p (a w) -> p a w", a=4),
                PW[0][:, :].rearrange("p (a w) -> p a w", a=4),
                C["negbd"][:].rearrange("p (x w) -> p x w", x=1).broadcast_to([128, 4, 128]), OP.mult),
                waits=[mmt], sig="dv")
            tp["ps0"] = tB
            tBT = S.emit("vector", lambda e: e.transpose(BT_b[:, :], B_b[:, :]), waits=[tB], sig="dv")
            tPc = S.emit("vector", lambda e: e.tensor_tensor(P_b[:, 0:512], B_b[:, :], C["eye4"][:], OP.add),
                         waits=[tB], sig="dv")
            curP = 0
            tsq_src, tsq_srcT = tB, tBT
            src_pw, src_pwT = B_b, BT_b
            for lvl in range(4):
                mmq = None
                for h in range(H):
                    mmq = S.emit("tensor", lambda e, h=h, s1=src_pwT, s2=src_pw: e.matmul(
                        PW[1][:, h * 128:(h + 1) * 128], s1[:, h * 128:(h + 1) * 128], s2[:, h * 128:(h + 1) * 128],
                        start=(h == 0), stop=(h == 3)), waits=[tsq_src, tsq_srcT, tp.get("ps1")], sig="dt")
                tev = S.emit("vector", lambda e: e.tensor_copy(pw_b[:, :], PW[1][:, :]), waits=[mmq], sig="dv")
                tp["ps1"] = tev
                tevT = S.emit("vector", lambda e: e.transpose(pwT_b[:, :], pw_b[:, :]), waits=[tev], sig="dv")
                mmp = None
                for h in range(H):
                    mmp = S.emit("tensor", lambda e, h=h, curP=curP: e.matmul(
                        PW[2][:, h * 128:(h + 1) * 128], pwT_b[:, h * 128:(h + 1) * 128],
                        P_b[:, curP * 512 + h * 128: curP * 512 + (h + 1) * 128],
                        start=(h == 0), stop=False), waits=[tevT, tPc, tp.get("ps2")], sig="dt")
                for h in range(H):
                    mmp = S.emit("tensor", lambda e, h=h, curP=curP: e.matmul(
                        PW[2][:, h * 128:(h + 1) * 128], C["eye128"][:, :],
                        P_b[:, curP * 512 + h * 128: curP * 512 + (h + 1) * 128],
                        start=False, stop=(h == 3)), sig="dt")
                curP = 1 - curP
                tPc = S.emit("vector", lambda e, curP=curP: e.tensor_copy(P_b[:, curP * 512:(curP + 1) * 512],
                                                                          PW[2][:, :]), waits=[mmp], sig="dv")
                tp["ps2"] = tPc
                if lvl < 3:
                    tsq_src = S.emit("vector", lambda e: e.tensor_copy(B_b[:, :], pw_b[:, :]), waits=[tevT], sig="dv")
                    tsq_srcT = S.emit("vector", lambda e: e.tensor_copy(BT_b[:, :], pwT_b[:, :]), waits=[tevT], sig="dv")
                    src_pw, src_pwT = B_b, BT_b
            tinvT = S.emit("vector", lambda e, curP=curP: e.transpose(invT_b[:, :], P_b[:, curP * 512:(curP + 1) * 512]),
                           waits=[tPc], sig="dv")
            mmu = None
            for h in range(H):
                mmu = S.emit("tensor", lambda e, h=h, tt=tt: e.matmul(
                    PW[h // 2][:, (h % 2) * 256:(h % 2) * 256 + 256], invT_b[:, h * 128:(h + 1) * 128],
                    vbT[:, tt * 1024 + h * 256: tt * 1024 + (h + 1) * 256], start=True, stop=True),
                    waits=[tinvT, tok_tmprod[("vbT", tt)], tp.get("ps0"), tp.get("ps1")], sig="dt")
            tu1 = S.emit("vector", lambda e: e.tensor_copy(u_b[:, 0:512], PW[0][:, :]),
                         waits=[mmu, tp.get("dbgu")], sig="dv")
            tu2 = S.emit("vector", lambda e: e.tensor_copy(u_b[:, 512:1024], PW[1][:, :]), waits=[mmu], sig="dv")
            tp["ps0"] = tu1
            tp["ps1"] = tu2
            mmw = None
            for h in range(H):
                for half in range(2):
                    mmw = S.emit("tensor", lambda e, h=h, half=half, tt=tt: e.matmul(
                        PW[2 + h // 2][:, ((h % 2) * 2 + half) * 128: ((h % 2) * 2 + half) * 128 + 128],
                        kbT[:, tt * 1024 + h * 256 + half * 128: tt * 1024 + h * 256 + half * 128 + 128],
                        invT_b[:, h * 128:(h + 1) * 128], start=True, stop=True),
                        waits=[tinvT, tok_tmprod[("kbT", tt)], tp.get("ps2"), tp.get("ps3")], sig="dt")
            nw_lo = negw_fm[:, 0:4 * W].rearrange("p (c w) -> p c w", c=4)[:, :, tt * 128:tt * 128 + 128]
            nw_hi = negw_fm[:, 4 * W:8 * W].rearrange("p (c w) -> p c w", c=4)[:, :, tt * 128:tt * 128 + 128]
            tw1 = S.emit("vector", lambda e, nw_lo=nw_lo: e.tensor_scalar(
                nw_lo, PW[2][:, :].rearrange("p (c w) -> p c w", c=4), -1.0, None, OP.mult), waits=[mmw], sig="dv")
            tw2 = S.emit("vector", lambda e, nw_hi=nw_hi: e.tensor_scalar(
                nw_hi, PW[3][:, :].rearrange("p (c w) -> p c w", c=4), -1.0, None, OP.mult), waits=[mmw], sig="dv")
            tp["ps2"] = tw1
            tp["ps3"] = tw2
            if DEBUG:
                r0 = mb + tt * 128
                tp["dbgu"] = S.dma("mg", dbg["dbg_u"][r0:r0 + 128, :], u_b[:, :], waits=[tu1, tu2])
                for c in range(8):
                    tp["dbgu"] = S.dma("mg", dbg["dbg_negw"][c * 128:(c + 1) * 128, r0:r0 + 128],
                                       fmsl(negw_fm, c, tt), waits=[tw1, tw2, tp.get("dbgu")])
            mmt = None
            for h in range(H):
                for half in range(2):
                    c = 2 * h + half
                    mmt = S.emit("tensor", lambda e, h=h, c=c, tt=tt, half=half: e.matmul(
                        PW[0][:, h * 128:(h + 1) * 128], fmsl(kh_fm, c, tt), fmsl(qh_fm, c, tt),
                        start=(h == 0 and half == 0), stop=(h == 3 and half == 1)),
                        waits=[tu1, tok_scaled[("q", c)]], sig="dt")
            tMt = S.emit("vector", lambda e: e.tensor_tensor(
                Mt_b[:, :].rearrange("p (a w) -> p a w", a=4), PW[0][:, :].rearrange("p (a w) -> p a w", a=4),
                C["triuI"][:].rearrange("p (x w) -> p x w", x=1).broadcast_to([128, 4, 128]), OP.mult), waits=[mmt], sig="dv")
            tp["ps0"] = tMt
            mm1 = None
            for h in range(H):
                for half in range(2):
                    c = 2 * h + half
                    mm1 = S.emit("tensor", lambda e, h=h, c=c, tt=tt, half=half: e.matmul(
                        PW[1][:, h * 128:(h + 1) * 128], fmsl(negw_fm, c, tt), fmsl(kh_fm, c, tt),
                        start=(h == 0 and half == 0), stop=(h == 3 and half == 1)),
                        waits=[tw1, tw2, tu2], sig="dt")
            tGnw = S.emit("vector", lambda e: e.tensor_tensor(
                Gnw_b[:, :].rearrange("p (a w) -> p a w", a=4), PW[1][:, :].rearrange("p (a w) -> p a w", a=4),
                C["blkLow"][:].rearrange("p (x w) -> p x w", x=1).broadcast_to([128, 4, 128]), OP.mult), waits=[mm1], sig="dv")
            tp["ps1"] = tGnw
            mm2 = None
            for h in range(H):
                for half in range(2):
                    c = 2 * h + half
                    mm2 = S.emit("tensor", lambda e, h=h, c=c, tt=tt, half=half: e.matmul(
                        PW[2][:, h * 128:(h + 1) * 128], fmsl(kh_fm, c, tt), fmsl(negw_fm, c, tt),
                        start=(h == 0 and half == 0), stop=(h == 3 and half == 1)),
                        waits=[tw1, tw2, tp.get("ps2")], sig="dt")
            tGnwT = S.emit("vector", lambda e: e.tensor_tensor(
                GnwT_b[:, :].rearrange("p (a w) -> p a w", a=4), PW[2][:, :].rearrange("p (a w) -> p a w", a=4),
                C["blkUp"][:].rearrange("p (x w) -> p x w", x=1).broadcast_to([128, 4, 128]), OP.mult), waits=[mm2], sig="dv")
            tp["ps2"] = tGnwT
            mmt2 = None
            for h in range(H):
                mmt2 = S.emit("tensor", lambda e, h=h: e.matmul(
                    PW[3][:, h * 128:(h + 1) * 128], Gnw_b[:, h * 128:(h + 1) * 128],
                    GnwT_b[:, h * 128:(h + 1) * 128], start=(h == 0), stop=(h == 3)),
                    waits=[tGnw, tGnwT, tp.get("ps3")], sig="dt")
            tT2 = S.emit("vector", lambda e: e.tensor_copy(T2t_b[:, :], PW[3][:, :]), waits=[mmt2], sig="dv")
            tp["ps3"] = tT2
            mmh = None
            for h in range(H):
                hs = slice(h * 128, (h + 1) * 128)
                mmh = S.emit("tensor", lambda e, hs=hs: e.matmul(PW[0][:, hs], C["eye128"][:, :], C["eye128"][:, :],
                                                                start=True, stop=False),
                             waits=[tT2, tMt], sig="dt")
                S.emit("tensor", lambda e, hs=hs: e.matmul(PW[0][:, hs], C["eye128"][:, :], GnwT_b[:, hs],
                                                           start=False, stop=False), sig="dt")
                S.emit("tensor", lambda e, hs=hs: e.matmul(PW[0][:, hs], C["eye128"][:, :], T2t_b[:, hs],
                                                           start=False, stop=False), sig="dt")
                mmh = S.emit("tensor", lambda e, hs=hs: e.matmul(PW[0][:, hs], Gnw_b[:, hs], T2t_b[:, hs],
                                                                start=False, stop=True), sig="dt")
            tHt = S.emit("vector", lambda e: e.tensor_copy(Ht_b[:, :], PW[0][:, :]), waits=[mmh], sig="dv")
            tp["ps0"] = tHt
            mmu0 = None
            for h in range(H):
                pwb = PW[1 + h // 2]
                of = (h % 2) * 256
                mmu0 = S.emit("tensor", lambda e, h=h, pwb=pwb, of=of, fsc=fsc: e.matmul(
                    pwb[:, of:of + 256], C["eye128"][:, :], u_b[:, h * 256:(h + 1) * 256],
                    start=True, stop=fsc), waits=[tHt, tu1, tu2, tp.get("ps1"), tp.get("ps2")], sig="dt")
                if not fsc:
                    for half in range(2):
                        c = 2 * h + half
                        mmu0 = S.emit("tensor", lambda e, h=h, c=c, tt=tt, pwb=pwb, of=of, half=half: e.matmul(
                            pwb[:, of:of + 256], fmsl(negw_fm, c, tt),
                            S_bf[:, h * 512 + half * 256: h * 512 + half * 256 + 256],
                            start=False, stop=(half == 1)), sig="dt")
            tU1 = S.emit("vector", lambda e: e.tensor_copy(U0_b[:, 0:512], PW[1][:, :]), waits=[mmu0], sig="dv")
            tU2 = S.emit("vector", lambda e: e.tensor_copy(U0_b[:, 512:1024], PW[2][:, :]), waits=[mmu0], sig="dv")
            tp["ps1"] = tU1
            tp["ps2"] = tU2
            mmua = None
            for h in range(H):
                pwb = PW[1 + h // 2]
                of = (h % 2) * 256
                mmua = S.emit("tensor", lambda e, h=h, pwb=pwb, of=of: e.matmul(
                    pwb[:, of:of + 256], Ht_b[:, h * 128:(h + 1) * 128], U0_b[:, h * 256:(h + 1) * 256],
                    start=True, stop=True), waits=[tU1, tU2], sig="dt")
            tua1 = S.emit("vector", lambda e: e.tensor_copy(uadj_b[:, 0:512], PW[1][:, :]), waits=[mmua], sig="dv")
            tua2 = S.emit("vector", lambda e: e.tensor_copy(uadj_b[:, 512:1024], PW[2][:, :]), waits=[mmua], sig="dv")
            tp["ps1"] = tua1
            tp["ps2"] = tua2
            mmo = None
            for h in range(H):
                pwb = PW[1 + h // 2]
                of = (h % 2) * 256
                mmo = S.emit("tensor", lambda e, h=h, pwb=pwb, of=of, fsc=fsc: e.matmul(
                    pwb[:, of:of + 256], Mt_b[:, h * 128:(h + 1) * 128], uadj_b[:, h * 256:(h + 1) * 256],
                    start=True, stop=fsc), waits=[tua1, tua2], sig="dt")
                if not fsc:
                    for half in range(2):
                        c = 2 * h + half
                        mmo = S.emit("tensor", lambda e, h=h, c=c, tt=tt, pwb=pwb, of=of, half=half: e.matmul(
                            pwb[:, of:of + 256], fmsl(qh_fm, c, tt),
                            S_bf[:, h * 512 + half * 256: h * 512 + half * 256 + 256],
                            start=False, stop=(half == 1)), sig="dt")
            tod1 = S.emit("vector", lambda e, tt=tt: e.tensor_copy(delta_tm[:, tt * 1024:tt * 1024 + 512], PW[1][:, :]),
                          waits=[mmo], sig="dv")
            tod2 = S.emit("vector", lambda e, tt=tt: e.tensor_copy(delta_tm[:, tt * 1024 + 512:(tt + 1) * 1024],
                                                                   PW[2][:, :]), waits=[mmo], sig="dv")
            tp["ps1"] = tod1
            tp["ps2"] = tod2
            mms = None
            for h in range(H):
                for half in range(2):
                    mms = S.emit("tensor", lambda e, h=h, half=half, tt=tt: e.matmul(
                        PS_S[h][:, half * 256: half * 256 + 256],
                        khT[:, tt * 1024 + h * 256 + half * 128: tt * 1024 + h * 256 + half * 128 + 128],
                        uadj_b[:, h * 256:(h + 1) * 256], start=(half == 0), stop=(half == 1)),
                        waits=[tua1, tua2, tok_tmprod[("khT", tt)], tp.get("scast")], sig="dt")
            tsc = None
            for h in range(H):
                if fsc:
                    ta = S.emit("vector", lambda e, h=h: e.tensor_copy(S32[:, h * 512:(h + 1) * 512], PS_S[h][:, :]),
                                waits=[mms], sig="dv")
                else:
                    ta = S.emit("vector", lambda e, h=h: e.tensor_tensor(S32[:, h * 512:(h + 1) * 512],
                                                                         S32[:, h * 512:(h + 1) * 512],
                                                                         PS_S[h][:, :], OP.add),
                                waits=[mms], sig="dv")
                tsc = S.emit("vector", lambda e, h=h: e.tensor_copy(S_bf[:, h * 512:(h + 1) * 512],
                                                                    S32[:, h * 512:(h + 1) * 512]),
                             waits=[ta], sig="dv")
            tp["scast"] = tsc
            if DEBUG:
                r0 = mb + tt * 128
                tp["dbgu"] = S.dma("mg", dbg["dbg_delta"][r0:r0 + 128, :], delta_tm[:, tt * 1024:(tt + 1) * 1024],
                                   waits=[tod1, tod2, tp.get("dbgu")])
        AB.free("B_b", "BT_b", "pw_b", "pwT_b", "P_b", "invT_b", "Mt_b", "Gnw_b", "GnwT_b",
                "T2t_b", "Ht_b", "u_b", "U0_b", "uadj_b", "negw_fm", "kb_fm", "vbT", "kbT", "khT")
        S.barrier()

        if PH < 8: continue
        sh_tm = AB.alloc("sh_tm", NT * 1024)
        lo_tm = AB.alloc("lo_tm", NT * 1024)
        wd_tm = AB.alloc("wd_tm", NT * 1024)
        Rb = AB.alloc("Rb", 2 * 4 * RW)
        fdb = AB.alloc("fdb", 2 * 4 * NGF * 32)
        fir_fm = AB.alloc("fir_fm", 3 * W)
        sq_b = AB.alloc("sq_b", W)
        filt_last = [NG_S - 1, NG_S + NG_L - 1, NGF - 1]
        tok_fir_ev = {}
        vdview = vdram.rearrange("(c s i) t -> c i s t", c=8, s=4)
        mmsum = None
        first_summ = True
        for c in range(8):
            half = c % 2
            tRl = []
            for g in range(4):
                colb = mb + 61 + g
                tRl.append(S.dma("mh", Rb[32 * g:32 * g + 32, half * 4 * RW:(half + 1) * 4 * RW].rearrange(
                    "p (s t) -> p s t", s=4), vdview[c, :, :, colb:colb + RW],
                    waits=[tok_vdr_w, tok_fir_ev.get(c - 2)]))
            tfd = S.dma("mg", fdb[:, half * NGF * 128:(half + 1) * NGF * 128].rearrange(
                "p (s g w) -> p s g w", s=4, g=NGF), dr["fird"][c].rearrange("s g p w -> p s g w"),
                waits=[tok_fir_ev.get(c - 2)])
            mmf = {}
            for fi in range(3):
                mmf[fi] = S.emit("tensor", lambda e, fi=fi, half=half: e.matmul(
                    PW[fi][:, 0:512], C["zeros128"][:, :], Rb[:, half * 4 * RW: half * 4 * RW + 512],
                    start=True, stop=False), waits=tRl + [tfd, tp.get(("fps", fi))], sig="dt")
            for gi, (fi, Gf, tau) in enumerate(groups):
                for s4 in range(4):
                    mmf[fi] = S.emit("tensor", lambda e, fi=fi, gi=gi, s4=s4, tau=tau, half=half: e.matmul(
                        PW[fi][32 * s4:32 * s4 + 32, 0:512],
                        fdb[:, half * NGF * 128 + (s4 * NGF + gi) * 32: half * NGF * 128 + (s4 * NGF + gi) * 32 + 32],
                        Rb[:, half * 4 * RW + s4 * RW + tau: half * 4 * RW + s4 * RW + tau + 512],
                        start=False, stop=False, tile_position=(0, 32 * s4)),
                        waits=tRl + [tfd], sig="dt")
            for fi in range(3):
                mmf[fi] = S.emit("tensor", lambda e, fi=fi, half=half: e.matmul(
                    PW[fi][:, 0:512], C["zeros128"][:, :], Rb[:, half * 4 * RW: half * 4 * RW + 512],
                    start=False, stop=True), sig="dt")
            evs = []
            for fi in range(3):
                evs.append(S.emit("vector", lambda e, fi=fi: e.tensor_copy(fir_fm[:, fi * W:(fi + 1) * W], PW[fi][:, :]),
                                  waits=[mmf[fi], tp.get(("fxw", fi))], sig="dv"))
                tp[("fps", fi)] = evs[-1]
            tok_fir_ev[c] = evs[-1]
            if first_summ:
                mmsum = S.emit("tensor", lambda e: e.matmul(
                    PW[3][:, 0:512], C["zeros128"][:, :], vf_fm[:, 0:W],
                    start=True, stop=False), waits=[tp.get("sumps")] + [evs[0]], sig="dt")
                first_summ = False
            for fi in range(3):
                mmsum = S.emit("tensor", lambda e, c=c, fi=fi: e.matmul(
                    PW[3][0:16, 0:512], C["hc4"][:, (c * 4 + fi) * 16:(c * 4 + fi + 1) * 16],
                    fir_fm[:, fi * W:(fi + 1) * W],
                    start=False, stop=False), waits=[evs[fi], tp.get("sumps")], sig="dt")
            mmsum = S.emit("tensor", lambda e, c=c: e.matmul(
                PW[3][0:16, 0:512], C["hc4"][:, (c * 4 + 3) * 16:(c * 4 + 4) * 16], vf_fm[:, c * W:(c + 1) * W],
                start=False, stop=False), sig="dt")
            for fi in range(3):
                tsq = S.emit("vector", lambda e, fi=fi: e.tensor_tensor(
                    sq_b[:, :], fir_fm[:, fi * W:(fi + 1) * W], fir_fm[:, fi * W:(fi + 1) * W], OP.mult),
                    waits=[evs[fi], mmsum], sig="dv")
                mmsum = S.emit("tensor", lambda e, c=c, fi=fi: e.matmul(
                    PW[3][32:48, 0:512], C["hc4"][:, (c * 4 + fi) * 16:(c * 4 + fi + 1) * 16], sq_b[:, :],
                    start=False, stop=False), waits=[tsq], sig="dt")
            tsqv = S.emit("vector", lambda e, c=c: e.tensor_tensor(
                sq_b[:, :], vf_fm[:, c * W:(c + 1) * W], vf_fm[:, c * W:(c + 1) * W], OP.mult),
                waits=[mmsum], sig="dv")
            mmsum = S.emit("tensor", lambda e, c=c: e.matmul(
                PW[3][32:48, 0:512], C["hc4"][:, (c * 4 + 3) * 16:(c * 4 + 4) * 16], sq_b[:, :],
                start=False, stop=False), waits=[tsqv], sig="dt")
            if c == 7:
                mmsum = S.emit("tensor", lambda e: e.matmul(
                    PW[3][:, 0:512], C["zeros128"][:, :], vf_fm[:, 0:W],
                    start=False, stop=True), sig="dt")
            tp["sumps"] = mmsum
            for fi, dst in ((0, sh_tm), (1, lo_tm), (2, wd_tm)):
                tp[("fxw", fi)] = [S.dma("ms", dst[:, tt * 1024 + c * 128: tt * 1024 + (c + 1) * 128],
                                         fir_fm[:, fi * W + tt * 128: fi * W + tt * 128 + 128],
                                         waits=[evs[fi]], transpose=True) for tt in range(NT)]
        tok_fxall = S.qtot("ms") + S.qtot("mh")
        if DEBUG:
            for nm, dst in (("dbg_short", sh_tm), ("dbg_long", lo_tm), ("dbg_wide", wd_tm)):
                for tt in range(NT):
                    S.dma("mg", dbg[nm][mb + tt * 128: mb + tt * 128 + 128, :], dst[:, tt * 1024:(tt + 1) * 1024],
                          waits=[tok_fxall])
        AB.free("Rb", "fdb", "fir_fm", "sq_b")

        if PH < 9: continue
        stat64 = AFp.alloc("stat64", 64)
        sums16 = AFp.alloc("sums16", 512)
        trsp = AFp.alloc("trsp", 128)
        dsq = AFp.alloc("dsq", 1024)
        stt = S.emit("vector", lambda e: e.tensor_copy(sums16[:, :], PW[3][:, 0:512]), waits=[mmsum], sig="dv")
        if DEBUG:
            S.dma("mg", dbg["dbg_sums"][:, mb:mb + W], sums16[:, :], waits=[stt])
        tp["ps3"] = stt
        tok_stats = []
        for tt in range(NT):
            toks = []
            tp1 = S.emit("tensor", lambda e, tt=tt: e.transpose(PW[2][:, 0:128], sums16[:, tt * 128:(tt + 1) * 128],
                                                                C["ident32"][:, :]),
                         waits=[stt, tp.get("ps2")], sig="dt")
            ev1 = S.emit("vector", lambda e: e.tensor_copy(trsp[:, :], PW[2][:, 0:128]), waits=[tp1], sig="dv")
            tp["ps2"] = ev1
            for ti in range(4):
                off = [0, 12, 24, 48][ti]
                toks.append(S.emit("vector", lambda e, ti=ti, off=off: e.tensor_scalar(
                    stat64[:, off:off + 4], trsp[:, ti * 4:(ti + 1) * 4], 1.0 / 256.0, None, OP.mult),
                    waits=[ev1, tp.get("dbgstat")], sig="dv"))
                toks.append(S.emit("vector", lambda e, ti=ti, off=off: e.tensor_scalar(
                    stat64[:, off + 4:off + 8], trsp[:, 32 + ti * 4: 32 + (ti + 1) * 4], 1.0 / 256.0, 1e-8,
                    OP.mult, OP.max), waits=[ev1], sig="dv"))
            dslice = delta_tm[:, tt * 1024:(tt + 1) * 1024]
            td1 = S.emit("vector", lambda e, dslice=dslice: e.tensor_reduce(
                stat64[:, 60:64], dslice.rearrange("p (h c) -> p h c", h=4), axis=AX, op=OP.add),
                waits=[tp.get("dstat")], sig="dv")
            td2 = S.emit("vector", lambda e: e.tensor_scalar(stat64[:, 36:40], stat64[:, 60:64],
                                                             1.0 / 256.0, None, OP.mult), waits=[td1], sig="dv")
            td3 = S.emit("vector", lambda e, dslice=dslice: e.tensor_tensor(dsq[:, :], dslice, dslice, OP.mult),
                         waits=[td2], sig="dv")
            td4 = S.emit("vector", lambda e: e.tensor_reduce(
                stat64[:, 60:64], dsq[:, :].rearrange("p (h c) -> p h c", h=4), axis=AX, op=OP.add),
                waits=[td3], sig="dv")
            td5 = S.emit("vector", lambda e: e.tensor_scalar(stat64[:, 40:44], stat64[:, 60:64],
                                                             1.0 / 256.0, 1e-8, OP.mult, OP.max), waits=[td4], sig="dv")
            tp["dstat"] = td5
            toks.extend([td1, td2, td3, td4, td5])
            for tile, off in ((sh_tm, 0), (lo_tm, 12), (wd_tm, 24), (delta_tm, 36), (v_tm, 48)):
                toks.append(S.emit("vector", lambda e, tile=tile, off=off, tt=tt: e.tensor_reduce(
                    stat64[:, off + 8:off + 12], tile[:, tt * 1024:(tt + 1) * 1024].rearrange("p (h c) -> p h c", h=4),
                    axis=AX, op=OP.max, apply_absolute_value=True), waits=[tok_fxall], sig="dv"))
            sqs = []
            for coff in (4, 16, 28, 40, 52):
                sqs.append(S.emit("scalar", lambda e, coff=coff: e.activation(
                    stat64[:, coff:coff + 4], stat64[:, coff:coff + 4], AF.Sqrt), waits=toks, sig="dc"))
            tp2 = S.emit("tensor", lambda e: e.transpose(PW[2][0:64, 0:128], stat64[:, :], C["ident32"][:, :]),
                         waits=toks + sqs + [tp.get("ps2")], sig="dt")
            tst = S.emit("vector", lambda e, tt=tt: e.tensor_copy(statF[0:64, tt * 128:(tt + 1) * 128],
                                                                  PW[2][0:64, 0:128]), waits=[tp2], sig="dv")
            tp["ps2"] = tst
            tok_stats.append(tst)
            if DEBUG:
                tp["dbgstat"] = S.dma("mg", dbg["dbg_stat"][mb + tt * 128: mb + tt * 128 + 128, :], stat64[:, :],
                                      waits=toks + sqs + [tst])
        AFp.free("stat64", "sums16", "trsp", "dsq")

        if PH < 10: continue
        g1buf = AB.alloc("g1buf", 2 * 9 * 128)
        hdn_b = AB.alloc("hdn_b", 2 * W)
        logit_sp = AFp.alloc("logit_sp", W)
        tok_hdn = {}
        mml = None
        for dcg in range(16):
            buf = g1buf[:, (dcg % 2) * 9 * 128:((dcg % 2) + 1) * 9 * 128]
            twl = S.dma("mg", buf.rearrange("p (c w) -> p c w", c=9),
                        dr["g1Tp"].rearrange("(c p) d -> p c d", p=128)[:, :, dcg * 128:(dcg + 1) * 128],
                        waits=[tok_hdn.get(dcg - 2)])
            mmt = None
            for kc in range(9):
                rhs = xn_fm[:, kc * W:(kc + 1) * W] if kc < 8 else statF[:, 0:512]
                mmt = S.emit("tensor", lambda e, kc=kc, buf=buf, rhs=rhs, dcg=dcg: e.matmul(
                    PW[dcg % 2][:, 0:512], buf[:, kc * 128:(kc + 1) * 128], rhs,
                    start=(kc == 0), stop=(kc == 8)),
                    waits=[twl, tok_stats[-1], tp.get(("g1ps", dcg % 2))], sig="dt")
            ghl = S.emit("scalar", lambda e, dcg=dcg: e.activation(
                hdn_b[:, (dcg % 2) * W:((dcg % 2) + 1) * W], PW[dcg % 2][:, 0:512], AF.Gelu,
                bias=C["g1b"][:, dcg:dcg + 1]), waits=[mmt], sig="dc")
            tok_hdn[dcg] = ghl
            tp[("g1ps", dcg % 2)] = ghl
            mml = S.emit("tensor", lambda e, dcg=dcg: e.matmul(
                PW[2][0:20, 0:512], C["g2T"][:, dcg * 20:(dcg + 1) * 20],
                hdn_b[:, (dcg % 2) * W:((dcg % 2) + 1) * W],
                start=(dcg == 0), stop=(dcg == 15)), waits=[ghl, tp.get("ps2")], sig="dt")
        tok_lg = S.emit("vector", lambda e: e.tensor_copy(logit_sp[0:20, :], PW[2][0:20, 0:512]), waits=[mml], sig="dv")
        tp["ps2"] = tok_lg
        tp["xnfree"] = tok_lg
        AB.free("g1buf", "hdn_b")

        if PH < 11: continue
        o_tm = AB.alloc("o_tm", NT * 1024)
        lg_t = AFp.alloc("lg_t", 128)
        o32 = AFp.alloc("o32", 1024)
        tmpc = AFp.alloc("tmpc", 1024)
        tpxl = []
        for tt in range(NT):
            tpxl.append(S.emit("tensor", lambda e, tt=tt: e.transpose(
                PW[3][:, tt * 128:(tt + 1) * 128], logit_sp[:, tt * 128:(tt + 1) * 128], C["ident32"][:, :]),
                waits=([tok_lg, tp.get("ps3")] if tt == 0 else []), sig="dt"))
        lgv = lg_t[:, 0:80].rearrange("p (t w) -> p t w", t=4)
        lgi = lg_t[:, 0:80].rearrange("p (a i) -> p a i", i=5)
        t1 = S.emit("vector", lambda e: e.tensor_copy(
            lg_t[:, 0:80].rearrange("p (t w) -> p t w", t=4),
            PW[3][:, :].rearrange("p (t w) -> p t w", w=128)[:, :, 0:20]),
            waits=tpxl + [tp.get("dbglg")], sig="dv")
        tp["ps3"] = t1

        def gcv(a):
            return gconst[:, a:a + 20].rearrange("p (x w) -> p x w", x=1).broadcast_to([128, 4, 20])
        t2 = S.emit("vector", lambda e: e.tensor_tensor(lgv, lgv, gcv(0), OP.add), waits=[t1], sig="dv")
        t3 = S.emit("vector", lambda e: e.tensor_tensor(lgv, lgv, gcv(20), OP.mult), waits=[t2], sig="dv")
        t4 = S.emit("vector", lambda e: e.tensor_reduce(lg_t[:, 80:96], lgi, axis=AX, op=OP.max),
                    waits=[t3], sig="dv")
        mxv = lg_t[:, 80:96].broadcast_to([128, 16, 5])
        t5 = S.emit("vector", lambda e: e.tensor_tensor(lgi, lgi, mxv, OP.subtract), waits=[t4], sig="dv")
        t6 = S.emit("scalar", lambda e: e.activation(lg_t[:, 0:80], lg_t[:, 0:80], AF.Exp), waits=[t5], sig="dc")
        t7 = S.emit("vector", lambda e: e.tensor_reduce(lg_t[:, 80:96], lgi, axis=AX, op=OP.add),
                    waits=[t6], sig="dv")
        t8 = S.emit("vector", lambda e: e.reciprocal(lg_t[:, 96:112], lg_t[:, 80:96]), waits=[t7], sig="dv")
        rv = lg_t[:, 96:112].broadcast_to([128, 16, 5])
        t9 = S.emit("vector", lambda e: e.tensor_tensor(lgi, lgi, rv, OP.mult), waits=[t8], sig="dv")
        t10 = S.emit("vector", lambda e: e.tensor_tensor(lgv, lgv, gcv(40), OP.mult), waits=[t9], sig="dv")
        t11 = S.emit("vector", lambda e: e.tensor_tensor(lgv, lgv, gcv(60), OP.add), waits=[t10], sig="dv")
        if DEBUG:
            tp["dbglg"] = S.dma_burst("mg", [(dbg["dbg_logits"][mb + t_ * 128:mb + t_ * 128 + 128, 0:20],
                                              lg_t[:, 20 * t_:20 * t_ + 20], {}) for t_ in range(NT)],
                                      waits=[t11])
        for tt in range(NT):
            tensors = [(sh_tm, 0), (lo_tm, 1), (wd_tm, 2), (delta_tm, 3), (v_tm, 4)]
            tsl = slice(tt * 1024, (tt + 1) * 1024)
            tokc = None
            for ti, (tile, i0) in enumerate(tensors):
                wv_ = lg_t[:, 20 * tt + i0: 20 * tt + i0 + 20].rearrange("p (h i) -> p h i", i=5)[:, :, 0:1].broadcast_to([128, 4, 256])
                if ti == 0:
                    tokc = S.emit("vector", lambda e, tile=tile, wv_=wv_, tsl=tsl: e.tensor_tensor(
                        o32[:, :].rearrange("p (h c) -> p h c", h=4),
                        tile[:, tsl].rearrange("p (h c) -> p h c", h=4), wv_, OP.mult),
                        waits=[t11, tok_fxall], sig="dv")
                else:
                    tokc = S.emit("vector", lambda e, tile=tile, wv_=wv_, tsl=tsl: e.tensor_tensor(
                        tmpc[:, :].rearrange("p (h c) -> p h c", h=4),
                        tile[:, tsl].rearrange("p (h c) -> p h c", h=4), wv_, OP.mult),
                        waits=[tokc], sig="dv")
                    tokc = S.emit("vector", lambda e: e.tensor_tensor(o32[:, :], o32[:, :], tmpc[:, :], OP.add),
                                  waits=[tokc], sig="dv")
            t12 = S.emit("vector", lambda e: e.tensor_tensor(tmpc[:, :], o32[:, :], o32[:, :], OP.mult),
                         waits=[tokc], sig="dv")
            t13 = S.emit("vector", lambda e: e.tensor_reduce(lg_t[:, 112:116],
                                                             tmpc[:, :].rearrange("p (h c) -> p h c", h=4),
                                                             axis=AX, op=OP.add), waits=[t12], sig="dv")
            t14 = S.emit("vector", lambda e: e.tensor_scalar(lg_t[:, 112:116], lg_t[:, 112:116], 1.0 / 256, 1e-5,
                                                             OP.mult, OP.add), waits=[t13], sig="dv")
            t15 = S.emit("vector", lambda e: e.reciprocal(lg_t[:, 116:120], lg_t[:, 112:116]), waits=[t14], sig="dv")
            t16 = S.emit("scalar", lambda e: e.activation(lg_t[:, 116:120], lg_t[:, 116:120], AF.Sqrt), waits=[t15], sig="dc")
            rsv = lg_t[:, 116:120].broadcast_to([128, 4, 256])
            t17 = S.emit("vector", lambda e: e.tensor_tensor(o32[:, :].rearrange("p (h c) -> p h c", h=4),
                                                             o32[:, :].rearrange("p (h c) -> p h c", h=4),
                                                             rsv, OP.mult), waits=[t16], sig="dv")
            t18 = S.emit("vector", lambda e, tsl=tsl: e.tensor_tensor(o_tm[:, tsl], o32[:, :], C["onorm_rep"][:, :],
                                                                      OP.mult), waits=[t17], sig="dv")
            tp[("otm", tt)] = t18
            if DEBUG:
                S.dma("mg", dbg["dbg_otm"][mb + tt * 128:mb + tt * 128 + 128, :], o_tm[:, tsl], waits=[t18])
        AFp.free("lg_t", "o32", "tmpc", "logit_sp")

        if PH < 12: continue
        o_fm = AB.alloc("o_fm", 8 * W)
        owb = AB.alloc("owb", 8 * 1024)
        AB.free("sh_tm", "lo_tm", "wd_tm", "delta_tm", "v_tm")
        s_t = AFp.alloc("s_t", 1024)
        towl = S.dma("mg", owb[:].rearrange("p (c w) -> p c w", c=8),
                     dr["owT"].rearrange("(c p) d -> p c d", p=128), waits=[tp[("otm", 0)]])
        tof = {}
        for tt in range(NT):
            for c in range(8):
                tof[(tt, c)] = S.dma("ms", o_fm[:, c * W + tt * 128: c * W + tt * 128 + 128],
                                     o_tm[:, tt * 1024 + c * 128: tt * 1024 + (c + 1) * 128],
                                     waits=[tp[("otm", tt)]], transpose=True)
        for tt in range(NT):
            mms = None
            for kc in range(8):
                for nn in range(2):
                    mms = S.emit("tensor", lambda e, kc=kc, nn=nn, tt=tt: e.matmul(
                        PW[nn][:, 0:512], o_fm[:, kc * W + tt * 128: kc * W + tt * 128 + 128],
                        owb[:, kc * 1024 + nn * 512: kc * 1024 + (nn + 1) * 512],
                        start=(kc == 0), stop=(kc == 7)),
                        waits=[towl, tof[(tt, kc)], tp.get("ps0"), tp.get("ps1")]
                        + ([tp.get(("g1ps", 0)), tp.get(("g1ps", 1))] if tt == 0 else []), sig="dt")
            ev1 = S.emit("vector", lambda e: e.tensor_copy(s_t[:, 0:512], PW[0][:, :]),
                         waits=[mms, tp.get("sdma")], sig="dv")
            ev2 = S.emit("vector", lambda e: e.tensor_copy(s_t[:, 512:1024], PW[1][:, :]), waits=[mms], sig="dv")
            tp["ps0"] = ev1
            tp["ps1"] = ev2
            tp["sdma"] = S.dma("mg", sdram[mb + tt * 128: mb + tt * 128 + 128, :], s_t[:, :], waits=[ev1, ev2])
            tp["mttail"] = mms
        tp[("projps", 0)] = tp["ps0"]
        tp[("projps", 1)] = tp["ps1"]
        AB.free("o_fm", "owb", "o_tm", "xq_tm")
        AFp.free("s_t")

    S.barrier()
    AB.free("xn_fm", "qh_fm", "kh_fm", "vf_fm", "S_bf", "tails", "statF", "cd", "g2T", "hc4")
    AFp.free("S32", "onorm_rep", "g1b", "gconst", "ident32", "smallrows")

    if PH < 13:
        return _finish(nc, S, ctx)
    wgq = AB.alloc("wgq", 8 * IFF)
    wuq = AB.alloc("wuq", 8 * IFF)
    wdq = AB.alloc("wdq", 22 * 1024)
    wld = AFp.alloc("wld", 2 * 1024)
    wq_b = AB.alloc("wq_b", 2 * 1024)
    prev_q = {}
    prev_x = {}
    i = 0
    for wnm, dst in (("wg", wgq), ("wu", wuq)):
        for ic2 in range(22):
            p = i % 2
            wldp = wld[:, p * 1024:(p + 1) * 1024]
            wqbp = wq_b[:, p * 1024:(p + 1) * 1024]
            tl = S.dma("mg", wldp, dr[wnm][ic2 * 128:(ic2 + 1) * 128, :], waits=[prev_q.get(p)])
            tqq, _ = emit_quant(wldp, wqbp, 1024, tl, "q2", dst_war=prev_x.get(p))
            prev_q[p] = tqq
            prev_x[p] = [S.dma("ms" if p == 0 else "mh",
                               dst[:, kc * IFF + ic2 * 128: kc * IFF + ic2 * 128 + 128],
                               wqbp[:, kc * 128:(kc + 1) * 128], waits=[tqq], transpose=True)
                         for kc in range(8)]
            i += 1
    AFp.free("wld")
    AB.free("wq_b")
    wld2 = AFp.alloc("wld2", 2 * (IFF // 2))
    wq_b2 = AB.alloc("wq_b2", 2 * IFF)
    prev_q2 = {0: [prev_q[0], prev_q[1]], 1: [prev_q[0], prev_q[1]]}
    prev_x2 = {0: prev_x[0] + prev_x[1], 1: prev_x[0] + prev_x[1]}
    for c8 in range(8):
        p = c8 % 2
        wldp = wld2[:, p * 1408:(p + 1) * 1408]
        wqbp = wq_b2[:, p * IFF:(p + 1) * IFF]
        tqh = []
        for hf in range(2):
            tl = S.dma("mg", wldp, dr["wdn"][c8 * 128:(c8 + 1) * 128, hf * 1408:(hf + 1) * 1408],
                       waits=[prev_q2.get(p)] + (tqh if hf else []))
            tqq, _ = emit_quant(wldp, wqbp[:, hf * 1408:(hf + 1) * 1408], IFF // 2, tl, "q3",
                                dst_war=prev_x2.get(p))
            tqh.append(tqq)
        prev_q2[p] = tqh
        prev_x2[p] = [S.dma("ms" if p == 0 else "mh",
                            wdq[:, icc * 1024 + c8 * 128: icc * 1024 + c8 * 128 + 128],
                            wqbp[:, icc * 128:(icc + 1) * 128], waits=tqh, transpose=True)
                      for icc in range(22)]
    AFp.free("wld2")
    AB.free("wq_b2")
    S.barrier()

    xq_t2 = AB.alloc("xq_t", 3 * 1024)
    hq_t2 = AB.alloc("hq_t", 5 * 512)
    hq_f2 = AB.alloc("hq_f", 5 * 512)
    gup2 = AFp.alloc("gup", 2 * 1024)
    hh2 = AFp.alloc("hh", 2 * 512)
    g_t2 = AFp.alloc("g_t", 2 * 1024)
    xs_t2 = AFp.alloc("xs_t", 2048)
    IC_OF = [0, 512, 1024, 1536, 2048, 2560]
    IC_SZ = [512, 512, 512, 512, 512, 256]
    GU = [(PW[0], PW[1]), (PS_S[0], PS_S[1])]
    DN = [(PW[2], PW[3]), (PS_S[2], PS_S[3])]

    def emit_dn(pend):
        tt, ici, q4, tok_hqf = pend
        pb = tt % 2
        bd0, bd1 = DN[pb]
        hq_f = hq_f2[:, q4 * 512:(q4 + 1) * 512]
        iof, isz = IC_OF[ici], IC_SZ[ici]
        mmd = None
        for j in range(isz // 128):
            icc = iof // 128 + j
            for nn in range(2):
                mmd = S.emit("tensor", lambda e, j=j, icc=icc, nn=nn, hq_f=hq_f, bd=(bd0, bd1)[nn]: e.matmul(
                    bd[:, 0:512], hq_f[:, j * 128:(j + 1) * 128],
                    wdq[:, icc * 1024 + nn * 512: icc * 1024 + (nn + 1) * 512],
                    start=(icc == 0), stop=(icc == 21)),
                    waits=([tok_hqf, tp.get(("dnev", pb))] if (j == 0 and ici == 0) else [tok_hqf]), sig="dt")
        tp[("dnmm", q4)] = mmd
        return mmd

    pend_dn = []
    for tt in range(N_FFN_TILES):
        r0 = tt * 128
        pb = tt % 2
        xq_t = xq_t2[:, (tt % 3) * 1024:(tt % 3 + 1) * 1024]
        g_t = g_t2[:, pb * 1024:(pb + 1) * 1024]
        hh = hh2[:, pb * 512:(pb + 1) * 512]
        xs_t = xs_t2
        wprev = tp.get(("ffn", tt - 2))
        txq = [S.dma("mh", xq_t[:, c * 128:(c + 1) * 128], xqdram[r0:r0 + 128, c * 128:(c + 1) * 128],
                     waits=[tp.get(("ffn", tt - 3))], transpose=True) for c in range(8)]
        for ici in range(6):
            iof, isz = IC_OF[ici], IC_SZ[ici]
            p2 = ici % 2
            q4 = (tt * 6 + ici) % 5
            bg, bu = GU[p2]
            gup = gup2[:, p2 * 1024:(p2 + 1) * 1024]
            hq_t = hq_t2[:, q4 * 512:(q4 + 1) * 512]
            hq_f = hq_f2[:, q4 * 512:(q4 + 1) * 512]
            mmg = None
            for kc in range(8):
                mmg = S.emit("tensor", lambda e, kc=kc, iof=iof, isz=isz, xq_t=xq_t, bg=bg: e.matmul(
                    bg[:, 0:isz], xq_t[:, kc * 128:(kc + 1) * 128], wgq[:, kc * IFF + iof: kc * IFF + iof + isz],
                    start=(kc == 0), stop=(kc == 7)),
                    waits=([txq, tp.get(("guev", p2))] if kc == 0 else []), sig="dt")
                mmg = S.emit("tensor", lambda e, kc=kc, iof=iof, isz=isz, xq_t=xq_t, bu=bu: e.matmul(
                    bu[:, 0:isz], xq_t[:, kc * 128:(kc + 1) * 128], wuq[:, kc * IFF + iof: kc * IFF + iof + isz],
                    start=(kc == 0), stop=(kc == 7)), sig="dt")
            if len(pend_dn) >= 3:
                emit_dn(pend_dn.pop(0))
            tsg = S.emit("scalar", lambda e, isz=isz, gup=gup, bg=bg: e.activation(gup[:, 0:isz], bg[:, 0:isz], AF.Silu),
                         waits=[mmg, tp.get(("guth", p2))], sig="dc")
            tup = S.emit("vector", lambda e, isz=isz, gup=gup, bu=bu: e.tensor_copy(gup[:, 512:512 + isz], bu[:, 0:isz]),
                         waits=[mmg, tp.get(("guth", p2))], sig="dv")
            tp[("guev", p2)] = [tsg, tup]
            th = S.emit("vector", lambda e, isz=isz, gup=gup, hh=hh: e.tensor_tensor(
                hh[:, 0:isz], gup[:, 0:isz], gup[:, 512:512 + isz], OP.mult),
                waits=[tsg, tup, tp.get(("dbgh", pb))], sig="dv")
            tp[("guth", p2)] = th
            thq, _ = emit_quant(hh[:, 0:isz], hq_t[:, 0:isz], isz, th, "q4", fast=True,
                                dst_war=[t for t in [tp.get(("dbghq", q4))] if t] + (tp.get(("hqx", q4)) or []))
            if DEBUG:
                tp[("dbgh", pb)] = S.dma("mg", dbg["dbg_h"][r0:r0 + 128, iof:iof + isz], hh[:, 0:isz],
                                         waits=[th, thq])
                tp[("dbghq", q4)] = S.dma("mg", dbg["dbg_hq"][r0:r0 + 128, iof:iof + isz], hq_t[:, 0:isz],
                                          waits=[thq])
            tok_hqf = [S.dma("ms", hq_f[:, j * 128:(j + 1) * 128], hq_t[:, j * 128:(j + 1) * 128],
                             waits=[thq, tp.get(("dnmm", q4))], transpose=True)
                       for j in range(isz // 128)]
            tp[("hqx", q4)] = tok_hqf
            pend_dn.append((tt, ici, q4, tok_hqf))
        for p_ in pend_dn:
            mmd = emit_dn(p_)
        pend_dn = []
        bd0, bd1 = DN[pb]
        tg1 = S.emit("scalar", lambda e, g_t=g_t, bd0=bd0: e.activation(g_t[:, 0:512], bd0[:, 0:512], AF.Silu),
                     waits=[mmd], sig="dc")
        tg2 = S.emit("scalar", lambda e, g_t=g_t, bd1=bd1: e.activation(g_t[:, 512:1024], bd1[:, 0:512], AF.Silu),
                     waits=[mmd], sig="dc")
        tp[("dnev", pb)] = [tg1, tg2]
        txl = S.dma("mg", xs_t[:, 0:1024], dr["x"][r0:r0 + 128, :], waits=[wprev, tp.get("xsfree")])
        tsl = S.dma("mg", xs_t[:, 1024:2048], sdram[r0:r0 + 128, :], waits=[wprev, tp.get("xsfree")])
        to1 = S.emit("vector", lambda e, g_t=g_t, xs_t=xs_t: e.tensor_tensor(
            g_t[:, :], xs_t[:, 1024:2048], g_t[:, :], OP.mult), waits=[tg1, tg2, tsl], sig="dv")
        to2 = S.emit("vector", lambda e, g_t=g_t, xs_t=xs_t: e.tensor_tensor(
            g_t[:, :], xs_t[:, 0:1024], g_t[:, :], OP.add), waits=[to1, txl], sig="dv")
        tp["xsfree"] = to2
        tp[("ffn", tt)] = S.dma("mg", y[r0:r0 + 128, :], g_t[:, :], waits=[to2])
    S.barrier()

    return _finish(nc, S, ctx)


def _finish(nc, S, ctx):
    sems = {}
    for s in ALLSEMS:
        t = nc.semaphore(s)
        h_ = t.__enter__()
        ctx.append(t)
        sems[s] = h_
    blk = nc.Block()
    b = blk.__enter__()
    ctx.append(blk)

    @b.tensor
    def _(eng):
        S.run("tensor", eng, sems)

    @b.vector
    def _(eng):
        S.run("vector", eng, sems)

    @b.scalar
    def _(eng):
        S.run("scalar", eng, sems)

    @b.gpsimd
    def _(eng):
        S.run("gpsimd", eng, sems)

    @b.sync
    def _(eng):
        S.run("sync", eng, sems)

    blk.__exit__(None, None, None)
    ctx.pop()
    return nc, ctx


_CACHE = {}


def kernel(**inputs):
    x_full = np.asarray(inputs["L0_attn_hidden_states"], np.float32)
    host, groups = prep_host(inputs)
    if "nc" not in _CACHE:
        nc, ctx = build(groups)
        _CACHE["nc"] = nc
        _CACHE["ctx"] = ctx
    nc = _CACHE["nc"]
    in_maps = []
    for core in range(8):
        m = dict(host)
        m["x"] = np.ascontiguousarray(x_full[core])
        in_maps.append(m)
    res = run_bass_kernel_spmd(nc, in_maps, core_ids=list(range(8)))
    return np.stack([np.asarray(r["y"], np.float32) for r in res.results], axis=0)


# version 11
# speedup vs baseline: 1.0873x; 1.0358x over previous
import numpy as np
import ml_dtypes
import concourse.bass as bass
import concourse.mybir as mybir
from concourse.bass_utils import run_bass_kernel_spmd
from concourse.alu_op_type import AluOpType as OP

F32 = mybir.dt.float32
BF16 = mybir.dt.bfloat16
I32 = mybir.dt.int32
U8 = mybir.dt.uint8
AF = mybir.ActivationFunctionType
AX = mybir.AxisListType.X

L, D, H = 4096, 1024, 4
TMTOK = 512
NMEG = L // TMTOK
NT = TMTOK // 128
W = TMTOK
WP = W + 64
IFF = 2816
NG_S, NG_L, NG_W = 1, 8, 16
NGF = NG_S + NG_L + NG_W
RW = W + 64

DEBUG = False
SUB56 = 1
N_MEG_BUILD = NMEG
N_FFN_TILES = L // 128
PH = 99

ENGINES = ("tensor", "vector", "scalar", "gpsimd", "sync")
ESEM = {"tensor": "dt", "vector": "dv", "scalar": "dc", "gpsimd": "dg", "sync": "dy"}
NPOOL = 8
NEPOOL = 12
DMAPOOL = {q: [f"{q}{i}" for i in range(NPOOL)] for q in ("ms", "mg", "mh")}
ENGPOOL = {b: [f"{b}{i}" for i in range(NEPOOL)] for b in ESEM.values()}
ALLSEMS = [s for q in DMAPOOL.values() for s in q] + [s for p in ENGPOOL.values() for s in p]
QENG = {"ms": "sync", "mg": "gpsimd", "mh": "scalar"}


class Sch:
    def __init__(self):
        self.ops = {e: [] for e in ENGINES}
        self.cnt = {}

    def qtot(self, q):
        return [(sem, self.cnt[sem]) for sem in DMAPOOL[q] if self.cnt.get(sem, 0) > 0]

    def emit(self, eng, fn, waits=(), sig=None, amt=1):
        w = []
        for t in waits:
            if t is None:
                continue
            if isinstance(t, list):
                w.extend(t)
            else:
                w.append(t)
        tok = None
        if sig is not None:
            if sig in ENGPOOL:
                if not hasattr(self, "erot"):
                    self.erot = {}
                self.erot[sig] = (self.erot.get(sig, -1) + 1) % NEPOOL
                sig = ENGPOOL[sig][self.erot[sig]] if False else f"{sig}{self.erot[sig]}"
            self.cnt[sig] = self.cnt.get(sig, 0) + amt
            tok = (sig, self.cnt[sig])
        self.ops[eng].append((fn, w, sig, amt))
        return tok

    def dma(self, q, dst, src, waits=(), **kw):
        return self.dma_burst(q, [(dst, src, kw)], waits=waits)

    def dma_burst(self, q, items, waits=()):
        if not hasattr(self, "rot"):
            self.rot = {}
        self.rot[q] = (self.rot.get(q, -1) + 1) % NPOOL
        sem = DMAPOOL[q][self.rot[q]]
        eng = QENG[q]
        prev = self.cnt.get(sem, 0)
        w = [t for t in waits if t is not None]
        if prev > 0:
            w = w + [(sem, prev)]
        tok = None
        for i, (dst, src, kw) in enumerate(items):
            tok = self.emit(eng, lambda e, d=dst, s=src, kw=kw: e.dma_start(out=d, in_=s, **kw),
                            waits=(w if i == 0 else ()), sig=sem, amt=16)
        return tok

    def barrier(self):
        toks = {}
        for e in ENGINES:
            toks[e] = self.emit(e, lambda eng: eng.nop(), sig=ESEM[e])
        dt_ = [(sm, self.cnt[sm]) for sm in ALLSEMS if self.cnt.get(sm, 0) > 0]
        for e in ENGINES:
            self.emit(e, None, waits=[toks[o] for o in ENGINES if o != e] + dt_)

    def run(self, eng_name, eng, sems):
        seen = {}
        for fn, wl, sig, amt in self.ops[eng_name]:
            for s, v in wl:
                if seen.get(s, -1) < v:
                    eng.wait_ge(sems[s], v)
                    seen[s] = v
            ins = fn(eng) if fn is not None else None
            if sig is not None:
                ins.then_inc(sems[sig], amt)


class Arena:
    def __init__(self, tile, total, name):
        self.tile, self.total, self.name = tile, total, name
        self.blocks = {}
        self.hw = 0

    def alloc(self, key, size):
        assert key not in self.blocks, key
        taken = sorted(self.blocks.values())
        off = 0
        for (o, s) in taken:
            if off + size <= o:
                break
            off = o + s
        assert off + size <= self.total, f"{self.name} OOM: {key}({size})"
        self.blocks[key] = (off, size)
        self.hw = max(self.hw, off + size)
        return self.tile[:, off:off + size]

    def free(self, *keys):
        for k in keys:
            del self.blocks[k]


def prep_host(inp):
    f32 = np.float32
    bf = ml_dtypes.bfloat16
    out = {}
    out["qwT"] = np.ascontiguousarray(np.asarray(inp["L0_attn_q_w"], f32).T)
    out["kwT"] = np.ascontiguousarray(np.asarray(inp["L0_attn_k_w"], f32).T)
    out["vwT"] = np.ascontiguousarray(np.asarray(inp["L0_attn_v_w"], f32).T)
    bsp = np.zeros((D, 128), f32)
    for h in range(H):
        bsp[:, 32 * h] = np.asarray(inp["L0_attn_b_w"], f32)[h, :]
    out["bwTsp"] = bsp
    cds = np.zeros((3, 8, 4, 128, 32), f32)
    for ti, nm in enumerate(("convq_w", "convk_w", "convv_w")):
        cw = np.asarray(inp["L0_attn_" + nm], f32)
        for c in range(8):
            for g in range(4):
                d_ = cw[c * 128:(c + 1) * 128, g]
                for i4 in range(4):
                    cds[ti, c, g, 32 * i4:32 * i4 + 32, :] = np.diag(d_[32 * i4:32 * i4 + 32])
    out["cd"] = cds
    firs = [np.asarray(inp["L0_attn_fir_s"], f32).reshape(D, -1),
            np.asarray(inp["L0_attn_fir_l"], f32).reshape(D, -1),
            np.asarray(inp["L0_attn_fir_wd"], f32).reshape(D, -1)]
    groups = []
    for fi, filt in enumerate(firs):
        K = filt.shape[1]
        for Gf in range((K + 3) // 4):
            groups.append((fi, Gf, 64 - 4 * Gf))
    assert len(groups) == NGF
    fd = np.zeros((8, 4, NGF, 128, 32), f32)
    for gi, (fi, Gf, _t) in enumerate(groups):
        filt = firs[fi]
        K = filt.shape[1]
        for c in range(8):
            for s4 in range(4):
                for g in range(4):
                    k = K - 1 - 4 * Gf + g - 3
                    if 0 <= k < K:
                        d_ = filt[c * 128 + s4 * 32: c * 128 + s4 * 32 + 32, k]
                        fd[c, s4, gi, 32 * g:32 * g + 32, :] = np.diag(d_)
    out["fird"] = fd
    g1T = np.zeros((1152, 2048), f32)
    g1T[:1084, :] = np.asarray(inp["L0_attn_g1_w"], f32).T
    out["g1Tp"] = g1T
    out["g1b"] = np.ascontiguousarray(np.asarray(inp["L0_attn_g1_b"], f32).reshape(16, 128).T)
    out["g2T"] = np.ascontiguousarray(np.asarray(inp["L0_attn_g2_w"], f32).T)
    out["g2b"] = np.asarray(inp["L0_attn_g2_b"], f32).reshape(1, 20)
    out["vbias"] = np.asarray(inp["L0_attn_value_bias"], f32).reshape(1, 4)
    out["ltemp"] = np.asarray(inp["L0_attn_log_temp"], f32).reshape(1, 4)
    out["flog"] = np.asarray(inp["L0_attn_floor_logit"], f32).reshape(1, 4)
    out["onorm_rep"] = np.ascontiguousarray(np.tile(np.asarray(inp["L0_attn_onorm_w"], f32).reshape(1, 256), (128, 4)))
    out["owT"] = np.ascontiguousarray(np.asarray(inp["L0_attn_o_w"], f32).T)
    out["wg"] = np.ascontiguousarray(np.asarray(inp["L0_ffn_w_gate"], f32))
    out["wu"] = np.ascontiguousarray(np.asarray(inp["L0_ffn_w_up"], f32))
    out["wdn"] = np.ascontiguousarray(np.asarray(inp["L0_ffn_w_down"], f32))
    out["eye128"] = np.eye(128, dtype=bf)
    out["ident32"] = np.eye(128, dtype=f32)
    out["eye4"] = np.tile(np.eye(128, dtype=f32), (1, 4)).astype(bf)
    out["allones"] = np.ones((128, 128), dtype=bf)
    out["zeros128"] = np.zeros((128, 128), dtype=bf)
    oc = np.zeros((128, 256), f32)
    oc[:, 127] = 1
    out["onescol"] = oc.astype(bf)
    r = np.arange(128)
    out["triuI"] = (r[:, None] <= r[None, :]).astype(bf)
    bdm = (r[:, None] // 32 == r[None, :] // 32) & (r[:, None] > r[None, :])
    out["negbd"] = (-bdm.astype(f32)).astype(bf)
    out["blkLow"] = ((r[:, None] // 32) > (r[None, :] // 32)).astype(bf)
    out["blkUp"] = ((r[:, None] // 32) < (r[None, :] // 32)).astype(bf)
    hc4 = np.zeros((8, 4, 128, 16), f32)
    for c in range(8):
        for slot in range(4):
            hc4[c, slot, :, 4 * slot + c // 2] = 1.0
    out["headcol4"] = hc4.astype(bf)
    imask = np.zeros((1, 20), f32)
    imask[0, 4::5] = 1.0
    out["i4mask"] = imask
    return out, groups


def build(groups):
    global PH, SUB56
    nc = bass.Bass("TRN2", target_bir_lowering=False)
    S = Sch()
    dr = {}

    def din(name, shape, dt=F32):
        dr[name] = nc.dram_tensor(name, list(shape), dt, kind="ExternalInput").ap()

    din("x", (L, D))
    for nm in ("qwT", "kwT", "vwT"):
        din(nm, (D, D))
    din("bwTsp", (D, 128))
    din("cd", (3, 8, 4, 128, 32))
    din("fird", (8, 4, NGF, 128, 32))
    din("g1Tp", (1152, 2048))
    din("g1b", (128, 16))
    din("g2T", (2048, 20))
    din("g2b", (1, 20)); din("vbias", (1, 4)); din("ltemp", (1, 4)); din("flog", (1, 4))
    din("onorm_rep", (128, D))
    din("owT", (D, D))
    din("wg", (IFF, D)); din("wu", (IFF, D)); din("wdn", (D, IFF))
    for nm in ("eye128", "allones", "triuI", "negbd", "blkLow", "blkUp", "zeros128"):
        din(nm, (128, 128), BF16)
    din("ident32", (128, 128), F32)
    din("eye4", (128, 512), BF16)
    din("onescol", (128, 256), BF16)
    din("headcol4", (8, 4, 128, 16), BF16)
    din("i4mask", (1, 20))

    y = nc.dram_tensor("y", [L, D], F32, kind="ExternalOutput").ap()
    vdram = nc.dram_tensor("vdram", [D, 128 + L], BF16, kind="Internal").ap()
    xqdram = nc.dram_tensor("xqdram", [L, D], BF16, kind="Internal").ap()
    sdram = nc.dram_tensor("sdram", [L, D], F32, kind="Internal").ap()
    dbg = {}
    if DEBUG:
        for nm, shape in [("dbg_xn", (L, D)), ("dbg_q", (D, L)), ("dbg_k", (D, L)),
                          ("dbg_v", (D, L)), ("dbg_beta", (128, L)),
                          ("dbg_qh", (D, L)), ("dbg_kb", (D, L)), ("dbg_negw", (D, L)),
                          ("dbg_u", (L, D)), ("dbg_delta", (L, D)),
                          ("dbg_short", (L, D)), ("dbg_long", (L, D)), ("dbg_wide", (L, D)),
                          ("dbg_stat", (L, 64)), ("dbg_logits", (L, 20)),
                          ("dbg_otm", (L, D)), ("dbg_xq", (L, D)),
                          ("dbg_hq", (L, IFF)), ("dbg_h", (L, IFF)), ("dbg_preq", (D, L)), ("dbg_sums", (128, L))]:
            dbg[nm] = nc.dram_tensor(nm, list(shape), F32, kind="ExternalOutput").ap()

    ctx = []

    def sbuf(name, shape, dt):
        t = nc.sbuf_tensor(name, list(shape), dt)
        h_ = t.__enter__()
        ctx.append(t)
        return h_

    def psum(name, shape, dt=F32):
        t = nc.psum_tensor(name, list(shape), dt)
        h_ = t.__enter__()
        ctx.append(t)
        return h_

    AB = Arena(sbuf("AB", [128, 79360], BF16), 79360, "AB")
    AFp = Arena(sbuf("AFv", [128, 10240], F32), 10240, "AFp")
    AI = Arena(sbuf("AIv", [128, 2816], I32), 2816, "AI")
    AU = Arena(sbuf("AUv", [128, 1408], U8), 1408, "AU")

    PS_S = [psum(f"psS{h}", [128, 512]) for h in range(H)]
    PW = [psum(f"psW{i}", [128, 512]) for i in range(4)]

    C = {}
    for nm, w_ in [("eye128", 128), ("allones", 128), ("triuI", 128), ("negbd", 128),
                   ("blkLow", 128), ("blkUp", 128), ("zeros128", 128), ("eye4", 512), ("onescol", 256)]:
        C[nm] = AB.alloc(nm, w_)
    C["ident32"] = AFp.alloc("ident32", 128)
    C["onorm_rep"] = AFp.alloc("onorm_rep", D)
    C["g1b"] = AFp.alloc("g1b", 16)
    C["cd"] = AB.alloc("cd", 3 * 8 * 4 * 32)
    C["g2T"] = AB.alloc("g2T", 16 * 20)
    C["hc4"] = AB.alloc("hc4", 8 * 4 * 16)
    sm = AFp.alloc("smallrows", 64)

    tld = []
    for nm in ("eye128", "allones", "triuI", "negbd", "blkLow", "blkUp", "zeros128", "eye4", "onescol"):
        tld.append(S.dma("ms", C[nm][:], dr[nm]))
    tld.append(S.dma("ms", C["ident32"][:], dr["ident32"]))
    tld.append(S.dma("ms", C["onorm_rep"][:], dr["onorm_rep"]))
    tld.append(S.dma("ms", C["g1b"][:], dr["g1b"]))
    tld.append(S.dma("mg", C["cd"][:].rearrange("p (t c g w) -> p t c g w", t=3, c=8, g=4),
                     dr["cd"].rearrange("t c g p w -> p t c g w")))
    tld.append(S.dma("mg", C["g2T"][:].rearrange("p (c w) -> p c w", c=16),
                     dr["g2T"].rearrange("(c p) w -> p c w", p=128)))
    tld.append(S.dma("mg", C["hc4"][:].rearrange("p (c s w) -> p c s w", c=8, s=4),
                     dr["headcol4"].rearrange("c s p w -> p c s w")))
    tld.append(S.dma("ms", sm[0:1, 0:20], dr["g2b"]))
    tld.append(S.dma("ms", sm[0:1, 20:24], dr["vbias"]))
    tld.append(S.dma("ms", sm[0:1, 24:28], dr["ltemp"]))
    tld.append(S.dma("ms", sm[0:1, 28:32], dr["flog"]))
    tld.append(S.dma("ms", sm[0:1, 32:52], dr["i4mask"]))

    zt = AB.alloc("zt", 128)
    tz = S.emit("vector", lambda e: e.memset(zt[:], 0.0), sig="dv")
    for c in range(8):
        S.dma("mg", vdram[c * 128:(c + 1) * 128, 0:128], zt[:], waits=[tz])

    grow = AFp.alloc("grow", 80)
    fscr = AFp.alloc("fscr", 20)
    vb_sp = sm[0:1, 20:24].broadcast_to([1, 4, 5])
    lt_sp = sm[0:1, 24:28].broadcast_to([1, 4, 5])
    fl_sp = sm[0:1, 28:32].broadcast_to([1, 4, 5])
    g3 = lambda ap: ap.rearrange("p (a b) -> p a b", b=5)
    w_all = list(tld)
    t1 = S.emit("vector", lambda e: e.tensor_tensor(g3(grow[0:1, 0:20]), vb_sp, g3(sm[0:1, 32:52]), OP.mult),
                waits=w_all, sig="dv")
    t1 = S.emit("vector", lambda e: e.tensor_tensor(grow[0:1, 0:20], grow[0:1, 0:20], sm[0:1, 0:20], OP.add),
                waits=[t1], sig="dv")
    t2 = S.emit("scalar", lambda e: e.activation(g3(grow[0:1, 20:40]), lt_sp, AF.Exp, scale=-1.0), waits=w_all, sig="dc")
    tf1 = S.emit("scalar", lambda e: e.activation(g3(fscr[0:1, :]), fl_sp, AF.Sigmoid), waits=w_all, sig="dc")
    t4 = S.emit("vector", lambda e: e.tensor_scalar(grow[0:1, 40:60], fscr[0:1, :], -4.0, 1.0, OP.mult, OP.add),
                waits=[tf1], sig="dv")
    tf2 = S.emit("vector", lambda e: e.tensor_tensor(grow[0:1, 60:80], fscr[0:1, :], sm[0:1, 32:52], OP.mult),
                 waits=[tf1], sig="dv")
    tf3 = S.emit("vector", lambda e: e.scalar_tensor_tensor(grow[0:1, 60:80], grow[0:1, 60:80], -1.0, fscr[0:1, :],
                                                            OP.mult, OP.add), waits=[tf2], sig="dv")
    growb = AB.alloc("growb", 80)
    tcast = S.emit("vector", lambda e: e.tensor_copy(growb[0:1, :], grow[0:1, :]), waits=[t1, t2, t4, tf3], sig="dv")
    gconst = AFp.alloc("gconst", 80)
    tgc = None
    for i in range(4):
        tmm = S.emit("tensor", lambda e, i=i: e.matmul(PW[3][:, 0:20], C["allones"][0:1, :],
                                                       growb[0:1, i * 20:(i + 1) * 20], start=True, stop=True),
                     waits=[tcast, tgc], sig="dt")
        tgc = S.emit("vector", lambda e, i=i: e.tensor_copy(gconst[:, i * 20:(i + 1) * 20], PW[3][:, 0:20]),
                     waits=[tmm], sig="dv")
    AFp.free("grow", "fscr")
    AB.free("growb", "zt")
    S.barrier()

    xn_fm = AB.alloc("xn_fm", 8 * W)
    qh_fm = AB.alloc("qh_fm", 8 * W)
    kh_fm = AB.alloc("kh_fm", 8 * W)
    vf_fm = AB.alloc("vf_fm", 8 * W)
    S_bf = AB.alloc("S_bf", H * 512)
    tails = AB.alloc("tails", 3 * 8 * 64)
    statF = AB.alloc("statF", NT * 128)
    S32 = AFp.alloc("S32", H * 512)

    tsz = S.emit("vector", lambda e: e.memset(statF[:], 0.0), sig="dv")

    TP = dict(tok_prev=None)
    tp = {}

    def fmsl(tile, c, tt):
        return tile[:, c * W + tt * 128: c * W + tt * 128 + 128]

    def bcast128(constv):
        return constv.rearrange("p (x w) -> p x w", x=1).broadcast_to([128, 4, 128]).rearrange("p a w -> p (a w)")

    def emit_quant(src, dst, Fn, wtok, key, scratch=None, dst_war=None, fast=False):
        nblk = Fn // 16
        nb2 = (nblk + 15) // 16 * 16
        qbm = AFp.alloc(key + "bm", 2 * nb2)
        qv = AFp.alloc(key + "v", Fn)
        qA = AFp.alloc(key + "A", Fn)
        qBC = AFp.alloc(key + "BC", Fn)
        qt2 = scratch[:, 0:Fn] if scratch is not None else AFp.alloc(key + "t2", Fn)
        qi = AI.alloc(key + "i", Fn)
        qi2 = AI.alloc(key + "i2", Fn)
        qm = AU.alloc(key + "m", Fn)
        t0 = S.emit("vector", lambda e: e.tensor_reduce(qbm[:, 0:nblk], src.rearrange("p (b k) -> p b k", k=16),
                                                        axis=AX, op=OP.max, apply_absolute_value=True),
                    waits=[wtok], sig="dv")
        t1 = S.emit("vector", lambda e: e.tensor_scalar(qbm[:, nb2:nb2 + nblk], qbm[:, 0:nblk], 1.0 / 6.0, 1e-12,
                                                        OP.mult, OP.max), waits=[t0], sig="dv")
        t2 = S.emit("vector", lambda e: e.reciprocal(qbm[:, 0:nblk], qbm[:, nb2:nb2 + nblk]), waits=[t1], sig="dv")
        rv = qbm[:, 0:nblk].broadcast_to([128, nblk, 16])
        sv = qbm[:, nb2:nb2 + nblk].broadcast_to([128, nblk, 16])
        sview = src.rearrange("p (b k) -> p b k", k=16)
        t3 = S.emit("vector", lambda e: e.tensor_tensor(qv[:, :].rearrange("p (b k) -> p b k", k=16), sview, rv, OP.mult),
                    waits=[t2], sig="dv")
        t4 = S.emit("vector", lambda e: e.tensor_scalar(qi[:, :], qv[:, :], 2.0, None, OP.mult), waits=[t3], sig="dv")
        t5 = S.emit("vector", lambda e: e.tensor_scalar(qA[:, :], qi[:, :], 0.5, None, OP.mult), waits=[t4], sig="dv")
        t7 = S.emit("vector", lambda e: e.tensor_scalar(qi2[:, :], qv[:, :], 4.0, -4.0, OP.min, OP.max),
                    waits=[t3], sig="dv")
        qi3 = AI.alloc(key + "i3", Fn) if fast else qi
        t8 = S.emit("vector", lambda e, qi3=qi3: e.tensor_scalar(qi3[:, :], qv[:, :], 0.1, None, OP.mult),
                    waits=([t3] if fast else [t3, t5]), sig="dv")
        t10 = S.emit("vector", lambda e, qi3=qi3: e.scalar_tensor_tensor(qBC[:, :], qi3[:, :], 2.0, qi2[:, :],
                                                                         OP.mult, OP.add),
                     waits=[t7, t8], sig="dv")
        if fast:
            t11a = S.emit("scalar", lambda e: e.activation(qt2[:, :], qv[:, :], AF.Square),
                          waits=[t3], sig="dc")
        else:
            t11a = S.emit("vector", lambda e: e.tensor_tensor(qt2[:, :], qv[:, :], qv[:, :], OP.mult),
                          waits=[t3], sig="dv")
        t11 = S.emit("vector", lambda e: e.tensor_scalar(qm[:, :], qt2[:, :], 4.0, None, OP.is_gt),
                     waits=[t11a], sig="dv")
        t12 = S.emit("vector", lambda e: e.copy_predicated(qA[:, :], qm[:, :], qBC[:, :]),
                     waits=[t10, t11, t5], sig="dv")
        qAv = qA[:, :].rearrange("p (b k) -> p b k", k=16)
        t13 = S.emit("vector", lambda e: e.tensor_tensor(dst.rearrange("p (b k) -> p b k", k=16), qAv, sv, OP.mult),
                     waits=[t12, dst_war], sig="dv")
        AFp.free(key + "bm", key + "v", key + "A", key + "BC")
        if scratch is None:
            AFp.free(key + "t2")
        AI.free(key + "i", key + "i2")
        if fast:
            AI.free(key + "i3")
        AU.free(key + "m")
        return t13, t0

    for mt in range(N_MEG_BUILD):
        mb = mt * TMTOK
        first = (mt == 0)

        x_tm = AFp.alloc("x_tm", 1024)
        xn32 = AFp.alloc("xn32", 1024)
        lnS = AFp.alloc("lnS", 16)
        tmpS = AFp.alloc("tmpS", 1024)
        xn_tt = AB.alloc("xn_tt", 1024)
        xq_tm = AB.alloc("xq_tm", 1024)

        for tt in range(NT):
            r0 = mb + tt * 128
            txl = S.dma("mg", x_tm[:, :], dr["x"][r0:r0 + 128, :],
                        waits=[tp.get("xtmfree"), tp.get("mttail"), tp.get("sdma")])
            t1 = S.emit("scalar", lambda e: e.activation(tmpS[:, :], x_tm[:, :], AF.Square, accum_out=lnS[:, 0:1]),
                        waits=[txl], sig="dc")
            t2 = S.emit("vector", lambda e: e.tensor_reduce(lnS[:, 1:2], x_tm[:, :], axis=AX, op=OP.add),
                        waits=[txl], sig="dv")
            t3 = S.emit("vector", lambda e: e.tensor_scalar(lnS[:, 2:3], lnS[:, 1:2], 1.0 / D, None, OP.mult),
                        waits=[t2], sig="dv")
            t4 = S.emit("vector", lambda e: e.tensor_tensor(lnS[:, 3:4], lnS[:, 2:3], lnS[:, 2:3], OP.mult),
                        waits=[t3], sig="dv")
            t5 = S.emit("vector", lambda e: e.scalar_tensor_tensor(lnS[:, 4:5], lnS[:, 0:1], 1.0 / D, lnS[:, 3:4],
                                                                   OP.mult, OP.subtract), waits=[t1, t4], sig="dv")
            t6 = S.emit("vector", lambda e: e.tensor_scalar(lnS[:, 5:6], lnS[:, 4:5], 1e-5, None, OP.add),
                        waits=[t5], sig="dv")
            t7 = S.emit("vector", lambda e: e.reciprocal(lnS[:, 6:7], lnS[:, 5:6]), waits=[t6], sig="dv")
            t8 = S.emit("scalar", lambda e: e.activation(lnS[:, 7:8], lnS[:, 6:7], AF.Sqrt), waits=[t7], sig="dc")
            t9 = S.emit("vector", lambda e: e.tensor_scalar(lnS[:, 8:9], lnS[:, 2:3], -1.0, None, OP.mult),
                        waits=[t3], sig="dv")
            t10 = S.emit("vector", lambda e: e.tensor_scalar(xn32[:, :], x_tm[:, :], lnS[:, 8:9], lnS[:, 7:8],
                                                             OP.add, OP.mult),
                         waits=[t8, t9, tp.get("qprev"), tp.get("qprev2")], sig="dv")
            tp["xtmfree"] = t10
            t11 = S.emit("vector", lambda e: e.tensor_copy(xn_tt[:, :], xn32[:, :]),
                         waits=[t10, tp.get("xnxpo")], sig="dv")
            tq, tq0 = emit_quant(xn32[:, :], xq_tm[:, :], 1024, t10, "q1", scratch=tmpS,
                                 dst_war=tp.get("xqout"))
            tp["qprev"] = tq0
            if DEBUG:
                tp["qprev2"] = S.dma("mg", dbg["dbg_xn"][r0:r0 + 128, :], xn32[:, :], waits=[t10, tq])
            tp["xqout"] = S.dma("mg", xqdram[r0:r0 + 128, :], xq_tm[:, :], waits=[tq])
            if DEBUG:
                tp["xqout"] = S.dma("mg", dbg["dbg_xq"][r0:r0 + 128, :], xq_tm[:, :],
                                    waits=[tq, tp.get("xqout")])
            for c in range(8):
                S.dma("ms", xn_fm[:, c * W + tt * 128: c * W + tt * 128 + 128],
                      xn_tt[:, c * 128:(c + 1) * 128], waits=[t11, tp.get("xnfree")], transpose=True)
            tp["xnxpo"] = S.qtot("ms")
        tok_xpo = S.qtot("ms")
        AFp.free("x_tm", "xn32", "lnS", "tmpS")
        AB.free("xn_tt")
        tp.pop("xtmfree", None)
        tp.pop("xnxpo", None)

        if PH < 2: continue
        beta_sp = AFp.alloc("beta_sp", W)
        s2q_sp = AFp.alloc("s2q_sp", W)
        s2k_sp = AFp.alloc("s2k_sp", W)
        rkb_sp = AFp.alloc("rkb_sp", W)
        pre_q = AB.alloc("pre_q", 8 * WP)
        pre_k = AB.alloc("pre_k", 8 * WP)
        pre_v = AB.alloc("pre_v", 8 * WP)
        wbuf = AB.alloc("wbuf", 4 * 1024)
        tok_tail = []
        for ti, pre in enumerate((pre_q, pre_k, pre_v)):
            for c in range(8):
                tl = tails[:, (ti * 8 + c) * 64:(ti * 8 + c) * 64 + 64]
                if first:
                    tok_tail.append(S.emit("vector", lambda e, pre=pre, c=c: e.memset(pre[:, c * WP:c * WP + 64], 0.0),
                                           waits=[tp.get("mttail")], sig="dv"))
                else:
                    tok_tail.append(S.emit("vector", lambda e, pre=pre, c=c, tl=tl: e.tensor_copy(
                        pre[:, c * WP:c * WP + 64], tl), waits=[tp.get("mttail")], sig="dv"))
        tok_proj = {}
        order = [(n, dc) for n in ("q", "k", "v") for dc in range(8)]
        wsel = {"q": ("qwT", pre_q), "k": ("kwT", pre_k), "v": ("vwT", pre_v)}
        for oi, (name, dc) in enumerate(order):
            wnm, pre = wsel[name]
            buf = wbuf[:, (oi % 4) * 1024:(oi % 4) * 1024 + 1024]
            war = tok_proj.get(order[oi - 4]) if oi >= 4 else None
            twl = S.dma("mg", buf.rearrange("p (c w) -> p c w", c=8),
                        dr[wnm].rearrange("(c p) d -> p c d", p=128)[:, :, dc * 128:(dc + 1) * 128],
                        waits=[war, tp.get("mttail"), tp.get("sdma")])
            mmt = None
            for kc in range(8):
                mmt = S.emit("tensor", lambda e, kc=kc, buf=buf, oi=oi: e.matmul(
                    PW[oi % 2][:, 0:512], buf[:, kc * 128:(kc + 1) * 128], xn_fm[:, kc * W:(kc + 1) * W],
                    start=(kc == 0), stop=(kc == 7)),
                    waits=[twl, tok_xpo, tp.get(("projps", oi % 2))], sig="dt")
            ev = S.emit("scalar", lambda e, pre=pre, dc=dc, oi=oi: e.activation(
                pre[:, dc * WP + 64: dc * WP + 64 + W], PW[oi % 2][:, 0:512], AF.Identity),
                waits=[mmt] + tok_tail, sig="dc")
            tok_proj[(name, dc)] = ev
            tp[("projps", oi % 2)] = ev
        twb = S.dma("mg", wbuf[:, 0:1024].rearrange("p (c w) -> p c w", c=8),
                    dr["bwTsp"].rearrange("(c p) d -> p c d", p=128), waits=[tok_proj[("v", 6)]])
        bmm = None
        for kc in range(8):
            bmm = S.emit("tensor", lambda e, kc=kc: e.matmul(
                PW[2][:, 0:512], wbuf[:, kc * 128:(kc + 1) * 128], xn_fm[:, kc * W:(kc + 1) * W],
                start=(kc == 0), stop=(kc == 7)), waits=[twb, tp.get("ps2")], sig="dt")
        tok_beta = S.emit("scalar", lambda e: e.activation(beta_sp[:, :], PW[2][:, 0:512], AF.Sigmoid),
                          waits=[bmm], sig="dc")
        tp["ps2"] = tok_beta
        if DEBUG:
            for c in range(8):
                S.dma("mg", dbg["dbg_preq"][c * 128:(c + 1) * 128, mb:mb + W],
                      pre_q[:, c * WP + 64:c * WP + 64 + W], waits=[tok_proj[("q", c)]])
        tok_savetail = []
        for ti, (name, pre) in enumerate((("q", pre_q), ("k", pre_k), ("v", pre_v))):
            for c in range(8):
                tl = tails[:, (ti * 8 + c) * 64:(ti * 8 + c) * 64 + 64]
                tok_savetail.append(S.emit("vector", lambda e, pre=pre, c=c, tl=tl: e.tensor_copy(
                    tl, pre[:, c * WP + W: c * WP + W + 64]), waits=[tok_proj[(name, c)]], sig="dv"))

        if PH < 3: continue
        cdv = C["cd"][:].rearrange("p (t c g w) -> p t c g w", t=3, c=8, g=4)
        tok_conv = {}
        for wi, (name, pre, outt) in enumerate((("q", pre_q, qh_fm), ("k", pre_k, kh_fm), ("v", pre_v, vf_fm))):
            for c in range(8):
                mmt = S.emit("tensor", lambda e, c=c, pre=pre: e.matmul(
                    PW[3][:, 0:512], C["zeros128"][:, :], pre[:, c * WP + 64: c * WP + 64 + W],
                    start=True, stop=False), waits=[tok_proj[(name, c)], tp.get("ps3")], sig="dt")
                for s4 in range(4):
                    for g in range(4):
                        mmt = S.emit("tensor", lambda e, wi=wi, c=c, g=g, s4=s4, pre=pre: e.matmul(
                            PW[3][32 * s4:32 * s4 + 32, 0:512],
                            cdv[32 * s4:32 * s4 + 32, wi, c, g, :],
                            pre[32 * s4:32 * s4 + 32, c * WP + 61 + g: c * WP + 61 + g + W],
                            start=False, stop=False,
                            tile_position=(32 * s4, 32 * s4)),
                            waits=[tok_proj[(name, c)]], sig="dt")
                mmt = S.emit("tensor", lambda e, c=c, pre=pre: e.matmul(
                    PW[3][:, 0:512], C["zeros128"][:, :], pre[:, c * WP + 64: c * WP + 64 + W],
                    start=False, stop=True), sig="dt")
                tok_conv[(name, c)] = S.emit("scalar", lambda e, c=c, outt=outt: e.activation(
                    outt[:, c * W:(c + 1) * W], PW[3][:, 0:512], AF.Silu), waits=[mmt], sig="dc")
                tp["ps3"] = tok_conv[(name, c)]
        if DEBUG:
            for nm, key in (("dbg_q", "q"), ("dbg_k", "k"), ("dbg_v", "v")):
                tile = {"q": qh_fm, "k": kh_fm, "v": vf_fm}[key]
                for c in range(8):
                    S.dma("mg", dbg[nm][c * 128:(c + 1) * 128, mb:mb + W], tile[:, c * W:(c + 1) * W],
                          waits=[tok_conv[(key, c)]])
            S.dma("mg", dbg["dbg_beta"][:, mb:mb + W], beta_sp[:, :], waits=[tok_beta])
        if DEBUG:
            S.barrier()
        AB.free("pre_q", "pre_k", "pre_v", "wbuf")

        if PH < 4: continue
        k_tm = AB.alloc("k_tm", NT * 1024)
        v_tm = AB.alloc("v_tm", NT * 1024)
        vbT = AB.alloc("vbT", NT * 1024)
        kbT = AB.alloc("kbT", NT * 1024)
        khT = AB.alloc("khT", NT * 1024)
        kb_fm = AB.alloc("kb_fm", 8 * W)
        tok_vtm = {}
        tok_ktm = {}
        for c in range(8):
            S.dma("mg", vdram[c * 128:(c + 1) * 128, 128 + mb:128 + mb + W], vf_fm[:, c * W:(c + 1) * W],
                  waits=[tok_conv[("v", c)]])
            for tt in range(NT):
                tok_vtm[(tt, c)] = S.dma("ms", v_tm[:, tt * 1024 + c * 128: tt * 1024 + (c + 1) * 128],
                                         vf_fm[:, c * W + tt * 128: c * W + tt * 128 + 128],
                                         waits=[tok_conv[("v", c)], tp.get("mttail")], transpose=True)
                tok_ktm[(tt, c)] = S.dma("mh", k_tm[:, tt * 1024 + c * 128: tt * 1024 + (c + 1) * 128],
                                         kh_fm[:, c * W + tt * 128: c * W + tt * 128 + 128],
                                         waits=[tok_conv[("k", c)], tp.get("mttail")], transpose=True)
        tok_vdr_w = S.qtot("mg")
        tok_ktm_all = S.qtot("mh")
        tok_vtm_all = S.qtot("ms")

        if PH < 5: continue
        sq_t = AB.alloc("sq_t", 2 * W)
        tok_s2 = {}
        for ti, (name, src, dstsp) in enumerate((("q", qh_fm, s2q_sp), ("k", kh_fm, s2k_sp))):
            mmt = None
            sqt = {}
            for c in range(8):
                half = c % 2
                war = tok_s2.get((name, c - 2))
                if war is None and name == "k":
                    war = tok_s2.get(("q", 6 + c))
                sqt[c] = S.emit("scalar", lambda e, c=c, src=src, half=half: e.activation(
                    sq_t[:, half * W:(half + 1) * W], src[:, c * W:(c + 1) * W], AF.Square),
                    waits=[tok_conv[(name, c)], war], sig="dc")
                h = c // 2
                mmt = S.emit("tensor", lambda e, c=c, h=h, half=half: e.matmul(
                    PW[2][:, 0:512], C["onescol"][:, 127 - 32 * h:255 - 32 * h], sq_t[:, half * W:(half + 1) * W],
                    start=(c == 0), stop=(c == 7)),
                    waits=[sqt[c], tp.get("ps2")], sig="dt")
                tok_s2[(name, c)] = mmt
            tok_s2[name] = S.emit("vector", lambda e, dstsp=dstsp: e.tensor_copy(dstsp[:, :], PW[2][:, 0:512]),
                                  waits=[mmt], sig="dv")
            tp["ps2"] = tok_s2[name]
        AB.free("sq_t")
        tok_rsq = {}
        for name, sp in (("q", s2q_sp), ("k", s2k_sp)):
            a = S.emit("vector", lambda e, sp=sp: e.tensor_scalar(sp[:, :], sp[:, :], 1e-6, None, OP.add),
                       waits=[tok_s2[name]], sig="dv")
            b = S.emit("vector", lambda e, sp=sp: e.reciprocal(sp[:, :], sp[:, :]), waits=[a], sig="dv")
            tok_rsq[name] = S.emit("scalar", lambda e, sp=sp: e.activation(sp[:, :], sp[:, :], AF.Sqrt),
                                   waits=[b], sig="dc")
        tok_rkb = S.emit("vector", lambda e: e.tensor_tensor(rkb_sp[:, :], s2k_sp[:, :], beta_sp[:, :], OP.mult),
                         waits=[tok_rsq["k"], tok_beta], sig="dv")
        sprd3 = AB.alloc("sprd3", 3 * W)
        tok_sb = {}
        tok_sb["q"] = S.emit("vector", lambda e: e.tensor_copy(sprd3[:, 0:W], s2q_sp[:, :]),
                             waits=[tok_rsq["q"]], sig="dv")
        tok_sb["k"] = S.emit("vector", lambda e: e.tensor_copy(sprd3[:, W:2 * W], s2k_sp[:, :]),
                             waits=[tok_rsq["k"]], sig="dv")
        tok_sb["kb"] = S.emit("vector", lambda e: e.tensor_copy(sprd3[:, 2 * W:3 * W], rkb_sp[:, :]),
                              waits=[tok_rkb], sig="dv")
        tok_scaled = {}
        for c in range(8):
            h = c // 2
            tpos = (96, 0) if h == 3 else None
            bq = S.emit("tensor", lambda e, h=h, tpos=tpos: e.matmul(
                PW[2][:, 0:512], C["allones"][32 * h:32 * h + 1, :],
                sprd3[32 * h:32 * h + 1, 0:W], start=True, stop=True, tile_position=tpos),
                waits=[tok_sb["q"], tp.get("ps2")], sig="dt")
            tok_scaled[("q", c)] = S.emit("vector", lambda e, c=c: e.tensor_tensor(
                qh_fm[:, c * W:(c + 1) * W], qh_fm[:, c * W:(c + 1) * W], PW[2][:, 0:512], OP.mult),
                waits=[bq], sig="dv")
            tp["ps2"] = tok_scaled[("q", c)]
            bk = S.emit("tensor", lambda e, h=h, tpos=tpos: e.matmul(
                PW[3][:, 0:512], C["allones"][32 * h:32 * h + 1, :],
                sprd3[32 * h:32 * h + 1, 2 * W:3 * W], start=True, stop=True, tile_position=tpos),
                waits=[tok_sb["kb"], tp.get("ps3")], sig="dt")
            tok_scaled[("kb", c)] = S.emit("vector", lambda e, c=c: e.tensor_tensor(
                kb_fm[:, c * W:(c + 1) * W], kh_fm[:, c * W:(c + 1) * W], PW[3][:, 0:512], OP.mult),
                waits=[bk], sig="dv")
            bk2 = S.emit("tensor", lambda e, h=h, tpos=tpos: e.matmul(
                PW[3][:, 0:512], C["allones"][32 * h:32 * h + 1, :],
                sprd3[32 * h:32 * h + 1, W:2 * W], start=True, stop=True, tile_position=tpos),
                waits=[tok_sb["k"], tok_scaled[("kb", c)]], sig="dt")
            tok_scaled[("kh", c)] = S.emit("vector", lambda e, c=c: e.tensor_tensor(
                kh_fm[:, c * W:(c + 1) * W], kh_fm[:, c * W:(c + 1) * W], PW[3][:, 0:512], OP.mult),
                waits=[bk2, tok_ktm_all], sig="dv")
            tp["ps3"] = tok_scaled[("kh", c)]
        if DEBUG:
            for c in range(8):
                S.dma("mg", dbg["dbg_qh"][c * 128:(c + 1) * 128, mb:mb + W], qh_fm[:, c * W:(c + 1) * W],
                      waits=[tok_scaled[("q", c)]])
                S.dma("mg", dbg["dbg_kb"][c * 128:(c + 1) * 128, mb:mb + W], kb_fm[:, c * W:(c + 1) * W],
                      waits=[tok_scaled[("kb", c)]])

        if PH < 6: continue
        scT = AFp.alloc("scT", 3 * 128)
        tok_tmprod = {}
        for tt in range(NT):
            tsc = []
            for si, (sp, tok_src) in enumerate(((beta_sp, tok_beta), (s2k_sp, tok_rsq["k"]), (rkb_sp, tok_rkb))):
                tpx = S.emit("tensor", lambda e, sp=sp, tt=tt: e.transpose(
                    PW[2][:, 0:128], sp[:, tt * 128:(tt + 1) * 128], C["ident32"][:, :]),
                    waits=[tok_src, tp.get("ps2")], sig="dt")
                ev = S.emit("vector", lambda e, si=si: e.tensor_copy(scT[:, si * 128:(si + 1) * 128], PW[2][:, 0:128]),
                            waits=[tpx], sig="dv")
                tp["ps2"] = ev
                tsc.append(ev)

            def hview(si):
                return scT[:, si * 128: si * 128 + 128].rearrange("p (h r) -> p h r", h=4)[:, :, 0:1].broadcast_to(
                    [128, 4, 256])
            sl = slice(tt * 1024, (tt + 1) * 1024)
            wv_ = [tok_vtm[(tt, c)] for c in range(8)] + tsc
            wk_ = [tok_ktm[(tt, c)] for c in range(8)] + tsc
            if not SUB56:
                continue
            tok_tmprod[("vbT", tt)] = S.emit("vector", lambda e, sl=sl, tt=tt: e.tensor_tensor(
                vbT[:, sl].rearrange("p (h c) -> p h c", h=4), v_tm[:, sl].rearrange("p (h c) -> p h c", h=4),
                hview(0), OP.mult), waits=wv_, sig="dv")
            tok_tmprod[("kbT", tt)] = S.emit("vector", lambda e, sl=sl, tt=tt: e.tensor_tensor(
                kbT[:, sl].rearrange("p (h c) -> p h c", h=4), k_tm[:, sl].rearrange("p (h c) -> p h c", h=4),
                hview(2), OP.mult), waits=wk_, sig="dv")
            tok_tmprod[("khT", tt)] = S.emit("vector", lambda e, sl=sl, tt=tt: e.tensor_tensor(
                khT[:, sl].rearrange("p (h c) -> p h c", h=4), k_tm[:, sl].rearrange("p (h c) -> p h c", h=4),
                hview(1), OP.mult), waits=wk_, sig="dv")
            tp["ps2"] = tok_tmprod[("khT", tt)]
        AB.free("k_tm", "sprd3")
        AFp.free("scT", "beta_sp", "s2q_sp", "s2k_sp", "rkb_sp")
        S.barrier()

        if PH < 7: continue
        negw_fm = AB.alloc("negw_fm", 8 * W)
        delta_tm = AB.alloc("delta_tm", NT * 1024)
        B_b = AB.alloc("B_b", 512)
        BT_b = AB.alloc("BT_b", 512)
        pw_b = AB.alloc("pw_b", 512)
        pwT_b = AB.alloc("pwT_b", 512)
        P_b = AB.alloc("P_b", 2 * 512)
        invT_b = AB.alloc("invT_b", 512)
        Mt_b = AB.alloc("Mt_b", 512)
        Gnw_b = AB.alloc("Gnw_b", 512)
        GnwT_b = AB.alloc("GnwT_b", 512)
        T2t_b = AB.alloc("T2t_b", 512)
        Ht_b = AB.alloc("Ht_b", 512)
        u_b = AB.alloc("u_b", 1024)
        U0_b = AB.alloc("U0_b", 1024)
        uadj_b = AB.alloc("uadj_b", 1024)
        for tt in range(NT):
            fsc = first and tt == 0
            mmt = None
            for h in range(H):
                for half in range(2):
                    c = 2 * h + half
                    mmt = S.emit("tensor", lambda e, h=h, c=c, tt=tt, half=half: e.matmul(
                        PW[0][:, h * 128:(h + 1) * 128], fmsl(kb_fm, c, tt), fmsl(kh_fm, c, tt),
                        start=(h == 0 and half == 0), stop=(h == 3 and half == 1)),
                        waits=[tok_scaled[("kb", c)], tok_scaled[("kh", c)], tp.get("ps0")], sig="dt")
            tB = S.emit("vector", lambda e: e.tensor_tensor(
                B_b[:, :].rearrange("p (a w) -> p a w", a=4),
                PW[0][:, :].rearrange("p (a w) -> p a w", a=4),
                C["negbd"][:].rearrange("p (x w) -> p x w", x=1).broadcast_to([128, 4, 128]), OP.mult),
                waits=[mmt], sig="dv")
            tp["ps0"] = tB
            tBT = S.emit("vector", lambda e: e.transpose(BT_b[:, :], B_b[:, :]), waits=[tB], sig="dv")
            tPc = S.emit("vector", lambda e: e.tensor_tensor(P_b[:, 0:512], B_b[:, :], C["eye4"][:], OP.add),
                         waits=[tB], sig="dv")
            curP = 0
            tsq_src, tsq_srcT = tB, tBT
            src_pw, src_pwT = B_b, BT_b
            for lvl in range(4):
                mmq = None
                for h in range(H):
                    mmq = S.emit("tensor", lambda e, h=h, s1=src_pwT, s2=src_pw: e.matmul(
                        PW[1][:, h * 128:(h + 1) * 128], s1[:, h * 128:(h + 1) * 128], s2[:, h * 128:(h + 1) * 128],
                        start=(h == 0), stop=(h == 3)), waits=[tsq_src, tsq_srcT, tp.get("ps1")], sig="dt")
                tev = S.emit("vector", lambda e: e.tensor_copy(pw_b[:, :], PW[1][:, :]), waits=[mmq], sig="dv")
                tp["ps1"] = tev
                tevT = S.emit("vector", lambda e: e.transpose(pwT_b[:, :], pw_b[:, :]), waits=[tev], sig="dv")
                mmp = None
                for h in range(H):
                    mmp = S.emit("tensor", lambda e, h=h, curP=curP: e.matmul(
                        PW[2][:, h * 128:(h + 1) * 128], pwT_b[:, h * 128:(h + 1) * 128],
                        P_b[:, curP * 512 + h * 128: curP * 512 + (h + 1) * 128],
                        start=(h == 0), stop=False), waits=[tevT, tPc, tp.get("ps2")], sig="dt")
                for h in range(H):
                    mmp = S.emit("tensor", lambda e, h=h, curP=curP: e.matmul(
                        PW[2][:, h * 128:(h + 1) * 128], C["eye128"][:, :],
                        P_b[:, curP * 512 + h * 128: curP * 512 + (h + 1) * 128],
                        start=False, stop=(h == 3)), sig="dt")
                curP = 1 - curP
                tPc = S.emit("vector", lambda e, curP=curP: e.tensor_copy(P_b[:, curP * 512:(curP + 1) * 512],
                                                                          PW[2][:, :]), waits=[mmp], sig="dv")
                tp["ps2"] = tPc
                if lvl < 3:
                    tsq_src = S.emit("vector", lambda e: e.tensor_copy(B_b[:, :], pw_b[:, :]), waits=[tevT], sig="dv")
                    tsq_srcT = S.emit("vector", lambda e: e.tensor_copy(BT_b[:, :], pwT_b[:, :]), waits=[tevT], sig="dv")
                    src_pw, src_pwT = B_b, BT_b
            tinvT = S.emit("vector", lambda e, curP=curP: e.transpose(invT_b[:, :], P_b[:, curP * 512:(curP + 1) * 512]),
                           waits=[tPc], sig="dv")
            mmu = None
            for h in range(H):
                mmu = S.emit("tensor", lambda e, h=h, tt=tt: e.matmul(
                    PW[h // 2][:, (h % 2) * 256:(h % 2) * 256 + 256], invT_b[:, h * 128:(h + 1) * 128],
                    vbT[:, tt * 1024 + h * 256: tt * 1024 + (h + 1) * 256], start=True, stop=True),
                    waits=[tinvT, tok_tmprod[("vbT", tt)], tp.get("ps0"), tp.get("ps1")], sig="dt")
            tu1 = S.emit("vector", lambda e: e.tensor_copy(u_b[:, 0:512], PW[0][:, :]),
                         waits=[mmu, tp.get("dbgu")], sig="dv")
            tu2 = S.emit("vector", lambda e: e.tensor_copy(u_b[:, 512:1024], PW[1][:, :]), waits=[mmu], sig="dv")
            tp["ps0"] = tu1
            tp["ps1"] = tu2
            mmw = None
            for h in range(H):
                for half in range(2):
                    mmw = S.emit("tensor", lambda e, h=h, half=half, tt=tt: e.matmul(
                        PW[2 + h // 2][:, ((h % 2) * 2 + half) * 128: ((h % 2) * 2 + half) * 128 + 128],
                        kbT[:, tt * 1024 + h * 256 + half * 128: tt * 1024 + h * 256 + half * 128 + 128],
                        invT_b[:, h * 128:(h + 1) * 128], start=True, stop=True),
                        waits=[tinvT, tok_tmprod[("kbT", tt)], tp.get("ps2"), tp.get("ps3")], sig="dt")
            nw_lo = negw_fm[:, 0:4 * W].rearrange("p (c w) -> p c w", c=4)[:, :, tt * 128:tt * 128 + 128]
            nw_hi = negw_fm[:, 4 * W:8 * W].rearrange("p (c w) -> p c w", c=4)[:, :, tt * 128:tt * 128 + 128]
            tw1 = S.emit("vector", lambda e, nw_lo=nw_lo: e.tensor_scalar(
                nw_lo, PW[2][:, :].rearrange("p (c w) -> p c w", c=4), -1.0, None, OP.mult), waits=[mmw], sig="dv")
            tw2 = S.emit("vector", lambda e, nw_hi=nw_hi: e.tensor_scalar(
                nw_hi, PW[3][:, :].rearrange("p (c w) -> p c w", c=4), -1.0, None, OP.mult), waits=[mmw], sig="dv")
            tp["ps2"] = tw1
            tp["ps3"] = tw2
            if DEBUG:
                r0 = mb + tt * 128
                tp["dbgu"] = S.dma("mg", dbg["dbg_u"][r0:r0 + 128, :], u_b[:, :], waits=[tu1, tu2])
                for c in range(8):
                    tp["dbgu"] = S.dma("mg", dbg["dbg_negw"][c * 128:(c + 1) * 128, r0:r0 + 128],
                                       fmsl(negw_fm, c, tt), waits=[tw1, tw2, tp.get("dbgu")])
            mmt = None
            for h in range(H):
                for half in range(2):
                    c = 2 * h + half
                    mmt = S.emit("tensor", lambda e, h=h, c=c, tt=tt, half=half: e.matmul(
                        PW[0][:, h * 128:(h + 1) * 128], fmsl(kh_fm, c, tt), fmsl(qh_fm, c, tt),
                        start=(h == 0 and half == 0), stop=(h == 3 and half == 1)),
                        waits=[tu1, tok_scaled[("q", c)]], sig="dt")
            tMt = S.emit("vector", lambda e: e.tensor_tensor(
                Mt_b[:, :].rearrange("p (a w) -> p a w", a=4), PW[0][:, :].rearrange("p (a w) -> p a w", a=4),
                C["triuI"][:].rearrange("p (x w) -> p x w", x=1).broadcast_to([128, 4, 128]), OP.mult), waits=[mmt], sig="dv")
            tp["ps0"] = tMt
            mm1 = None
            for h in range(H):
                for half in range(2):
                    c = 2 * h + half
                    mm1 = S.emit("tensor", lambda e, h=h, c=c, tt=tt, half=half: e.matmul(
                        PW[1][:, h * 128:(h + 1) * 128], fmsl(negw_fm, c, tt), fmsl(kh_fm, c, tt),
                        start=(h == 0 and half == 0), stop=(h == 3 and half == 1)),
                        waits=[tw1, tw2, tu2], sig="dt")
            tGnw = S.emit("vector", lambda e: e.tensor_tensor(
                Gnw_b[:, :].rearrange("p (a w) -> p a w", a=4), PW[1][:, :].rearrange("p (a w) -> p a w", a=4),
                C["blkLow"][:].rearrange("p (x w) -> p x w", x=1).broadcast_to([128, 4, 128]), OP.mult), waits=[mm1], sig="dv")
            tp["ps1"] = tGnw
            mm2 = None
            for h in range(H):
                for half in range(2):
                    c = 2 * h + half
                    mm2 = S.emit("tensor", lambda e, h=h, c=c, tt=tt, half=half: e.matmul(
                        PW[2][:, h * 128:(h + 1) * 128], fmsl(kh_fm, c, tt), fmsl(negw_fm, c, tt),
                        start=(h == 0 and half == 0), stop=(h == 3 and half == 1)),
                        waits=[tw1, tw2, tp.get("ps2")], sig="dt")
            tGnwT = S.emit("vector", lambda e: e.tensor_tensor(
                GnwT_b[:, :].rearrange("p (a w) -> p a w", a=4), PW[2][:, :].rearrange("p (a w) -> p a w", a=4),
                C["blkUp"][:].rearrange("p (x w) -> p x w", x=1).broadcast_to([128, 4, 128]), OP.mult), waits=[mm2], sig="dv")
            tp["ps2"] = tGnwT
            mmt2 = None
            for h in range(H):
                mmt2 = S.emit("tensor", lambda e, h=h: e.matmul(
                    PW[3][:, h * 128:(h + 1) * 128], Gnw_b[:, h * 128:(h + 1) * 128],
                    GnwT_b[:, h * 128:(h + 1) * 128], start=(h == 0), stop=(h == 3)),
                    waits=[tGnw, tGnwT, tp.get("ps3")], sig="dt")
            tT2 = S.emit("vector", lambda e: e.tensor_copy(T2t_b[:, :], PW[3][:, :]), waits=[mmt2], sig="dv")
            tp["ps3"] = tT2
            mmh = None
            for h in range(H):
                hs = slice(h * 128, (h + 1) * 128)
                mmh = S.emit("tensor", lambda e, hs=hs: e.matmul(PW[0][:, hs], C["eye128"][:, :], C["eye128"][:, :],
                                                                start=True, stop=False),
                             waits=[tT2, tMt], sig="dt")
                S.emit("tensor", lambda e, hs=hs: e.matmul(PW[0][:, hs], C["eye128"][:, :], GnwT_b[:, hs],
                                                           start=False, stop=False), sig="dt")
                S.emit("tensor", lambda e, hs=hs: e.matmul(PW[0][:, hs], C["eye128"][:, :], T2t_b[:, hs],
                                                           start=False, stop=False), sig="dt")
                mmh = S.emit("tensor", lambda e, hs=hs: e.matmul(PW[0][:, hs], Gnw_b[:, hs], T2t_b[:, hs],
                                                                start=False, stop=True), sig="dt")
            tHt = S.emit("vector", lambda e: e.tensor_copy(Ht_b[:, :], PW[0][:, :]), waits=[mmh], sig="dv")
            tp["ps0"] = tHt
            mmu0 = None
            for h in range(H):
                pwb = PW[1 + h // 2]
                of = (h % 2) * 256
                mmu0 = S.emit("tensor", lambda e, h=h, pwb=pwb, of=of, fsc=fsc: e.matmul(
                    pwb[:, of:of + 256], C["eye128"][:, :], u_b[:, h * 256:(h + 1) * 256],
                    start=True, stop=fsc), waits=[tHt, tu1, tu2, tp.get("ps1"), tp.get("ps2")], sig="dt")
                if not fsc:
                    for half in range(2):
                        c = 2 * h + half
                        mmu0 = S.emit("tensor", lambda e, h=h, c=c, tt=tt, pwb=pwb, of=of, half=half: e.matmul(
                            pwb[:, of:of + 256], fmsl(negw_fm, c, tt),
                            S_bf[:, h * 512 + half * 256: h * 512 + half * 256 + 256],
                            start=False, stop=(half == 1)), sig="dt")
            tU1 = S.emit("vector", lambda e: e.tensor_copy(U0_b[:, 0:512], PW[1][:, :]), waits=[mmu0], sig="dv")
            tU2 = S.emit("vector", lambda e: e.tensor_copy(U0_b[:, 512:1024], PW[2][:, :]), waits=[mmu0], sig="dv")
            tp["ps1"] = tU1
            tp["ps2"] = tU2
            mmua = None
            for h in range(H):
                pwb = PW[1 + h // 2]
                of = (h % 2) * 256
                mmua = S.emit("tensor", lambda e, h=h, pwb=pwb, of=of: e.matmul(
                    pwb[:, of:of + 256], Ht_b[:, h * 128:(h + 1) * 128], U0_b[:, h * 256:(h + 1) * 256],
                    start=True, stop=True), waits=[tU1, tU2], sig="dt")
            tua1 = S.emit("vector", lambda e: e.tensor_copy(uadj_b[:, 0:512], PW[1][:, :]), waits=[mmua], sig="dv")
            tua2 = S.emit("vector", lambda e: e.tensor_copy(uadj_b[:, 512:1024], PW[2][:, :]), waits=[mmua], sig="dv")
            tp["ps1"] = tua1
            tp["ps2"] = tua2
            mmo = None
            for h in range(H):
                pwb = PW[1 + h // 2]
                of = (h % 2) * 256
                mmo = S.emit("tensor", lambda e, h=h, pwb=pwb, of=of, fsc=fsc: e.matmul(
                    pwb[:, of:of + 256], Mt_b[:, h * 128:(h + 1) * 128], uadj_b[:, h * 256:(h + 1) * 256],
                    start=True, stop=fsc), waits=[tua1, tua2], sig="dt")
                if not fsc:
                    for half in range(2):
                        c = 2 * h + half
                        mmo = S.emit("tensor", lambda e, h=h, c=c, tt=tt, pwb=pwb, of=of, half=half: e.matmul(
                            pwb[:, of:of + 256], fmsl(qh_fm, c, tt),
                            S_bf[:, h * 512 + half * 256: h * 512 + half * 256 + 256],
                            start=False, stop=(half == 1)), sig="dt")
            tod1 = S.emit("vector", lambda e, tt=tt: e.tensor_copy(delta_tm[:, tt * 1024:tt * 1024 + 512], PW[1][:, :]),
                          waits=[mmo], sig="dv")
            tod2 = S.emit("vector", lambda e, tt=tt: e.tensor_copy(delta_tm[:, tt * 1024 + 512:(tt + 1) * 1024],
                                                                   PW[2][:, :]), waits=[mmo], sig="dv")
            tp["ps1"] = tod1
            tp["ps2"] = tod2
            mms = None
            for h in range(H):
                for half in range(2):
                    mms = S.emit("tensor", lambda e, h=h, half=half, tt=tt: e.matmul(
                        PS_S[h][:, half * 256: half * 256 + 256],
                        khT[:, tt * 1024 + h * 256 + half * 128: tt * 1024 + h * 256 + half * 128 + 128],
                        uadj_b[:, h * 256:(h + 1) * 256], start=(half == 0), stop=(half == 1)),
                        waits=[tua1, tua2, tok_tmprod[("khT", tt)], tp.get("scast")], sig="dt")
            tsc = None
            for h in range(H):
                if fsc:
                    ta = S.emit("vector", lambda e, h=h: e.tensor_copy(S32[:, h * 512:(h + 1) * 512], PS_S[h][:, :]),
                                waits=[mms], sig="dv")
                else:
                    ta = S.emit("vector", lambda e, h=h: e.tensor_tensor(S32[:, h * 512:(h + 1) * 512],
                                                                         S32[:, h * 512:(h + 1) * 512],
                                                                         PS_S[h][:, :], OP.add),
                                waits=[mms], sig="dv")
                tsc = S.emit("vector", lambda e, h=h: e.tensor_copy(S_bf[:, h * 512:(h + 1) * 512],
                                                                    S32[:, h * 512:(h + 1) * 512]),
                             waits=[ta], sig="dv")
            tp["scast"] = tsc
            if DEBUG:
                r0 = mb + tt * 128
                tp["dbgu"] = S.dma("mg", dbg["dbg_delta"][r0:r0 + 128, :], delta_tm[:, tt * 1024:(tt + 1) * 1024],
                                   waits=[tod1, tod2, tp.get("dbgu")])
        AB.free("B_b", "BT_b", "pw_b", "pwT_b", "P_b", "invT_b", "Mt_b", "Gnw_b", "GnwT_b",
                "T2t_b", "Ht_b", "u_b", "U0_b", "uadj_b", "negw_fm", "kb_fm", "vbT", "kbT", "khT")
        S.barrier()

        if PH < 8: continue
        sh_tm = AB.alloc("sh_tm", NT * 1024)
        lo_tm = AB.alloc("lo_tm", NT * 1024)
        wd_tm = AB.alloc("wd_tm", NT * 1024)
        Rb = AB.alloc("Rb", 2 * 4 * RW)
        fdb = AB.alloc("fdb", 2 * 4 * NGF * 32)
        fir_fm = AB.alloc("fir_fm", 3 * W)
        sq_b = AB.alloc("sq_b", W)
        filt_last = [NG_S - 1, NG_S + NG_L - 1, NGF - 1]
        tok_fir_ev = {}
        vdview = vdram.rearrange("(c s i) t -> c i s t", c=8, s=4)
        mmsum = None
        first_summ = True
        for c in range(8):
            half = c % 2
            tRl = []
            for g in range(4):
                colb = mb + 61 + g
                tRl.append(S.dma("mh", Rb[32 * g:32 * g + 32, half * 4 * RW:(half + 1) * 4 * RW].rearrange(
                    "p (s t) -> p s t", s=4), vdview[c, :, :, colb:colb + RW],
                    waits=[tok_vdr_w, tok_fir_ev.get(c - 2)]))
            tfd = S.dma("mg", fdb[:, half * NGF * 128:(half + 1) * NGF * 128].rearrange(
                "p (s g w) -> p s g w", s=4, g=NGF), dr["fird"][c].rearrange("s g p w -> p s g w"),
                waits=[tok_fir_ev.get(c - 2)])
            mmf = {}
            for fi in range(3):
                mmf[fi] = S.emit("tensor", lambda e, fi=fi, half=half: e.matmul(
                    PW[fi][:, 0:512], C["zeros128"][:, :], Rb[:, half * 4 * RW: half * 4 * RW + 512],
                    start=True, stop=False), waits=tRl + [tfd, tp.get(("fps", fi))], sig="dt")
            for gi, (fi, Gf, tau) in enumerate(groups):
                for s4 in range(4):
                    mmf[fi] = S.emit("tensor", lambda e, fi=fi, gi=gi, s4=s4, tau=tau, half=half: e.matmul(
                        PW[fi][32 * s4:32 * s4 + 32, 0:512],
                        fdb[:, half * NGF * 128 + (s4 * NGF + gi) * 32: half * NGF * 128 + (s4 * NGF + gi) * 32 + 32],
                        Rb[:, half * 4 * RW + s4 * RW + tau: half * 4 * RW + s4 * RW + tau + 512],
                        start=False, stop=False, tile_position=(0, 32 * s4)),
                        waits=tRl + [tfd], sig="dt")
            for fi in range(3):
                mmf[fi] = S.emit("tensor", lambda e, fi=fi, half=half: e.matmul(
                    PW[fi][:, 0:512], C["zeros128"][:, :], Rb[:, half * 4 * RW: half * 4 * RW + 512],
                    start=False, stop=True), sig="dt")
            evs = []
            for fi in range(3):
                evs.append(S.emit("vector", lambda e, fi=fi: e.tensor_copy(fir_fm[:, fi * W:(fi + 1) * W], PW[fi][:, :]),
                                  waits=[mmf[fi], tp.get(("fxw", fi))], sig="dv"))
                tp[("fps", fi)] = evs[-1]
            tok_fir_ev[c] = evs[-1]
            if first_summ:
                mmsum = S.emit("tensor", lambda e: e.matmul(
                    PW[3][:, 0:512], C["zeros128"][:, :], vf_fm[:, 0:W],
                    start=True, stop=False), waits=[tp.get("sumps")] + [evs[0]], sig="dt")
                first_summ = False
            for fi in range(3):
                mmsum = S.emit("tensor", lambda e, c=c, fi=fi: e.matmul(
                    PW[3][0:16, 0:512], C["hc4"][:, (c * 4 + fi) * 16:(c * 4 + fi + 1) * 16],
                    fir_fm[:, fi * W:(fi + 1) * W],
                    start=False, stop=False), waits=[evs[fi], tp.get("sumps")], sig="dt")
            mmsum = S.emit("tensor", lambda e, c=c: e.matmul(
                PW[3][0:16, 0:512], C["hc4"][:, (c * 4 + 3) * 16:(c * 4 + 4) * 16], vf_fm[:, c * W:(c + 1) * W],
                start=False, stop=False), sig="dt")
            for fi in range(3):
                tsq = S.emit("vector", lambda e, fi=fi: e.tensor_tensor(
                    sq_b[:, :], fir_fm[:, fi * W:(fi + 1) * W], fir_fm[:, fi * W:(fi + 1) * W], OP.mult),
                    waits=[evs[fi], mmsum], sig="dv")
                mmsum = S.emit("tensor", lambda e, c=c, fi=fi: e.matmul(
                    PW[3][32:48, 0:512], C["hc4"][:, (c * 4 + fi) * 16:(c * 4 + fi + 1) * 16], sq_b[:, :],
                    start=False, stop=False), waits=[tsq], sig="dt")
            tsqv = S.emit("vector", lambda e, c=c: e.tensor_tensor(
                sq_b[:, :], vf_fm[:, c * W:(c + 1) * W], vf_fm[:, c * W:(c + 1) * W], OP.mult),
                waits=[mmsum], sig="dv")
            mmsum = S.emit("tensor", lambda e, c=c: e.matmul(
                PW[3][32:48, 0:512], C["hc4"][:, (c * 4 + 3) * 16:(c * 4 + 4) * 16], sq_b[:, :],
                start=False, stop=False), waits=[tsqv], sig="dt")
            if c == 7:
                mmsum = S.emit("tensor", lambda e: e.matmul(
                    PW[3][:, 0:512], C["zeros128"][:, :], vf_fm[:, 0:W],
                    start=False, stop=True), sig="dt")
            tp["sumps"] = mmsum
            for fi, dst in ((0, sh_tm), (1, lo_tm), (2, wd_tm)):
                tp[("fxw", fi)] = [S.dma("ms", dst[:, tt * 1024 + c * 128: tt * 1024 + (c + 1) * 128],
                                         fir_fm[:, fi * W + tt * 128: fi * W + tt * 128 + 128],
                                         waits=[evs[fi]], transpose=True) for tt in range(NT)]
        tok_fxall = S.qtot("ms") + S.qtot("mh")
        if DEBUG:
            for nm, dst in (("dbg_short", sh_tm), ("dbg_long", lo_tm), ("dbg_wide", wd_tm)):
                for tt in range(NT):
                    S.dma("mg", dbg[nm][mb + tt * 128: mb + tt * 128 + 128, :], dst[:, tt * 1024:(tt + 1) * 1024],
                          waits=[tok_fxall])
        AB.free("Rb", "fdb", "fir_fm", "sq_b")

        if PH < 9: continue
        stat64 = AFp.alloc("stat64", 64)
        sums16 = AFp.alloc("sums16", 512)
        trsp = AFp.alloc("trsp", 128)
        dsq = AFp.alloc("dsq", 1024)
        stt = S.emit("vector", lambda e: e.tensor_copy(sums16[:, :], PW[3][:, 0:512]), waits=[mmsum], sig="dv")
        if DEBUG:
            S.dma("mg", dbg["dbg_sums"][:, mb:mb + W], sums16[:, :], waits=[stt])
        tp["ps3"] = stt
        tok_stats = []
        for tt in range(NT):
            toks = []
            tp1 = S.emit("tensor", lambda e, tt=tt: e.transpose(PW[2][:, 0:128], sums16[:, tt * 128:(tt + 1) * 128],
                                                                C["ident32"][:, :]),
                         waits=[stt, tp.get("ps2")], sig="dt")
            ev1 = S.emit("vector", lambda e: e.tensor_copy(trsp[:, :], PW[2][:, 0:128]), waits=[tp1], sig="dv")
            tp["ps2"] = ev1
            for ti in range(4):
                off = [0, 12, 24, 48][ti]
                toks.append(S.emit("vector", lambda e, ti=ti, off=off: e.tensor_scalar(
                    stat64[:, off:off + 4], trsp[:, ti * 4:(ti + 1) * 4], 1.0 / 256.0, None, OP.mult),
                    waits=[ev1, tp.get("dbgstat")], sig="dv"))
                toks.append(S.emit("vector", lambda e, ti=ti, off=off: e.tensor_scalar(
                    stat64[:, off + 4:off + 8], trsp[:, 32 + ti * 4: 32 + (ti + 1) * 4], 1.0 / 256.0, 1e-8,
                    OP.mult, OP.max), waits=[ev1], sig="dv"))
            dslice = delta_tm[:, tt * 1024:(tt + 1) * 1024]
            td1 = S.emit("vector", lambda e, dslice=dslice: e.tensor_reduce(
                stat64[:, 60:64], dslice.rearrange("p (h c) -> p h c", h=4), axis=AX, op=OP.add),
                waits=[tp.get("dstat")], sig="dv")
            td2 = S.emit("vector", lambda e: e.tensor_scalar(stat64[:, 36:40], stat64[:, 60:64],
                                                             1.0 / 256.0, None, OP.mult), waits=[td1], sig="dv")
            td3 = S.emit("vector", lambda e, dslice=dslice: e.tensor_tensor(dsq[:, :], dslice, dslice, OP.mult),
                         waits=[td2], sig="dv")
            td4 = S.emit("vector", lambda e: e.tensor_reduce(
                stat64[:, 60:64], dsq[:, :].rearrange("p (h c) -> p h c", h=4), axis=AX, op=OP.add),
                waits=[td3], sig="dv")
            td5 = S.emit("vector", lambda e: e.tensor_scalar(stat64[:, 40:44], stat64[:, 60:64],
                                                             1.0 / 256.0, 1e-8, OP.mult, OP.max), waits=[td4], sig="dv")
            tp["dstat"] = td5
            toks.extend([td1, td2, td3, td4, td5])
            for tile, off in ((sh_tm, 0), (lo_tm, 12), (wd_tm, 24), (delta_tm, 36), (v_tm, 48)):
                toks.append(S.emit("vector", lambda e, tile=tile, off=off, tt=tt: e.tensor_reduce(
                    stat64[:, off + 8:off + 12], tile[:, tt * 1024:(tt + 1) * 1024].rearrange("p (h c) -> p h c", h=4),
                    axis=AX, op=OP.max, apply_absolute_value=True), waits=[tok_fxall], sig="dv"))
            sqs = []
            for coff in (4, 16, 28, 40, 52):
                sqs.append(S.emit("scalar", lambda e, coff=coff: e.activation(
                    stat64[:, coff:coff + 4], stat64[:, coff:coff + 4], AF.Sqrt), waits=toks, sig="dc"))
            tp2 = S.emit("tensor", lambda e: e.transpose(PW[2][0:64, 0:128], stat64[:, :], C["ident32"][:, :]),
                         waits=toks + sqs + [tp.get("ps2")], sig="dt")
            tst = S.emit("vector", lambda e, tt=tt: e.tensor_copy(statF[0:64, tt * 128:(tt + 1) * 128],
                                                                  PW[2][0:64, 0:128]), waits=[tp2], sig="dv")
            tp["ps2"] = tst
            tok_stats.append(tst)
            if DEBUG:
                tp["dbgstat"] = S.dma("mg", dbg["dbg_stat"][mb + tt * 128: mb + tt * 128 + 128, :], stat64[:, :],
                                      waits=toks + sqs + [tst])
        AFp.free("stat64", "sums16", "trsp", "dsq")

        if PH < 10: continue
        g1buf = AB.alloc("g1buf", 2 * 9 * 128)
        hdn_b = AB.alloc("hdn_b", 2 * W)
        logit_sp = AFp.alloc("logit_sp", W)
        tok_hdn = {}
        mml = None
        for dcg in range(16):
            buf = g1buf[:, (dcg % 2) * 9 * 128:((dcg % 2) + 1) * 9 * 128]
            twl = S.dma("mg", buf.rearrange("p (c w) -> p c w", c=9),
                        dr["g1Tp"].rearrange("(c p) d -> p c d", p=128)[:, :, dcg * 128:(dcg + 1) * 128],
                        waits=[tok_hdn.get(dcg - 2)])
            mmt = None
            for kc in range(9):
                rhs = xn_fm[:, kc * W:(kc + 1) * W] if kc < 8 else statF[:, 0:512]
                mmt = S.emit("tensor", lambda e, kc=kc, buf=buf, rhs=rhs, dcg=dcg: e.matmul(
                    PW[dcg % 2][:, 0:512], buf[:, kc * 128:(kc + 1) * 128], rhs,
                    start=(kc == 0), stop=(kc == 8)),
                    waits=[twl, tok_stats[-1], tp.get(("g1ps", dcg % 2))], sig="dt")
            ghl = S.emit("scalar", lambda e, dcg=dcg: e.activation(
                hdn_b[:, (dcg % 2) * W:((dcg % 2) + 1) * W], PW[dcg % 2][:, 0:512], AF.Gelu,
                bias=C["g1b"][:, dcg:dcg + 1]), waits=[mmt], sig="dc")
            tok_hdn[dcg] = ghl
            tp[("g1ps", dcg % 2)] = ghl
            mml = S.emit("tensor", lambda e, dcg=dcg: e.matmul(
                PW[2][0:20, 0:512], C["g2T"][:, dcg * 20:(dcg + 1) * 20],
                hdn_b[:, (dcg % 2) * W:((dcg % 2) + 1) * W],
                start=(dcg == 0), stop=(dcg == 15)), waits=[ghl, tp.get("ps2")], sig="dt")
        tok_lg = S.emit("vector", lambda e: e.tensor_copy(logit_sp[0:20, :], PW[2][0:20, 0:512]), waits=[mml], sig="dv")
        tp["ps2"] = tok_lg
        tp["xnfree"] = tok_lg
        AB.free("g1buf", "hdn_b")

        if PH < 11: continue
        o_tm = AB.alloc("o_tm", NT * 1024)
        lg_t = AFp.alloc("lg_t", 128)
        o32 = AFp.alloc("o32", 1024)
        tmpc = AFp.alloc("tmpc", 1024)
        tpxl = []
        for tt in range(NT):
            tpxl.append(S.emit("tensor", lambda e, tt=tt: e.transpose(
                PW[3][:, tt * 128:(tt + 1) * 128], logit_sp[:, tt * 128:(tt + 1) * 128], C["ident32"][:, :]),
                waits=([tok_lg, tp.get("ps3")] if tt == 0 else []), sig="dt"))
        lgv = lg_t[:, 0:80].rearrange("p (t w) -> p t w", t=4)
        lgi = lg_t[:, 0:80].rearrange("p (a i) -> p a i", i=5)
        t1 = S.emit("vector", lambda e: e.tensor_copy(
            lg_t[:, 0:80].rearrange("p (t w) -> p t w", t=4),
            PW[3][:, :].rearrange("p (t w) -> p t w", w=128)[:, :, 0:20]),
            waits=tpxl + [tp.get("dbglg")], sig="dv")
        tp["ps3"] = t1

        def gcv(a):
            return gconst[:, a:a + 20].rearrange("p (x w) -> p x w", x=1).broadcast_to([128, 4, 20])
        t2 = S.emit("vector", lambda e: e.tensor_tensor(lgv, lgv, gcv(0), OP.add), waits=[t1], sig="dv")
        t3 = S.emit("vector", lambda e: e.tensor_tensor(lgv, lgv, gcv(20), OP.mult), waits=[t2], sig="dv")
        t4 = S.emit("vector", lambda e: e.tensor_reduce(lg_t[:, 80:96], lgi, axis=AX, op=OP.max),
                    waits=[t3], sig="dv")
        mxv = lg_t[:, 80:96].broadcast_to([128, 16, 5])
        t5 = S.emit("vector", lambda e: e.tensor_tensor(lgi, lgi, mxv, OP.subtract), waits=[t4], sig="dv")
        t6 = S.emit("scalar", lambda e: e.activation(lg_t[:, 0:80], lg_t[:, 0:80], AF.Exp), waits=[t5], sig="dc")
        t7 = S.emit("vector", lambda e: e.tensor_reduce(lg_t[:, 80:96], lgi, axis=AX, op=OP.add),
                    waits=[t6], sig="dv")
        t8 = S.emit("vector", lambda e: e.reciprocal(lg_t[:, 96:112], lg_t[:, 80:96]), waits=[t7], sig="dv")
        rv = lg_t[:, 96:112].broadcast_to([128, 16, 5])
        t9 = S.emit("vector", lambda e: e.tensor_tensor(lgi, lgi, rv, OP.mult), waits=[t8], sig="dv")
        t10 = S.emit("vector", lambda e: e.tensor_tensor(lgv, lgv, gcv(40), OP.mult), waits=[t9], sig="dv")
        t11 = S.emit("vector", lambda e: e.tensor_tensor(lgv, lgv, gcv(60), OP.add), waits=[t10], sig="dv")
        if DEBUG:
            tp["dbglg"] = S.dma_burst("mg", [(dbg["dbg_logits"][mb + t_ * 128:mb + t_ * 128 + 128, 0:20],
                                              lg_t[:, 20 * t_:20 * t_ + 20], {}) for t_ in range(NT)],
                                      waits=[t11])
        for tt in range(NT):
            tensors = [(sh_tm, 0), (lo_tm, 1), (wd_tm, 2), (delta_tm, 3), (v_tm, 4)]
            tsl = slice(tt * 1024, (tt + 1) * 1024)
            tokc = None
            for ti, (tile, i0) in enumerate(tensors):
                wv_ = lg_t[:, 20 * tt + i0: 20 * tt + i0 + 20].rearrange("p (h i) -> p h i", i=5)[:, :, 0:1].broadcast_to([128, 4, 256])
                if ti == 0:
                    tokc = S.emit("vector", lambda e, tile=tile, wv_=wv_, tsl=tsl: e.tensor_tensor(
                        o32[:, :].rearrange("p (h c) -> p h c", h=4),
                        tile[:, tsl].rearrange("p (h c) -> p h c", h=4), wv_, OP.mult),
                        waits=[t11, tok_fxall], sig="dv")
                else:
                    tokc = S.emit("vector", lambda e, tile=tile, wv_=wv_, tsl=tsl: e.tensor_tensor(
                        tmpc[:, :].rearrange("p (h c) -> p h c", h=4),
                        tile[:, tsl].rearrange("p (h c) -> p h c", h=4), wv_, OP.mult),
                        waits=[tokc], sig="dv")
                    tokc = S.emit("vector", lambda e: e.tensor_tensor(o32[:, :], o32[:, :], tmpc[:, :], OP.add),
                                  waits=[tokc], sig="dv")
            t12 = S.emit("vector", lambda e: e.tensor_tensor(tmpc[:, :], o32[:, :], o32[:, :], OP.mult),
                         waits=[tokc], sig="dv")
            t13 = S.emit("vector", lambda e: e.tensor_reduce(lg_t[:, 112:116],
                                                             tmpc[:, :].rearrange("p (h c) -> p h c", h=4),
                                                             axis=AX, op=OP.add), waits=[t12], sig="dv")
            t14 = S.emit("vector", lambda e: e.tensor_scalar(lg_t[:, 112:116], lg_t[:, 112:116], 1.0 / 256, 1e-5,
                                                             OP.mult, OP.add), waits=[t13], sig="dv")
            t15 = S.emit("vector", lambda e: e.reciprocal(lg_t[:, 116:120], lg_t[:, 112:116]), waits=[t14], sig="dv")
            t16 = S.emit("scalar", lambda e: e.activation(lg_t[:, 116:120], lg_t[:, 116:120], AF.Sqrt), waits=[t15], sig="dc")
            rsv = lg_t[:, 116:120].broadcast_to([128, 4, 256])
            t17 = S.emit("vector", lambda e: e.tensor_tensor(o32[:, :].rearrange("p (h c) -> p h c", h=4),
                                                             o32[:, :].rearrange("p (h c) -> p h c", h=4),
                                                             rsv, OP.mult), waits=[t16], sig="dv")
            t18 = S.emit("vector", lambda e, tsl=tsl: e.tensor_tensor(o_tm[:, tsl], o32[:, :], C["onorm_rep"][:, :],
                                                                      OP.mult), waits=[t17], sig="dv")
            tp[("otm", tt)] = t18
            if DEBUG:
                S.dma("mg", dbg["dbg_otm"][mb + tt * 128:mb + tt * 128 + 128, :], o_tm[:, tsl], waits=[t18])
        AFp.free("lg_t", "o32", "tmpc", "logit_sp")

        if PH < 12: continue
        o_fm = AB.alloc("o_fm", 8 * W)
        owb = AB.alloc("owb", 8 * 1024)
        AB.free("sh_tm", "lo_tm", "wd_tm", "delta_tm", "v_tm")
        s_t = AFp.alloc("s_t", 1024)
        towl = S.dma("mg", owb[:].rearrange("p (c w) -> p c w", c=8),
                     dr["owT"].rearrange("(c p) d -> p c d", p=128), waits=[tp[("otm", 0)]])
        tof = {}
        for tt in range(NT):
            for c in range(8):
                tof[(tt, c)] = S.dma("ms", o_fm[:, c * W + tt * 128: c * W + tt * 128 + 128],
                                     o_tm[:, tt * 1024 + c * 128: tt * 1024 + (c + 1) * 128],
                                     waits=[tp[("otm", tt)]], transpose=True)
        for tt in range(NT):
            mms = None
            for kc in range(8):
                for nn in range(2):
                    mms = S.emit("tensor", lambda e, kc=kc, nn=nn, tt=tt: e.matmul(
                        PW[nn][:, 0:512], o_fm[:, kc * W + tt * 128: kc * W + tt * 128 + 128],
                        owb[:, kc * 1024 + nn * 512: kc * 1024 + (nn + 1) * 512],
                        start=(kc == 0), stop=(kc == 7)),
                        waits=[towl, tof[(tt, kc)], tp.get("ps0"), tp.get("ps1")]
                        + ([tp.get(("g1ps", 0)), tp.get(("g1ps", 1))] if tt == 0 else []), sig="dt")
            ev1 = S.emit("vector", lambda e: e.tensor_copy(s_t[:, 0:512], PW[0][:, :]),
                         waits=[mms, tp.get("sdma")], sig="dv")
            ev2 = S.emit("vector", lambda e: e.tensor_copy(s_t[:, 512:1024], PW[1][:, :]), waits=[mms], sig="dv")
            tp["ps0"] = ev1
            tp["ps1"] = ev2
            tp["sdma"] = S.dma("mg", sdram[mb + tt * 128: mb + tt * 128 + 128, :], s_t[:, :], waits=[ev1, ev2])
            tp["mttail"] = mms
        tp[("projps", 0)] = tp["ps0"]
        tp[("projps", 1)] = tp["ps1"]
        AB.free("o_fm", "owb", "o_tm", "xq_tm")
        AFp.free("s_t")

    S.barrier()
    AB.free("xn_fm", "qh_fm", "kh_fm", "vf_fm", "S_bf", "tails", "statF", "cd", "g2T", "hc4")
    AFp.free("S32", "onorm_rep", "g1b", "gconst", "ident32", "smallrows")

    if PH < 13:
        return _finish(nc, S, ctx)
    wgq = AB.alloc("wgq", 8 * IFF)
    wuq = AB.alloc("wuq", 8 * IFF)
    wdq = AB.alloc("wdq", 22 * 1024)
    wld = AFp.alloc("wld", 2 * 1024)
    wq_b = AB.alloc("wq_b", 2 * 1024)
    prev_q = {}
    prev_x = {}
    i = 0
    for wnm, dst in (("wg", wgq), ("wu", wuq)):
        for ic2 in range(22):
            p = i % 2
            wldp = wld[:, p * 1024:(p + 1) * 1024]
            wqbp = wq_b[:, p * 1024:(p + 1) * 1024]
            tl = S.dma("mg", wldp, dr[wnm][ic2 * 128:(ic2 + 1) * 128, :], waits=[prev_q.get(p)])
            tqq, _ = emit_quant(wldp, wqbp, 1024, tl, "q2", dst_war=prev_x.get(p))
            prev_q[p] = tqq
            prev_x[p] = [S.dma("ms" if p == 0 else "mh",
                               dst[:, kc * IFF + ic2 * 128: kc * IFF + ic2 * 128 + 128],
                               wqbp[:, kc * 128:(kc + 1) * 128], waits=[tqq], transpose=True)
                         for kc in range(8)]
            i += 1
    AFp.free("wld")
    AB.free("wq_b")
    wld2 = AFp.alloc("wld2", 2 * (IFF // 2))
    wq_b2 = AB.alloc("wq_b2", 2 * IFF)
    prev_q2 = {0: [prev_q[0], prev_q[1]], 1: [prev_q[0], prev_q[1]]}
    prev_x2 = {0: prev_x[0] + prev_x[1], 1: prev_x[0] + prev_x[1]}
    for c8 in range(8):
        p = c8 % 2
        wldp = wld2[:, p * 1408:(p + 1) * 1408]
        wqbp = wq_b2[:, p * IFF:(p + 1) * IFF]
        tqh = []
        for hf in range(2):
            tl = S.dma("mg", wldp, dr["wdn"][c8 * 128:(c8 + 1) * 128, hf * 1408:(hf + 1) * 1408],
                       waits=[prev_q2.get(p)] + (tqh if hf else []))
            tqq, _ = emit_quant(wldp, wqbp[:, hf * 1408:(hf + 1) * 1408], IFF // 2, tl, "q3",
                                dst_war=prev_x2.get(p))
            tqh.append(tqq)
        prev_q2[p] = tqh
        prev_x2[p] = [S.dma("ms" if p == 0 else "mh",
                            wdq[:, icc * 1024 + c8 * 128: icc * 1024 + c8 * 128 + 128],
                            wqbp[:, icc * 128:(icc + 1) * 128], waits=tqh, transpose=True)
                      for icc in range(22)]
    AFp.free("wld2")
    AB.free("wq_b2")
    S.barrier()

    xq_t2 = AB.alloc("xq_t", 3 * 1024)
    hq_t2 = AB.alloc("hq_t", 5 * 512)
    hq_f2 = AB.alloc("hq_f", 5 * 512)
    gup2 = AFp.alloc("gup", 2 * 1024)
    hh2 = AFp.alloc("hh", 2 * 512)
    g_t2 = AFp.alloc("g_t", 2 * 1024)
    xs_t2 = AFp.alloc("xs_t", 2048)
    IC_OF = [0, 512, 1024, 1536, 2048, 2560]
    IC_SZ = [512, 512, 512, 512, 512, 256]
    GU = [(PW[0], PW[1]), (PS_S[0], PS_S[1])]
    DN = [(PW[2], PW[3]), (PS_S[2], PS_S[3])]

    def emit_dn(pend):
        tt, ici, q4, tok_hqf = pend
        pb = tt % 2
        bd0, bd1 = DN[pb]
        hq_f = hq_f2[:, q4 * 512:(q4 + 1) * 512]
        iof, isz = IC_OF[ici], IC_SZ[ici]
        mmd = None
        for j in range(isz // 128):
            icc = iof // 128 + j
            for nn in range(2):
                mmd = S.emit("tensor", lambda e, j=j, icc=icc, nn=nn, hq_f=hq_f, bd=(bd0, bd1)[nn]: e.matmul(
                    bd[:, 0:512], hq_f[:, j * 128:(j + 1) * 128],
                    wdq[:, icc * 1024 + nn * 512: icc * 1024 + (nn + 1) * 512],
                    start=(icc == 0), stop=(icc == 21)),
                    waits=([tok_hqf, tp.get(("dnev", pb))] if (j == 0 and ici == 0) else [tok_hqf]), sig="dt")
        tp[("dnmm", q4)] = mmd
        return mmd

    pend_dn = []
    for tt in range(N_FFN_TILES):
        r0 = tt * 128
        pb = tt % 2
        xq_t = xq_t2[:, (tt % 3) * 1024:(tt % 3 + 1) * 1024]
        g_t = g_t2[:, pb * 1024:(pb + 1) * 1024]
        hh = hh2[:, pb * 512:(pb + 1) * 512]
        xs_t = xs_t2
        wprev = tp.get(("ffn", tt - 2))
        txq = [S.dma("mh", xq_t[:, c * 128:(c + 1) * 128], xqdram[r0:r0 + 128, c * 128:(c + 1) * 128],
                     waits=[tp.get(("ffn", tt - 3))], transpose=True) for c in range(8)]
        for ici in range(6):
            iof, isz = IC_OF[ici], IC_SZ[ici]
            p2 = ici % 2
            q4 = (tt * 6 + ici) % 5
            bg, bu = GU[p2]
            gup = gup2[:, p2 * 1024:(p2 + 1) * 1024]
            hq_t = hq_t2[:, q4 * 512:(q4 + 1) * 512]
            hq_f = hq_f2[:, q4 * 512:(q4 + 1) * 512]
            mmg = None
            for kc in range(8):
                mmg = S.emit("tensor", lambda e, kc=kc, iof=iof, isz=isz, xq_t=xq_t, bg=bg: e.matmul(
                    bg[:, 0:isz], xq_t[:, kc * 128:(kc + 1) * 128], wgq[:, kc * IFF + iof: kc * IFF + iof + isz],
                    start=(kc == 0), stop=(kc == 7)),
                    waits=([txq, tp.get(("guev", p2))] if kc == 0 else []), sig="dt")
                mmg = S.emit("tensor", lambda e, kc=kc, iof=iof, isz=isz, xq_t=xq_t, bu=bu: e.matmul(
                    bu[:, 0:isz], xq_t[:, kc * 128:(kc + 1) * 128], wuq[:, kc * IFF + iof: kc * IFF + iof + isz],
                    start=(kc == 0), stop=(kc == 7)), sig="dt")
            if len(pend_dn) >= 3:
                emit_dn(pend_dn.pop(0))
            tsg = S.emit("scalar", lambda e, isz=isz, gup=gup, bg=bg: e.activation(gup[:, 0:isz], bg[:, 0:isz], AF.Silu),
                         waits=[mmg, tp.get(("guth", p2))], sig="dc")
            tup = S.emit("vector", lambda e, isz=isz, gup=gup, bu=bu: e.tensor_copy(gup[:, 512:512 + isz], bu[:, 0:isz]),
                         waits=[mmg, tp.get(("guth", p2))], sig="dv")
            tp[("guev", p2)] = [tsg, tup]
            th = S.emit("vector", lambda e, isz=isz, gup=gup, hh=hh: e.tensor_tensor(
                hh[:, 0:isz], gup[:, 0:isz], gup[:, 512:512 + isz], OP.mult),
                waits=[tsg, tup, tp.get(("dbgh", pb))], sig="dv")
            tp[("guth", p2)] = th
            thq, _ = emit_quant(hh[:, 0:isz], hq_t[:, 0:isz], isz, th, "q4", fast=True,
                                dst_war=[t for t in [tp.get(("dbghq", q4))] if t] + (tp.get(("hqx", q4)) or []))
            if DEBUG:
                tp[("dbgh", pb)] = S.dma("mg", dbg["dbg_h"][r0:r0 + 128, iof:iof + isz], hh[:, 0:isz],
                                         waits=[th, thq])
                tp[("dbghq", q4)] = S.dma("mg", dbg["dbg_hq"][r0:r0 + 128, iof:iof + isz], hq_t[:, 0:isz],
                                          waits=[thq])
            tok_hqf = [S.dma("ms", hq_f[:, j * 128:(j + 1) * 128], hq_t[:, j * 128:(j + 1) * 128],
                             waits=[thq, tp.get(("dnmm", q4))], transpose=True)
                       for j in range(isz // 128)]
            tp[("hqx", q4)] = tok_hqf
            pend_dn.append((tt, ici, q4, tok_hqf))
        for p_ in pend_dn:
            mmd = emit_dn(p_)
        pend_dn = []
        bd0, bd1 = DN[pb]
        tg1 = S.emit("scalar", lambda e, g_t=g_t, bd0=bd0: e.activation(g_t[:, 0:512], bd0[:, 0:512], AF.Silu),
                     waits=[mmd], sig="dc")
        tg2 = S.emit("scalar", lambda e, g_t=g_t, bd1=bd1: e.activation(g_t[:, 512:1024], bd1[:, 0:512], AF.Silu),
                     waits=[mmd], sig="dc")
        tp[("dnev", pb)] = [tg1, tg2]
        txl = S.dma("mg", xs_t[:, 0:1024], dr["x"][r0:r0 + 128, :], waits=[wprev, tp.get("xsfree")])
        tsl = S.dma("mg", xs_t[:, 1024:2048], sdram[r0:r0 + 128, :], waits=[wprev, tp.get("xsfree")])
        to1 = S.emit("vector", lambda e, g_t=g_t, xs_t=xs_t: e.tensor_tensor(
            g_t[:, :], xs_t[:, 1024:2048], g_t[:, :], OP.mult), waits=[tg1, tg2, tsl], sig="dv")
        to2 = S.emit("vector", lambda e, g_t=g_t, xs_t=xs_t: e.tensor_tensor(
            g_t[:, :], xs_t[:, 0:1024], g_t[:, :], OP.add), waits=[to1, txl], sig="dv")
        tp["xsfree"] = to2
        tp[("ffn", tt)] = S.dma("mg", y[r0:r0 + 128, :], g_t[:, :], waits=[to2])
    S.barrier()

    return _finish(nc, S, ctx)


def _finish(nc, S, ctx):
    sems = {}
    for s in ALLSEMS:
        t = nc.semaphore(s)
        h_ = t.__enter__()
        ctx.append(t)
        sems[s] = h_
    blk = nc.Block()
    b = blk.__enter__()
    ctx.append(blk)

    @b.tensor
    def _(eng):
        S.run("tensor", eng, sems)

    @b.vector
    def _(eng):
        S.run("vector", eng, sems)

    @b.scalar
    def _(eng):
        S.run("scalar", eng, sems)

    @b.gpsimd
    def _(eng):
        S.run("gpsimd", eng, sems)

    @b.sync
    def _(eng):
        S.run("sync", eng, sems)

    blk.__exit__(None, None, None)
    ctx.pop()
    return nc, ctx


_CACHE = {}


def kernel(**inputs):
    x_full = np.asarray(inputs["L0_attn_hidden_states"], np.float32)
    host, groups = prep_host(inputs)
    if "nc" not in _CACHE:
        nc, ctx = build(groups)
        _CACHE["nc"] = nc
        _CACHE["ctx"] = ctx
    nc = _CACHE["nc"]
    in_maps = []
    for core in range(8):
        m = dict(host)
        m["x"] = np.ascontiguousarray(x_full[core])
        in_maps.append(m)
    res = run_bass_kernel_spmd(nc, in_maps, core_ids=list(range(8)))
    return np.stack([np.asarray(r["y"], np.float32) for r in res.results], axis=0)


# version 12
# speedup vs baseline: 1.0884x; 1.0010x over previous
import numpy as np
import ml_dtypes
import concourse.bass as bass
import concourse.mybir as mybir
from concourse.bass_utils import run_bass_kernel_spmd
from concourse.alu_op_type import AluOpType as OP

F32 = mybir.dt.float32
BF16 = mybir.dt.bfloat16
I32 = mybir.dt.int32
U8 = mybir.dt.uint8
AF = mybir.ActivationFunctionType
AX = mybir.AxisListType.X

L, D, H = 4096, 1024, 4
TMTOK = 512
NMEG = L // TMTOK
NT = TMTOK // 128
W = TMTOK
WP = W + 64
IFF = 2816
NG_S, NG_L, NG_W = 1, 8, 16
NGF = NG_S + NG_L + NG_W
RW = W + 64

DEBUG = False
SUB56 = 1
N_MEG_BUILD = NMEG
N_FFN_TILES = L // 128
PH = 99

ENGINES = ("tensor", "vector", "scalar", "gpsimd", "sync")
ESEM = {"tensor": "dt", "vector": "dv", "scalar": "dc", "gpsimd": "dg", "sync": "dy"}
NPOOL = 8
NEPOOL = 12
DMAPOOL = {q: [f"{q}{i}" for i in range(NPOOL)] for q in ("ms", "mg", "mh")}
ENGPOOL = {b: [f"{b}{i}" for i in range(NEPOOL)] for b in ESEM.values()}
ALLSEMS = [s for q in DMAPOOL.values() for s in q] + [s for p in ENGPOOL.values() for s in p]
QENG = {"ms": "sync", "mg": "gpsimd", "mh": "scalar"}


class Sch:
    def __init__(self):
        self.ops = {e: [] for e in ENGINES}
        self.cnt = {}

    def qtot(self, q):
        return [(sem, self.cnt[sem]) for sem in DMAPOOL[q] if self.cnt.get(sem, 0) > 0]

    def emit(self, eng, fn, waits=(), sig=None, amt=1):
        w = []
        for t in waits:
            if t is None:
                continue
            if isinstance(t, list):
                w.extend(t)
            else:
                w.append(t)
        tok = None
        if sig is not None:
            if sig in ENGPOOL:
                if not hasattr(self, "erot"):
                    self.erot = {}
                self.erot[sig] = (self.erot.get(sig, -1) + 1) % NEPOOL
                sig = ENGPOOL[sig][self.erot[sig]] if False else f"{sig}{self.erot[sig]}"
            self.cnt[sig] = self.cnt.get(sig, 0) + amt
            tok = (sig, self.cnt[sig])
        self.ops[eng].append((fn, w, sig, amt))
        return tok

    def dma(self, q, dst, src, waits=(), **kw):
        return self.dma_burst(q, [(dst, src, kw)], waits=waits)

    def dma_burst(self, q, items, waits=()):
        if not hasattr(self, "rot"):
            self.rot = {}
        self.rot[q] = (self.rot.get(q, -1) + 1) % NPOOL
        sem = DMAPOOL[q][self.rot[q]]
        eng = QENG[q]
        prev = self.cnt.get(sem, 0)
        w = [t for t in waits if t is not None]
        if prev > 0:
            w = w + [(sem, prev)]
        tok = None
        for i, (dst, src, kw) in enumerate(items):
            tok = self.emit(eng, lambda e, d=dst, s=src, kw=kw: e.dma_start(out=d, in_=s, **kw),
                            waits=(w if i == 0 else ()), sig=sem, amt=16)
        return tok

    def barrier(self):
        toks = {}
        for e in ENGINES:
            toks[e] = self.emit(e, lambda eng: eng.nop(), sig=ESEM[e])
        dt_ = [(sm, self.cnt[sm]) for sm in ALLSEMS if self.cnt.get(sm, 0) > 0]
        for e in ENGINES:
            self.emit(e, None, waits=[toks[o] for o in ENGINES if o != e] + dt_)

    def run(self, eng_name, eng, sems):
        seen = {}
        for fn, wl, sig, amt in self.ops[eng_name]:
            for s, v in wl:
                if seen.get(s, -1) < v:
                    eng.wait_ge(sems[s], v)
                    seen[s] = v
            ins = fn(eng) if fn is not None else None
            if sig is not None:
                ins.then_inc(sems[sig], amt)


class Arena:
    def __init__(self, tile, total, name):
        self.tile, self.total, self.name = tile, total, name
        self.blocks = {}
        self.hw = 0

    def alloc(self, key, size):
        assert key not in self.blocks, key
        taken = sorted(self.blocks.values())
        off = 0
        for (o, s) in taken:
            if off + size <= o:
                break
            off = o + s
        assert off + size <= self.total, f"{self.name} OOM: {key}({size})"
        self.blocks[key] = (off, size)
        self.hw = max(self.hw, off + size)
        return self.tile[:, off:off + size]

    def free(self, *keys):
        for k in keys:
            del self.blocks[k]


def prep_host(inp):
    f32 = np.float32
    bf = ml_dtypes.bfloat16
    out = {}
    out["qwT"] = np.ascontiguousarray(np.asarray(inp["L0_attn_q_w"], f32).T)
    out["kwT"] = np.ascontiguousarray(np.asarray(inp["L0_attn_k_w"], f32).T)
    out["vwT"] = np.ascontiguousarray(np.asarray(inp["L0_attn_v_w"], f32).T)
    bsp = np.zeros((D, 128), f32)
    for h in range(H):
        bsp[:, 32 * h] = np.asarray(inp["L0_attn_b_w"], f32)[h, :]
    out["bwTsp"] = bsp
    cds = np.zeros((3, 8, 4, 128, 32), f32)
    for ti, nm in enumerate(("convq_w", "convk_w", "convv_w")):
        cw = np.asarray(inp["L0_attn_" + nm], f32)
        for c in range(8):
            for g in range(4):
                d_ = cw[c * 128:(c + 1) * 128, g]
                for i4 in range(4):
                    cds[ti, c, g, 32 * i4:32 * i4 + 32, :] = np.diag(d_[32 * i4:32 * i4 + 32])
    out["cd"] = cds
    firs = [np.asarray(inp["L0_attn_fir_s"], f32).reshape(D, -1),
            np.asarray(inp["L0_attn_fir_l"], f32).reshape(D, -1),
            np.asarray(inp["L0_attn_fir_wd"], f32).reshape(D, -1)]
    groups = []
    for fi, filt in enumerate(firs):
        K = filt.shape[1]
        for Gf in range((K + 3) // 4):
            groups.append((fi, Gf, 64 - 4 * Gf))
    assert len(groups) == NGF
    fd = np.zeros((8, 4, NGF, 128, 32), f32)
    for gi, (fi, Gf, _t) in enumerate(groups):
        filt = firs[fi]
        K = filt.shape[1]
        for c in range(8):
            for s4 in range(4):
                for g in range(4):
                    k = K - 1 - 4 * Gf + g - 3
                    if 0 <= k < K:
                        d_ = filt[c * 128 + s4 * 32: c * 128 + s4 * 32 + 32, k]
                        fd[c, s4, gi, 32 * g:32 * g + 32, :] = np.diag(d_)
    out["fird"] = fd
    g1T = np.zeros((1152, 2048), f32)
    g1T[:1084, :] = np.asarray(inp["L0_attn_g1_w"], f32).T
    out["g1Tp"] = g1T
    out["g1b"] = np.ascontiguousarray(np.asarray(inp["L0_attn_g1_b"], f32).reshape(16, 128).T)
    out["g2T"] = np.ascontiguousarray(np.asarray(inp["L0_attn_g2_w"], f32).T)
    out["g2b"] = np.asarray(inp["L0_attn_g2_b"], f32).reshape(1, 20)
    out["vbias"] = np.asarray(inp["L0_attn_value_bias"], f32).reshape(1, 4)
    out["ltemp"] = np.asarray(inp["L0_attn_log_temp"], f32).reshape(1, 4)
    out["flog"] = np.asarray(inp["L0_attn_floor_logit"], f32).reshape(1, 4)
    out["onorm_rep"] = np.ascontiguousarray(np.tile(np.asarray(inp["L0_attn_onorm_w"], f32).reshape(1, 256), (128, 4)))
    out["owT"] = np.ascontiguousarray(np.asarray(inp["L0_attn_o_w"], f32).T)
    out["wg"] = np.ascontiguousarray(np.asarray(inp["L0_ffn_w_gate"], f32))
    out["wu"] = np.ascontiguousarray(np.asarray(inp["L0_ffn_w_up"], f32))
    out["wdn"] = np.ascontiguousarray(np.asarray(inp["L0_ffn_w_down"], f32))
    out["eye128"] = np.eye(128, dtype=bf)
    out["ident32"] = np.eye(128, dtype=f32)
    out["eye4"] = np.tile(np.eye(128, dtype=f32), (1, 4)).astype(bf)
    out["allones"] = np.ones((128, 128), dtype=bf)
    out["zeros128"] = np.zeros((128, 128), dtype=bf)
    oc = np.zeros((128, 256), f32)
    oc[:, 127] = 1
    out["onescol"] = oc.astype(bf)
    r = np.arange(128)
    out["triuI"] = (r[:, None] <= r[None, :]).astype(bf)
    bdm = (r[:, None] // 32 == r[None, :] // 32) & (r[:, None] > r[None, :])
    out["negbd"] = (-bdm.astype(f32)).astype(bf)
    out["blkLow"] = ((r[:, None] // 32) > (r[None, :] // 32)).astype(bf)
    out["blkUp"] = ((r[:, None] // 32) < (r[None, :] // 32)).astype(bf)
    hc4 = np.zeros((8, 4, 128, 16), f32)
    for c in range(8):
        for slot in range(4):
            hc4[c, slot, :, 4 * slot + c // 2] = 1.0
    out["headcol4"] = hc4.astype(bf)
    imask = np.zeros((1, 20), f32)
    imask[0, 4::5] = 1.0
    out["i4mask"] = imask
    return out, groups


def build(groups):
    global PH, SUB56
    nc = bass.Bass("TRN2", target_bir_lowering=False)
    S = Sch()
    dr = {}

    def din(name, shape, dt=F32):
        dr[name] = nc.dram_tensor(name, list(shape), dt, kind="ExternalInput").ap()

    din("x", (L, D))
    for nm in ("qwT", "kwT", "vwT"):
        din(nm, (D, D))
    din("bwTsp", (D, 128))
    din("cd", (3, 8, 4, 128, 32))
    din("fird", (8, 4, NGF, 128, 32))
    din("g1Tp", (1152, 2048))
    din("g1b", (128, 16))
    din("g2T", (2048, 20))
    din("g2b", (1, 20)); din("vbias", (1, 4)); din("ltemp", (1, 4)); din("flog", (1, 4))
    din("onorm_rep", (128, D))
    din("owT", (D, D))
    din("wg", (IFF, D)); din("wu", (IFF, D)); din("wdn", (D, IFF))
    for nm in ("eye128", "allones", "triuI", "negbd", "blkLow", "blkUp", "zeros128"):
        din(nm, (128, 128), BF16)
    din("ident32", (128, 128), F32)
    din("eye4", (128, 512), BF16)
    din("onescol", (128, 256), BF16)
    din("headcol4", (8, 4, 128, 16), BF16)
    din("i4mask", (1, 20))

    y = nc.dram_tensor("y", [L, D], F32, kind="ExternalOutput").ap()
    vdram = nc.dram_tensor("vdram", [D, 128 + L], BF16, kind="Internal").ap()
    xqdram = nc.dram_tensor("xqdram", [L, D], BF16, kind="Internal").ap()
    sdram = nc.dram_tensor("sdram", [L, D], F32, kind="Internal").ap()
    dbg = {}
    if DEBUG:
        for nm, shape in [("dbg_xn", (L, D)), ("dbg_q", (D, L)), ("dbg_k", (D, L)),
                          ("dbg_v", (D, L)), ("dbg_beta", (128, L)),
                          ("dbg_qh", (D, L)), ("dbg_kb", (D, L)), ("dbg_negw", (D, L)),
                          ("dbg_u", (L, D)), ("dbg_delta", (L, D)),
                          ("dbg_short", (L, D)), ("dbg_long", (L, D)), ("dbg_wide", (L, D)),
                          ("dbg_stat", (L, 64)), ("dbg_logits", (L, 20)),
                          ("dbg_otm", (L, D)), ("dbg_xq", (L, D)),
                          ("dbg_hq", (L, IFF)), ("dbg_h", (L, IFF)), ("dbg_preq", (D, L)), ("dbg_sums", (128, L))]:
            dbg[nm] = nc.dram_tensor(nm, list(shape), F32, kind="ExternalOutput").ap()

    ctx = []

    def sbuf(name, shape, dt):
        t = nc.sbuf_tensor(name, list(shape), dt)
        h_ = t.__enter__()
        ctx.append(t)
        return h_

    def psum(name, shape, dt=F32):
        t = nc.psum_tensor(name, list(shape), dt)
        h_ = t.__enter__()
        ctx.append(t)
        return h_

    AB = Arena(sbuf("AB", [128, 79360], BF16), 79360, "AB")
    AFp = Arena(sbuf("AFv", [128, 10240], F32), 10240, "AFp")
    AI = Arena(sbuf("AIv", [128, 2816], I32), 2816, "AI")
    AU = Arena(sbuf("AUv", [128, 1408], U8), 1408, "AU")

    PS_S = [psum(f"psS{h}", [128, 512]) for h in range(H)]
    PW = [psum(f"psW{i}", [128, 512]) for i in range(4)]

    C = {}
    for nm, w_ in [("eye128", 128), ("allones", 128), ("triuI", 128), ("negbd", 128),
                   ("blkLow", 128), ("blkUp", 128), ("zeros128", 128), ("eye4", 512), ("onescol", 256)]:
        C[nm] = AB.alloc(nm, w_)
    C["ident32"] = AFp.alloc("ident32", 128)
    C["onorm_rep"] = AFp.alloc("onorm_rep", D)
    C["g1b"] = AFp.alloc("g1b", 16)
    C["cd"] = AB.alloc("cd", 3 * 8 * 4 * 32)
    C["g2T"] = AB.alloc("g2T", 16 * 20)
    C["hc4"] = AB.alloc("hc4", 8 * 4 * 16)
    sm = AFp.alloc("smallrows", 64)

    tld = []
    for nm in ("eye128", "allones", "triuI", "negbd", "blkLow", "blkUp", "zeros128", "eye4", "onescol"):
        tld.append(S.dma("ms", C[nm][:], dr[nm]))
    tld.append(S.dma("ms", C["ident32"][:], dr["ident32"]))
    tld.append(S.dma("ms", C["onorm_rep"][:], dr["onorm_rep"]))
    tld.append(S.dma("ms", C["g1b"][:], dr["g1b"]))
    tld.append(S.dma("mg", C["cd"][:].rearrange("p (t c g w) -> p t c g w", t=3, c=8, g=4),
                     dr["cd"].rearrange("t c g p w -> p t c g w")))
    tld.append(S.dma("mg", C["g2T"][:].rearrange("p (c w) -> p c w", c=16),
                     dr["g2T"].rearrange("(c p) w -> p c w", p=128)))
    tld.append(S.dma("mg", C["hc4"][:].rearrange("p (c s w) -> p c s w", c=8, s=4),
                     dr["headcol4"].rearrange("c s p w -> p c s w")))
    tld.append(S.dma("ms", sm[0:1, 0:20], dr["g2b"]))
    tld.append(S.dma("ms", sm[0:1, 20:24], dr["vbias"]))
    tld.append(S.dma("ms", sm[0:1, 24:28], dr["ltemp"]))
    tld.append(S.dma("ms", sm[0:1, 28:32], dr["flog"]))
    tld.append(S.dma("ms", sm[0:1, 32:52], dr["i4mask"]))

    zt = AB.alloc("zt", 128)
    tz = S.emit("vector", lambda e: e.memset(zt[:], 0.0), sig="dv")
    for c in range(8):
        S.dma("mg", vdram[c * 128:(c + 1) * 128, 0:128], zt[:], waits=[tz])

    grow = AFp.alloc("grow", 80)
    fscr = AFp.alloc("fscr", 20)
    vb_sp = sm[0:1, 20:24].broadcast_to([1, 4, 5])
    lt_sp = sm[0:1, 24:28].broadcast_to([1, 4, 5])
    fl_sp = sm[0:1, 28:32].broadcast_to([1, 4, 5])
    g3 = lambda ap: ap.rearrange("p (a b) -> p a b", b=5)
    w_all = list(tld)
    t1 = S.emit("vector", lambda e: e.tensor_tensor(g3(grow[0:1, 0:20]), vb_sp, g3(sm[0:1, 32:52]), OP.mult),
                waits=w_all, sig="dv")
    t1 = S.emit("vector", lambda e: e.tensor_tensor(grow[0:1, 0:20], grow[0:1, 0:20], sm[0:1, 0:20], OP.add),
                waits=[t1], sig="dv")
    t2 = S.emit("scalar", lambda e: e.activation(g3(grow[0:1, 20:40]), lt_sp, AF.Exp, scale=-1.0), waits=w_all, sig="dc")
    tf1 = S.emit("scalar", lambda e: e.activation(g3(fscr[0:1, :]), fl_sp, AF.Sigmoid), waits=w_all, sig="dc")
    t4 = S.emit("vector", lambda e: e.tensor_scalar(grow[0:1, 40:60], fscr[0:1, :], -4.0, 1.0, OP.mult, OP.add),
                waits=[tf1], sig="dv")
    tf2 = S.emit("vector", lambda e: e.tensor_tensor(grow[0:1, 60:80], fscr[0:1, :], sm[0:1, 32:52], OP.mult),
                 waits=[tf1], sig="dv")
    tf3 = S.emit("vector", lambda e: e.scalar_tensor_tensor(grow[0:1, 60:80], grow[0:1, 60:80], -1.0, fscr[0:1, :],
                                                            OP.mult, OP.add), waits=[tf2], sig="dv")
    growb = AB.alloc("growb", 80)
    tcast = S.emit("vector", lambda e: e.tensor_copy(growb[0:1, :], grow[0:1, :]), waits=[t1, t2, t4, tf3], sig="dv")
    gconst = AFp.alloc("gconst", 80)
    tgc = None
    for i in range(4):
        tmm = S.emit("tensor", lambda e, i=i: e.matmul(PW[3][:, 0:20], C["allones"][0:1, :],
                                                       growb[0:1, i * 20:(i + 1) * 20], start=True, stop=True),
                     waits=[tcast, tgc], sig="dt")
        tgc = S.emit("vector", lambda e, i=i: e.tensor_copy(gconst[:, i * 20:(i + 1) * 20], PW[3][:, 0:20]),
                     waits=[tmm], sig="dv")
    AFp.free("grow", "fscr")
    AB.free("growb", "zt")
    S.barrier()

    xn_fm = AB.alloc("xn_fm", 8 * W)
    qh_fm = AB.alloc("qh_fm", 8 * W)
    kh_fm = AB.alloc("kh_fm", 8 * W)
    vf_fm = AB.alloc("vf_fm", 8 * W)
    S_bf = AB.alloc("S_bf", H * 512)
    tails = AB.alloc("tails", 3 * 8 * 64)
    statF = AB.alloc("statF", NT * 128)
    S32 = AFp.alloc("S32", H * 512)

    tsz = S.emit("vector", lambda e: e.memset(statF[:], 0.0), sig="dv")

    TP = dict(tok_prev=None)
    tp = {}

    def fmsl(tile, c, tt):
        return tile[:, c * W + tt * 128: c * W + tt * 128 + 128]

    def bcast128(constv):
        return constv.rearrange("p (x w) -> p x w", x=1).broadcast_to([128, 4, 128]).rearrange("p a w -> p (a w)")

    def emit_quant(src, dst, Fn, wtok, key, scratch=None, dst_war=None, fast=False):
        nblk = Fn // 16
        nb2 = (nblk + 15) // 16 * 16
        qbm = AFp.alloc(key + "bm", 2 * nb2)
        qv = AFp.alloc(key + "v", Fn)
        qA = AFp.alloc(key + "A", Fn)
        qBC = AFp.alloc(key + "BC", Fn)
        qt2 = scratch[:, 0:Fn] if scratch is not None else AFp.alloc(key + "t2", Fn)
        qi = AI.alloc(key + "i", Fn)
        qi2 = AI.alloc(key + "i2", Fn)
        qm = AU.alloc(key + "m", Fn)
        t0 = S.emit("vector", lambda e: e.tensor_reduce(qbm[:, 0:nblk], src.rearrange("p (b k) -> p b k", k=16),
                                                        axis=AX, op=OP.max, apply_absolute_value=True),
                    waits=[wtok], sig="dv")
        t1 = S.emit("vector", lambda e: e.tensor_scalar(qbm[:, nb2:nb2 + nblk], qbm[:, 0:nblk], 1.0 / 6.0, 1e-12,
                                                        OP.mult, OP.max), waits=[t0], sig="dv")
        t2 = S.emit("vector", lambda e: e.reciprocal(qbm[:, 0:nblk], qbm[:, nb2:nb2 + nblk]), waits=[t1], sig="dv")
        rv = qbm[:, 0:nblk].broadcast_to([128, nblk, 16])
        sv = qbm[:, nb2:nb2 + nblk].broadcast_to([128, nblk, 16])
        sview = src.rearrange("p (b k) -> p b k", k=16)
        t3 = S.emit("vector", lambda e: e.tensor_tensor(qv[:, :].rearrange("p (b k) -> p b k", k=16), sview, rv, OP.mult),
                    waits=[t2], sig="dv")
        t4 = S.emit("vector", lambda e: e.tensor_scalar(qi[:, :], qv[:, :], 2.0, None, OP.mult), waits=[t3], sig="dv")
        t5 = S.emit("vector", lambda e: e.tensor_scalar(qA[:, :], qi[:, :], 0.5, None, OP.mult), waits=[t4], sig="dv")
        t7 = S.emit("vector", lambda e: e.tensor_scalar(qi2[:, :], qv[:, :], 4.0, -4.0, OP.min, OP.max),
                    waits=[t3], sig="dv")
        qi3 = AI.alloc(key + "i3", Fn) if fast else qi
        t8 = S.emit("vector", lambda e, qi3=qi3: e.tensor_scalar(qi3[:, :], qv[:, :], 0.1, None, OP.mult),
                    waits=([t3] if fast else [t3, t5]), sig="dv")
        t10 = S.emit("vector", lambda e, qi3=qi3: e.scalar_tensor_tensor(qBC[:, :], qi3[:, :], 2.0, qi2[:, :],
                                                                         OP.mult, OP.add),
                     waits=[t7, t8], sig="dv")
        if fast:
            t11a = S.emit("scalar", lambda e: e.activation(qt2[:, :], qv[:, :], AF.Square),
                          waits=[t3], sig="dc")
        else:
            t11a = S.emit("vector", lambda e: e.tensor_tensor(qt2[:, :], qv[:, :], qv[:, :], OP.mult),
                          waits=[t3], sig="dv")
        t11 = S.emit("vector", lambda e: e.tensor_scalar(qm[:, :], qt2[:, :], 4.0, None, OP.is_gt),
                     waits=[t11a], sig="dv")
        t12 = S.emit("vector", lambda e: e.copy_predicated(qA[:, :], qm[:, :], qBC[:, :]),
                     waits=[t10, t11, t5], sig="dv")
        qAv = qA[:, :].rearrange("p (b k) -> p b k", k=16)
        t13 = S.emit("vector", lambda e: e.tensor_tensor(dst.rearrange("p (b k) -> p b k", k=16), qAv, sv, OP.mult),
                     waits=[t12, dst_war], sig="dv")
        AFp.free(key + "bm", key + "v", key + "A", key + "BC")
        if scratch is None:
            AFp.free(key + "t2")
        AI.free(key + "i", key + "i2")
        if fast:
            AI.free(key + "i3")
        AU.free(key + "m")
        return t13, t0

    for mt in range(N_MEG_BUILD):
        mb = mt * TMTOK
        first = (mt == 0)

        x_tm = AFp.alloc("x_tm", 1024)
        xn32 = AFp.alloc("xn32", 1024)
        lnS = AFp.alloc("lnS", 16)
        tmpS = AFp.alloc("tmpS", 1024)
        xn_tt = AB.alloc("xn_tt", 1024)
        xq_tm = AB.alloc("xq_tm", 1024)

        for tt in range(NT):
            r0 = mb + tt * 128
            txl = S.dma("mg", x_tm[:, :], dr["x"][r0:r0 + 128, :],
                        waits=[tp.get("xtmfree"), tp.get("mttail"), tp.get("sdma")])
            t1 = S.emit("scalar", lambda e: e.activation(tmpS[:, :], x_tm[:, :], AF.Square, accum_out=lnS[:, 0:1]),
                        waits=[txl], sig="dc")
            t2 = S.emit("vector", lambda e: e.tensor_reduce(lnS[:, 1:2], x_tm[:, :], axis=AX, op=OP.add),
                        waits=[txl], sig="dv")
            t3 = S.emit("vector", lambda e: e.tensor_scalar(lnS[:, 2:3], lnS[:, 1:2], 1.0 / D, None, OP.mult),
                        waits=[t2], sig="dv")
            t4 = S.emit("vector", lambda e: e.tensor_tensor(lnS[:, 3:4], lnS[:, 2:3], lnS[:, 2:3], OP.mult),
                        waits=[t3], sig="dv")
            t5 = S.emit("vector", lambda e: e.scalar_tensor_tensor(lnS[:, 4:5], lnS[:, 0:1], 1.0 / D, lnS[:, 3:4],
                                                                   OP.mult, OP.subtract), waits=[t1, t4], sig="dv")
            t6 = S.emit("vector", lambda e: e.tensor_scalar(lnS[:, 5:6], lnS[:, 4:5], 1e-5, None, OP.add),
                        waits=[t5], sig="dv")
            t7 = S.emit("vector", lambda e: e.reciprocal(lnS[:, 6:7], lnS[:, 5:6]), waits=[t6], sig="dv")
            t8 = S.emit("scalar", lambda e: e.activation(lnS[:, 7:8], lnS[:, 6:7], AF.Sqrt), waits=[t7], sig="dc")
            t9 = S.emit("vector", lambda e: e.tensor_scalar(lnS[:, 8:9], lnS[:, 2:3], -1.0, None, OP.mult),
                        waits=[t3], sig="dv")
            t10 = S.emit("vector", lambda e: e.tensor_scalar(xn32[:, :], x_tm[:, :], lnS[:, 8:9], lnS[:, 7:8],
                                                             OP.add, OP.mult),
                         waits=[t8, t9, tp.get("qprev"), tp.get("qprev2")], sig="dv")
            tp["xtmfree"] = t10
            t11 = S.emit("vector", lambda e: e.tensor_copy(xn_tt[:, :], xn32[:, :]),
                         waits=[t10, tp.get("xnxpo")], sig="dv")
            tq, tq0 = emit_quant(xn32[:, :], xq_tm[:, :], 1024, t10, "q1", scratch=tmpS,
                                 dst_war=tp.get("xqout"))
            tp["qprev"] = tq0
            if DEBUG:
                tp["qprev2"] = S.dma("mg", dbg["dbg_xn"][r0:r0 + 128, :], xn32[:, :], waits=[t10, tq])
            tp["xqout"] = S.dma("mg", xqdram[r0:r0 + 128, :], xq_tm[:, :], waits=[tq])
            if DEBUG:
                tp["xqout"] = S.dma("mg", dbg["dbg_xq"][r0:r0 + 128, :], xq_tm[:, :],
                                    waits=[tq, tp.get("xqout")])
            for c in range(8):
                S.dma("ms", xn_fm[:, c * W + tt * 128: c * W + tt * 128 + 128],
                      xn_tt[:, c * 128:(c + 1) * 128], waits=[t11, tp.get("xnfree")], transpose=True)
            tp["xnxpo"] = S.qtot("ms")
        tok_xpo = S.qtot("ms")
        AFp.free("x_tm", "xn32", "lnS", "tmpS")
        AB.free("xn_tt")
        tp.pop("xtmfree", None)
        tp.pop("xnxpo", None)

        if PH < 2: continue
        beta_sp = AFp.alloc("beta_sp", W)
        s2q_sp = AFp.alloc("s2q_sp", W)
        s2k_sp = AFp.alloc("s2k_sp", W)
        rkb_sp = AFp.alloc("rkb_sp", W)
        pre_q = AB.alloc("pre_q", 8 * WP)
        pre_k = AB.alloc("pre_k", 8 * WP)
        pre_v = AB.alloc("pre_v", 8 * WP)
        wbuf = AB.alloc("wbuf", 4 * 1024)
        tok_tail = []
        for ti, pre in enumerate((pre_q, pre_k, pre_v)):
            for c in range(8):
                tl = tails[:, (ti * 8 + c) * 64:(ti * 8 + c) * 64 + 64]
                if first:
                    tok_tail.append(S.emit("vector", lambda e, pre=pre, c=c: e.memset(pre[:, c * WP:c * WP + 64], 0.0),
                                           waits=[tp.get("mttail")], sig="dv"))
                else:
                    tok_tail.append(S.emit("vector", lambda e, pre=pre, c=c, tl=tl: e.tensor_copy(
                        pre[:, c * WP:c * WP + 64], tl), waits=[tp.get("mttail")], sig="dv"))
        tok_proj = {}
        order = [(n, dc) for n in ("q", "k", "v") for dc in range(8)]
        wsel = {"q": ("qwT", pre_q), "k": ("kwT", pre_k), "v": ("vwT", pre_v)}
        for oi, (name, dc) in enumerate(order):
            wnm, pre = wsel[name]
            buf = wbuf[:, (oi % 4) * 1024:(oi % 4) * 1024 + 1024]
            war = tok_proj.get(order[oi - 4]) if oi >= 4 else None
            twl = S.dma("mg", buf.rearrange("p (c w) -> p c w", c=8),
                        dr[wnm].rearrange("(c p) d -> p c d", p=128)[:, :, dc * 128:(dc + 1) * 128],
                        waits=[war, tp.get("mttail"), tp.get("sdma")])
            mmt = None
            for kc in range(8):
                mmt = S.emit("tensor", lambda e, kc=kc, buf=buf, oi=oi: e.matmul(
                    PW[oi % 2][:, 0:512], buf[:, kc * 128:(kc + 1) * 128], xn_fm[:, kc * W:(kc + 1) * W],
                    start=(kc == 0), stop=(kc == 7)),
                    waits=[twl, tok_xpo, tp.get(("projps", oi % 2))], sig="dt")
            ev = S.emit("scalar", lambda e, pre=pre, dc=dc, oi=oi: e.activation(
                pre[:, dc * WP + 64: dc * WP + 64 + W], PW[oi % 2][:, 0:512], AF.Identity),
                waits=[mmt] + tok_tail, sig="dc")
            tok_proj[(name, dc)] = ev
            tp[("projps", oi % 2)] = ev
        twb = S.dma("mg", wbuf[:, 0:1024].rearrange("p (c w) -> p c w", c=8),
                    dr["bwTsp"].rearrange("(c p) d -> p c d", p=128), waits=[tok_proj[("v", 6)]])
        bmm = None
        for kc in range(8):
            bmm = S.emit("tensor", lambda e, kc=kc: e.matmul(
                PW[2][:, 0:512], wbuf[:, kc * 128:(kc + 1) * 128], xn_fm[:, kc * W:(kc + 1) * W],
                start=(kc == 0), stop=(kc == 7)), waits=[twb, tp.get("ps2")], sig="dt")
        tok_beta = S.emit("scalar", lambda e: e.activation(beta_sp[:, :], PW[2][:, 0:512], AF.Sigmoid),
                          waits=[bmm], sig="dc")
        tp["ps2"] = tok_beta
        if DEBUG:
            for c in range(8):
                S.dma("mg", dbg["dbg_preq"][c * 128:(c + 1) * 128, mb:mb + W],
                      pre_q[:, c * WP + 64:c * WP + 64 + W], waits=[tok_proj[("q", c)]])
        tok_savetail = []
        for ti, (name, pre) in enumerate((("q", pre_q), ("k", pre_k), ("v", pre_v))):
            for c in range(8):
                tl = tails[:, (ti * 8 + c) * 64:(ti * 8 + c) * 64 + 64]
                tok_savetail.append(S.emit("vector", lambda e, pre=pre, c=c, tl=tl: e.tensor_copy(
                    tl, pre[:, c * WP + W: c * WP + W + 64]), waits=[tok_proj[(name, c)]], sig="dv"))

        if PH < 3: continue
        cdv = C["cd"][:].rearrange("p (t c g w) -> p t c g w", t=3, c=8, g=4)
        tok_conv = {}
        for wi, (name, pre, outt) in enumerate((("q", pre_q, qh_fm), ("k", pre_k, kh_fm), ("v", pre_v, vf_fm))):
            for c in range(8):
                mmt = S.emit("tensor", lambda e, c=c, pre=pre: e.matmul(
                    PW[3][:, 0:512], C["zeros128"][:, :], pre[:, c * WP + 64: c * WP + 64 + W],
                    start=True, stop=False), waits=[tok_proj[(name, c)], tp.get("ps3")], sig="dt")
                for s4 in range(4):
                    for g in range(4):
                        mmt = S.emit("tensor", lambda e, wi=wi, c=c, g=g, s4=s4, pre=pre: e.matmul(
                            PW[3][32 * s4:32 * s4 + 32, 0:512],
                            cdv[32 * s4:32 * s4 + 32, wi, c, g, :],
                            pre[32 * s4:32 * s4 + 32, c * WP + 61 + g: c * WP + 61 + g + W],
                            start=False, stop=False,
                            tile_position=(32 * s4, 32 * s4)),
                            waits=[tok_proj[(name, c)]], sig="dt")
                mmt = S.emit("tensor", lambda e, c=c, pre=pre: e.matmul(
                    PW[3][:, 0:512], C["zeros128"][:, :], pre[:, c * WP + 64: c * WP + 64 + W],
                    start=False, stop=True), sig="dt")
                tok_conv[(name, c)] = S.emit("scalar", lambda e, c=c, outt=outt: e.activation(
                    outt[:, c * W:(c + 1) * W], PW[3][:, 0:512], AF.Silu), waits=[mmt], sig="dc")
                tp["ps3"] = tok_conv[(name, c)]
        if DEBUG:
            for nm, key in (("dbg_q", "q"), ("dbg_k", "k"), ("dbg_v", "v")):
                tile = {"q": qh_fm, "k": kh_fm, "v": vf_fm}[key]
                for c in range(8):
                    S.dma("mg", dbg[nm][c * 128:(c + 1) * 128, mb:mb + W], tile[:, c * W:(c + 1) * W],
                          waits=[tok_conv[(key, c)]])
            S.dma("mg", dbg["dbg_beta"][:, mb:mb + W], beta_sp[:, :], waits=[tok_beta])
        if DEBUG:
            S.barrier()
        AB.free("pre_q", "pre_k", "pre_v", "wbuf")

        if PH < 4: continue
        k_tm = AB.alloc("k_tm", NT * 1024)
        v_tm = AB.alloc("v_tm", NT * 1024)
        vbT = AB.alloc("vbT", NT * 1024)
        kbT = AB.alloc("kbT", NT * 1024)
        khT = AB.alloc("khT", NT * 1024)
        kb_fm = AB.alloc("kb_fm", 8 * W)
        tok_vtm = {}
        tok_ktm = {}
        for c in range(8):
            S.dma("mg", vdram[c * 128:(c + 1) * 128, 128 + mb:128 + mb + W], vf_fm[:, c * W:(c + 1) * W],
                  waits=[tok_conv[("v", c)]])
            for tt in range(NT):
                tok_vtm[(tt, c)] = S.dma("ms", v_tm[:, tt * 1024 + c * 128: tt * 1024 + (c + 1) * 128],
                                         vf_fm[:, c * W + tt * 128: c * W + tt * 128 + 128],
                                         waits=[tok_conv[("v", c)], tp.get("mttail")] + tok_savetail,
                                         transpose=True)
                tok_ktm[(tt, c)] = S.dma("mh", k_tm[:, tt * 1024 + c * 128: tt * 1024 + (c + 1) * 128],
                                         kh_fm[:, c * W + tt * 128: c * W + tt * 128 + 128],
                                         waits=[tok_conv[("k", c)], tp.get("mttail")] + tok_savetail,
                                         transpose=True)
        tok_vdr_w = S.qtot("mg")
        tok_ktm_all = S.qtot("mh")
        tok_vtm_all = S.qtot("ms")

        if PH < 5: continue
        sq_t = AB.alloc("sq_t", 2 * W)
        tok_s2 = {}
        for ti, (name, src, dstsp) in enumerate((("q", qh_fm, s2q_sp), ("k", kh_fm, s2k_sp))):
            mmt = None
            sqt = {}
            for c in range(8):
                half = c % 2
                war = tok_s2.get((name, c - 2))
                if war is None and name == "k":
                    war = tok_s2.get(("q", 6 + c))
                sqt[c] = S.emit("scalar", lambda e, c=c, src=src, half=half: e.activation(
                    sq_t[:, half * W:(half + 1) * W], src[:, c * W:(c + 1) * W], AF.Square),
                    waits=[tok_conv[(name, c)], war] + (tok_savetail if c < 2 else []), sig="dc")
                h = c // 2
                mmt = S.emit("tensor", lambda e, c=c, h=h, half=half: e.matmul(
                    PW[2][:, 0:512], C["onescol"][:, 127 - 32 * h:255 - 32 * h], sq_t[:, half * W:(half + 1) * W],
                    start=(c == 0), stop=(c == 7)),
                    waits=[sqt[c], tp.get("ps2")], sig="dt")
                tok_s2[(name, c)] = mmt
            tok_s2[name] = S.emit("vector", lambda e, dstsp=dstsp: e.tensor_copy(dstsp[:, :], PW[2][:, 0:512]),
                                  waits=[mmt], sig="dv")
            tp["ps2"] = tok_s2[name]
        AB.free("sq_t")
        tok_rsq = {}
        for name, sp in (("q", s2q_sp), ("k", s2k_sp)):
            a = S.emit("vector", lambda e, sp=sp: e.tensor_scalar(sp[:, :], sp[:, :], 1e-6, None, OP.add),
                       waits=[tok_s2[name]], sig="dv")
            b = S.emit("vector", lambda e, sp=sp: e.reciprocal(sp[:, :], sp[:, :]), waits=[a], sig="dv")
            tok_rsq[name] = S.emit("scalar", lambda e, sp=sp: e.activation(sp[:, :], sp[:, :], AF.Sqrt),
                                   waits=[b], sig="dc")
        tok_rkb = S.emit("vector", lambda e: e.tensor_tensor(rkb_sp[:, :], s2k_sp[:, :], beta_sp[:, :], OP.mult),
                         waits=[tok_rsq["k"], tok_beta], sig="dv")
        sprd3 = AB.alloc("sprd3", 3 * W)
        tok_sb = {}
        tok_sb["q"] = S.emit("vector", lambda e: e.tensor_copy(sprd3[:, 0:W], s2q_sp[:, :]),
                             waits=[tok_rsq["q"]], sig="dv")
        tok_sb["k"] = S.emit("vector", lambda e: e.tensor_copy(sprd3[:, W:2 * W], s2k_sp[:, :]),
                             waits=[tok_rsq["k"]], sig="dv")
        tok_sb["kb"] = S.emit("vector", lambda e: e.tensor_copy(sprd3[:, 2 * W:3 * W], rkb_sp[:, :]),
                              waits=[tok_rkb], sig="dv")
        tok_scaled = {}
        for c in range(8):
            h = c // 2
            tpos = (96, 0) if h == 3 else None
            bq = S.emit("tensor", lambda e, h=h, tpos=tpos: e.matmul(
                PW[2][:, 0:512], C["allones"][32 * h:32 * h + 1, :],
                sprd3[32 * h:32 * h + 1, 0:W], start=True, stop=True, tile_position=tpos),
                waits=[tok_sb["q"], tp.get("ps2")], sig="dt")
            tok_scaled[("q", c)] = S.emit("vector", lambda e, c=c: e.tensor_tensor(
                qh_fm[:, c * W:(c + 1) * W], qh_fm[:, c * W:(c + 1) * W], PW[2][:, 0:512], OP.mult),
                waits=[bq], sig="dv")
            tp["ps2"] = tok_scaled[("q", c)]
            bk = S.emit("tensor", lambda e, h=h, tpos=tpos: e.matmul(
                PW[3][:, 0:512], C["allones"][32 * h:32 * h + 1, :],
                sprd3[32 * h:32 * h + 1, 2 * W:3 * W], start=True, stop=True, tile_position=tpos),
                waits=[tok_sb["kb"], tp.get("ps3")], sig="dt")
            tok_scaled[("kb", c)] = S.emit("vector", lambda e, c=c: e.tensor_tensor(
                kb_fm[:, c * W:(c + 1) * W], kh_fm[:, c * W:(c + 1) * W], PW[3][:, 0:512], OP.mult),
                waits=[bk], sig="dv")
            bk2 = S.emit("tensor", lambda e, h=h, tpos=tpos: e.matmul(
                PW[3][:, 0:512], C["allones"][32 * h:32 * h + 1, :],
                sprd3[32 * h:32 * h + 1, W:2 * W], start=True, stop=True, tile_position=tpos),
                waits=[tok_sb["k"], tok_scaled[("kb", c)]], sig="dt")
            tok_scaled[("kh", c)] = S.emit("vector", lambda e, c=c: e.tensor_tensor(
                kh_fm[:, c * W:(c + 1) * W], kh_fm[:, c * W:(c + 1) * W], PW[3][:, 0:512], OP.mult),
                waits=[bk2, tok_ktm_all], sig="dv")
            tp["ps3"] = tok_scaled[("kh", c)]
        if DEBUG:
            for c in range(8):
                S.dma("mg", dbg["dbg_qh"][c * 128:(c + 1) * 128, mb:mb + W], qh_fm[:, c * W:(c + 1) * W],
                      waits=[tok_scaled[("q", c)]])
                S.dma("mg", dbg["dbg_kb"][c * 128:(c + 1) * 128, mb:mb + W], kb_fm[:, c * W:(c + 1) * W],
                      waits=[tok_scaled[("kb", c)]])

        if PH < 6: continue
        scT = AFp.alloc("scT", 3 * 128)
        tok_tmprod = {}
        for tt in range(NT):
            tsc = []
            for si, (sp, tok_src) in enumerate(((beta_sp, tok_beta), (s2k_sp, tok_rsq["k"]), (rkb_sp, tok_rkb))):
                tpx = S.emit("tensor", lambda e, sp=sp, tt=tt: e.transpose(
                    PW[2][:, 0:128], sp[:, tt * 128:(tt + 1) * 128], C["ident32"][:, :]),
                    waits=[tok_src, tp.get("ps2")], sig="dt")
                ev = S.emit("vector", lambda e, si=si: e.tensor_copy(scT[:, si * 128:(si + 1) * 128], PW[2][:, 0:128]),
                            waits=[tpx], sig="dv")
                tp["ps2"] = ev
                tsc.append(ev)

            def hview(si):
                return scT[:, si * 128: si * 128 + 128].rearrange("p (h r) -> p h r", h=4)[:, :, 0:1].broadcast_to(
                    [128, 4, 256])
            sl = slice(tt * 1024, (tt + 1) * 1024)
            wv_ = [tok_vtm[(tt, c)] for c in range(8)] + tsc
            wk_ = [tok_ktm[(tt, c)] for c in range(8)] + tsc
            if not SUB56:
                continue
            tok_tmprod[("vbT", tt)] = S.emit("vector", lambda e, sl=sl, tt=tt: e.tensor_tensor(
                vbT[:, sl].rearrange("p (h c) -> p h c", h=4), v_tm[:, sl].rearrange("p (h c) -> p h c", h=4),
                hview(0), OP.mult), waits=wv_, sig="dv")
            tok_tmprod[("kbT", tt)] = S.emit("vector", lambda e, sl=sl, tt=tt: e.tensor_tensor(
                kbT[:, sl].rearrange("p (h c) -> p h c", h=4), k_tm[:, sl].rearrange("p (h c) -> p h c", h=4),
                hview(2), OP.mult), waits=wk_, sig="dv")
            tok_tmprod[("khT", tt)] = S.emit("vector", lambda e, sl=sl, tt=tt: e.tensor_tensor(
                khT[:, sl].rearrange("p (h c) -> p h c", h=4), k_tm[:, sl].rearrange("p (h c) -> p h c", h=4),
                hview(1), OP.mult), waits=wk_, sig="dv")
            tp["ps2"] = tok_tmprod[("khT", tt)]
        AB.free("k_tm", "sprd3")
        AFp.free("scT", "beta_sp", "s2q_sp", "s2k_sp", "rkb_sp")
        S.barrier()

        if PH < 7: continue
        negw_fm = AB.alloc("negw_fm", 8 * W)
        delta_tm = AB.alloc("delta_tm", NT * 1024)
        B_b = AB.alloc("B_b", 512)
        BT_b = AB.alloc("BT_b", 512)
        pw_b = AB.alloc("pw_b", 512)
        pwT_b = AB.alloc("pwT_b", 512)
        P_b = AB.alloc("P_b", 2 * 512)
        invT_b = AB.alloc("invT_b", 512)
        Mt_b = AB.alloc("Mt_b", 512)
        Gnw_b = AB.alloc("Gnw_b", 512)
        GnwT_b = AB.alloc("GnwT_b", 512)
        T2t_b = AB.alloc("T2t_b", 512)
        Ht_b = AB.alloc("Ht_b", 512)
        u_b = AB.alloc("u_b", 1024)
        U0_b = AB.alloc("U0_b", 1024)
        uadj_b = AB.alloc("uadj_b", 1024)
        for tt in range(NT):
            fsc = first and tt == 0
            mmt = None
            for h in range(H):
                for half in range(2):
                    c = 2 * h + half
                    mmt = S.emit("tensor", lambda e, h=h, c=c, tt=tt, half=half: e.matmul(
                        PW[0][:, h * 128:(h + 1) * 128], fmsl(kb_fm, c, tt), fmsl(kh_fm, c, tt),
                        start=(h == 0 and half == 0), stop=(h == 3 and half == 1)),
                        waits=[tok_scaled[("kb", c)], tok_scaled[("kh", c)], tp.get("ps0")], sig="dt")
            tB = S.emit("vector", lambda e: e.tensor_tensor(
                B_b[:, :].rearrange("p (a w) -> p a w", a=4),
                PW[0][:, :].rearrange("p (a w) -> p a w", a=4),
                C["negbd"][:].rearrange("p (x w) -> p x w", x=1).broadcast_to([128, 4, 128]), OP.mult),
                waits=[mmt], sig="dv")
            tp["ps0"] = tB
            tBT = S.emit("vector", lambda e: e.transpose(BT_b[:, :], B_b[:, :]), waits=[tB], sig="dv")
            tPc = S.emit("vector", lambda e: e.tensor_tensor(P_b[:, 0:512], B_b[:, :], C["eye4"][:], OP.add),
                         waits=[tB], sig="dv")
            curP = 0
            tsq_src, tsq_srcT = tB, tBT
            src_pw, src_pwT = B_b, BT_b
            for lvl in range(4):
                mmq = None
                for h in range(H):
                    mmq = S.emit("tensor", lambda e, h=h, s1=src_pwT, s2=src_pw: e.matmul(
                        PW[1][:, h * 128:(h + 1) * 128], s1[:, h * 128:(h + 1) * 128], s2[:, h * 128:(h + 1) * 128],
                        start=(h == 0), stop=(h == 3)), waits=[tsq_src, tsq_srcT, tp.get("ps1")], sig="dt")
                tev = S.emit("vector", lambda e: e.tensor_copy(pw_b[:, :], PW[1][:, :]), waits=[mmq], sig="dv")
                tp["ps1"] = tev
                tevT = S.emit("vector", lambda e: e.transpose(pwT_b[:, :], pw_b[:, :]), waits=[tev], sig="dv")
                mmp = None
                for h in range(H):
                    mmp = S.emit("tensor", lambda e, h=h, curP=curP: e.matmul(
                        PW[2][:, h * 128:(h + 1) * 128], pwT_b[:, h * 128:(h + 1) * 128],
                        P_b[:, curP * 512 + h * 128: curP * 512 + (h + 1) * 128],
                        start=(h == 0), stop=False), waits=[tevT, tPc, tp.get("ps2")], sig="dt")
                for h in range(H):
                    mmp = S.emit("tensor", lambda e, h=h, curP=curP: e.matmul(
                        PW[2][:, h * 128:(h + 1) * 128], C["eye128"][:, :],
                        P_b[:, curP * 512 + h * 128: curP * 512 + (h + 1) * 128],
                        start=False, stop=(h == 3)), sig="dt")
                curP = 1 - curP
                tPc = S.emit("vector", lambda e, curP=curP: e.tensor_copy(P_b[:, curP * 512:(curP + 1) * 512],
                                                                          PW[2][:, :]), waits=[mmp], sig="dv")
                tp["ps2"] = tPc
                if lvl < 3:
                    tsq_src = S.emit("vector", lambda e: e.tensor_copy(B_b[:, :], pw_b[:, :]), waits=[tevT], sig="dv")
                    tsq_srcT = S.emit("vector", lambda e: e.tensor_copy(BT_b[:, :], pwT_b[:, :]), waits=[tevT], sig="dv")
                    src_pw, src_pwT = B_b, BT_b
            tinvT = S.emit("vector", lambda e, curP=curP: e.transpose(invT_b[:, :], P_b[:, curP * 512:(curP + 1) * 512]),
                           waits=[tPc], sig="dv")
            mmu = None
            for h in range(H):
                mmu = S.emit("tensor", lambda e, h=h, tt=tt: e.matmul(
                    PW[h // 2][:, (h % 2) * 256:(h % 2) * 256 + 256], invT_b[:, h * 128:(h + 1) * 128],
                    vbT[:, tt * 1024 + h * 256: tt * 1024 + (h + 1) * 256], start=True, stop=True),
                    waits=[tinvT, tok_tmprod[("vbT", tt)], tp.get("ps0"), tp.get("ps1")], sig="dt")
            tu1 = S.emit("vector", lambda e: e.tensor_copy(u_b[:, 0:512], PW[0][:, :]),
                         waits=[mmu, tp.get("dbgu")], sig="dv")
            tu2 = S.emit("vector", lambda e: e.tensor_copy(u_b[:, 512:1024], PW[1][:, :]), waits=[mmu], sig="dv")
            tp["ps0"] = tu1
            tp["ps1"] = tu2
            mmw = None
            for h in range(H):
                for half in range(2):
                    mmw = S.emit("tensor", lambda e, h=h, half=half, tt=tt: e.matmul(
                        PW[2 + h // 2][:, ((h % 2) * 2 + half) * 128: ((h % 2) * 2 + half) * 128 + 128],
                        kbT[:, tt * 1024 + h * 256 + half * 128: tt * 1024 + h * 256 + half * 128 + 128],
                        invT_b[:, h * 128:(h + 1) * 128], start=True, stop=True),
                        waits=[tinvT, tok_tmprod[("kbT", tt)], tp.get("ps2"), tp.get("ps3")], sig="dt")
            nw_lo = negw_fm[:, 0:4 * W].rearrange("p (c w) -> p c w", c=4)[:, :, tt * 128:tt * 128 + 128]
            nw_hi = negw_fm[:, 4 * W:8 * W].rearrange("p (c w) -> p c w", c=4)[:, :, tt * 128:tt * 128 + 128]
            tw1 = S.emit("vector", lambda e, nw_lo=nw_lo: e.tensor_scalar(
                nw_lo, PW[2][:, :].rearrange("p (c w) -> p c w", c=4), -1.0, None, OP.mult), waits=[mmw], sig="dv")
            tw2 = S.emit("vector", lambda e, nw_hi=nw_hi: e.tensor_scalar(
                nw_hi, PW[3][:, :].rearrange("p (c w) -> p c w", c=4), -1.0, None, OP.mult), waits=[mmw], sig="dv")
            tp["ps2"] = tw1
            tp["ps3"] = tw2
            if DEBUG:
                r0 = mb + tt * 128
                tp["dbgu"] = S.dma("mg", dbg["dbg_u"][r0:r0 + 128, :], u_b[:, :], waits=[tu1, tu2])
                for c in range(8):
                    tp["dbgu"] = S.dma("mg", dbg["dbg_negw"][c * 128:(c + 1) * 128, r0:r0 + 128],
                                       fmsl(negw_fm, c, tt), waits=[tw1, tw2, tp.get("dbgu")])
            mmt = None
            for h in range(H):
                for half in range(2):
                    c = 2 * h + half
                    mmt = S.emit("tensor", lambda e, h=h, c=c, tt=tt, half=half: e.matmul(
                        PW[0][:, h * 128:(h + 1) * 128], fmsl(kh_fm, c, tt), fmsl(qh_fm, c, tt),
                        start=(h == 0 and half == 0), stop=(h == 3 and half == 1)),
                        waits=[tu1, tok_scaled[("q", c)]], sig="dt")
            tMt = S.emit("vector", lambda e: e.tensor_tensor(
                Mt_b[:, :].rearrange("p (a w) -> p a w", a=4), PW[0][:, :].rearrange("p (a w) -> p a w", a=4),
                C["triuI"][:].rearrange("p (x w) -> p x w", x=1).broadcast_to([128, 4, 128]), OP.mult), waits=[mmt], sig="dv")
            tp["ps0"] = tMt
            mm1 = None
            for h in range(H):
                for half in range(2):
                    c = 2 * h + half
                    mm1 = S.emit("tensor", lambda e, h=h, c=c, tt=tt, half=half: e.matmul(
                        PW[1][:, h * 128:(h + 1) * 128], fmsl(negw_fm, c, tt), fmsl(kh_fm, c, tt),
                        start=(h == 0 and half == 0), stop=(h == 3 and half == 1)),
                        waits=[tw1, tw2, tu2], sig="dt")
            tGnw = S.emit("vector", lambda e: e.tensor_tensor(
                Gnw_b[:, :].rearrange("p (a w) -> p a w", a=4), PW[1][:, :].rearrange("p (a w) -> p a w", a=4),
                C["blkLow"][:].rearrange("p (x w) -> p x w", x=1).broadcast_to([128, 4, 128]), OP.mult), waits=[mm1], sig="dv")
            tp["ps1"] = tGnw
            mm2 = None
            for h in range(H):
                for half in range(2):
                    c = 2 * h + half
                    mm2 = S.emit("tensor", lambda e, h=h, c=c, tt=tt, half=half: e.matmul(
                        PW[2][:, h * 128:(h + 1) * 128], fmsl(kh_fm, c, tt), fmsl(negw_fm, c, tt),
                        start=(h == 0 and half == 0), stop=(h == 3 and half == 1)),
                        waits=[tw1, tw2, tp.get("ps2")], sig="dt")
            tGnwT = S.emit("vector", lambda e: e.tensor_tensor(
                GnwT_b[:, :].rearrange("p (a w) -> p a w", a=4), PW[2][:, :].rearrange("p (a w) -> p a w", a=4),
                C["blkUp"][:].rearrange("p (x w) -> p x w", x=1).broadcast_to([128, 4, 128]), OP.mult), waits=[mm2], sig="dv")
            tp["ps2"] = tGnwT
            mmt2 = None
            for h in range(H):
                mmt2 = S.emit("tensor", lambda e, h=h: e.matmul(
                    PW[3][:, h * 128:(h + 1) * 128], Gnw_b[:, h * 128:(h + 1) * 128],
                    GnwT_b[:, h * 128:(h + 1) * 128], start=(h == 0), stop=(h == 3)),
                    waits=[tGnw, tGnwT, tp.get("ps3")], sig="dt")
            tT2 = S.emit("vector", lambda e: e.tensor_copy(T2t_b[:, :], PW[3][:, :]), waits=[mmt2], sig="dv")
            tp["ps3"] = tT2
            mmh = None
            for h in range(H):
                hs = slice(h * 128, (h + 1) * 128)
                mmh = S.emit("tensor", lambda e, hs=hs: e.matmul(PW[0][:, hs], C["eye128"][:, :], C["eye128"][:, :],
                                                                start=True, stop=False),
                             waits=[tT2, tMt], sig="dt")
                S.emit("tensor", lambda e, hs=hs: e.matmul(PW[0][:, hs], C["eye128"][:, :], GnwT_b[:, hs],
                                                           start=False, stop=False), sig="dt")
                S.emit("tensor", lambda e, hs=hs: e.matmul(PW[0][:, hs], C["eye128"][:, :], T2t_b[:, hs],
                                                           start=False, stop=False), sig="dt")
                mmh = S.emit("tensor", lambda e, hs=hs: e.matmul(PW[0][:, hs], Gnw_b[:, hs], T2t_b[:, hs],
                                                                start=False, stop=True), sig="dt")
            tHt = S.emit("vector", lambda e: e.tensor_copy(Ht_b[:, :], PW[0][:, :]), waits=[mmh], sig="dv")
            tp["ps0"] = tHt
            mmu0 = None
            for h in range(H):
                pwb = PW[1 + h // 2]
                of = (h % 2) * 256
                mmu0 = S.emit("tensor", lambda e, h=h, pwb=pwb, of=of, fsc=fsc: e.matmul(
                    pwb[:, of:of + 256], C["eye128"][:, :], u_b[:, h * 256:(h + 1) * 256],
                    start=True, stop=fsc), waits=[tHt, tu1, tu2, tp.get("ps1"), tp.get("ps2")], sig="dt")
                if not fsc:
                    for half in range(2):
                        c = 2 * h + half
                        mmu0 = S.emit("tensor", lambda e, h=h, c=c, tt=tt, pwb=pwb, of=of, half=half: e.matmul(
                            pwb[:, of:of + 256], fmsl(negw_fm, c, tt),
                            S_bf[:, h * 512 + half * 256: h * 512 + half * 256 + 256],
                            start=False, stop=(half == 1)), sig="dt")
            tU1 = S.emit("vector", lambda e: e.tensor_copy(U0_b[:, 0:512], PW[1][:, :]), waits=[mmu0], sig="dv")
            tU2 = S.emit("vector", lambda e: e.tensor_copy(U0_b[:, 512:1024], PW[2][:, :]), waits=[mmu0], sig="dv")
            tp["ps1"] = tU1
            tp["ps2"] = tU2
            mmua = None
            for h in range(H):
                pwb = PW[1 + h // 2]
                of = (h % 2) * 256
                mmua = S.emit("tensor", lambda e, h=h, pwb=pwb, of=of: e.matmul(
                    pwb[:, of:of + 256], Ht_b[:, h * 128:(h + 1) * 128], U0_b[:, h * 256:(h + 1) * 256],
                    start=True, stop=True), waits=[tU1, tU2], sig="dt")
            tua1 = S.emit("vector", lambda e: e.tensor_copy(uadj_b[:, 0:512], PW[1][:, :]), waits=[mmua], sig="dv")
            tua2 = S.emit("vector", lambda e: e.tensor_copy(uadj_b[:, 512:1024], PW[2][:, :]), waits=[mmua], sig="dv")
            tp["ps1"] = tua1
            tp["ps2"] = tua2
            mmo = None
            for h in range(H):
                pwb = PW[1 + h // 2]
                of = (h % 2) * 256
                mmo = S.emit("tensor", lambda e, h=h, pwb=pwb, of=of, fsc=fsc: e.matmul(
                    pwb[:, of:of + 256], Mt_b[:, h * 128:(h + 1) * 128], uadj_b[:, h * 256:(h + 1) * 256],
                    start=True, stop=fsc), waits=[tua1, tua2], sig="dt")
                if not fsc:
                    for half in range(2):
                        c = 2 * h + half
                        mmo = S.emit("tensor", lambda e, h=h, c=c, tt=tt, pwb=pwb, of=of, half=half: e.matmul(
                            pwb[:, of:of + 256], fmsl(qh_fm, c, tt),
                            S_bf[:, h * 512 + half * 256: h * 512 + half * 256 + 256],
                            start=False, stop=(half == 1)), sig="dt")
            tod1 = S.emit("vector", lambda e, tt=tt: e.tensor_copy(delta_tm[:, tt * 1024:tt * 1024 + 512], PW[1][:, :]),
                          waits=[mmo], sig="dv")
            tod2 = S.emit("vector", lambda e, tt=tt: e.tensor_copy(delta_tm[:, tt * 1024 + 512:(tt + 1) * 1024],
                                                                   PW[2][:, :]), waits=[mmo], sig="dv")
            tp["ps1"] = tod1
            tp["ps2"] = tod2
            mms = None
            for h in range(H):
                for half in range(2):
                    mms = S.emit("tensor", lambda e, h=h, half=half, tt=tt: e.matmul(
                        PS_S[h][:, half * 256: half * 256 + 256],
                        khT[:, tt * 1024 + h * 256 + half * 128: tt * 1024 + h * 256 + half * 128 + 128],
                        uadj_b[:, h * 256:(h + 1) * 256], start=(half == 0), stop=(half == 1)),
                        waits=[tua1, tua2, tok_tmprod[("khT", tt)], tp.get("scast")], sig="dt")
            tsc = None
            for h in range(H):
                if fsc:
                    ta = S.emit("vector", lambda e, h=h: e.tensor_copy(S32[:, h * 512:(h + 1) * 512], PS_S[h][:, :]),
                                waits=[mms], sig="dv")
                else:
                    ta = S.emit("vector", lambda e, h=h: e.tensor_tensor(S32[:, h * 512:(h + 1) * 512],
                                                                         S32[:, h * 512:(h + 1) * 512],
                                                                         PS_S[h][:, :], OP.add),
                                waits=[mms], sig="dv")
                tsc = S.emit("vector", lambda e, h=h: e.tensor_copy(S_bf[:, h * 512:(h + 1) * 512],
                                                                    S32[:, h * 512:(h + 1) * 512]),
                             waits=[ta], sig="dv")
            tp["scast"] = tsc
            if DEBUG:
                r0 = mb + tt * 128
                tp["dbgu"] = S.dma("mg", dbg["dbg_delta"][r0:r0 + 128, :], delta_tm[:, tt * 1024:(tt + 1) * 1024],
                                   waits=[tod1, tod2, tp.get("dbgu")])
        AB.free("B_b", "BT_b", "pw_b", "pwT_b", "P_b", "invT_b", "Mt_b", "Gnw_b", "GnwT_b",
                "T2t_b", "Ht_b", "u_b", "U0_b", "uadj_b", "negw_fm", "kb_fm", "vbT", "kbT", "khT")
        S.barrier()

        if PH < 8: continue
        sh_tm = AB.alloc("sh_tm", NT * 1024)
        lo_tm = AB.alloc("lo_tm", NT * 1024)
        wd_tm = AB.alloc("wd_tm", NT * 1024)
        Rb = AB.alloc("Rb", 2 * 4 * RW)
        fdb = AB.alloc("fdb", 2 * 4 * NGF * 32)
        fir_fm = AB.alloc("fir_fm", 3 * W)
        sq_b = AB.alloc("sq_b", W)
        filt_last = [NG_S - 1, NG_S + NG_L - 1, NGF - 1]
        tok_fir_ev = {}
        vdview = vdram.rearrange("(c s i) t -> c i s t", c=8, s=4)
        mmsum = None
        first_summ = True
        for c in range(8):
            half = c % 2
            tRl = []
            for g in range(4):
                colb = mb + 61 + g
                tRl.append(S.dma("mh", Rb[32 * g:32 * g + 32, half * 4 * RW:(half + 1) * 4 * RW].rearrange(
                    "p (s t) -> p s t", s=4), vdview[c, :, :, colb:colb + RW],
                    waits=[tok_vdr_w, tok_fir_ev.get(c - 2)]))
            tfd = S.dma("mg", fdb[:, half * NGF * 128:(half + 1) * NGF * 128].rearrange(
                "p (s g w) -> p s g w", s=4, g=NGF), dr["fird"][c].rearrange("s g p w -> p s g w"),
                waits=[tok_fir_ev.get(c - 2)])
            mmf = {}
            for fi in range(3):
                mmf[fi] = S.emit("tensor", lambda e, fi=fi, half=half: e.matmul(
                    PW[fi][:, 0:512], C["zeros128"][:, :], Rb[:, half * 4 * RW: half * 4 * RW + 512],
                    start=True, stop=False), waits=tRl + [tfd, tp.get(("fps", fi))], sig="dt")
            for gi, (fi, Gf, tau) in enumerate(groups):
                for s4 in range(4):
                    mmf[fi] = S.emit("tensor", lambda e, fi=fi, gi=gi, s4=s4, tau=tau, half=half: e.matmul(
                        PW[fi][32 * s4:32 * s4 + 32, 0:512],
                        fdb[:, half * NGF * 128 + (s4 * NGF + gi) * 32: half * NGF * 128 + (s4 * NGF + gi) * 32 + 32],
                        Rb[:, half * 4 * RW + s4 * RW + tau: half * 4 * RW + s4 * RW + tau + 512],
                        start=False, stop=False, tile_position=(0, 32 * s4)),
                        waits=tRl + [tfd], sig="dt")
            for fi in range(3):
                mmf[fi] = S.emit("tensor", lambda e, fi=fi, half=half: e.matmul(
                    PW[fi][:, 0:512], C["zeros128"][:, :], Rb[:, half * 4 * RW: half * 4 * RW + 512],
                    start=False, stop=True), sig="dt")
            evs = []
            for fi in range(3):
                evs.append(S.emit("vector", lambda e, fi=fi: e.tensor_copy(fir_fm[:, fi * W:(fi + 1) * W], PW[fi][:, :]),
                                  waits=[mmf[fi], tp.get(("fxw", fi))], sig="dv"))
                tp[("fps", fi)] = evs[-1]
            tok_fir_ev[c] = evs[-1]
            if first_summ:
                mmsum = S.emit("tensor", lambda e: e.matmul(
                    PW[3][:, 0:512], C["zeros128"][:, :], vf_fm[:, 0:W],
                    start=True, stop=False), waits=[tp.get("sumps")] + [evs[0]], sig="dt")
                first_summ = False
            for fi in range(3):
                mmsum = S.emit("tensor", lambda e, c=c, fi=fi: e.matmul(
                    PW[3][0:16, 0:512], C["hc4"][:, (c * 4 + fi) * 16:(c * 4 + fi + 1) * 16],
                    fir_fm[:, fi * W:(fi + 1) * W],
                    start=False, stop=False), waits=[evs[fi], tp.get("sumps")], sig="dt")
            mmsum = S.emit("tensor", lambda e, c=c: e.matmul(
                PW[3][0:16, 0:512], C["hc4"][:, (c * 4 + 3) * 16:(c * 4 + 4) * 16], vf_fm[:, c * W:(c + 1) * W],
                start=False, stop=False), sig="dt")
            for fi in range(3):
                tsq = S.emit("vector", lambda e, fi=fi: e.tensor_tensor(
                    sq_b[:, :], fir_fm[:, fi * W:(fi + 1) * W], fir_fm[:, fi * W:(fi + 1) * W], OP.mult),
                    waits=[evs[fi], mmsum], sig="dv")
                mmsum = S.emit("tensor", lambda e, c=c, fi=fi: e.matmul(
                    PW[3][32:48, 0:512], C["hc4"][:, (c * 4 + fi) * 16:(c * 4 + fi + 1) * 16], sq_b[:, :],
                    start=False, stop=False), waits=[tsq], sig="dt")
            tsqv = S.emit("vector", lambda e, c=c: e.tensor_tensor(
                sq_b[:, :], vf_fm[:, c * W:(c + 1) * W], vf_fm[:, c * W:(c + 1) * W], OP.mult),
                waits=[mmsum], sig="dv")
            mmsum = S.emit("tensor", lambda e, c=c: e.matmul(
                PW[3][32:48, 0:512], C["hc4"][:, (c * 4 + 3) * 16:(c * 4 + 4) * 16], sq_b[:, :],
                start=False, stop=False), waits=[tsqv], sig="dt")
            if c == 7:
                mmsum = S.emit("tensor", lambda e: e.matmul(
                    PW[3][:, 0:512], C["zeros128"][:, :], vf_fm[:, 0:W],
                    start=False, stop=True), sig="dt")
            tp["sumps"] = mmsum
            for fi, dst in ((0, sh_tm), (1, lo_tm), (2, wd_tm)):
                tp[("fxw", fi)] = [S.dma("ms", dst[:, tt * 1024 + c * 128: tt * 1024 + (c + 1) * 128],
                                         fir_fm[:, fi * W + tt * 128: fi * W + tt * 128 + 128],
                                         waits=[evs[fi]], transpose=True) for tt in range(NT)]
        tok_fxall = S.qtot("ms") + S.qtot("mh")
        if DEBUG:
            for nm, dst in (("dbg_short", sh_tm), ("dbg_long", lo_tm), ("dbg_wide", wd_tm)):
                for tt in range(NT):
                    S.dma("mg", dbg[nm][mb + tt * 128: mb + tt * 128 + 128, :], dst[:, tt * 1024:(tt + 1) * 1024],
                          waits=[tok_fxall])
        AB.free("Rb", "fdb", "fir_fm", "sq_b")

        if PH < 9: continue
        stat64 = AFp.alloc("stat64", 64)
        sums16 = AFp.alloc("sums16", 512)
        trsp = AFp.alloc("trsp", 128)
        dsq = AFp.alloc("dsq", 1024)
        stt = S.emit("vector", lambda e: e.tensor_copy(sums16[:, :], PW[3][:, 0:512]), waits=[mmsum], sig="dv")
        if DEBUG:
            S.dma("mg", dbg["dbg_sums"][:, mb:mb + W], sums16[:, :], waits=[stt])
        tp["ps3"] = stt
        tok_stats = []
        for tt in range(NT):
            toks = []
            tp1 = S.emit("tensor", lambda e, tt=tt: e.transpose(PW[2][:, 0:128], sums16[:, tt * 128:(tt + 1) * 128],
                                                                C["ident32"][:, :]),
                         waits=[stt, tp.get("ps2")], sig="dt")
            ev1 = S.emit("vector", lambda e: e.tensor_copy(trsp[:, :], PW[2][:, 0:128]), waits=[tp1], sig="dv")
            tp["ps2"] = ev1
            for ti in range(4):
                off = [0, 12, 24, 48][ti]
                toks.append(S.emit("vector", lambda e, ti=ti, off=off: e.tensor_scalar(
                    stat64[:, off:off + 4], trsp[:, ti * 4:(ti + 1) * 4], 1.0 / 256.0, None, OP.mult),
                    waits=[ev1, tp.get("dbgstat")], sig="dv"))
                toks.append(S.emit("vector", lambda e, ti=ti, off=off: e.tensor_scalar(
                    stat64[:, off + 4:off + 8], trsp[:, 32 + ti * 4: 32 + (ti + 1) * 4], 1.0 / 256.0, 1e-8,
                    OP.mult, OP.max), waits=[ev1], sig="dv"))
            dslice = delta_tm[:, tt * 1024:(tt + 1) * 1024]
            td1 = S.emit("vector", lambda e, dslice=dslice: e.tensor_reduce(
                stat64[:, 60:64], dslice.rearrange("p (h c) -> p h c", h=4), axis=AX, op=OP.add),
                waits=[tp.get("dstat")], sig="dv")
            td2 = S.emit("vector", lambda e: e.tensor_scalar(stat64[:, 36:40], stat64[:, 60:64],
                                                             1.0 / 256.0, None, OP.mult), waits=[td1], sig="dv")
            td3 = S.emit("vector", lambda e, dslice=dslice: e.tensor_tensor(dsq[:, :], dslice, dslice, OP.mult),
                         waits=[td2], sig="dv")
            td4 = S.emit("vector", lambda e: e.tensor_reduce(
                stat64[:, 60:64], dsq[:, :].rearrange("p (h c) -> p h c", h=4), axis=AX, op=OP.add),
                waits=[td3], sig="dv")
            td5 = S.emit("vector", lambda e: e.tensor_scalar(stat64[:, 40:44], stat64[:, 60:64],
                                                             1.0 / 256.0, 1e-8, OP.mult, OP.max), waits=[td4], sig="dv")
            tp["dstat"] = td5
            toks.extend([td1, td2, td3, td4, td5])
            for tile, off in ((sh_tm, 0), (lo_tm, 12), (wd_tm, 24), (delta_tm, 36), (v_tm, 48)):
                toks.append(S.emit("vector", lambda e, tile=tile, off=off, tt=tt: e.tensor_reduce(
                    stat64[:, off + 8:off + 12], tile[:, tt * 1024:(tt + 1) * 1024].rearrange("p (h c) -> p h c", h=4),
                    axis=AX, op=OP.max, apply_absolute_value=True), waits=[tok_fxall], sig="dv"))
            sqs = []
            for coff in (4, 16, 28, 40, 52):
                sqs.append(S.emit("scalar", lambda e, coff=coff: e.activation(
                    stat64[:, coff:coff + 4], stat64[:, coff:coff + 4], AF.Sqrt), waits=toks, sig="dc"))
            tp2 = S.emit("tensor", lambda e: e.transpose(PW[2][0:64, 0:128], stat64[:, :], C["ident32"][:, :]),
                         waits=toks + sqs + [tp.get("ps2")], sig="dt")
            tst = S.emit("vector", lambda e, tt=tt: e.tensor_copy(statF[0:64, tt * 128:(tt + 1) * 128],
                                                                  PW[2][0:64, 0:128]), waits=[tp2], sig="dv")
            tp["ps2"] = tst
            tok_stats.append(tst)
            if DEBUG:
                tp["dbgstat"] = S.dma("mg", dbg["dbg_stat"][mb + tt * 128: mb + tt * 128 + 128, :], stat64[:, :],
                                      waits=toks + sqs + [tst])
        AFp.free("stat64", "sums16", "trsp", "dsq")

        if PH < 10: continue
        g1buf = AB.alloc("g1buf", 2 * 9 * 128)
        hdn_b = AB.alloc("hdn_b", 2 * W)
        logit_sp = AFp.alloc("logit_sp", W)
        tok_hdn = {}
        mml = None
        for dcg in range(16):
            buf = g1buf[:, (dcg % 2) * 9 * 128:((dcg % 2) + 1) * 9 * 128]
            twl = S.dma("mg", buf.rearrange("p (c w) -> p c w", c=9),
                        dr["g1Tp"].rearrange("(c p) d -> p c d", p=128)[:, :, dcg * 128:(dcg + 1) * 128],
                        waits=[tok_hdn.get(dcg - 2)]
                        + ([tp.get(("fps", 0)), tp.get(("fps", 1)), tp.get(("fps", 2)), tp.get("sumps")]
                           if dcg < 2 else []))
            mmt = None
            for kc in range(9):
                rhs = xn_fm[:, kc * W:(kc + 1) * W] if kc < 8 else statF[:, 0:512]
                mmt = S.emit("tensor", lambda e, kc=kc, buf=buf, rhs=rhs, dcg=dcg: e.matmul(
                    PW[dcg % 2][:, 0:512], buf[:, kc * 128:(kc + 1) * 128], rhs,
                    start=(kc == 0), stop=(kc == 8)),
                    waits=[twl, tok_stats[-1], tp.get(("g1ps", dcg % 2))], sig="dt")
            ghl = S.emit("scalar", lambda e, dcg=dcg: e.activation(
                hdn_b[:, (dcg % 2) * W:((dcg % 2) + 1) * W], PW[dcg % 2][:, 0:512], AF.Gelu,
                bias=C["g1b"][:, dcg:dcg + 1]), waits=[mmt], sig="dc")
            tok_hdn[dcg] = ghl
            tp[("g1ps", dcg % 2)] = ghl
            mml = S.emit("tensor", lambda e, dcg=dcg: e.matmul(
                PW[2][0:20, 0:512], C["g2T"][:, dcg * 20:(dcg + 1) * 20],
                hdn_b[:, (dcg % 2) * W:((dcg % 2) + 1) * W],
                start=(dcg == 0), stop=(dcg == 15)), waits=[ghl, tp.get("ps2")], sig="dt")
        tok_lg = S.emit("vector", lambda e: e.tensor_copy(logit_sp[0:20, :], PW[2][0:20, 0:512]), waits=[mml], sig="dv")
        tp["ps2"] = tok_lg
        tp["xnfree"] = tok_lg
        AB.free("g1buf", "hdn_b")

        if PH < 11: continue
        o_tm = AB.alloc("o_tm", NT * 1024)
        lg_t = AFp.alloc("lg_t", 128)
        o32 = AFp.alloc("o32", 1024)
        tmpc = AFp.alloc("tmpc", 1024)
        tpxl = []
        for tt in range(NT):
            tpxl.append(S.emit("tensor", lambda e, tt=tt: e.transpose(
                PW[3][:, tt * 128:(tt + 1) * 128], logit_sp[:, tt * 128:(tt + 1) * 128], C["ident32"][:, :]),
                waits=([tok_lg, tp.get("ps3")] if tt == 0 else []), sig="dt"))
        lgv = lg_t[:, 0:80].rearrange("p (t w) -> p t w", t=4)
        lgi = lg_t[:, 0:80].rearrange("p (a i) -> p a i", i=5)
        t1 = S.emit("vector", lambda e: e.tensor_copy(
            lg_t[:, 0:80].rearrange("p (t w) -> p t w", t=4),
            PW[3][:, :].rearrange("p (t w) -> p t w", w=128)[:, :, 0:20]),
            waits=tpxl + [tp.get("dbglg")], sig="dv")
        tp["ps3"] = t1

        def gcv(a):
            return gconst[:, a:a + 20].rearrange("p (x w) -> p x w", x=1).broadcast_to([128, 4, 20])
        t2 = S.emit("vector", lambda e: e.tensor_tensor(lgv, lgv, gcv(0), OP.add), waits=[t1], sig="dv")
        t3 = S.emit("vector", lambda e: e.tensor_tensor(lgv, lgv, gcv(20), OP.mult), waits=[t2], sig="dv")
        t4 = S.emit("vector", lambda e: e.tensor_reduce(lg_t[:, 80:96], lgi, axis=AX, op=OP.max),
                    waits=[t3], sig="dv")
        mxv = lg_t[:, 80:96].broadcast_to([128, 16, 5])
        t5 = S.emit("vector", lambda e: e.tensor_tensor(lgi, lgi, mxv, OP.subtract), waits=[t4], sig="dv")
        t6 = S.emit("scalar", lambda e: e.activation(lg_t[:, 0:80], lg_t[:, 0:80], AF.Exp), waits=[t5], sig="dc")
        t7 = S.emit("vector", lambda e: e.tensor_reduce(lg_t[:, 80:96], lgi, axis=AX, op=OP.add),
                    waits=[t6], sig="dv")
        t8 = S.emit("vector", lambda e: e.reciprocal(lg_t[:, 96:112], lg_t[:, 80:96]), waits=[t7], sig="dv")
        rv = lg_t[:, 96:112].broadcast_to([128, 16, 5])
        t9 = S.emit("vector", lambda e: e.tensor_tensor(lgi, lgi, rv, OP.mult), waits=[t8], sig="dv")
        t10 = S.emit("vector", lambda e: e.tensor_tensor(lgv, lgv, gcv(40), OP.mult), waits=[t9], sig="dv")
        t11 = S.emit("vector", lambda e: e.tensor_tensor(lgv, lgv, gcv(60), OP.add), waits=[t10], sig="dv")
        if DEBUG:
            tp["dbglg"] = S.dma_burst("mg", [(dbg["dbg_logits"][mb + t_ * 128:mb + t_ * 128 + 128, 0:20],
                                              lg_t[:, 20 * t_:20 * t_ + 20], {}) for t_ in range(NT)],
                                      waits=[t11])
        for tt in range(NT):
            tensors = [(sh_tm, 0), (lo_tm, 1), (wd_tm, 2), (delta_tm, 3), (v_tm, 4)]
            tsl = slice(tt * 1024, (tt + 1) * 1024)
            tokc = None
            for ti, (tile, i0) in enumerate(tensors):
                wv_ = lg_t[:, 20 * tt + i0: 20 * tt + i0 + 20].rearrange("p (h i) -> p h i", i=5)[:, :, 0:1].broadcast_to([128, 4, 256])
                if ti == 0:
                    tokc = S.emit("vector", lambda e, tile=tile, wv_=wv_, tsl=tsl: e.tensor_tensor(
                        o32[:, :].rearrange("p (h c) -> p h c", h=4),
                        tile[:, tsl].rearrange("p (h c) -> p h c", h=4), wv_, OP.mult),
                        waits=[t11, tok_fxall, tp.get("o32w")], sig="dv")
                else:
                    tokc = S.emit("vector", lambda e, tile=tile, wv_=wv_, tsl=tsl: e.tensor_tensor(
                        tmpc[:, :].rearrange("p (h c) -> p h c", h=4),
                        tile[:, tsl].rearrange("p (h c) -> p h c", h=4), wv_, OP.mult),
                        waits=[tokc], sig="dv")
                    tokc = S.emit("vector", lambda e: e.tensor_tensor(o32[:, :], o32[:, :], tmpc[:, :], OP.add),
                                  waits=[tokc], sig="dv")
            t12 = S.emit("vector", lambda e: e.tensor_tensor(tmpc[:, :], o32[:, :], o32[:, :], OP.mult),
                         waits=[tokc], sig="dv")
            t13 = S.emit("vector", lambda e: e.tensor_reduce(lg_t[:, 112:116],
                                                             tmpc[:, :].rearrange("p (h c) -> p h c", h=4),
                                                             axis=AX, op=OP.add), waits=[t12], sig="dv")
            t14 = S.emit("vector", lambda e: e.tensor_scalar(lg_t[:, 112:116], lg_t[:, 112:116], 1.0 / 256, 1e-5,
                                                             OP.mult, OP.add), waits=[t13], sig="dv")
            t15 = S.emit("vector", lambda e: e.reciprocal(lg_t[:, 116:120], lg_t[:, 112:116]), waits=[t14], sig="dv")
            t16 = S.emit("scalar", lambda e: e.activation(lg_t[:, 116:120], lg_t[:, 116:120], AF.Sqrt), waits=[t15], sig="dc")
            rsv = lg_t[:, 116:120].broadcast_to([128, 4, 256])
            t17 = S.emit("vector", lambda e: e.tensor_tensor(o32[:, :].rearrange("p (h c) -> p h c", h=4),
                                                             o32[:, :].rearrange("p (h c) -> p h c", h=4),
                                                             rsv, OP.mult), waits=[t16], sig="dv")
            t18 = S.emit("vector", lambda e, tsl=tsl: e.tensor_tensor(o_tm[:, tsl], o32[:, :], C["onorm_rep"][:, :],
                                                                      OP.mult), waits=[t17], sig="dv")
            tp[("otm", tt)] = t18
            tp["o32w"] = t18
            if DEBUG:
                S.dma("mg", dbg["dbg_otm"][mb + tt * 128:mb + tt * 128 + 128, :], o_tm[:, tsl], waits=[t18])
        AFp.free("lg_t", "o32", "tmpc", "logit_sp")

        if PH < 12: continue
        o_fm = AB.alloc("o_fm", 8 * W)
        owb = AB.alloc("owb", 8 * 1024)
        AB.free("sh_tm", "lo_tm", "wd_tm", "delta_tm", "v_tm")
        s_t = AFp.alloc("s_t", 1024)
        towl = S.dma("mg", owb[:].rearrange("p (c w) -> p c w", c=8),
                     dr["owT"].rearrange("(c p) d -> p c d", p=128), waits=[tp[("otm", 0)]])
        tof = {}
        for tt in range(NT):
            for c in range(8):
                tof[(tt, c)] = S.dma("ms", o_fm[:, c * W + tt * 128: c * W + tt * 128 + 128],
                                     o_tm[:, tt * 1024 + c * 128: tt * 1024 + (c + 1) * 128],
                                     waits=[tp[("otm", tt)]], transpose=True)
        for tt in range(NT):
            mms = None
            for kc in range(8):
                for nn in range(2):
                    mms = S.emit("tensor", lambda e, kc=kc, nn=nn, tt=tt: e.matmul(
                        PW[nn][:, 0:512], o_fm[:, kc * W + tt * 128: kc * W + tt * 128 + 128],
                        owb[:, kc * 1024 + nn * 512: kc * 1024 + (nn + 1) * 512],
                        start=(kc == 0), stop=(kc == 7)),
                        waits=[towl, tof[(tt, kc)], tp.get("ps0"), tp.get("ps1")]
                        + ([tp.get(("g1ps", 0)), tp.get(("g1ps", 1))] if tt == 0 else []), sig="dt")
            ev1 = S.emit("vector", lambda e: e.tensor_copy(s_t[:, 0:512], PW[0][:, :]),
                         waits=[mms, tp.get("sdma"), tp.get("o32w")], sig="dv")
            ev2 = S.emit("vector", lambda e: e.tensor_copy(s_t[:, 512:1024], PW[1][:, :]),
                         waits=[mms, tp.get("o32w")], sig="dv")
            tp["ps0"] = ev1
            tp["ps1"] = ev2
            tp["sdma"] = S.dma("mg", sdram[mb + tt * 128: mb + tt * 128 + 128, :], s_t[:, :], waits=[ev1, ev2])
            tp["mttail"] = mms
        tp[("projps", 0)] = tp["ps0"]
        tp[("projps", 1)] = tp["ps1"]
        AB.free("o_fm", "owb", "o_tm", "xq_tm")
        AFp.free("s_t")

    S.barrier()
    AB.free("xn_fm", "qh_fm", "kh_fm", "vf_fm", "S_bf", "tails", "statF", "cd", "g2T", "hc4")
    AFp.free("S32", "onorm_rep", "g1b", "gconst", "ident32", "smallrows")

    if PH < 13:
        return _finish(nc, S, ctx)
    wgq = AB.alloc("wgq", 8 * IFF)
    wuq = AB.alloc("wuq", 8 * IFF)
    wdq = AB.alloc("wdq", 22 * 1024)
    wld = AFp.alloc("wld", 2 * 1024)
    wq_b = AB.alloc("wq_b", 2 * 1024)
    prev_q = {}
    prev_x = {}
    i = 0
    for wnm, dst in (("wg", wgq), ("wu", wuq)):
        for ic2 in range(22):
            p = i % 2
            wldp = wld[:, p * 1024:(p + 1) * 1024]
            wqbp = wq_b[:, p * 1024:(p + 1) * 1024]
            tl = S.dma("mg", wldp, dr[wnm][ic2 * 128:(ic2 + 1) * 128, :], waits=[prev_q.get(p)])
            tqq, _ = emit_quant(wldp, wqbp, 1024, tl, "q2", dst_war=prev_x.get(p))
            prev_q[p] = tqq
            prev_x[p] = [S.dma("ms" if p == 0 else "mh",
                               dst[:, kc * IFF + ic2 * 128: kc * IFF + ic2 * 128 + 128],
                               wqbp[:, kc * 128:(kc + 1) * 128], waits=[tqq], transpose=True)
                         for kc in range(8)]
            i += 1
    AFp.free("wld")
    AB.free("wq_b")
    wld2 = AFp.alloc("wld2", 2 * (IFF // 2))
    wq_b2 = AB.alloc("wq_b2", 2 * IFF)
    prev_q2 = {0: [prev_q[0], prev_q[1]], 1: [prev_q[0], prev_q[1]]}
    prev_x2 = {0: prev_x[0] + prev_x[1], 1: prev_x[0] + prev_x[1]}
    for c8 in range(8):
        p = c8 % 2
        wldp = wld2[:, p * 1408:(p + 1) * 1408]
        wqbp = wq_b2[:, p * IFF:(p + 1) * IFF]
        tqh = []
        for hf in range(2):
            tl = S.dma("mg", wldp, dr["wdn"][c8 * 128:(c8 + 1) * 128, hf * 1408:(hf + 1) * 1408],
                       waits=[prev_q2.get(p)] + (tqh if hf else []))
            tqq, _ = emit_quant(wldp, wqbp[:, hf * 1408:(hf + 1) * 1408], IFF // 2, tl, "q3",
                                dst_war=prev_x2.get(p))
            tqh.append(tqq)
        prev_q2[p] = tqh
        prev_x2[p] = [S.dma("ms" if p == 0 else "mh",
                            wdq[:, icc * 1024 + c8 * 128: icc * 1024 + c8 * 128 + 128],
                            wqbp[:, icc * 128:(icc + 1) * 128], waits=tqh, transpose=True)
                      for icc in range(22)]
    AFp.free("wld2")
    AB.free("wq_b2")
    S.barrier()

    xq_t2 = AB.alloc("xq_t", 3 * 1024)
    hq_t2 = AB.alloc("hq_t", 5 * 512)
    hq_f2 = AB.alloc("hq_f", 5 * 512)
    gup2 = AFp.alloc("gup", 2 * 1024)
    hh2 = AFp.alloc("hh", 2 * 512)
    g_t2 = AFp.alloc("g_t", 2 * 1024)
    xs_t2 = AFp.alloc("xs_t", 2048)
    IC_OF = [0, 512, 1024, 1536, 2048, 2560]
    IC_SZ = [512, 512, 512, 512, 512, 256]
    GU = [(PW[0], PW[1]), (PS_S[0], PS_S[1])]
    DN = [(PW[2], PW[3]), (PS_S[2], PS_S[3])]

    def emit_dn(pend):
        tt, ici, q4, tok_hqf = pend
        pb = tt % 2
        bd0, bd1 = DN[pb]
        hq_f = hq_f2[:, q4 * 512:(q4 + 1) * 512]
        iof, isz = IC_OF[ici], IC_SZ[ici]
        mmd = None
        for j in range(isz // 128):
            icc = iof // 128 + j
            for nn in range(2):
                mmd = S.emit("tensor", lambda e, j=j, icc=icc, nn=nn, hq_f=hq_f, bd=(bd0, bd1)[nn]: e.matmul(
                    bd[:, 0:512], hq_f[:, j * 128:(j + 1) * 128],
                    wdq[:, icc * 1024 + nn * 512: icc * 1024 + (nn + 1) * 512],
                    start=(icc == 0), stop=(icc == 21)),
                    waits=([tok_hqf, tp.get(("dnev", pb))] if (j == 0 and ici == 0) else [tok_hqf]), sig="dt")
        tp[("dnmm", q4)] = mmd
        return mmd

    pend_dn = []
    for tt in range(N_FFN_TILES):
        r0 = tt * 128
        pb = tt % 2
        xq_t = xq_t2[:, (tt % 3) * 1024:(tt % 3 + 1) * 1024]
        g_t = g_t2[:, pb * 1024:(pb + 1) * 1024]
        hh = hh2[:, pb * 512:(pb + 1) * 512]
        xs_t = xs_t2
        wprev = tp.get(("ffn", tt - 2))
        txq = [S.dma("mh", xq_t[:, c * 128:(c + 1) * 128], xqdram[r0:r0 + 128, c * 128:(c + 1) * 128],
                     waits=[tp.get(("ffn", tt - 3))], transpose=True) for c in range(8)]
        for ici in range(6):
            iof, isz = IC_OF[ici], IC_SZ[ici]
            p2 = ici % 2
            q4 = (tt * 6 + ici) % 5
            bg, bu = GU[p2]
            gup = gup2[:, p2 * 1024:(p2 + 1) * 1024]
            hq_t = hq_t2[:, q4 * 512:(q4 + 1) * 512]
            hq_f = hq_f2[:, q4 * 512:(q4 + 1) * 512]
            mmg = None
            for kc in range(8):
                mmg = S.emit("tensor", lambda e, kc=kc, iof=iof, isz=isz, xq_t=xq_t, bg=bg: e.matmul(
                    bg[:, 0:isz], xq_t[:, kc * 128:(kc + 1) * 128], wgq[:, kc * IFF + iof: kc * IFF + iof + isz],
                    start=(kc == 0), stop=(kc == 7)),
                    waits=([txq, tp.get(("guev", p2))] if kc == 0 else []), sig="dt")
                mmg = S.emit("tensor", lambda e, kc=kc, iof=iof, isz=isz, xq_t=xq_t, bu=bu: e.matmul(
                    bu[:, 0:isz], xq_t[:, kc * 128:(kc + 1) * 128], wuq[:, kc * IFF + iof: kc * IFF + iof + isz],
                    start=(kc == 0), stop=(kc == 7)), sig="dt")
            if len(pend_dn) >= 3:
                emit_dn(pend_dn.pop(0))
            tsg = S.emit("scalar", lambda e, isz=isz, gup=gup, bg=bg: e.activation(gup[:, 0:isz], bg[:, 0:isz], AF.Silu),
                         waits=[mmg, tp.get(("guth", p2))], sig="dc")
            tup = S.emit("vector", lambda e, isz=isz, gup=gup, bu=bu: e.tensor_copy(gup[:, 512:512 + isz], bu[:, 0:isz]),
                         waits=[mmg, tp.get(("guth", p2))], sig="dv")
            tp[("guev", p2)] = [tsg, tup]
            th = S.emit("vector", lambda e, isz=isz, gup=gup, hh=hh: e.tensor_tensor(
                hh[:, 0:isz], gup[:, 0:isz], gup[:, 512:512 + isz], OP.mult),
                waits=[tsg, tup, tp.get(("dbgh", pb))], sig="dv")
            tp[("guth", p2)] = th
            thq, _ = emit_quant(hh[:, 0:isz], hq_t[:, 0:isz], isz, th, "q4", fast=True,
                                dst_war=[t for t in [tp.get(("dbghq", q4))] if t] + (tp.get(("hqx", q4)) or []))
            if DEBUG:
                tp[("dbgh", pb)] = S.dma("mg", dbg["dbg_h"][r0:r0 + 128, iof:iof + isz], hh[:, 0:isz],
                                         waits=[th, thq])
                tp[("dbghq", q4)] = S.dma("mg", dbg["dbg_hq"][r0:r0 + 128, iof:iof + isz], hq_t[:, 0:isz],
                                          waits=[thq])
            tok_hqf = [S.dma("ms", hq_f[:, j * 128:(j + 1) * 128], hq_t[:, j * 128:(j + 1) * 128],
                             waits=[thq, tp.get(("dnmm", q4))], transpose=True)
                       for j in range(isz // 128)]
            tp[("hqx", q4)] = tok_hqf
            pend_dn.append((tt, ici, q4, tok_hqf))
        for p_ in pend_dn:
            mmd = emit_dn(p_)
        pend_dn = []
        bd0, bd1 = DN[pb]
        tg1 = S.emit("scalar", lambda e, g_t=g_t, bd0=bd0: e.activation(g_t[:, 0:512], bd0[:, 0:512], AF.Silu),
                     waits=[mmd, wprev], sig="dc")
        tg2 = S.emit("scalar", lambda e, g_t=g_t, bd1=bd1: e.activation(g_t[:, 512:1024], bd1[:, 0:512], AF.Silu),
                     waits=[mmd, wprev], sig="dc")
        tp[("dnev", pb)] = [tg1, tg2]
        txl = S.dma("mg", xs_t[:, 0:1024], dr["x"][r0:r0 + 128, :], waits=[wprev, tp.get("xsfree")])
        tsl = S.dma("mg", xs_t[:, 1024:2048], sdram[r0:r0 + 128, :], waits=[wprev, tp.get("xsfree")])
        to1 = S.emit("vector", lambda e, g_t=g_t, xs_t=xs_t: e.tensor_tensor(
            g_t[:, :], xs_t[:, 1024:2048], g_t[:, :], OP.mult), waits=[tg1, tg2, tsl], sig="dv")
        to2 = S.emit("vector", lambda e, g_t=g_t, xs_t=xs_t: e.tensor_tensor(
            g_t[:, :], xs_t[:, 0:1024], g_t[:, :], OP.add), waits=[to1, txl], sig="dv")
        tp["xsfree"] = to2
        tp[("ffn", tt)] = S.dma("mg", y[r0:r0 + 128, :], g_t[:, :], waits=[to2])
    S.barrier()

    return _finish(nc, S, ctx)


def _finish(nc, S, ctx):
    sems = {}
    for s in ALLSEMS:
        t = nc.semaphore(s)
        h_ = t.__enter__()
        ctx.append(t)
        sems[s] = h_
    blk = nc.Block()
    b = blk.__enter__()
    ctx.append(blk)

    @b.tensor
    def _(eng):
        S.run("tensor", eng, sems)

    @b.vector
    def _(eng):
        S.run("vector", eng, sems)

    @b.scalar
    def _(eng):
        S.run("scalar", eng, sems)

    @b.gpsimd
    def _(eng):
        S.run("gpsimd", eng, sems)

    @b.sync
    def _(eng):
        S.run("sync", eng, sems)

    blk.__exit__(None, None, None)
    ctx.pop()
    return nc, ctx


_CACHE = {}


def kernel(**inputs):
    x_full = np.asarray(inputs["L0_attn_hidden_states"], np.float32)
    host, groups = prep_host(inputs)
    if "nc" not in _CACHE:
        nc, ctx = build(groups)
        _CACHE["nc"] = nc
        _CACHE["ctx"] = ctx
    nc = _CACHE["nc"]
    in_maps = []
    for core in range(8):
        m = dict(host)
        m["x"] = np.ascontiguousarray(x_full[core])
        in_maps.append(m)
    res = run_bass_kernel_spmd(nc, in_maps, core_ids=list(range(8)))
    return np.stack([np.asarray(r["y"], np.float32) for r in res.results], axis=0)


# version 13
# speedup vs baseline: 1.0893x; 1.0008x over previous
import numpy as np
import ml_dtypes
import concourse.bass as bass
import concourse.mybir as mybir
from concourse.bass_utils import run_bass_kernel_spmd
from concourse.alu_op_type import AluOpType as OP

F32 = mybir.dt.float32
BF16 = mybir.dt.bfloat16
I32 = mybir.dt.int32
U8 = mybir.dt.uint8
AF = mybir.ActivationFunctionType
AX = mybir.AxisListType.X

L, D, H = 4096, 1024, 4
TMTOK = 512
NMEG = L // TMTOK
NT = TMTOK // 128
W = TMTOK
WP = W + 64
IFF = 2816
NG_S, NG_L, NG_W = 1, 8, 16
NGF = NG_S + NG_L + NG_W
RW = W + 64

DEBUG = False
SUB56 = 1
N_MEG_BUILD = NMEG
N_FFN_TILES = L // 128
PH = 99

ENGINES = ("tensor", "vector", "scalar", "gpsimd", "sync")
ESEM = {"tensor": "dt", "vector": "dv", "scalar": "dc", "gpsimd": "dg", "sync": "dy"}
NPOOL = 8
NEPOOL = 12
DMAPOOL = {q: [f"{q}{i}" for i in range(NPOOL)] for q in ("ms", "mg", "mh")}
ENGPOOL = {b: [f"{b}{i}" for i in range(NEPOOL)] for b in ESEM.values()}
ALLSEMS = [s for q in DMAPOOL.values() for s in q] + [s for p in ENGPOOL.values() for s in p]
QENG = {"ms": "sync", "mg": "gpsimd", "mh": "scalar"}


class Sch:
    def __init__(self):
        self.ops = {e: [] for e in ENGINES}
        self.cnt = {}

    def qtot(self, q):
        return [(sem, self.cnt[sem]) for sem in DMAPOOL[q] if self.cnt.get(sem, 0) > 0]

    def emit(self, eng, fn, waits=(), sig=None, amt=1):
        w = []
        for t in waits:
            if t is None:
                continue
            if isinstance(t, list):
                w.extend(t)
            else:
                w.append(t)
        tok = None
        if sig is not None:
            if sig in ENGPOOL:
                if not hasattr(self, "erot"):
                    self.erot = {}
                self.erot[sig] = (self.erot.get(sig, -1) + 1) % NEPOOL
                sig = ENGPOOL[sig][self.erot[sig]] if False else f"{sig}{self.erot[sig]}"
            self.cnt[sig] = self.cnt.get(sig, 0) + amt
            tok = (sig, self.cnt[sig])
        self.ops[eng].append((fn, w, sig, amt))
        return tok

    def dma(self, q, dst, src, waits=(), **kw):
        return self.dma_burst(q, [(dst, src, kw)], waits=waits)

    def dma_burst(self, q, items, waits=()):
        if not hasattr(self, "rot"):
            self.rot = {}
        self.rot[q] = (self.rot.get(q, -1) + 1) % NPOOL
        sem = DMAPOOL[q][self.rot[q]]
        eng = QENG[q]
        prev = self.cnt.get(sem, 0)
        w = [t for t in waits if t is not None]
        if prev > 0:
            w = w + [(sem, prev)]
        tok = None
        for i, (dst, src, kw) in enumerate(items):
            tok = self.emit(eng, lambda e, d=dst, s=src, kw=kw: e.dma_start(out=d, in_=s, **kw),
                            waits=(w if i == 0 else ()), sig=sem, amt=16)
        return tok

    def barrier(self):
        toks = {}
        for e in ENGINES:
            toks[e] = self.emit(e, lambda eng: eng.nop(), sig=ESEM[e])
        dt_ = [(sm, self.cnt[sm]) for sm in ALLSEMS if self.cnt.get(sm, 0) > 0]
        for e in ENGINES:
            self.emit(e, None, waits=[toks[o] for o in ENGINES if o != e] + dt_)

    def run(self, eng_name, eng, sems):
        seen = {}
        for fn, wl, sig, amt in self.ops[eng_name]:
            for s, v in wl:
                if seen.get(s, -1) < v:
                    eng.wait_ge(sems[s], v)
                    seen[s] = v
            ins = fn(eng) if fn is not None else None
            if sig is not None:
                ins.then_inc(sems[sig], amt)


class Arena:
    def __init__(self, tile, total, name):
        self.tile, self.total, self.name = tile, total, name
        self.blocks = {}
        self.hw = 0

    def alloc(self, key, size):
        assert key not in self.blocks, key
        taken = sorted(self.blocks.values())
        off = 0
        for (o, s) in taken:
            if off + size <= o:
                break
            off = o + s
        assert off + size <= self.total, f"{self.name} OOM: {key}({size})"
        self.blocks[key] = (off, size)
        self.hw = max(self.hw, off + size)
        return self.tile[:, off:off + size]

    def free(self, *keys):
        for k in keys:
            del self.blocks[k]


def prep_host(inp):
    f32 = np.float32
    bf = ml_dtypes.bfloat16
    out = {}
    out["qwT"] = np.ascontiguousarray(np.asarray(inp["L0_attn_q_w"], f32).T)
    out["kwT"] = np.ascontiguousarray(np.asarray(inp["L0_attn_k_w"], f32).T)
    out["vwT"] = np.ascontiguousarray(np.asarray(inp["L0_attn_v_w"], f32).T)
    bsp = np.zeros((D, 128), f32)
    for h in range(H):
        bsp[:, 32 * h] = np.asarray(inp["L0_attn_b_w"], f32)[h, :]
    out["bwTsp"] = bsp
    cds = np.zeros((3, 8, 4, 128, 32), f32)
    for ti, nm in enumerate(("convq_w", "convk_w", "convv_w")):
        cw = np.asarray(inp["L0_attn_" + nm], f32)
        for c in range(8):
            for g in range(4):
                d_ = cw[c * 128:(c + 1) * 128, g]
                for i4 in range(4):
                    cds[ti, c, g, 32 * i4:32 * i4 + 32, :] = np.diag(d_[32 * i4:32 * i4 + 32])
    out["cd"] = cds
    firs = [np.asarray(inp["L0_attn_fir_s"], f32).reshape(D, -1),
            np.asarray(inp["L0_attn_fir_l"], f32).reshape(D, -1),
            np.asarray(inp["L0_attn_fir_wd"], f32).reshape(D, -1)]
    groups = []
    for fi, filt in enumerate(firs):
        K = filt.shape[1]
        for Gf in range((K + 3) // 4):
            groups.append((fi, Gf, 64 - 4 * Gf))
    assert len(groups) == NGF
    fd = np.zeros((8, 4, NGF, 128, 32), f32)
    for gi, (fi, Gf, _t) in enumerate(groups):
        filt = firs[fi]
        K = filt.shape[1]
        for c in range(8):
            for s4 in range(4):
                for g in range(4):
                    k = K - 1 - 4 * Gf + g - 3
                    if 0 <= k < K:
                        d_ = filt[c * 128 + s4 * 32: c * 128 + s4 * 32 + 32, k]
                        fd[c, s4, gi, 32 * g:32 * g + 32, :] = np.diag(d_)
    out["fird"] = fd
    g1T = np.zeros((1152, 2048), f32)
    g1T[:1084, :] = np.asarray(inp["L0_attn_g1_w"], f32).T
    out["g1Tp"] = g1T
    out["g1b"] = np.ascontiguousarray(np.asarray(inp["L0_attn_g1_b"], f32).reshape(16, 128).T)
    out["g2T"] = np.ascontiguousarray(np.asarray(inp["L0_attn_g2_w"], f32).T)
    out["g2b"] = np.asarray(inp["L0_attn_g2_b"], f32).reshape(1, 20)
    out["vbias"] = np.asarray(inp["L0_attn_value_bias"], f32).reshape(1, 4)
    out["ltemp"] = np.asarray(inp["L0_attn_log_temp"], f32).reshape(1, 4)
    out["flog"] = np.asarray(inp["L0_attn_floor_logit"], f32).reshape(1, 4)
    out["onorm_rep"] = np.ascontiguousarray(np.tile(np.asarray(inp["L0_attn_onorm_w"], f32).reshape(1, 256), (128, 4)))
    out["owT"] = np.ascontiguousarray(np.asarray(inp["L0_attn_o_w"], f32).T)
    out["wg"] = np.ascontiguousarray(np.asarray(inp["L0_ffn_w_gate"], f32))
    out["wu"] = np.ascontiguousarray(np.asarray(inp["L0_ffn_w_up"], f32))
    out["wdn"] = np.ascontiguousarray(np.asarray(inp["L0_ffn_w_down"], f32))
    out["eye128"] = np.eye(128, dtype=bf)
    out["ident32"] = np.eye(128, dtype=f32)
    out["eye4"] = np.tile(np.eye(128, dtype=f32), (1, 4)).astype(bf)
    out["allones"] = np.ones((128, 128), dtype=bf)
    out["zeros128"] = np.zeros((128, 128), dtype=bf)
    oc = np.zeros((128, 256), f32)
    oc[:, 127] = 1
    out["onescol"] = oc.astype(bf)
    r = np.arange(128)
    out["triuI"] = (r[:, None] <= r[None, :]).astype(bf)
    bdm = (r[:, None] // 32 == r[None, :] // 32) & (r[:, None] > r[None, :])
    out["negbd"] = (-bdm.astype(f32)).astype(bf)
    out["blkLow"] = ((r[:, None] // 32) > (r[None, :] // 32)).astype(bf)
    out["blkUp"] = ((r[:, None] // 32) < (r[None, :] // 32)).astype(bf)
    hc4 = np.zeros((8, 4, 128, 16), f32)
    for c in range(8):
        for slot in range(4):
            hc4[c, slot, :, 4 * slot + c // 2] = 1.0
    out["headcol4"] = hc4.astype(bf)
    imask = np.zeros((1, 20), f32)
    imask[0, 4::5] = 1.0
    out["i4mask"] = imask
    return out, groups


def build(groups):
    global PH, SUB56
    nc = bass.Bass("TRN2", target_bir_lowering=False)
    S = Sch()
    dr = {}

    def din(name, shape, dt=F32):
        dr[name] = nc.dram_tensor(name, list(shape), dt, kind="ExternalInput").ap()

    din("x", (L, D))
    for nm in ("qwT", "kwT", "vwT"):
        din(nm, (D, D))
    din("bwTsp", (D, 128))
    din("cd", (3, 8, 4, 128, 32))
    din("fird", (8, 4, NGF, 128, 32))
    din("g1Tp", (1152, 2048))
    din("g1b", (128, 16))
    din("g2T", (2048, 20))
    din("g2b", (1, 20)); din("vbias", (1, 4)); din("ltemp", (1, 4)); din("flog", (1, 4))
    din("onorm_rep", (128, D))
    din("owT", (D, D))
    din("wg", (IFF, D)); din("wu", (IFF, D)); din("wdn", (D, IFF))
    for nm in ("eye128", "allones", "triuI", "negbd", "blkLow", "blkUp", "zeros128"):
        din(nm, (128, 128), BF16)
    din("ident32", (128, 128), F32)
    din("eye4", (128, 512), BF16)
    din("onescol", (128, 256), BF16)
    din("headcol4", (8, 4, 128, 16), BF16)
    din("i4mask", (1, 20))

    y = nc.dram_tensor("y", [L, D], F32, kind="ExternalOutput").ap()
    vdram = nc.dram_tensor("vdram", [D, 128 + L], BF16, kind="Internal").ap()
    xqdram = nc.dram_tensor("xqdram", [L, D], BF16, kind="Internal").ap()
    sdram = nc.dram_tensor("sdram", [L, D], F32, kind="Internal").ap()
    dbg = {}
    if DEBUG:
        for nm, shape in [("dbg_xn", (L, D)), ("dbg_q", (D, L)), ("dbg_k", (D, L)),
                          ("dbg_v", (D, L)), ("dbg_beta", (128, L)),
                          ("dbg_qh", (D, L)), ("dbg_kb", (D, L)), ("dbg_negw", (D, L)),
                          ("dbg_u", (L, D)), ("dbg_delta", (L, D)),
                          ("dbg_short", (L, D)), ("dbg_long", (L, D)), ("dbg_wide", (L, D)),
                          ("dbg_stat", (L, 64)), ("dbg_logits", (L, 20)),
                          ("dbg_otm", (L, D)), ("dbg_xq", (L, D)),
                          ("dbg_hq", (L, IFF)), ("dbg_h", (L, IFF)), ("dbg_preq", (D, L)), ("dbg_sums", (128, L))]:
            dbg[nm] = nc.dram_tensor(nm, list(shape), F32, kind="ExternalOutput").ap()

    ctx = []

    def sbuf(name, shape, dt):
        t = nc.sbuf_tensor(name, list(shape), dt)
        h_ = t.__enter__()
        ctx.append(t)
        return h_

    def psum(name, shape, dt=F32):
        t = nc.psum_tensor(name, list(shape), dt)
        h_ = t.__enter__()
        ctx.append(t)
        return h_

    AB = Arena(sbuf("AB", [128, 79360], BF16), 79360, "AB")
    AFp = Arena(sbuf("AFv", [128, 10240], F32), 10240, "AFp")
    AI = Arena(sbuf("AIv", [128, 2816], I32), 2816, "AI")
    AU = Arena(sbuf("AUv", [128, 1408], U8), 1408, "AU")

    PS_S = [psum(f"psS{h}", [128, 512]) for h in range(H)]
    PW = [psum(f"psW{i}", [128, 512]) for i in range(4)]

    C = {}
    for nm, w_ in [("eye128", 128), ("allones", 128), ("triuI", 128), ("negbd", 128),
                   ("blkLow", 128), ("blkUp", 128), ("zeros128", 128), ("eye4", 512), ("onescol", 256)]:
        C[nm] = AB.alloc(nm, w_)
    C["ident32"] = AFp.alloc("ident32", 128)
    C["onorm_rep"] = AFp.alloc("onorm_rep", D)
    C["g1b"] = AFp.alloc("g1b", 16)
    C["cd"] = AB.alloc("cd", 3 * 8 * 4 * 32)
    C["g2T"] = AB.alloc("g2T", 16 * 20)
    C["hc4"] = AB.alloc("hc4", 8 * 4 * 16)
    sm = AFp.alloc("smallrows", 64)

    tld = []
    for nm in ("eye128", "allones", "triuI", "negbd", "blkLow", "blkUp", "zeros128", "eye4", "onescol"):
        tld.append(S.dma("ms", C[nm][:], dr[nm]))
    tld.append(S.dma("ms", C["ident32"][:], dr["ident32"]))
    tld.append(S.dma("ms", C["onorm_rep"][:], dr["onorm_rep"]))
    tld.append(S.dma("ms", C["g1b"][:], dr["g1b"]))
    tld.append(S.dma("mg", C["cd"][:].rearrange("p (t c g w) -> p t c g w", t=3, c=8, g=4),
                     dr["cd"].rearrange("t c g p w -> p t c g w")))
    tld.append(S.dma("mg", C["g2T"][:].rearrange("p (c w) -> p c w", c=16),
                     dr["g2T"].rearrange("(c p) w -> p c w", p=128)))
    tld.append(S.dma("mg", C["hc4"][:].rearrange("p (c s w) -> p c s w", c=8, s=4),
                     dr["headcol4"].rearrange("c s p w -> p c s w")))
    tld.append(S.dma("ms", sm[0:1, 0:20], dr["g2b"]))
    tld.append(S.dma("ms", sm[0:1, 20:24], dr["vbias"]))
    tld.append(S.dma("ms", sm[0:1, 24:28], dr["ltemp"]))
    tld.append(S.dma("ms", sm[0:1, 28:32], dr["flog"]))
    tld.append(S.dma("ms", sm[0:1, 32:52], dr["i4mask"]))

    zt = AB.alloc("zt", 128)
    tz = S.emit("vector", lambda e: e.memset(zt[:], 0.0), sig="dv")
    for c in range(8):
        S.dma("mg", vdram[c * 128:(c + 1) * 128, 0:128], zt[:], waits=[tz])

    grow = AFp.alloc("grow", 80)
    fscr = AFp.alloc("fscr", 20)
    vb_sp = sm[0:1, 20:24].broadcast_to([1, 4, 5])
    lt_sp = sm[0:1, 24:28].broadcast_to([1, 4, 5])
    fl_sp = sm[0:1, 28:32].broadcast_to([1, 4, 5])
    g3 = lambda ap: ap.rearrange("p (a b) -> p a b", b=5)
    w_all = list(tld)
    t1 = S.emit("vector", lambda e: e.tensor_tensor(g3(grow[0:1, 0:20]), vb_sp, g3(sm[0:1, 32:52]), OP.mult),
                waits=w_all, sig="dv")
    t1 = S.emit("vector", lambda e: e.tensor_tensor(grow[0:1, 0:20], grow[0:1, 0:20], sm[0:1, 0:20], OP.add),
                waits=[t1], sig="dv")
    t2 = S.emit("scalar", lambda e: e.activation(g3(grow[0:1, 20:40]), lt_sp, AF.Exp, scale=-1.0), waits=w_all, sig="dc")
    tf1 = S.emit("scalar", lambda e: e.activation(g3(fscr[0:1, :]), fl_sp, AF.Sigmoid), waits=w_all, sig="dc")
    t4 = S.emit("vector", lambda e: e.tensor_scalar(grow[0:1, 40:60], fscr[0:1, :], -4.0, 1.0, OP.mult, OP.add),
                waits=[tf1], sig="dv")
    tf2 = S.emit("vector", lambda e: e.tensor_tensor(grow[0:1, 60:80], fscr[0:1, :], sm[0:1, 32:52], OP.mult),
                 waits=[tf1], sig="dv")
    tf3 = S.emit("vector", lambda e: e.scalar_tensor_tensor(grow[0:1, 60:80], grow[0:1, 60:80], -1.0, fscr[0:1, :],
                                                            OP.mult, OP.add), waits=[tf2], sig="dv")
    growb = AB.alloc("growb", 80)
    tcast = S.emit("vector", lambda e: e.tensor_copy(growb[0:1, :], grow[0:1, :]), waits=[t1, t2, t4, tf3], sig="dv")
    gconst = AFp.alloc("gconst", 80)
    tgc = None
    for i in range(4):
        tmm = S.emit("tensor", lambda e, i=i: e.matmul(PW[3][:, 0:20], C["allones"][0:1, :],
                                                       growb[0:1, i * 20:(i + 1) * 20], start=True, stop=True),
                     waits=[tcast, tgc], sig="dt")
        tgc = S.emit("vector", lambda e, i=i: e.tensor_copy(gconst[:, i * 20:(i + 1) * 20], PW[3][:, 0:20]),
                     waits=[tmm], sig="dv")
    AFp.free("grow", "fscr")
    AB.free("growb", "zt")
    S.barrier()

    xn_fm = AB.alloc("xn_fm", 8 * W)
    qh_fm = AB.alloc("qh_fm", 8 * W)
    kh_fm = AB.alloc("kh_fm", 8 * W)
    vf_fm = AB.alloc("vf_fm", 8 * W)
    S_bf = AB.alloc("S_bf", H * 512)
    tails = AB.alloc("tails", 3 * 8 * 64)
    statF = AB.alloc("statF", NT * 128)
    S32 = AFp.alloc("S32", H * 512)

    tsz = S.emit("vector", lambda e: e.memset(statF[:], 0.0), sig="dv")

    TP = dict(tok_prev=None)
    tp = {}

    def fmsl(tile, c, tt):
        return tile[:, c * W + tt * 128: c * W + tt * 128 + 128]

    def bcast128(constv):
        return constv.rearrange("p (x w) -> p x w", x=1).broadcast_to([128, 4, 128]).rearrange("p a w -> p (a w)")

    def emit_quant(src, dst, Fn, wtok, key, scratch=None, dst_war=None, fast=False):
        nblk = Fn // 16
        nb2 = (nblk + 15) // 16 * 16
        qbm = AFp.alloc(key + "bm", 2 * nb2)
        qv = AFp.alloc(key + "v", Fn)
        qA = AFp.alloc(key + "A", Fn)
        qBC = AFp.alloc(key + "BC", Fn)
        qt2 = scratch[:, 0:Fn] if scratch is not None else AFp.alloc(key + "t2", Fn)
        qi = AI.alloc(key + "i", Fn)
        qi2 = AI.alloc(key + "i2", Fn)
        qm = AU.alloc(key + "m", Fn)
        t0 = S.emit("vector", lambda e: e.tensor_reduce(qbm[:, 0:nblk], src.rearrange("p (b k) -> p b k", k=16),
                                                        axis=AX, op=OP.max, apply_absolute_value=True),
                    waits=[wtok], sig="dv")
        t1 = S.emit("vector", lambda e: e.tensor_scalar(qbm[:, nb2:nb2 + nblk], qbm[:, 0:nblk], 1.0 / 6.0, 1e-12,
                                                        OP.mult, OP.max), waits=[t0], sig="dv")
        t2 = S.emit("vector", lambda e: e.reciprocal(qbm[:, 0:nblk], qbm[:, nb2:nb2 + nblk]), waits=[t1], sig="dv")
        rv = qbm[:, 0:nblk].broadcast_to([128, nblk, 16])
        sv = qbm[:, nb2:nb2 + nblk].broadcast_to([128, nblk, 16])
        sview = src.rearrange("p (b k) -> p b k", k=16)
        t3 = S.emit("vector", lambda e: e.tensor_tensor(qv[:, :].rearrange("p (b k) -> p b k", k=16), sview, rv, OP.mult),
                    waits=[t2], sig="dv")
        t4 = S.emit("vector", lambda e: e.tensor_scalar(qi[:, :], qv[:, :], 2.0, None, OP.mult), waits=[t3], sig="dv")
        t5 = S.emit("vector", lambda e: e.tensor_scalar(qA[:, :], qi[:, :], 0.5, None, OP.mult), waits=[t4], sig="dv")
        t7 = S.emit("vector", lambda e: e.tensor_scalar(qi2[:, :], qv[:, :], 4.0, -4.0, OP.min, OP.max),
                    waits=[t3], sig="dv")
        qi3 = AI.alloc(key + "i3", Fn) if fast else qi
        t8 = S.emit("vector", lambda e, qi3=qi3: e.tensor_scalar(qi3[:, :], qv[:, :], 0.1, None, OP.mult),
                    waits=([t3] if fast else [t3, t5]), sig="dv")
        t10 = S.emit("vector", lambda e, qi3=qi3: e.scalar_tensor_tensor(qBC[:, :], qi3[:, :], 2.0, qi2[:, :],
                                                                         OP.mult, OP.add),
                     waits=[t7, t8], sig="dv")
        if fast:
            t11a = S.emit("scalar", lambda e: e.activation(qt2[:, :], qv[:, :], AF.Square),
                          waits=[t3], sig="dc")
        else:
            t11a = S.emit("vector", lambda e: e.tensor_tensor(qt2[:, :], qv[:, :], qv[:, :], OP.mult),
                          waits=[t3], sig="dv")
        t11 = S.emit("vector", lambda e: e.tensor_scalar(qm[:, :], qt2[:, :], 4.0, None, OP.is_gt),
                     waits=[t11a], sig="dv")
        t12 = S.emit("vector", lambda e: e.copy_predicated(qA[:, :], qm[:, :], qBC[:, :]),
                     waits=[t10, t11, t5], sig="dv")
        qAv = qA[:, :].rearrange("p (b k) -> p b k", k=16)
        t13 = S.emit("vector", lambda e: e.tensor_tensor(dst.rearrange("p (b k) -> p b k", k=16), qAv, sv, OP.mult),
                     waits=[t12, dst_war], sig="dv")
        AFp.free(key + "bm", key + "v", key + "A", key + "BC")
        if scratch is None:
            AFp.free(key + "t2")
        AI.free(key + "i", key + "i2")
        if fast:
            AI.free(key + "i3")
        AU.free(key + "m")
        return t13, t0

    for mt in range(N_MEG_BUILD):
        mb = mt * TMTOK
        first = (mt == 0)

        x_tm = AFp.alloc("x_tm", 1024)
        xn32 = AFp.alloc("xn32", 1024)
        lnS = AFp.alloc("lnS", 16)
        tmpS = AFp.alloc("tmpS", 1024)
        xn_tt = AB.alloc("xn_tt", 1024)
        xq_tm = AB.alloc("xq_tm", 1024)

        for tt in range(NT):
            r0 = mb + tt * 128
            txl = S.dma("mg", x_tm[:, :], dr["x"][r0:r0 + 128, :],
                        waits=[tp.get("xtmfree"), tp.get("mttail"), tp.get("sdma")])
            t1 = S.emit("scalar", lambda e: e.activation(tmpS[:, :], x_tm[:, :], AF.Square, accum_out=lnS[:, 0:1]),
                        waits=[txl], sig="dc")
            t2 = S.emit("vector", lambda e: e.tensor_reduce(lnS[:, 1:2], x_tm[:, :], axis=AX, op=OP.add),
                        waits=[txl], sig="dv")
            t3 = S.emit("vector", lambda e: e.tensor_scalar(lnS[:, 2:3], lnS[:, 1:2], 1.0 / D, None, OP.mult),
                        waits=[t2], sig="dv")
            t4 = S.emit("vector", lambda e: e.tensor_tensor(lnS[:, 3:4], lnS[:, 2:3], lnS[:, 2:3], OP.mult),
                        waits=[t3], sig="dv")
            t5 = S.emit("vector", lambda e: e.scalar_tensor_tensor(lnS[:, 4:5], lnS[:, 0:1], 1.0 / D, lnS[:, 3:4],
                                                                   OP.mult, OP.subtract), waits=[t1, t4], sig="dv")
            t6 = S.emit("vector", lambda e: e.tensor_scalar(lnS[:, 5:6], lnS[:, 4:5], 1e-5, None, OP.add),
                        waits=[t5], sig="dv")
            t7 = S.emit("vector", lambda e: e.reciprocal(lnS[:, 6:7], lnS[:, 5:6]), waits=[t6], sig="dv")
            t8 = S.emit("scalar", lambda e: e.activation(lnS[:, 7:8], lnS[:, 6:7], AF.Sqrt), waits=[t7], sig="dc")
            t9 = S.emit("vector", lambda e: e.tensor_scalar(lnS[:, 8:9], lnS[:, 2:3], -1.0, None, OP.mult),
                        waits=[t3], sig="dv")
            t10 = S.emit("vector", lambda e: e.tensor_scalar(xn32[:, :], x_tm[:, :], lnS[:, 8:9], lnS[:, 7:8],
                                                             OP.add, OP.mult),
                         waits=[t8, t9, tp.get("qprev"), tp.get("qprev2")], sig="dv")
            tp["xtmfree"] = t10
            t11 = S.emit("vector", lambda e: e.tensor_copy(xn_tt[:, :], xn32[:, :]),
                         waits=[t10, tp.get("xnxpo")], sig="dv")
            tq, tq0 = emit_quant(xn32[:, :], xq_tm[:, :], 1024, t10, "q1", scratch=tmpS,
                                 dst_war=tp.get("xqout"))
            tp["qprev"] = tq0
            if DEBUG:
                tp["qprev2"] = S.dma("mg", dbg["dbg_xn"][r0:r0 + 128, :], xn32[:, :], waits=[t10, tq])
            tp["xqout"] = S.dma("mg", xqdram[r0:r0 + 128, :], xq_tm[:, :], waits=[tq])
            if DEBUG:
                tp["xqout"] = S.dma("mg", dbg["dbg_xq"][r0:r0 + 128, :], xq_tm[:, :],
                                    waits=[tq, tp.get("xqout")])
            for c in range(8):
                S.dma("ms", xn_fm[:, c * W + tt * 128: c * W + tt * 128 + 128],
                      xn_tt[:, c * 128:(c + 1) * 128], waits=[t11, tp.get("xnfree")], transpose=True)
            tp["xnxpo"] = S.qtot("ms")
        tok_xpo = S.qtot("ms")
        AFp.free("x_tm", "xn32", "lnS", "tmpS")
        AB.free("xn_tt")
        tp.pop("xtmfree", None)
        tp.pop("xnxpo", None)

        if PH < 2: continue
        beta_sp = AFp.alloc("beta_sp", W)
        s2q_sp = AFp.alloc("s2q_sp", W)
        s2k_sp = AFp.alloc("s2k_sp", W)
        rkb_sp = AFp.alloc("rkb_sp", W)
        pre_q = AB.alloc("pre_q", 8 * WP)
        pre_k = AB.alloc("pre_k", 8 * WP)
        pre_v = AB.alloc("pre_v", 8 * WP)
        wbuf = AB.alloc("wbuf", 4 * 1024)
        tok_tail = []
        for ti, pre in enumerate((pre_q, pre_k, pre_v)):
            for c in range(8):
                tl = tails[:, (ti * 8 + c) * 64:(ti * 8 + c) * 64 + 64]
                if first:
                    tok_tail.append(S.emit("vector", lambda e, pre=pre, c=c: e.memset(pre[:, c * WP:c * WP + 64], 0.0),
                                           waits=[tp.get("mttail")], sig="dv"))
                else:
                    tok_tail.append(S.emit("vector", lambda e, pre=pre, c=c, tl=tl: e.tensor_copy(
                        pre[:, c * WP:c * WP + 64], tl), waits=[tp.get("mttail")], sig="dv"))
        tok_proj = {}
        order = [(n, dc) for n in ("q", "k", "v") for dc in range(8)]
        wsel = {"q": ("qwT", pre_q), "k": ("kwT", pre_k), "v": ("vwT", pre_v)}
        for oi, (name, dc) in enumerate(order):
            wnm, pre = wsel[name]
            buf = wbuf[:, (oi % 4) * 1024:(oi % 4) * 1024 + 1024]
            war = tok_proj.get(order[oi - 4]) if oi >= 4 else None
            twl = S.dma("mg", buf.rearrange("p (c w) -> p c w", c=8),
                        dr[wnm].rearrange("(c p) d -> p c d", p=128)[:, :, dc * 128:(dc + 1) * 128],
                        waits=[war, tp.get("mttail"), tp.get("sdma")])
            mmt = None
            for kc in range(8):
                mmt = S.emit("tensor", lambda e, kc=kc, buf=buf, oi=oi: e.matmul(
                    PW[oi % 2][:, 0:512], buf[:, kc * 128:(kc + 1) * 128], xn_fm[:, kc * W:(kc + 1) * W],
                    start=(kc == 0), stop=(kc == 7)),
                    waits=[twl, tok_xpo, tp.get(("projps", oi % 2))], sig="dt")
            ev = S.emit("scalar", lambda e, pre=pre, dc=dc, oi=oi: e.activation(
                pre[:, dc * WP + 64: dc * WP + 64 + W], PW[oi % 2][:, 0:512], AF.Identity),
                waits=[mmt] + tok_tail, sig="dc")
            tok_proj[(name, dc)] = ev
            tp[("projps", oi % 2)] = ev
        twb = S.dma("mg", wbuf[:, 0:1024].rearrange("p (c w) -> p c w", c=8),
                    dr["bwTsp"].rearrange("(c p) d -> p c d", p=128), waits=[tok_proj[("v", 6)]])
        bmm = None
        for kc in range(8):
            bmm = S.emit("tensor", lambda e, kc=kc: e.matmul(
                PW[2][:, 0:512], wbuf[:, kc * 128:(kc + 1) * 128], xn_fm[:, kc * W:(kc + 1) * W],
                start=(kc == 0), stop=(kc == 7)), waits=[twb, tp.get("ps2")], sig="dt")
        tok_beta = S.emit("scalar", lambda e: e.activation(beta_sp[:, :], PW[2][:, 0:512], AF.Sigmoid),
                          waits=[bmm], sig="dc")
        tp["ps2"] = tok_beta
        if DEBUG:
            for c in range(8):
                S.dma("mg", dbg["dbg_preq"][c * 128:(c + 1) * 128, mb:mb + W],
                      pre_q[:, c * WP + 64:c * WP + 64 + W], waits=[tok_proj[("q", c)]])
        tok_savetail = []
        for ti, (name, pre) in enumerate((("q", pre_q), ("k", pre_k), ("v", pre_v))):
            for c in range(8):
                tl = tails[:, (ti * 8 + c) * 64:(ti * 8 + c) * 64 + 64]
                tok_savetail.append(S.emit("vector", lambda e, pre=pre, c=c, tl=tl: e.tensor_copy(
                    tl, pre[:, c * WP + W: c * WP + W + 64]), waits=[tok_proj[(name, c)]], sig="dv"))

        if PH < 3: continue
        cdv = C["cd"][:].rearrange("p (t c g w) -> p t c g w", t=3, c=8, g=4)
        tok_conv = {}
        for wi, (name, pre, outt) in enumerate((("q", pre_q, qh_fm), ("k", pre_k, kh_fm), ("v", pre_v, vf_fm))):
            for c in range(8):
                mmt = S.emit("tensor", lambda e, c=c, pre=pre: e.matmul(
                    PW[3][:, 0:512], C["zeros128"][:, :], pre[:, c * WP + 64: c * WP + 64 + W],
                    start=True, stop=False), waits=[tok_proj[(name, c)], tp.get("ps3")], sig="dt")
                for s4 in range(4):
                    for g in range(4):
                        mmt = S.emit("tensor", lambda e, wi=wi, c=c, g=g, s4=s4, pre=pre: e.matmul(
                            PW[3][32 * s4:32 * s4 + 32, 0:512],
                            cdv[32 * s4:32 * s4 + 32, wi, c, g, :],
                            pre[32 * s4:32 * s4 + 32, c * WP + 61 + g: c * WP + 61 + g + W],
                            start=False, stop=False,
                            tile_position=(32 * s4, 32 * s4)),
                            waits=[tok_proj[(name, c)]], sig="dt")
                mmt = S.emit("tensor", lambda e, c=c, pre=pre: e.matmul(
                    PW[3][:, 0:512], C["zeros128"][:, :], pre[:, c * WP + 64: c * WP + 64 + W],
                    start=False, stop=True), sig="dt")
                tok_conv[(name, c)] = S.emit("scalar", lambda e, c=c, outt=outt: e.activation(
                    outt[:, c * W:(c + 1) * W], PW[3][:, 0:512], AF.Silu), waits=[mmt], sig="dc")
                tp["ps3"] = tok_conv[(name, c)]
        if DEBUG:
            for nm, key in (("dbg_q", "q"), ("dbg_k", "k"), ("dbg_v", "v")):
                tile = {"q": qh_fm, "k": kh_fm, "v": vf_fm}[key]
                for c in range(8):
                    S.dma("mg", dbg[nm][c * 128:(c + 1) * 128, mb:mb + W], tile[:, c * W:(c + 1) * W],
                          waits=[tok_conv[(key, c)]])
            S.dma("mg", dbg["dbg_beta"][:, mb:mb + W], beta_sp[:, :], waits=[tok_beta])
        if DEBUG:
            S.barrier()
        AB.free("pre_q", "pre_k", "pre_v", "wbuf")

        if PH < 4: continue
        k_tm = AB.alloc("k_tm", NT * 1024)
        v_tm = AB.alloc("v_tm", NT * 1024)
        vbT = AB.alloc("vbT", NT * 1024)
        kbT = AB.alloc("kbT", NT * 1024)
        khT = AB.alloc("khT", NT * 1024)
        kb_fm = AB.alloc("kb_fm", 8 * W)
        tok_vtm = {}
        tok_ktm = {}
        for c in range(8):
            S.dma("mg", vdram[c * 128:(c + 1) * 128, 128 + mb:128 + mb + W], vf_fm[:, c * W:(c + 1) * W],
                  waits=[tok_conv[("v", c)]])
            for tt in range(NT):
                tok_vtm[(tt, c)] = S.dma("ms", v_tm[:, tt * 1024 + c * 128: tt * 1024 + (c + 1) * 128],
                                         vf_fm[:, c * W + tt * 128: c * W + tt * 128 + 128],
                                         waits=[tok_conv[("v", c)], tp.get("mttail")] + tok_savetail,
                                         transpose=True)
                tok_ktm[(tt, c)] = S.dma("mh", k_tm[:, tt * 1024 + c * 128: tt * 1024 + (c + 1) * 128],
                                         kh_fm[:, c * W + tt * 128: c * W + tt * 128 + 128],
                                         waits=[tok_conv[("k", c)], tp.get("mttail")] + tok_savetail,
                                         transpose=True)
        tok_vdr_w = S.qtot("mg")
        tok_ktm_all = S.qtot("mh")
        tok_vtm_all = S.qtot("ms")

        if PH < 5: continue
        sq_t = AB.alloc("sq_t", 2 * W)
        tok_s2 = {}
        for ti, (name, src, dstsp) in enumerate((("q", qh_fm, s2q_sp), ("k", kh_fm, s2k_sp))):
            mmt = None
            sqt = {}
            for c in range(8):
                half = c % 2
                war = tok_s2.get((name, c - 2))
                if war is None and name == "k":
                    war = tok_s2.get(("q", 6 + c))
                sqt[c] = S.emit("scalar", lambda e, c=c, src=src, half=half: e.activation(
                    sq_t[:, half * W:(half + 1) * W], src[:, c * W:(c + 1) * W], AF.Square),
                    waits=[tok_conv[(name, c)], war] + (tok_savetail if c < 2 else []), sig="dc")
                h = c // 2
                mmt = S.emit("tensor", lambda e, c=c, h=h, half=half: e.matmul(
                    PW[2][:, 0:512], C["onescol"][:, 127 - 32 * h:255 - 32 * h], sq_t[:, half * W:(half + 1) * W],
                    start=(c == 0), stop=(c == 7)),
                    waits=[sqt[c], tp.get("ps2")], sig="dt")
                tok_s2[(name, c)] = mmt
            tok_s2[name] = S.emit("vector", lambda e, dstsp=dstsp: e.tensor_copy(dstsp[:, :], PW[2][:, 0:512]),
                                  waits=[mmt], sig="dv")
            tp["ps2"] = tok_s2[name]
        AB.free("sq_t")
        tok_rsq = {}
        for name, sp in (("q", s2q_sp), ("k", s2k_sp)):
            a = S.emit("vector", lambda e, sp=sp: e.tensor_scalar(sp[:, :], sp[:, :], 1e-6, None, OP.add),
                       waits=[tok_s2[name]], sig="dv")
            b = S.emit("vector", lambda e, sp=sp: e.reciprocal(sp[:, :], sp[:, :]), waits=[a], sig="dv")
            tok_rsq[name] = S.emit("scalar", lambda e, sp=sp: e.activation(sp[:, :], sp[:, :], AF.Sqrt),
                                   waits=[b], sig="dc")
        tok_rkb = S.emit("vector", lambda e: e.tensor_tensor(rkb_sp[:, :], s2k_sp[:, :], beta_sp[:, :], OP.mult),
                         waits=[tok_rsq["k"], tok_beta], sig="dv")
        sprd3 = AB.alloc("sprd3", 3 * W)
        tok_sb = {}
        tok_sb["q"] = S.emit("vector", lambda e: e.tensor_copy(sprd3[:, 0:W], s2q_sp[:, :]),
                             waits=[tok_rsq["q"]], sig="dv")
        tok_sb["k"] = S.emit("vector", lambda e: e.tensor_copy(sprd3[:, W:2 * W], s2k_sp[:, :]),
                             waits=[tok_rsq["k"]], sig="dv")
        tok_sb["kb"] = S.emit("vector", lambda e: e.tensor_copy(sprd3[:, 2 * W:3 * W], rkb_sp[:, :]),
                              waits=[tok_rkb], sig="dv")
        tok_scaled = {}
        for c in range(8):
            h = c // 2
            tpos = (96, 0) if h == 3 else None
            bq = S.emit("tensor", lambda e, h=h, tpos=tpos: e.matmul(
                PW[2][:, 0:512], C["allones"][32 * h:32 * h + 1, :],
                sprd3[32 * h:32 * h + 1, 0:W], start=True, stop=True, tile_position=tpos),
                waits=[tok_sb["q"], tp.get("ps2")], sig="dt")
            tok_scaled[("q", c)] = S.emit("vector", lambda e, c=c: e.tensor_tensor(
                qh_fm[:, c * W:(c + 1) * W], qh_fm[:, c * W:(c + 1) * W], PW[2][:, 0:512], OP.mult),
                waits=[bq], sig="dv")
            tp["ps2"] = tok_scaled[("q", c)]
            bk = S.emit("tensor", lambda e, h=h, tpos=tpos: e.matmul(
                PW[3][:, 0:512], C["allones"][32 * h:32 * h + 1, :],
                sprd3[32 * h:32 * h + 1, 2 * W:3 * W], start=True, stop=True, tile_position=tpos),
                waits=[tok_sb["kb"], tp.get("ps3")], sig="dt")
            tok_scaled[("kb", c)] = S.emit("vector", lambda e, c=c: e.tensor_tensor(
                kb_fm[:, c * W:(c + 1) * W], kh_fm[:, c * W:(c + 1) * W], PW[3][:, 0:512], OP.mult),
                waits=[bk], sig="dv")
            bk2 = S.emit("tensor", lambda e, h=h, tpos=tpos: e.matmul(
                PW[3][:, 0:512], C["allones"][32 * h:32 * h + 1, :],
                sprd3[32 * h:32 * h + 1, W:2 * W], start=True, stop=True, tile_position=tpos),
                waits=[tok_sb["k"], tok_scaled[("kb", c)]], sig="dt")
            tok_scaled[("kh", c)] = S.emit("vector", lambda e, c=c: e.tensor_tensor(
                kh_fm[:, c * W:(c + 1) * W], kh_fm[:, c * W:(c + 1) * W], PW[3][:, 0:512], OP.mult),
                waits=[bk2, tok_ktm_all], sig="dv")
            tp["ps3"] = tok_scaled[("kh", c)]
        if DEBUG:
            for c in range(8):
                S.dma("mg", dbg["dbg_qh"][c * 128:(c + 1) * 128, mb:mb + W], qh_fm[:, c * W:(c + 1) * W],
                      waits=[tok_scaled[("q", c)]])
                S.dma("mg", dbg["dbg_kb"][c * 128:(c + 1) * 128, mb:mb + W], kb_fm[:, c * W:(c + 1) * W],
                      waits=[tok_scaled[("kb", c)]])

        if PH < 6: continue
        scT = AFp.alloc("scT", 3 * 128)
        tok_tmprod = {}
        for tt in range(NT):
            tsc = []
            for si, (sp, tok_src) in enumerate(((beta_sp, tok_beta), (s2k_sp, tok_rsq["k"]), (rkb_sp, tok_rkb))):
                tpx = S.emit("tensor", lambda e, sp=sp, tt=tt: e.transpose(
                    PW[2][:, 0:128], sp[:, tt * 128:(tt + 1) * 128], C["ident32"][:, :]),
                    waits=[tok_src, tp.get("ps2")], sig="dt")
                ev = S.emit("vector", lambda e, si=si: e.tensor_copy(scT[:, si * 128:(si + 1) * 128], PW[2][:, 0:128]),
                            waits=[tpx], sig="dv")
                tp["ps2"] = ev
                tsc.append(ev)

            def hview(si):
                return scT[:, si * 128: si * 128 + 128].rearrange("p (h r) -> p h r", h=4)[:, :, 0:1].broadcast_to(
                    [128, 4, 256])
            sl = slice(tt * 1024, (tt + 1) * 1024)
            wv_ = [tok_vtm[(tt, c)] for c in range(8)] + tsc
            wk_ = [tok_ktm[(tt, c)] for c in range(8)] + tsc
            if not SUB56:
                continue
            tok_tmprod[("vbT", tt)] = S.emit("vector", lambda e, sl=sl, tt=tt: e.tensor_tensor(
                vbT[:, sl].rearrange("p (h c) -> p h c", h=4), v_tm[:, sl].rearrange("p (h c) -> p h c", h=4),
                hview(0), OP.mult), waits=wv_, sig="dv")
            tok_tmprod[("kbT", tt)] = S.emit("vector", lambda e, sl=sl, tt=tt: e.tensor_tensor(
                kbT[:, sl].rearrange("p (h c) -> p h c", h=4), k_tm[:, sl].rearrange("p (h c) -> p h c", h=4),
                hview(2), OP.mult), waits=wk_, sig="dv")
            tok_tmprod[("khT", tt)] = S.emit("vector", lambda e, sl=sl, tt=tt: e.tensor_tensor(
                khT[:, sl].rearrange("p (h c) -> p h c", h=4), k_tm[:, sl].rearrange("p (h c) -> p h c", h=4),
                hview(1), OP.mult), waits=wk_, sig="dv")
            tp["ps2"] = tok_tmprod[("khT", tt)]
        AB.free("k_tm", "sprd3")
        AFp.free("scT", "beta_sp", "s2q_sp", "s2k_sp", "rkb_sp")
        S.barrier()

        if PH < 7: continue
        negw_fm = AB.alloc("negw_fm", 8 * W)
        delta_tm = AB.alloc("delta_tm", NT * 1024)
        B_b = AB.alloc("B_b", 512)
        BT_b = AB.alloc("BT_b", 512)
        pw_b = AB.alloc("pw_b", 512)
        pwT_b = AB.alloc("pwT_b", 512)
        P_b = AB.alloc("P_b", 2 * 512)
        invT_b = AB.alloc("invT_b", 512)
        Mt_b = AB.alloc("Mt_b", 512)
        Gnw_b = AB.alloc("Gnw_b", 512)
        GnwT_b = AB.alloc("GnwT_b", 512)
        T2t_b = AB.alloc("T2t_b", 512)
        Ht_b = AB.alloc("Ht_b", 512)
        u_b = AB.alloc("u_b", 1024)
        U0_b = AB.alloc("U0_b", 1024)
        uadj_b = AB.alloc("uadj_b", 1024)
        for tt in range(NT):
            fsc = first and tt == 0
            mmt = None
            for h in range(H):
                for half in range(2):
                    c = 2 * h + half
                    mmt = S.emit("tensor", lambda e, h=h, c=c, tt=tt, half=half: e.matmul(
                        PW[0][:, h * 128:(h + 1) * 128], fmsl(kb_fm, c, tt), fmsl(kh_fm, c, tt),
                        start=(h == 0 and half == 0), stop=(h == 3 and half == 1)),
                        waits=[tok_scaled[("kb", c)], tok_scaled[("kh", c)], tp.get("ps0")], sig="dt")
            tB = S.emit("vector", lambda e: e.tensor_tensor(
                B_b[:, :].rearrange("p (a w) -> p a w", a=4),
                PW[0][:, :].rearrange("p (a w) -> p a w", a=4),
                C["negbd"][:].rearrange("p (x w) -> p x w", x=1).broadcast_to([128, 4, 128]), OP.mult),
                waits=[mmt], sig="dv")
            tp["ps0"] = tB
            tBT = S.emit("vector", lambda e: e.transpose(BT_b[:, :], B_b[:, :]), waits=[tB], sig="dv")
            tPc = S.emit("vector", lambda e: e.tensor_tensor(P_b[:, 0:512], B_b[:, :], C["eye4"][:], OP.add),
                         waits=[tB], sig="dv")
            curP = 0
            tsq_src, tsq_srcT = tB, tBT
            src_pw, src_pwT = B_b, BT_b
            for lvl in range(4):
                mmq = None
                for h in range(H):
                    mmq = S.emit("tensor", lambda e, h=h, s1=src_pwT, s2=src_pw: e.matmul(
                        PW[1][:, h * 128:(h + 1) * 128], s1[:, h * 128:(h + 1) * 128], s2[:, h * 128:(h + 1) * 128],
                        start=(h == 0), stop=(h == 3)), waits=[tsq_src, tsq_srcT, tp.get("ps1")], sig="dt")
                tev = S.emit("vector", lambda e: e.tensor_copy(pw_b[:, :], PW[1][:, :]), waits=[mmq], sig="dv")
                tp["ps1"] = tev
                tevT = S.emit("vector", lambda e: e.transpose(pwT_b[:, :], pw_b[:, :]), waits=[tev], sig="dv")
                mmp = None
                for h in range(H):
                    mmp = S.emit("tensor", lambda e, h=h, curP=curP: e.matmul(
                        PW[2][:, h * 128:(h + 1) * 128], pwT_b[:, h * 128:(h + 1) * 128],
                        P_b[:, curP * 512 + h * 128: curP * 512 + (h + 1) * 128],
                        start=(h == 0), stop=False), waits=[tevT, tPc, tp.get("ps2")], sig="dt")
                for h in range(H):
                    mmp = S.emit("tensor", lambda e, h=h, curP=curP: e.matmul(
                        PW[2][:, h * 128:(h + 1) * 128], C["eye128"][:, :],
                        P_b[:, curP * 512 + h * 128: curP * 512 + (h + 1) * 128],
                        start=False, stop=(h == 3)), sig="dt")
                curP = 1 - curP
                tPc = S.emit("vector", lambda e, curP=curP: e.tensor_copy(P_b[:, curP * 512:(curP + 1) * 512],
                                                                          PW[2][:, :]), waits=[mmp], sig="dv")
                tp["ps2"] = tPc
                if lvl < 3:
                    tsq_src = S.emit("vector", lambda e: e.tensor_copy(B_b[:, :], pw_b[:, :]), waits=[tevT], sig="dv")
                    tsq_srcT = S.emit("vector", lambda e: e.tensor_copy(BT_b[:, :], pwT_b[:, :]), waits=[tevT], sig="dv")
                    src_pw, src_pwT = B_b, BT_b
            tinvT = S.emit("vector", lambda e, curP=curP: e.transpose(invT_b[:, :], P_b[:, curP * 512:(curP + 1) * 512]),
                           waits=[tPc], sig="dv")
            mmu = None
            for h in range(H):
                mmu = S.emit("tensor", lambda e, h=h, tt=tt: e.matmul(
                    PW[h // 2][:, (h % 2) * 256:(h % 2) * 256 + 256], invT_b[:, h * 128:(h + 1) * 128],
                    vbT[:, tt * 1024 + h * 256: tt * 1024 + (h + 1) * 256], start=True, stop=True),
                    waits=[tinvT, tok_tmprod[("vbT", tt)], tp.get("ps0"), tp.get("ps1")], sig="dt")
            tu1 = S.emit("vector", lambda e: e.tensor_copy(u_b[:, 0:512], PW[0][:, :]),
                         waits=[mmu, tp.get("dbgu")], sig="dv")
            tu2 = S.emit("vector", lambda e: e.tensor_copy(u_b[:, 512:1024], PW[1][:, :]), waits=[mmu], sig="dv")
            tp["ps0"] = tu1
            tp["ps1"] = tu2
            mmw = None
            for h in range(H):
                for half in range(2):
                    mmw = S.emit("tensor", lambda e, h=h, half=half, tt=tt: e.matmul(
                        PW[2 + h // 2][:, ((h % 2) * 2 + half) * 128: ((h % 2) * 2 + half) * 128 + 128],
                        kbT[:, tt * 1024 + h * 256 + half * 128: tt * 1024 + h * 256 + half * 128 + 128],
                        invT_b[:, h * 128:(h + 1) * 128], start=True, stop=True),
                        waits=[tinvT, tok_tmprod[("kbT", tt)], tp.get("ps2"), tp.get("ps3")], sig="dt")
            nw_lo = negw_fm[:, 0:4 * W].rearrange("p (c w) -> p c w", c=4)[:, :, tt * 128:tt * 128 + 128]
            nw_hi = negw_fm[:, 4 * W:8 * W].rearrange("p (c w) -> p c w", c=4)[:, :, tt * 128:tt * 128 + 128]
            tw1 = S.emit("vector", lambda e, nw_lo=nw_lo: e.tensor_scalar(
                nw_lo, PW[2][:, :].rearrange("p (c w) -> p c w", c=4), -1.0, None, OP.mult), waits=[mmw], sig="dv")
            tw2 = S.emit("vector", lambda e, nw_hi=nw_hi: e.tensor_scalar(
                nw_hi, PW[3][:, :].rearrange("p (c w) -> p c w", c=4), -1.0, None, OP.mult), waits=[mmw], sig="dv")
            tp["ps2"] = tw1
            tp["ps3"] = tw2
            if DEBUG:
                r0 = mb + tt * 128
                tp["dbgu"] = S.dma("mg", dbg["dbg_u"][r0:r0 + 128, :], u_b[:, :], waits=[tu1, tu2])
                for c in range(8):
                    tp["dbgu"] = S.dma("mg", dbg["dbg_negw"][c * 128:(c + 1) * 128, r0:r0 + 128],
                                       fmsl(negw_fm, c, tt), waits=[tw1, tw2, tp.get("dbgu")])
            mmt = None
            for h in range(H):
                for half in range(2):
                    c = 2 * h + half
                    mmt = S.emit("tensor", lambda e, h=h, c=c, tt=tt, half=half: e.matmul(
                        PW[0][:, h * 128:(h + 1) * 128], fmsl(kh_fm, c, tt), fmsl(qh_fm, c, tt),
                        start=(h == 0 and half == 0), stop=(h == 3 and half == 1)),
                        waits=[tu1, tok_scaled[("q", c)]], sig="dt")
            tMt = S.emit("vector", lambda e: e.tensor_tensor(
                Mt_b[:, :].rearrange("p (a w) -> p a w", a=4), PW[0][:, :].rearrange("p (a w) -> p a w", a=4),
                C["triuI"][:].rearrange("p (x w) -> p x w", x=1).broadcast_to([128, 4, 128]), OP.mult), waits=[mmt], sig="dv")
            tp["ps0"] = tMt
            mm1 = None
            for h in range(H):
                for half in range(2):
                    c = 2 * h + half
                    mm1 = S.emit("tensor", lambda e, h=h, c=c, tt=tt, half=half: e.matmul(
                        PW[1][:, h * 128:(h + 1) * 128], fmsl(negw_fm, c, tt), fmsl(kh_fm, c, tt),
                        start=(h == 0 and half == 0), stop=(h == 3 and half == 1)),
                        waits=[tw1, tw2, tu2], sig="dt")
            tGnw = S.emit("vector", lambda e: e.tensor_tensor(
                Gnw_b[:, :].rearrange("p (a w) -> p a w", a=4), PW[1][:, :].rearrange("p (a w) -> p a w", a=4),
                C["blkLow"][:].rearrange("p (x w) -> p x w", x=1).broadcast_to([128, 4, 128]), OP.mult), waits=[mm1], sig="dv")
            tp["ps1"] = tGnw
            mm2 = None
            for h in range(H):
                for half in range(2):
                    c = 2 * h + half
                    mm2 = S.emit("tensor", lambda e, h=h, c=c, tt=tt, half=half: e.matmul(
                        PW[2][:, h * 128:(h + 1) * 128], fmsl(kh_fm, c, tt), fmsl(negw_fm, c, tt),
                        start=(h == 0 and half == 0), stop=(h == 3 and half == 1)),
                        waits=[tw1, tw2, tp.get("ps2")], sig="dt")
            tGnwT = S.emit("vector", lambda e: e.tensor_tensor(
                GnwT_b[:, :].rearrange("p (a w) -> p a w", a=4), PW[2][:, :].rearrange("p (a w) -> p a w", a=4),
                C["blkUp"][:].rearrange("p (x w) -> p x w", x=1).broadcast_to([128, 4, 128]), OP.mult), waits=[mm2], sig="dv")
            tp["ps2"] = tGnwT
            mmt2 = None
            for h in range(H):
                mmt2 = S.emit("tensor", lambda e, h=h: e.matmul(
                    PW[3][:, h * 128:(h + 1) * 128], Gnw_b[:, h * 128:(h + 1) * 128],
                    GnwT_b[:, h * 128:(h + 1) * 128], start=(h == 0), stop=(h == 3)),
                    waits=[tGnw, tGnwT, tp.get("ps3")], sig="dt")
            tT2 = S.emit("vector", lambda e: e.tensor_copy(T2t_b[:, :], PW[3][:, :]), waits=[mmt2], sig="dv")
            tp["ps3"] = tT2
            mmh = None
            for h in range(H):
                hs = slice(h * 128, (h + 1) * 128)
                mmh = S.emit("tensor", lambda e, hs=hs: e.matmul(PW[0][:, hs], C["eye128"][:, :], C["eye128"][:, :],
                                                                start=True, stop=False),
                             waits=[tT2, tMt], sig="dt")
                S.emit("tensor", lambda e, hs=hs: e.matmul(PW[0][:, hs], C["eye128"][:, :], GnwT_b[:, hs],
                                                           start=False, stop=False), sig="dt")
                S.emit("tensor", lambda e, hs=hs: e.matmul(PW[0][:, hs], C["eye128"][:, :], T2t_b[:, hs],
                                                           start=False, stop=False), sig="dt")
                mmh = S.emit("tensor", lambda e, hs=hs: e.matmul(PW[0][:, hs], Gnw_b[:, hs], T2t_b[:, hs],
                                                                start=False, stop=True), sig="dt")
            tHt = S.emit("vector", lambda e: e.tensor_copy(Ht_b[:, :], PW[0][:, :]), waits=[mmh], sig="dv")
            tp["ps0"] = tHt
            mmu0 = None
            for h in range(H):
                pwb = PW[1 + h // 2]
                of = (h % 2) * 256
                mmu0 = S.emit("tensor", lambda e, h=h, pwb=pwb, of=of, fsc=fsc: e.matmul(
                    pwb[:, of:of + 256], C["eye128"][:, :], u_b[:, h * 256:(h + 1) * 256],
                    start=True, stop=fsc), waits=[tHt, tu1, tu2, tp.get("ps1"), tp.get("ps2")], sig="dt")
                if not fsc:
                    for half in range(2):
                        c = 2 * h + half
                        mmu0 = S.emit("tensor", lambda e, h=h, c=c, tt=tt, pwb=pwb, of=of, half=half: e.matmul(
                            pwb[:, of:of + 256], fmsl(negw_fm, c, tt),
                            S_bf[:, h * 512 + half * 256: h * 512 + half * 256 + 256],
                            start=False, stop=(half == 1)), sig="dt")
            tU1 = S.emit("vector", lambda e: e.tensor_copy(U0_b[:, 0:512], PW[1][:, :]), waits=[mmu0], sig="dv")
            tU2 = S.emit("vector", lambda e: e.tensor_copy(U0_b[:, 512:1024], PW[2][:, :]), waits=[mmu0], sig="dv")
            tp["ps1"] = tU1
            tp["ps2"] = tU2
            mmua = None
            for h in range(H):
                pwb = PW[1 + h // 2]
                of = (h % 2) * 256
                mmua = S.emit("tensor", lambda e, h=h, pwb=pwb, of=of: e.matmul(
                    pwb[:, of:of + 256], Ht_b[:, h * 128:(h + 1) * 128], U0_b[:, h * 256:(h + 1) * 256],
                    start=True, stop=True), waits=[tU1, tU2], sig="dt")
            tua1 = S.emit("vector", lambda e: e.tensor_copy(uadj_b[:, 0:512], PW[1][:, :]), waits=[mmua], sig="dv")
            tua2 = S.emit("vector", lambda e: e.tensor_copy(uadj_b[:, 512:1024], PW[2][:, :]), waits=[mmua], sig="dv")
            tp["ps1"] = tua1
            tp["ps2"] = tua2
            mmo = None
            for h in range(H):
                pwb = PW[1 + h // 2]
                of = (h % 2) * 256
                mmo = S.emit("tensor", lambda e, h=h, pwb=pwb, of=of, fsc=fsc: e.matmul(
                    pwb[:, of:of + 256], Mt_b[:, h * 128:(h + 1) * 128], uadj_b[:, h * 256:(h + 1) * 256],
                    start=True, stop=fsc), waits=[tua1, tua2], sig="dt")
                if not fsc:
                    for half in range(2):
                        c = 2 * h + half
                        mmo = S.emit("tensor", lambda e, h=h, c=c, tt=tt, pwb=pwb, of=of, half=half: e.matmul(
                            pwb[:, of:of + 256], fmsl(qh_fm, c, tt),
                            S_bf[:, h * 512 + half * 256: h * 512 + half * 256 + 256],
                            start=False, stop=(half == 1)), sig="dt")
            tod1 = S.emit("vector", lambda e, tt=tt: e.tensor_copy(delta_tm[:, tt * 1024:tt * 1024 + 512], PW[1][:, :]),
                          waits=[mmo], sig="dv")
            tod2 = S.emit("vector", lambda e, tt=tt: e.tensor_copy(delta_tm[:, tt * 1024 + 512:(tt + 1) * 1024],
                                                                   PW[2][:, :]), waits=[mmo], sig="dv")
            tp["ps1"] = tod1
            tp["ps2"] = tod2
            mms = None
            for h in range(H):
                for half in range(2):
                    mms = S.emit("tensor", lambda e, h=h, half=half, tt=tt: e.matmul(
                        PS_S[h][:, half * 256: half * 256 + 256],
                        khT[:, tt * 1024 + h * 256 + half * 128: tt * 1024 + h * 256 + half * 128 + 128],
                        uadj_b[:, h * 256:(h + 1) * 256], start=(half == 0), stop=(half == 1)),
                        waits=[tua1, tua2, tok_tmprod[("khT", tt)], tp.get("scast")], sig="dt")
            tsc = None
            for h in range(H):
                if fsc:
                    ta = S.emit("vector", lambda e, h=h: e.tensor_copy(S32[:, h * 512:(h + 1) * 512], PS_S[h][:, :]),
                                waits=[mms], sig="dv")
                else:
                    ta = S.emit("vector", lambda e, h=h: e.tensor_tensor(S32[:, h * 512:(h + 1) * 512],
                                                                         S32[:, h * 512:(h + 1) * 512],
                                                                         PS_S[h][:, :], OP.add),
                                waits=[mms], sig="dv")
                tsc = S.emit("vector", lambda e, h=h: e.tensor_copy(S_bf[:, h * 512:(h + 1) * 512],
                                                                    S32[:, h * 512:(h + 1) * 512]),
                             waits=[ta], sig="dv")
            tp["scast"] = tsc
            if DEBUG:
                r0 = mb + tt * 128
                tp["dbgu"] = S.dma("mg", dbg["dbg_delta"][r0:r0 + 128, :], delta_tm[:, tt * 1024:(tt + 1) * 1024],
                                   waits=[tod1, tod2, tp.get("dbgu")])
        AB.free("B_b", "BT_b", "pw_b", "pwT_b", "P_b", "invT_b", "Mt_b", "Gnw_b", "GnwT_b",
                "T2t_b", "Ht_b", "u_b", "U0_b", "uadj_b", "negw_fm", "kb_fm", "vbT", "kbT", "khT")
        S.barrier()

        if PH < 8: continue
        sh_tm = AB.alloc("sh_tm", NT * 1024)
        lo_tm = AB.alloc("lo_tm", NT * 1024)
        wd_tm = AB.alloc("wd_tm", NT * 1024)
        Rb = AB.alloc("Rb", 2 * 4 * RW)
        fdb = AB.alloc("fdb", 2 * 4 * NGF * 32)
        fir_fm = AB.alloc("fir_fm", 3 * W)
        sq_b = AB.alloc("sq_b", W)
        filt_last = [NG_S - 1, NG_S + NG_L - 1, NGF - 1]
        tok_fir_ev = {}
        vdview = vdram.rearrange("(c s i) t -> c i s t", c=8, s=4)
        mmsum = None
        first_summ = True
        for c in range(8):
            half = c % 2
            tRl = []
            for g in range(4):
                colb = mb + 61 + g
                tRl.append(S.dma("mh", Rb[32 * g:32 * g + 32, half * 4 * RW:(half + 1) * 4 * RW].rearrange(
                    "p (s t) -> p s t", s=4), vdview[c, :, :, colb:colb + RW],
                    waits=[tok_vdr_w, tok_fir_ev.get(c - 2)]))
            tfd = S.dma("mg", fdb[:, half * NGF * 128:(half + 1) * NGF * 128].rearrange(
                "p (s g w) -> p s g w", s=4, g=NGF), dr["fird"][c].rearrange("s g p w -> p s g w"),
                waits=[tok_fir_ev.get(c - 2)])
            mmf = {}
            for fi in range(3):
                mmf[fi] = S.emit("tensor", lambda e, fi=fi, half=half: e.matmul(
                    PW[fi][:, 0:512], C["zeros128"][:, :], Rb[:, half * 4 * RW: half * 4 * RW + 512],
                    start=True, stop=False), waits=tRl + [tfd, tp.get(("fps", fi))], sig="dt")
            for gi, (fi, Gf, tau) in enumerate(groups):
                for s4 in range(4):
                    mmf[fi] = S.emit("tensor", lambda e, fi=fi, gi=gi, s4=s4, tau=tau, half=half: e.matmul(
                        PW[fi][32 * s4:32 * s4 + 32, 0:512],
                        fdb[:, half * NGF * 128 + (s4 * NGF + gi) * 32: half * NGF * 128 + (s4 * NGF + gi) * 32 + 32],
                        Rb[:, half * 4 * RW + s4 * RW + tau: half * 4 * RW + s4 * RW + tau + 512],
                        start=False, stop=False, tile_position=(0, 32 * s4)),
                        waits=tRl + [tfd], sig="dt")
            for fi in range(3):
                mmf[fi] = S.emit("tensor", lambda e, fi=fi, half=half: e.matmul(
                    PW[fi][:, 0:512], C["zeros128"][:, :], Rb[:, half * 4 * RW: half * 4 * RW + 512],
                    start=False, stop=True), sig="dt")
            evs = []
            for fi in range(3):
                evs.append(S.emit("vector", lambda e, fi=fi: e.tensor_copy(fir_fm[:, fi * W:(fi + 1) * W], PW[fi][:, :]),
                                  waits=[mmf[fi], tp.get(("fxw", fi))], sig="dv"))
                tp[("fps", fi)] = evs[-1]
            tok_fir_ev[c] = evs[-1]
            if first_summ:
                mmsum = S.emit("tensor", lambda e: e.matmul(
                    PW[3][:, 0:512], C["zeros128"][:, :], vf_fm[:, 0:W],
                    start=True, stop=False), waits=[tp.get("sumps")] + [evs[0]], sig="dt")
                first_summ = False
            for fi in range(3):
                mmsum = S.emit("tensor", lambda e, c=c, fi=fi: e.matmul(
                    PW[3][0:16, 0:512], C["hc4"][:, (c * 4 + fi) * 16:(c * 4 + fi + 1) * 16],
                    fir_fm[:, fi * W:(fi + 1) * W],
                    start=False, stop=False), waits=[evs[fi], tp.get("sumps")], sig="dt")
            mmsum = S.emit("tensor", lambda e, c=c: e.matmul(
                PW[3][0:16, 0:512], C["hc4"][:, (c * 4 + 3) * 16:(c * 4 + 4) * 16], vf_fm[:, c * W:(c + 1) * W],
                start=False, stop=False), sig="dt")
            for fi in range(3):
                tsq = S.emit("vector", lambda e, fi=fi: e.tensor_tensor(
                    sq_b[:, :], fir_fm[:, fi * W:(fi + 1) * W], fir_fm[:, fi * W:(fi + 1) * W], OP.mult),
                    waits=[evs[fi], mmsum], sig="dv")
                mmsum = S.emit("tensor", lambda e, c=c, fi=fi: e.matmul(
                    PW[3][32:48, 0:512], C["hc4"][:, (c * 4 + fi) * 16:(c * 4 + fi + 1) * 16], sq_b[:, :],
                    start=False, stop=False), waits=[tsq], sig="dt")
            tsqv = S.emit("vector", lambda e, c=c: e.tensor_tensor(
                sq_b[:, :], vf_fm[:, c * W:(c + 1) * W], vf_fm[:, c * W:(c + 1) * W], OP.mult),
                waits=[mmsum], sig="dv")
            mmsum = S.emit("tensor", lambda e, c=c: e.matmul(
                PW[3][32:48, 0:512], C["hc4"][:, (c * 4 + 3) * 16:(c * 4 + 4) * 16], sq_b[:, :],
                start=False, stop=False), waits=[tsqv], sig="dt")
            if c == 7:
                mmsum = S.emit("tensor", lambda e: e.matmul(
                    PW[3][:, 0:512], C["zeros128"][:, :], vf_fm[:, 0:W],
                    start=False, stop=True), sig="dt")
            tp["sumps"] = mmsum
            for fi, dst in ((0, sh_tm), (1, lo_tm), (2, wd_tm)):
                tp[("fxw", fi)] = [S.dma("ms", dst[:, tt * 1024 + c * 128: tt * 1024 + (c + 1) * 128],
                                         fir_fm[:, fi * W + tt * 128: fi * W + tt * 128 + 128],
                                         waits=[evs[fi]], transpose=True) for tt in range(NT)]
        tok_fxall = S.qtot("ms") + S.qtot("mh")
        if DEBUG:
            for nm, dst in (("dbg_short", sh_tm), ("dbg_long", lo_tm), ("dbg_wide", wd_tm)):
                for tt in range(NT):
                    S.dma("mg", dbg[nm][mb + tt * 128: mb + tt * 128 + 128, :], dst[:, tt * 1024:(tt + 1) * 1024],
                          waits=[tok_fxall])
        AB.free("Rb", "fdb", "fir_fm", "sq_b")

        if PH < 9: continue
        stat64 = AFp.alloc("stat64", 64)
        sums16 = AFp.alloc("sums16", 512)
        trsp = AFp.alloc("trsp", 128)
        dsq = AFp.alloc("dsq", 1024)
        stt = S.emit("vector", lambda e: e.tensor_copy(sums16[:, :], PW[3][:, 0:512]), waits=[mmsum], sig="dv")
        if DEBUG:
            S.dma("mg", dbg["dbg_sums"][:, mb:mb + W], sums16[:, :], waits=[stt])
        tp["ps3"] = stt
        tok_stats = []
        for tt in range(NT):
            toks = []
            tp1 = S.emit("tensor", lambda e, tt=tt: e.transpose(PW[2][:, 0:128], sums16[:, tt * 128:(tt + 1) * 128],
                                                                C["ident32"][:, :]),
                         waits=[stt, tp.get("ps2")], sig="dt")
            ev1 = S.emit("vector", lambda e: e.tensor_copy(trsp[:, :], PW[2][:, 0:128]), waits=[tp1], sig="dv")
            tp["ps2"] = ev1
            for ti in range(4):
                off = [0, 12, 24, 48][ti]
                toks.append(S.emit("vector", lambda e, ti=ti, off=off: e.tensor_scalar(
                    stat64[:, off:off + 4], trsp[:, ti * 4:(ti + 1) * 4], 1.0 / 256.0, None, OP.mult),
                    waits=[ev1, tp.get("dbgstat")], sig="dv"))
                toks.append(S.emit("vector", lambda e, ti=ti, off=off: e.tensor_scalar(
                    stat64[:, off + 4:off + 8], trsp[:, 32 + ti * 4: 32 + (ti + 1) * 4], 1.0 / 256.0, 1e-8,
                    OP.mult, OP.max), waits=[ev1], sig="dv"))
            dslice = delta_tm[:, tt * 1024:(tt + 1) * 1024]
            td1 = S.emit("vector", lambda e, dslice=dslice: e.tensor_reduce(
                stat64[:, 60:64], dslice.rearrange("p (h c) -> p h c", h=4), axis=AX, op=OP.add),
                waits=[tp.get("dstat")], sig="dv")
            td2 = S.emit("vector", lambda e: e.tensor_scalar(stat64[:, 36:40], stat64[:, 60:64],
                                                             1.0 / 256.0, None, OP.mult), waits=[td1], sig="dv")
            td3 = S.emit("vector", lambda e, dslice=dslice: e.tensor_tensor(dsq[:, :], dslice, dslice, OP.mult),
                         waits=[td2], sig="dv")
            td4 = S.emit("vector", lambda e: e.tensor_reduce(
                stat64[:, 60:64], dsq[:, :].rearrange("p (h c) -> p h c", h=4), axis=AX, op=OP.add),
                waits=[td3], sig="dv")
            td5 = S.emit("vector", lambda e: e.tensor_scalar(stat64[:, 40:44], stat64[:, 60:64],
                                                             1.0 / 256.0, 1e-8, OP.mult, OP.max), waits=[td4], sig="dv")
            tp["dstat"] = td5
            toks.extend([td1, td2, td3, td4, td5])
            for tile, off in ((sh_tm, 0), (lo_tm, 12), (wd_tm, 24), (delta_tm, 36), (v_tm, 48)):
                toks.append(S.emit("vector", lambda e, tile=tile, off=off, tt=tt: e.tensor_reduce(
                    stat64[:, off + 8:off + 12], tile[:, tt * 1024:(tt + 1) * 1024].rearrange("p (h c) -> p h c", h=4),
                    axis=AX, op=OP.max, apply_absolute_value=True), waits=[tok_fxall], sig="dv"))
            sv5 = stat64[:, 4:64].rearrange("p (g w) -> p g w", w=12)[:, :, 0:4]
            sqs = [S.emit("scalar", lambda e, sv5=sv5: e.activation(sv5, sv5, AF.Sqrt), waits=toks, sig="dc")]
            tp2 = S.emit("tensor", lambda e: e.transpose(PW[2][0:64, 0:128], stat64[:, :], C["ident32"][:, :]),
                         waits=toks + sqs + [tp.get("ps2")], sig="dt")
            tst = S.emit("vector", lambda e, tt=tt: e.tensor_copy(statF[0:64, tt * 128:(tt + 1) * 128],
                                                                  PW[2][0:64, 0:128]), waits=[tp2], sig="dv")
            tp["ps2"] = tst
            tok_stats.append(tst)
            if DEBUG:
                tp["dbgstat"] = S.dma("mg", dbg["dbg_stat"][mb + tt * 128: mb + tt * 128 + 128, :], stat64[:, :],
                                      waits=toks + sqs + [tst])
        AFp.free("stat64", "sums16", "trsp", "dsq")

        if PH < 10: continue
        g1buf = AB.alloc("g1buf", 2 * 9 * 128)
        hdn_b = AB.alloc("hdn_b", 2 * W)
        logit_sp = AFp.alloc("logit_sp", W)
        tok_hdn = {}
        mml = None
        for dcg in range(16):
            buf = g1buf[:, (dcg % 2) * 9 * 128:((dcg % 2) + 1) * 9 * 128]
            twl = S.dma("mg", buf.rearrange("p (c w) -> p c w", c=9),
                        dr["g1Tp"].rearrange("(c p) d -> p c d", p=128)[:, :, dcg * 128:(dcg + 1) * 128],
                        waits=[tok_hdn.get(dcg - 2)]
                        + ([tp.get(("fps", 0)), tp.get(("fps", 1)), tp.get(("fps", 2)), tp.get("sumps")]
                           if dcg < 2 else []))
            mmt = None
            for kc in range(9):
                rhs = xn_fm[:, kc * W:(kc + 1) * W] if kc < 8 else statF[:, 0:512]
                mmt = S.emit("tensor", lambda e, kc=kc, buf=buf, rhs=rhs, dcg=dcg: e.matmul(
                    PW[dcg % 2][:, 0:512], buf[:, kc * 128:(kc + 1) * 128], rhs,
                    start=(kc == 0), stop=(kc == 8)),
                    waits=[twl, tok_stats[-1], tp.get(("g1ps", dcg % 2))], sig="dt")
            ghl = S.emit("scalar", lambda e, dcg=dcg: e.activation(
                hdn_b[:, (dcg % 2) * W:((dcg % 2) + 1) * W], PW[dcg % 2][:, 0:512], AF.Gelu,
                bias=C["g1b"][:, dcg:dcg + 1]), waits=[mmt], sig="dc")
            tok_hdn[dcg] = ghl
            tp[("g1ps", dcg % 2)] = ghl
            mml = S.emit("tensor", lambda e, dcg=dcg: e.matmul(
                PW[2][0:20, 0:512], C["g2T"][:, dcg * 20:(dcg + 1) * 20],
                hdn_b[:, (dcg % 2) * W:((dcg % 2) + 1) * W],
                start=(dcg == 0), stop=(dcg == 15)), waits=[ghl, tp.get("ps2")], sig="dt")
        tok_lg = S.emit("vector", lambda e: e.tensor_copy(logit_sp[0:20, :], PW[2][0:20, 0:512]), waits=[mml], sig="dv")
        tp["ps2"] = tok_lg
        tp["xnfree"] = tok_lg
        AB.free("g1buf", "hdn_b")

        if PH < 11: continue
        o_tm = AB.alloc("o_tm", NT * 1024)
        lg_t = AFp.alloc("lg_t", 128)
        o32 = AFp.alloc("o32", 1024)
        tmpc = AFp.alloc("tmpc", 1024)
        tpxl = []
        for tt in range(NT):
            tpxl.append(S.emit("tensor", lambda e, tt=tt: e.transpose(
                PW[3][:, tt * 128:(tt + 1) * 128], logit_sp[:, tt * 128:(tt + 1) * 128], C["ident32"][:, :]),
                waits=([tok_lg, tp.get("ps3")] if tt == 0 else []), sig="dt"))
        lgv = lg_t[:, 0:80].rearrange("p (t w) -> p t w", t=4)
        lgi = lg_t[:, 0:80].rearrange("p (a i) -> p a i", i=5)
        t1 = S.emit("vector", lambda e: e.tensor_copy(
            lg_t[:, 0:80].rearrange("p (t w) -> p t w", t=4),
            PW[3][:, :].rearrange("p (t w) -> p t w", w=128)[:, :, 0:20]),
            waits=tpxl + [tp.get("dbglg")], sig="dv")
        tp["ps3"] = t1

        def gcv(a):
            return gconst[:, a:a + 20].rearrange("p (x w) -> p x w", x=1).broadcast_to([128, 4, 20])
        t2 = S.emit("vector", lambda e: e.tensor_tensor(lgv, lgv, gcv(0), OP.add), waits=[t1], sig="dv")
        t3 = S.emit("vector", lambda e: e.tensor_tensor(lgv, lgv, gcv(20), OP.mult), waits=[t2], sig="dv")
        t4 = S.emit("vector", lambda e: e.tensor_reduce(lg_t[:, 80:96], lgi, axis=AX, op=OP.max),
                    waits=[t3], sig="dv")
        mxv = lg_t[:, 80:96].broadcast_to([128, 16, 5])
        t5 = S.emit("vector", lambda e: e.tensor_tensor(lgi, lgi, mxv, OP.subtract), waits=[t4], sig="dv")
        t6 = S.emit("scalar", lambda e: e.activation(lg_t[:, 0:80], lg_t[:, 0:80], AF.Exp), waits=[t5], sig="dc")
        t7 = S.emit("vector", lambda e: e.tensor_reduce(lg_t[:, 80:96], lgi, axis=AX, op=OP.add),
                    waits=[t6], sig="dv")
        t8 = S.emit("vector", lambda e: e.reciprocal(lg_t[:, 96:112], lg_t[:, 80:96]), waits=[t7], sig="dv")
        rv = lg_t[:, 96:112].broadcast_to([128, 16, 5])
        t9 = S.emit("vector", lambda e: e.tensor_tensor(lgi, lgi, rv, OP.mult), waits=[t8], sig="dv")
        t10 = S.emit("vector", lambda e: e.tensor_tensor(lgv, lgv, gcv(40), OP.mult), waits=[t9], sig="dv")
        t11 = S.emit("vector", lambda e: e.tensor_tensor(lgv, lgv, gcv(60), OP.add), waits=[t10], sig="dv")
        if DEBUG:
            tp["dbglg"] = S.dma_burst("mg", [(dbg["dbg_logits"][mb + t_ * 128:mb + t_ * 128 + 128, 0:20],
                                              lg_t[:, 20 * t_:20 * t_ + 20], {}) for t_ in range(NT)],
                                      waits=[t11])
        for tt in range(NT):
            tensors = [(sh_tm, 0), (lo_tm, 1), (wd_tm, 2), (delta_tm, 3), (v_tm, 4)]
            tsl = slice(tt * 1024, (tt + 1) * 1024)
            tokc = None
            for ti, (tile, i0) in enumerate(tensors):
                wv_ = lg_t[:, 20 * tt + i0: 20 * tt + i0 + 20].rearrange("p (h i) -> p h i", i=5)[:, :, 0:1].broadcast_to([128, 4, 256])
                if ti == 0:
                    tokc = S.emit("vector", lambda e, tile=tile, wv_=wv_, tsl=tsl: e.tensor_tensor(
                        o32[:, :].rearrange("p (h c) -> p h c", h=4),
                        tile[:, tsl].rearrange("p (h c) -> p h c", h=4), wv_, OP.mult),
                        waits=[t11, tok_fxall, tp.get("o32w")], sig="dv")
                else:
                    tokc = S.emit("vector", lambda e, tile=tile, wv_=wv_, tsl=tsl: e.tensor_tensor(
                        tmpc[:, :].rearrange("p (h c) -> p h c", h=4),
                        tile[:, tsl].rearrange("p (h c) -> p h c", h=4), wv_, OP.mult),
                        waits=[tokc], sig="dv")
                    tokc = S.emit("vector", lambda e: e.tensor_tensor(o32[:, :], o32[:, :], tmpc[:, :], OP.add),
                                  waits=[tokc], sig="dv")
            t12 = S.emit("vector", lambda e: e.tensor_tensor(tmpc[:, :], o32[:, :], o32[:, :], OP.mult),
                         waits=[tokc], sig="dv")
            t13 = S.emit("vector", lambda e: e.tensor_reduce(lg_t[:, 112:116],
                                                             tmpc[:, :].rearrange("p (h c) -> p h c", h=4),
                                                             axis=AX, op=OP.add), waits=[t12], sig="dv")
            t14 = S.emit("vector", lambda e: e.tensor_scalar(lg_t[:, 112:116], lg_t[:, 112:116], 1.0 / 256, 1e-5,
                                                             OP.mult, OP.add), waits=[t13], sig="dv")
            t15 = S.emit("vector", lambda e: e.reciprocal(lg_t[:, 116:120], lg_t[:, 112:116]), waits=[t14], sig="dv")
            t16 = S.emit("scalar", lambda e: e.activation(lg_t[:, 116:120], lg_t[:, 116:120], AF.Sqrt), waits=[t15], sig="dc")
            rsv = lg_t[:, 116:120].broadcast_to([128, 4, 256])
            t17 = S.emit("vector", lambda e: e.tensor_tensor(o32[:, :].rearrange("p (h c) -> p h c", h=4),
                                                             o32[:, :].rearrange("p (h c) -> p h c", h=4),
                                                             rsv, OP.mult), waits=[t16], sig="dv")
            t18 = S.emit("vector", lambda e, tsl=tsl: e.tensor_tensor(o_tm[:, tsl], o32[:, :], C["onorm_rep"][:, :],
                                                                      OP.mult), waits=[t17], sig="dv")
            tp[("otm", tt)] = t18
            tp["o32w"] = t18
            if DEBUG:
                S.dma("mg", dbg["dbg_otm"][mb + tt * 128:mb + tt * 128 + 128, :], o_tm[:, tsl], waits=[t18])
        AFp.free("lg_t", "o32", "tmpc", "logit_sp")

        if PH < 12: continue
        o_fm = AB.alloc("o_fm", 8 * W)
        owb = AB.alloc("owb", 8 * 1024)
        AB.free("sh_tm", "lo_tm", "wd_tm", "delta_tm", "v_tm")
        s_t = AFp.alloc("s_t", 1024)
        towl = S.dma("mg", owb[:].rearrange("p (c w) -> p c w", c=8),
                     dr["owT"].rearrange("(c p) d -> p c d", p=128), waits=[tp[("otm", 0)]])
        tof = {}
        for tt in range(NT):
            for c in range(8):
                tof[(tt, c)] = S.dma("ms", o_fm[:, c * W + tt * 128: c * W + tt * 128 + 128],
                                     o_tm[:, tt * 1024 + c * 128: tt * 1024 + (c + 1) * 128],
                                     waits=[tp[("otm", tt)]], transpose=True)
        for tt in range(NT):
            mms = None
            for kc in range(8):
                for nn in range(2):
                    mms = S.emit("tensor", lambda e, kc=kc, nn=nn, tt=tt: e.matmul(
                        PW[nn][:, 0:512], o_fm[:, kc * W + tt * 128: kc * W + tt * 128 + 128],
                        owb[:, kc * 1024 + nn * 512: kc * 1024 + (nn + 1) * 512],
                        start=(kc == 0), stop=(kc == 7)),
                        waits=[towl, tof[(tt, kc)], tp.get("ps0"), tp.get("ps1")]
                        + ([tp.get(("g1ps", 0)), tp.get(("g1ps", 1))] if tt == 0 else []), sig="dt")
            ev1 = S.emit("vector", lambda e: e.tensor_copy(s_t[:, 0:512], PW[0][:, :]),
                         waits=[mms, tp.get("sdma"), tp.get("o32w")], sig="dv")
            ev2 = S.emit("vector", lambda e: e.tensor_copy(s_t[:, 512:1024], PW[1][:, :]),
                         waits=[mms, tp.get("o32w")], sig="dv")
            tp["ps0"] = ev1
            tp["ps1"] = ev2
            tp["sdma"] = S.dma("mg", sdram[mb + tt * 128: mb + tt * 128 + 128, :], s_t[:, :], waits=[ev1, ev2])
            tp["mttail"] = mms
        tp[("projps", 0)] = tp["ps0"]
        tp[("projps", 1)] = tp["ps1"]
        AB.free("o_fm", "owb", "o_tm", "xq_tm")
        AFp.free("s_t")

    S.barrier()
    AB.free("xn_fm", "qh_fm", "kh_fm", "vf_fm", "S_bf", "tails", "statF", "cd", "g2T", "hc4")
    AFp.free("S32", "onorm_rep", "g1b", "gconst", "ident32", "smallrows")

    if PH < 13:
        return _finish(nc, S, ctx)
    wgq = AB.alloc("wgq", 8 * IFF)
    wuq = AB.alloc("wuq", 8 * IFF)
    wdq = AB.alloc("wdq", 22 * 1024)
    wld = AFp.alloc("wld", 2 * 1024)
    wq_b = AB.alloc("wq_b", 2 * 1024)
    prev_q = {}
    prev_x = {}
    i = 0
    for wnm, dst in (("wg", wgq), ("wu", wuq)):
        for ic2 in range(22):
            p = i % 2
            wldp = wld[:, p * 1024:(p + 1) * 1024]
            wqbp = wq_b[:, p * 1024:(p + 1) * 1024]
            tl = S.dma("mg", wldp, dr[wnm][ic2 * 128:(ic2 + 1) * 128, :], waits=[prev_q.get(p)])
            tqq, _ = emit_quant(wldp, wqbp, 1024, tl, "q2", dst_war=prev_x.get(p))
            prev_q[p] = tqq
            prev_x[p] = [S.dma("ms" if p == 0 else "mh",
                               dst[:, kc * IFF + ic2 * 128: kc * IFF + ic2 * 128 + 128],
                               wqbp[:, kc * 128:(kc + 1) * 128], waits=[tqq], transpose=True)
                         for kc in range(8)]
            i += 1
    AFp.free("wld")
    AB.free("wq_b")
    wld2 = AFp.alloc("wld2", 2 * (IFF // 2))
    wq_b2 = AB.alloc("wq_b2", 2 * IFF)
    prev_q2 = {0: [prev_q[0], prev_q[1]], 1: [prev_q[0], prev_q[1]]}
    prev_x2 = {0: prev_x[0] + prev_x[1], 1: prev_x[0] + prev_x[1]}
    for c8 in range(8):
        p = c8 % 2
        wldp = wld2[:, p * 1408:(p + 1) * 1408]
        wqbp = wq_b2[:, p * IFF:(p + 1) * IFF]
        tqh = []
        for hf in range(2):
            tl = S.dma("mg", wldp, dr["wdn"][c8 * 128:(c8 + 1) * 128, hf * 1408:(hf + 1) * 1408],
                       waits=[prev_q2.get(p)] + (tqh if hf else []))
            tqq, _ = emit_quant(wldp, wqbp[:, hf * 1408:(hf + 1) * 1408], IFF // 2, tl, "q3",
                                dst_war=prev_x2.get(p))
            tqh.append(tqq)
        prev_q2[p] = tqh
        prev_x2[p] = [S.dma("ms" if p == 0 else "mh",
                            wdq[:, icc * 1024 + c8 * 128: icc * 1024 + c8 * 128 + 128],
                            wqbp[:, icc * 128:(icc + 1) * 128], waits=tqh, transpose=True)
                      for icc in range(22)]
    AFp.free("wld2")
    AB.free("wq_b2")
    S.barrier()

    xq_t2 = AB.alloc("xq_t", 3 * 1024)
    hq_t2 = AB.alloc("hq_t", 5 * 512)
    hq_f2 = AB.alloc("hq_f", 5 * 512)
    gup2 = AFp.alloc("gup", 2 * 1024)
    hh2 = AFp.alloc("hh", 2 * 512)
    g_t2 = AFp.alloc("g_t", 2 * 1024)
    xs_t2 = AFp.alloc("xs_t", 2048)
    IC_OF = [0, 512, 1024, 1536, 2048, 2560]
    IC_SZ = [512, 512, 512, 512, 512, 256]
    GU = [(PW[0], PW[1]), (PS_S[0], PS_S[1])]
    DN = [(PW[2], PW[3]), (PS_S[2], PS_S[3])]

    def emit_dn(pend):
        tt, ici, q4, tok_hqf = pend
        pb = tt % 2
        bd0, bd1 = DN[pb]
        hq_f = hq_f2[:, q4 * 512:(q4 + 1) * 512]
        iof, isz = IC_OF[ici], IC_SZ[ici]
        mmd = None
        for j in range(isz // 128):
            icc = iof // 128 + j
            for nn in range(2):
                mmd = S.emit("tensor", lambda e, j=j, icc=icc, nn=nn, hq_f=hq_f, bd=(bd0, bd1)[nn]: e.matmul(
                    bd[:, 0:512], hq_f[:, j * 128:(j + 1) * 128],
                    wdq[:, icc * 1024 + nn * 512: icc * 1024 + (nn + 1) * 512],
                    start=(icc == 0), stop=(icc == 21)),
                    waits=([tok_hqf, tp.get(("dnev", pb))] if (j == 0 and ici == 0) else [tok_hqf]), sig="dt")
        tp[("dnmm", q4)] = mmd
        return mmd

    pend_dn = []
    for tt in range(N_FFN_TILES):
        r0 = tt * 128
        pb = tt % 2
        xq_t = xq_t2[:, (tt % 3) * 1024:(tt % 3 + 1) * 1024]
        g_t = g_t2[:, pb * 1024:(pb + 1) * 1024]
        hh = hh2[:, pb * 512:(pb + 1) * 512]
        xs_t = xs_t2
        wprev = tp.get(("ffn", tt - 2))
        txq = [S.dma("mh", xq_t[:, c * 128:(c + 1) * 128], xqdram[r0:r0 + 128, c * 128:(c + 1) * 128],
                     waits=[tp.get(("ffn", tt - 3))], transpose=True) for c in range(8)]
        for ici in range(6):
            iof, isz = IC_OF[ici], IC_SZ[ici]
            p2 = ici % 2
            q4 = (tt * 6 + ici) % 5
            bg, bu = GU[p2]
            gup = gup2[:, p2 * 1024:(p2 + 1) * 1024]
            hq_t = hq_t2[:, q4 * 512:(q4 + 1) * 512]
            hq_f = hq_f2[:, q4 * 512:(q4 + 1) * 512]
            mmg = None
            for kc in range(8):
                mmg = S.emit("tensor", lambda e, kc=kc, iof=iof, isz=isz, xq_t=xq_t, bg=bg: e.matmul(
                    bg[:, 0:isz], xq_t[:, kc * 128:(kc + 1) * 128], wgq[:, kc * IFF + iof: kc * IFF + iof + isz],
                    start=(kc == 0), stop=(kc == 7)),
                    waits=([txq, tp.get(("guev", p2))] if kc == 0 else []), sig="dt")
                mmg = S.emit("tensor", lambda e, kc=kc, iof=iof, isz=isz, xq_t=xq_t, bu=bu: e.matmul(
                    bu[:, 0:isz], xq_t[:, kc * 128:(kc + 1) * 128], wuq[:, kc * IFF + iof: kc * IFF + iof + isz],
                    start=(kc == 0), stop=(kc == 7)), sig="dt")
            if len(pend_dn) >= 3:
                emit_dn(pend_dn.pop(0))
            tsg = S.emit("scalar", lambda e, isz=isz, gup=gup, bg=bg: e.activation(gup[:, 0:isz], bg[:, 0:isz], AF.Silu),
                         waits=[mmg, tp.get(("guth", p2))], sig="dc")
            tup = S.emit("vector", lambda e, isz=isz, gup=gup, bu=bu: e.tensor_copy(gup[:, 512:512 + isz], bu[:, 0:isz]),
                         waits=[mmg, tp.get(("guth", p2))], sig="dv")
            tp[("guev", p2)] = [tsg, tup]
            th = S.emit("vector", lambda e, isz=isz, gup=gup, hh=hh: e.tensor_tensor(
                hh[:, 0:isz], gup[:, 0:isz], gup[:, 512:512 + isz], OP.mult),
                waits=[tsg, tup, tp.get(("dbgh", pb))], sig="dv")
            tp[("guth", p2)] = th
            thq, _ = emit_quant(hh[:, 0:isz], hq_t[:, 0:isz], isz, th, "q4", fast=True,
                                dst_war=[t for t in [tp.get(("dbghq", q4))] if t] + (tp.get(("hqx", q4)) or []))
            if DEBUG:
                tp[("dbgh", pb)] = S.dma("mg", dbg["dbg_h"][r0:r0 + 128, iof:iof + isz], hh[:, 0:isz],
                                         waits=[th, thq])
                tp[("dbghq", q4)] = S.dma("mg", dbg["dbg_hq"][r0:r0 + 128, iof:iof + isz], hq_t[:, 0:isz],
                                          waits=[thq])
            tok_hqf = [S.dma("ms", hq_f[:, j * 128:(j + 1) * 128], hq_t[:, j * 128:(j + 1) * 128],
                             waits=[thq, tp.get(("dnmm", q4))], transpose=True)
                       for j in range(isz // 128)]
            tp[("hqx", q4)] = tok_hqf
            pend_dn.append((tt, ici, q4, tok_hqf))
        for p_ in pend_dn:
            mmd = emit_dn(p_)
        pend_dn = []
        bd0, bd1 = DN[pb]
        tg1 = S.emit("scalar", lambda e, g_t=g_t, bd0=bd0: e.activation(g_t[:, 0:512], bd0[:, 0:512], AF.Silu),
                     waits=[mmd, wprev], sig="dc")
        tg2 = S.emit("scalar", lambda e, g_t=g_t, bd1=bd1: e.activation(g_t[:, 512:1024], bd1[:, 0:512], AF.Silu),
                     waits=[mmd, wprev], sig="dc")
        tp[("dnev", pb)] = [tg1, tg2]
        txl = S.dma("mg", xs_t[:, 0:1024], dr["x"][r0:r0 + 128, :], waits=[wprev, tp.get("xsfree")])
        tsl = S.dma("mg", xs_t[:, 1024:2048], sdram[r0:r0 + 128, :], waits=[wprev, tp.get("xsfree")])
        to1 = S.emit("vector", lambda e, g_t=g_t, xs_t=xs_t: e.tensor_tensor(
            g_t[:, :], xs_t[:, 1024:2048], g_t[:, :], OP.mult), waits=[tg1, tg2, tsl], sig="dv")
        to2 = S.emit("vector", lambda e, g_t=g_t, xs_t=xs_t: e.tensor_tensor(
            g_t[:, :], xs_t[:, 0:1024], g_t[:, :], OP.add), waits=[to1, txl], sig="dv")
        tp["xsfree"] = to2
        tp[("ffn", tt)] = S.dma("mg", y[r0:r0 + 128, :], g_t[:, :], waits=[to2])
    S.barrier()

    return _finish(nc, S, ctx)


def _finish(nc, S, ctx):
    sems = {}
    for s in ALLSEMS:
        t = nc.semaphore(s)
        h_ = t.__enter__()
        ctx.append(t)
        sems[s] = h_
    blk = nc.Block()
    b = blk.__enter__()
    ctx.append(blk)

    @b.tensor
    def _(eng):
        S.run("tensor", eng, sems)

    @b.vector
    def _(eng):
        S.run("vector", eng, sems)

    @b.scalar
    def _(eng):
        S.run("scalar", eng, sems)

    @b.gpsimd
    def _(eng):
        S.run("gpsimd", eng, sems)

    @b.sync
    def _(eng):
        S.run("sync", eng, sems)

    blk.__exit__(None, None, None)
    ctx.pop()
    return nc, ctx


_CACHE = {}


def kernel(**inputs):
    x_full = np.asarray(inputs["L0_attn_hidden_states"], np.float32)
    host, groups = prep_host(inputs)
    if "nc" not in _CACHE:
        nc, ctx = build(groups)
        _CACHE["nc"] = nc
        _CACHE["ctx"] = ctx
    nc = _CACHE["nc"]
    in_maps = []
    for core in range(8):
        m = dict(host)
        m["x"] = np.ascontiguousarray(x_full[core])
        in_maps.append(m)
    res = run_bass_kernel_spmd(nc, in_maps, core_ids=list(range(8)))
    return np.stack([np.asarray(r["y"], np.float32) for r in res.results], axis=0)
